# version 02
# baseline (speedup 1.0000x reference)
import contextlib
import numpy as np
import concourse.bass as bass
import concourse.mybir as mybir
from concourse.bass_utils import run_bass_kernel_spmd

F32 = mybir.dt.float32
BF16 = mybir.dt.bfloat16
I32 = mybir.dt.int32
AF = mybir.ActivationFunctionType
ALU = mybir.AluOpType
AX = mybir.AxisListType.X

D = 1024
NDT = 8
TFULL = 4096
HT = 2048
NTT = HT // 128


class _AutoProxy:

    def __init__(self, eng):
        object.__setattr__(self, "_e", eng)

    def __getattr__(self, name):
        e = self._e
        f = getattr(e.raw, name)
        if name in ("wait_ge",) or not callable(f):
            return f

        def wrapped(*a, **k):
            if e.n > 0:
                e.raw.wait_ge(e.sem, e.n)
            r = f(*a, **k)
            if hasattr(r, "then_inc"):
                r.then_inc(e.sem, 1)
                e.n += 1
            return r
        return wrapped


class _DmaProxy:

    def __init__(self, eng):
        object.__setattr__(self, "_e", eng)

    def __getattr__(self, name):
        e = self._e
        f = getattr(e.raw, name)
        if name != "dma_start" or not callable(f):
            return f

        def wrapped(*a, **k):
            if e.n > 0:
                e.raw.wait_ge(e.sem, e.n)
            r = f(*a, **k)
            r.then_inc(e.sem, 16)
            e.n += 16
            return r
        return wrapped


class Eng:
    def __init__(self, prog, handle, name, step=1, auto=False, dma=False):
        self.p = prog
        self.raw = handle
        self.name = name
        self.step = step
        self.auto = auto
        self.sem = prog.es.enter_context(prog.nc.semaphore("sem_" + name))
        self.n = 0
        self._waited = {}
        if auto:
            self.h = _AutoProxy(self)
        elif dma:
            self.h = _DmaProxy(self)
        else:
            self.h = handle

    def inc(self, instr, k=None):
        if self.auto or isinstance(self.h, _DmaProxy):
            return self.n
        k = self.step if k is None else k
        instr.then_inc(self.sem, k)
        self.n += k
        return self.n


class Prog:
    def __init__(self):
        self.nc = bass.Bass("TRN2", target_bir_lowering=False)
        self.es = contextlib.ExitStack()
        nc = self.nc
        self.pe = Eng(self, nc.tensor, "pe")
        self.act = Eng(self, nc.scalar, "act", auto=True)
        self.dve = Eng(self, nc.vector, "dve", auto=True)
        self.qs = Eng(self, nc.sync, "qs", step=16, dma=True)
        self.qg = Eng(self, nc.gpsimd, "qg", step=16, dma=True)
        self.engines = [self.pe, self.act, self.dve, self.qs, self.qg]

    def wait(self, consumer, producer, n=None):
        tgt = producer.n if n is None else n
        if tgt <= 0:
            return
        key = consumer.name
        last = producer._waited.get(key, -1)
        if tgt <= last:
            return
        consumer.h.wait_ge(producer.sem, tgt)
        producer._waited[key] = tgt

    def push(self):
        if not hasattr(self, "_stacks"):
            self._stacks = []
        self._stacks.append(contextlib.ExitStack())

    def pop(self):
        self._stacks.pop().close()

    def _stack(self):
        st = getattr(self, "_stacks", None)
        return st[-1] if st else self.es

    def barrier(self):
        for c in self.engines:
            for pr in self.engines:
                if c is not pr:
                    self.wait(c, pr)

    def sb(self, name, shape, dt):
        name = f"{name}_{self._uid()}"
        return self._stack().enter_context(self.nc.sbuf_tensor(name, list(shape), dt))

    def ps(self, name, shape, dt=F32):
        name = f"{name}_{self._uid()}"
        return self._stack().enter_context(self.nc.psum_tensor(name, list(shape), dt))

    def _uid(self):
        self._n = getattr(self, "_n", 0) + 1
        return self._n

    def din(self, name, shape, dt=F32):
        return self.nc.dram_tensor(name, list(shape), dt, kind="ExternalInput").ap()

    def dout(self, name, shape, dt=F32):
        return self.nc.dram_tensor(name, list(shape), dt, kind="ExternalOutput").ap()

    def dint(self, name, shape, dt=F32):
        return self.nc.dram_tensor(name, list(shape), dt)

    def finish(self):
        for e in self.engines[1:]:
            self.wait(self.engines[0], e)
        self.es.close()
        return self.nc


class Pool8:

    def __init__(self, p, n=8):
        self.p = p
        self.t = [p.ps(f"bank{i}", [128, 512]) for i in range(n)]
        self.guard = [None] * n
        self.i = 0

    def get(self):
        i = self.i % len(self.t)
        self.i += 1
        g = self.guard[i]
        if g is not None:
            self.p.wait(self.p.pe, g[0], g[1])
        return i, self.t[i]

    def consumed(self, i, eng):
        self.guard[i] = (eng, eng.n)


def fm3(dram_ap, p_=128):
    return dram_ap.rearrange("(dt p) t -> p dt t", p=p_)


def sb3(sbuf, ndt, T):
    return sbuf[:, 0:ndt * T].rearrange("p (dt t) -> p dt t", t=T)


def fm_load(p, q, sbuf, dram_ap, Dfull, T, cast=False):
    nd = Dfull // 128
    q.inc(q.h.dma_start(out=sb3(sbuf, nd, T), in_=fm3(dram_ap)))


def fm_store(p, q, dram_ap, sbuf, Dfull, T):
    nd = Dfull // 128
    q.inc(q.h.dma_start(out=fm3(dram_ap), in_=sb3(sbuf, nd, T)))


def row_stats(p, banks, src32, sq_sc, onesc, rows_s, rows_q, T, ndt=NDT, HTl=HT, t0=0):
    pe, act, dve = p.pe, p.act, p.dve
    nch = (T + 511) // 512
    for c in range(nch):
        c0 = c * 512
        cw = min(512, T - c0)
        p.wait(p.act, pe, pe.n)
        for dt in range(ndt):
            last = act.h.activation(sq_sc[:, dt * 512: dt * 512 + cw],
                                    src32[:, dt * HTl + t0 + c0: dt * HTl + t0 + c0 + cw], AF.Copy)
        act.inc(last)
        p.wait(pe, act, act.n)
        bi1, b1 = banks.get()
        for dt in range(ndt):
            last = pe.h.matmul(b1[0:1, 0:cw], onesc[:, 0:1],
                               sq_sc[:, dt * 512: dt * 512 + cw],
                               start=(dt == 0), stop=(dt == ndt - 1))
        pe.inc(last)
        p.wait(act, pe, pe.n)
        for dt in range(ndt):
            last = act.h.activation(sq_sc[:, dt * 512: dt * 512 + cw],
                                    sq_sc[:, dt * 512: dt * 512 + cw], AF.Square)
        act.inc(last)
        p.wait(pe, act, act.n)
        bi2, b2 = banks.get()
        for dt in range(ndt):
            last = pe.h.matmul(b2[0:1, 0:cw], onesc[:, 0:1],
                               sq_sc[:, dt * 512: dt * 512 + cw],
                               start=(dt == 0), stop=(dt == ndt - 1))
        pe.inc(last)
        p.wait(dve, pe, pe.n)
        dve.h.tensor_copy(rows_s[0:1, c0:c0 + cw], b1[0:1, 0:cw])
        dve.inc(dve.h.tensor_copy(rows_q[0:1, c0:c0 + cw], b2[0:1, 0:cw]))
        banks.consumed(bi1, dve)
        banks.consumed(bi2, dve)


def ln_finalize_rows(p, rows_s, rows_q, tmp_row, T, nd, eps):
    act, dve = p.act, p.dve
    dve.h.tensor_scalar_mul(rows_s[0:1, 0:T], rows_s[0:1, 0:T], 1.0 / nd)
    dve.h.tensor_scalar_mul(rows_q[0:1, 0:T], rows_q[0:1, 0:T], 1.0 / nd)
    dve.h.tensor_tensor(tmp_row[0:1, 0:T], rows_s[0:1, 0:T], rows_s[0:1, 0:T], ALU.mult)
    dve.inc(dve.h.tensor_tensor(rows_q[0:1, 0:T], rows_q[0:1, 0:T], tmp_row[0:1, 0:T], ALU.subtract))
    dve.inc(dve.h.tensor_scalar_add(rows_q[0:1, 0:T], rows_q[0:1, 0:T], float(eps)))
    p.wait(act, dve, dve.n)
    act.inc(act.h.activation(rows_q[0:1, 0:T], rows_q[0:1, 0:T], AF.Sqrt))
    p.wait(dve, act, act.n)
    dve.h.reciprocal(rows_q[0:1, 0:T], rows_q[0:1, 0:T])
    dve.h.tensor_tensor(rows_s[0:1, 0:T], rows_s[0:1, 0:T], rows_q[0:1, 0:T], ALU.mult)
    dve.inc(dve.h.tensor_scalar_mul(rows_s[0:1, 0:T], rows_s[0:1, 0:T], -1.0))


def emit_k6(p, IO):
    pe, act, dve, qs, qg = p.pe, p.act, p.dve, p.qs, p.qg
    x_d = IO["x"]; wr_d = IO["wr_t"]; br_d = IO["br_r"]; w1_d = IO["w1t"]
    b1_d = IO["b1c"]; w2_d = IO["w2t"]; b2_d = IO["b2r"]; lnw_d = IO["lnw_c"]
    lnb_d = IO["lnb_c"]; id_d = IO["ident"]; y_d = IO["y"]

    FC = 1024
    NFC = 4096 // FC
    NFT = FC // 128

    acc = p.sb("acc", [128, NDT * HT], F32)
    x16 = p.sb("x16", [128, NDT * HT], BF16)
    w1c = p.sb("w1c", [128, NDT * FC], BF16)
    w2c = p.sb("w2c", [128, NFT * D], BF16)
    hbuf = p.sb("hbuf", [128, NFT * HT], BF16)
    wr_s = p.sb("wr_s", [128, NDT * 8], BF16)
    br_s = p.sb("br_s", [1, 8], BF16)
    id32 = p.sb("id32", [128, 128], F32)
    b1_s = p.sb("b1_s", [128, 32], F32)
    b2_s = p.sb("b2_s", [1, D], BF16)
    lnw_s = p.sb("lnw_s", [128, 8], F32)
    lnb_s = p.sb("lnb_s", [128, 8], F32)
    gts = p.sb("gts", [128, NTT * 8], F32)
    logit = p.sb("logit", [128, NTT * 8], F32)
    wrow = p.sb("wrow", [1, HT], BF16)
    wbc = p.sb("wbc", [128, HT], BF16)
    ones1 = p.sb("ones1", [1, 128], BF16)
    ones1f = p.sb("ones1f", [1, 128], F32)
    onesc = p.sb("onesc", [128, 1], BF16)
    rows_s = p.sb("rows_s", [1, 1024], F32)
    rows_q = p.sb("rows_q", [1, 1024], F32)
    tmp_row = p.sb("tmp_row", [1, 1024], F32)
    sq_sc = p.sb("sq_sc", [128, NDT * 512], BF16)
    x3c = p.sb("x3c", [128, 4 * 512], F32)
    tsc = p.sb("tsc", [128, 512], F32)
    tsc2 = p.sb("tsc2", [128, 512], F32)

    banks = Pool8(p)

    fm_load(p, qs, acc, x_d, D, HT)
    qg.inc(qg.h.dma_start(out=sb3(x16, NDT, HT), in_=fm3(x_d)))
    qg.inc(qg.h.dma_start(out=sb3(wr_s, NDT, 8), in_=fm3(wr_d)))
    qg.inc(qg.h.dma_start(out=br_s[:, :], in_=br_d))
    qs.inc(qs.h.dma_start(out=id32[:, :], in_=id_d))
    qs.inc(qs.h.dma_start(out=lnw_s[:, :], in_=lnw_d))
    qs.inc(qs.h.dma_start(out=lnb_s[:, :], in_=lnb_d))
    dve.h.memset(ones1[:, :], 1.0)
    dve.h.memset(ones1f[:, :], 1.0)
    dve.inc(dve.h.memset(onesc[:, :], 1.0))

    p.wait(pe, qg, qg.n)
    p.wait(pe, qs, qs.n)
    p.wait(dve, qs, qs.n)
    p.wait(act, qs, qs.n)
    p.wait(pe, dve, dve.n)
    bi, bank = banks.get()
    for tt in range(NTT):
        for dt in range(NDT):
            pe.h.matmul(bank[:, tt * 8:(tt + 1) * 8],
                        x16[:, dt * HT + tt * 128: dt * HT + (tt + 1) * 128],
                        wr_s[:, dt * 8:(dt + 1) * 8],
                        start=(dt == 0), stop=False)
        last = pe.h.matmul(bank[:, tt * 8:(tt + 1) * 8], ones1[:, :], br_s[:, :],
                           start=False, stop=True)
    pe.inc(last)
    p.wait(dve, pe, pe.n)
    dve.inc(dve.h.tensor_copy(logit[:, :], bank[:, 0:NTT * 8]))
    banks.consumed(bi, dve)

    m1 = p.sb("m1", [128, NTT], F32)
    nm1 = p.sb("nm1", [128, NTT], F32)
    m2 = p.sb("m2", [128, NTT], F32)
    zs = p.sb("zs", [128, NTT], F32)
    tg = p.sb("tg", [128, 8], F32)
    for tt in range(NTT):
        l8 = logit[:, tt * 8:(tt + 1) * 8]
        g8 = gts[:, tt * 8:(tt + 1) * 8]
        p.wait(dve, act, act.n)
        dve.h.reduce_max(m1[:, tt:tt + 1], l8, axis=AX)
        dve.h.reduce_max(nm1[:, tt:tt + 1], l8, axis=AX, negate=True)
        dve.h.tensor_scalar(tg[:, :], l8, m1[:, tt:tt + 1], None, ALU.is_ge)
        dve.h.scalar_tensor_tensor(tg[:, :], tg[:, :], -1e30, l8, ALU.mult, ALU.add)
        dve.h.reduce_max(m2[:, tt:tt + 1], tg[:, :], axis=AX)
        dve.inc(dve.h.tensor_scalar(g8, l8, m2[:, tt:tt + 1], None, ALU.is_ge))
        p.wait(act, dve, dve.n)
        act.inc(act.h.activation(tg[:, :], l8, AF.Exp, bias=nm1[:, tt:tt + 1]))
        p.wait(dve, act, act.n)
        dve.h.tensor_tensor(g8, g8, tg[:, :], ALU.mult)
        dve.h.reduce_sum(zs[:, tt:tt + 1], g8, axis=AX)
        dve.h.reciprocal(zs[:, tt:tt + 1], zs[:, tt:tt + 1])
        dve.inc(dve.h.tensor_scalar_mul(g8, g8, zs[:, tt:tt + 1]))

    for e in range(8):
        p.wait(pe, dve, dve.n)
        wr_banks = []
        for c in range(4):
            bi, bank = banks.get()
            for j in range(4):
                tt = c * 4 + j
                last = pe.h.matmul(bank[0:1, j * 128:(j + 1) * 128],
                                   gts[:, tt * 8 + e: tt * 8 + e + 1], id32[:, :],
                                   start=True, stop=True)
            wr_banks.append((bi, bank, c))
        pe.inc(last)
        p.wait(dve, pe, pe.n)
        for bi, bank, c in wr_banks:
            last = dve.h.tensor_copy(wrow[0:1, c * 512:(c + 1) * 512], bank[0:1, 0:512])
        dve.inc(last)
        for bi, bank, c in wr_banks:
            banks.consumed(bi, dve)
        p.wait(pe, dve, dve.n)
        wb_banks = []
        for c in range(4):
            bi, bank = banks.get()
            last = pe.h.matmul(bank[:, :], ones1[:, :], wrow[0:1, c * 512:(c + 1) * 512],
                               start=True, stop=True)
            wb_banks.append((bi, bank, c))
        pe.inc(last)
        p.wait(dve, pe, pe.n)
        for bi, bank, c in wb_banks:
            last = dve.h.tensor_copy(wbc[:, c * 512:(c + 1) * 512], bank[:, :])
        dve.inc(last)
        for bi, bank, c in wb_banks:
            banks.consumed(bi, dve)

        for fc in range(NFC):
            p.wait(qg, pe, pe.n)
            p.wait(qg, act, act.n)
            qg.inc(qg.h.dma_start(
                out=sb3(w1c, NDT, FC),
                in_=fm3(w1_d[e, :, fc * FC:(fc + 1) * FC])))
            qg.inc(qg.h.dma_start(
                out=sb3(w2c, NFT, D),
                in_=fm3(w2_d[e, fc * FC:(fc + 1) * FC, :])))
            qg.inc(qg.h.dma_start(out=b1_s[:, 0:NFT],
                                  in_=b1_d[e, :, fc * NFT:(fc + 1) * NFT]))
            if fc == NFC - 1:
                qg.inc(qg.h.dma_start(out=b2_s[:, :], in_=b2_d[e, 0:1, :]))
            p.wait(pe, qg, qg.n)
            p.wait(pe, dve, dve.n)
            for ft in range(NFT):
                for tb in range(4):
                    bi, bank = banks.get()
                    for dt in range(NDT):
                        last = pe.h.matmul(bank[:, :],
                                    w1c[:, dt * FC + ft * 128: dt * FC + (ft + 1) * 128],
                                    x16[:, dt * HT + tb * 512: dt * HT + (tb + 1) * 512],
                                    start=(dt == 0), stop=(dt == NDT - 1))
                    pe.inc(last)
                    p.wait(act, pe, pe.n)
                    act.inc(act.h.activation(
                        hbuf[:, ft * HT + tb * 512: ft * HT + (tb + 1) * 512],
                        bank[:, :], AF.Gelu, bias=b1_s[:, ft:ft + 1]))
                    banks.consumed(bi, act)
            p.wait(dve, act, act.n)
            for ft in range(NFT):
                last = dve.h.tensor_tensor(hbuf[:, ft * HT:(ft + 1) * HT],
                                    hbuf[:, ft * HT:(ft + 1) * HT],
                                    wbc[:, :], ALU.mult)
            dve.inc(last)
            p.wait(pe, dve, dve.n)
            for mt in range(NDT):
                for tb in range(4):
                    bi, bank = banks.get()
                    last_real = (fc != NFC - 1)
                    for ft in range(NFT):
                        last = pe.h.matmul(bank[:, :],
                                    w2c[:, ft * D + mt * 128: ft * D + (mt + 1) * 128],
                                    hbuf[:, ft * HT + tb * 512: ft * HT + (tb + 1) * 512],
                                    start=(ft == 0),
                                    stop=(last_real and ft == NFT - 1))
                    if fc == NFC - 1:
                        last = pe.h.matmul(bank[:, :], b2_s[0:1, mt * 128:(mt + 1) * 128],
                                    wrow[0:1, tb * 512:(tb + 1) * 512],
                                    start=False, stop=True)
                    pe.inc(last)
                    p.wait(dve, pe, pe.n)
                    sl = acc[:, mt * HT + tb * 512: mt * HT + (tb + 1) * 512]
                    dve.inc(dve.h.tensor_tensor(sl, sl, bank[:, :], ALU.add))
                    banks.consumed(bi, dve)

    for ln_i in range(2):
        for hf in range(2):
            t0 = hf * 1024
            p.wait(act, dve, dve.n)
            row_stats(p, banks, acc, sq_sc, onesc, rows_s, rows_q, 1024, t0=t0)
            p.wait(dve, pe, pe.n)
            eps = 1e-12 if ln_i == 0 else 1e-5
            ln_finalize_rows(p, rows_s, rows_q, tmp_row, 1024, float(D), eps)
            for c in range(2):
                c0 = t0 + c * 512
                r0 = c * 512
                p.wait(pe, dve, dve.n)
                bi1, bankR = banks.get()
                pe.h.matmul(bankR[:, :], ones1f[:, :], rows_q[0:1, r0:r0 + 512], start=True, stop=True)
                bi2, bankM = banks.get()
                last = pe.h.matmul(bankM[:, :], ones1f[:, :], rows_s[0:1, r0:r0 + 512], start=True, stop=True)
                pe.inc(last)
                p.wait(dve, pe, pe.n)
                if ln_i == 0:
                    p.wait(dve, qs, qs.n)
                for dt in range(NDT):
                    if ln_i == 0 and dt % 4 == 0:
                        p.wait(qs, dve, dve.n)
                        qs.inc(qs.h.dma_start(
                            out=sb3(x3c, 4, 512),
                            in_=fm3(x_d[dt * 128:(dt + 4) * 128, c0:c0 + 512])))
                        p.wait(dve, qs, qs.n)
                    sl = acc[:, dt * HT + c0: dt * HT + c0 + 512]
                    dve.h.tensor_tensor(tsc[:, :], sl, bankR[:, :], ALU.mult)
                    dve.inc(dve.h.tensor_tensor(tsc[:, :], tsc[:, :], bankM[:, :], ALU.add))
                    if ln_i == 0:
                        dve.h.tensor_scalar(tsc2[:, :], tsc[:, :],
                                            lnw_s[:, dt:dt + 1], lnb_s[:, dt:dt + 1],
                                            ALU.mult, ALU.add)
                        dve.inc(dve.h.tensor_tensor(sl, tsc2[:, :],
                                                    x3c[:, (dt % 4) * 512:(dt % 4 + 1) * 512], ALU.add))
                    else:
                        dve.inc(dve.h.tensor_copy(sl, tsc[:, :]))
                banks.consumed(bi1, dve)
                banks.consumed(bi2, dve)

    p.wait(qs, dve, dve.n)
    fm_store(p, qs, y_d, acc, D, HT)


def build_k6():
    p = Prog()
    IO = {n: p.din(n, s) for n, s in [("x", [D, HT]), ("wr_t", [D, 8]), ("br_r", [1, 8]),
          ("w1t", [8, D, 4096]), ("b1c", [8, 128, 32]), ("w2t", [8, 4096, D]),
          ("b2r", [8, 1, D]), ("lnw_c", [128, 8]), ("lnb_c", [128, 8]),
          ("ident", [128, 128])]}
    IO["y"] = p.dout("y", [D, HT])
    p.push(); emit_k6(p, IO); p.pop()
    return p.finish()



def k6_inputs(inputs, x3_full):
    c = np.ascontiguousarray
    wr_t = c(inputs["L1_ffn_Wr"].T)
    br_r = c(inputs["L1_ffn_br"].reshape(1, 8))
    w1t = c(inputs["L1_ffn_W1"].transpose(0, 2, 1))
    b1c = c(inputs["L1_ffn_b1"].reshape(8, 32, 128).transpose(0, 2, 1))
    w2t = c(inputs["L1_ffn_W2"].transpose(0, 2, 1))
    b2r = c(inputs["L1_ffn_b2"].reshape(8, 1, 1024))
    lnw_c = c(inputs["L1_ffn_ln_w"].reshape(8, 128).T)
    lnb_c = c(inputs["L1_ffn_ln_b"].reshape(8, 128).T)
    ident = np.eye(128, dtype=np.float32)
    maps = []
    for core in range(8):
        b, h = core // 2, core % 2
        x = c(x3_full[b, h * HT:(h + 1) * HT, :].T)
        maps.append(dict(x=x, wr_t=wr_t, br_r=br_r, w1t=w1t, b1c=b1c,
                         w2t=w2t, b2r=b2r, lnw_c=lnw_c, lnb_c=lnb_c, ident=ident))
    return maps


def k6_collect(res):
    out = np.zeros((4, TFULL, D), dtype=np.float32)
    for core in range(8):
        b, h = core // 2, core % 2
        out[b, h * HT:(h + 1) * HT, :] = res.results[core]["y"].T
    return out


def plain_ln_to_acc(p, banks, acc, sq_sc, onesc, ones1f, rows_s, rows_q, tmp_row,
                    tsc, eps, per_dt=None, x_extra=None, x3c=None, HTl=HT):
    pe, act, dve, qs = p.pe, p.act, p.dve, p.qs
    for hf in range(HTl // 1024):
        t0 = hf * 1024
        p.wait(act, dve, dve.n)
        row_stats(p, banks, acc, sq_sc, onesc, rows_s, rows_q, 1024, t0=t0, HTl=HTl)
        p.wait(dve, pe, pe.n)
        ln_finalize_rows(p, rows_s, rows_q, tmp_row, 1024, float(D), eps)
        for c in range(2):
            c0 = t0 + c * 512
            r0 = c * 512
            p.wait(pe, dve, dve.n)
            bi1, bankR = banks.get()
            pe.h.matmul(bankR[:, :], ones1f[:, :], rows_q[0:1, r0:r0 + 512], start=True, stop=True)
            bi2, bankM = banks.get()
            last = pe.h.matmul(bankM[:, :], ones1f[:, :], rows_s[0:1, r0:r0 + 512], start=True, stop=True)
            pe.inc(last)
            p.wait(dve, pe, pe.n)
            for dt in range(NDT):
                sl = acc[:, dt * HTl + c0: dt * HTl + c0 + 512]
                dve.h.tensor_tensor(tsc[:, :], sl, bankR[:, :], ALU.mult)
                dve.h.tensor_tensor(tsc[:, :], tsc[:, :], bankM[:, :], ALU.add)
                if per_dt is None:
                    dve.h.tensor_copy(sl, tsc[:, :])
                else:
                    per_dt(dt, c0, tsc, sl)
            banks.consumed(bi1, dve)
            banks.consumed(bi2, dve)


def add_dram_fm(p, banks, acc, dram_ap, x3c, HTl=HT):
    dve, qs = p.dve, p.qs
    for c in range(HTl // 512):
        c0 = c * 512
        for half in range(2):
            dt0 = half * 4
            p.wait(qs, dve, dve.n)
            qs.inc(qs.h.dma_start(
                out=sb3(x3c, 4, 512),
                in_=fm3(dram_ap[dt0 * 128:(dt0 + 4) * 128, c0:c0 + 512])))
            p.wait(dve, qs, qs.n)
            for dt in range(dt0, dt0 + 4):
                sl = acc[:, dt * HTl + c0: dt * HTl + c0 + 512]
                dve.h.tensor_tensor(sl, sl, x3c[:, (dt % 4) * 512:(dt % 4 + 1) * 512], ALU.add)


def topk2_gates(p, banks, x16, w_s, b_row, gts, logit, scratch, HTl=HT, nbias=True):
    pe, act, dve = p.pe, p.act, p.dve
    ones1, m1, nm1, m2, zs, tg = scratch
    ntt = HTl // 128
    bi, bank = banks.get()
    for tt in range(ntt):
        lastmm = None
        for dt in range(NDT):
            lastmm = pe.h.matmul(bank[:, tt * 8:(tt + 1) * 8],
                        x16[:, dt * HTl + tt * 128: dt * HTl + (tt + 1) * 128],
                        w_s[:, dt * 8:(dt + 1) * 8],
                        start=(dt == 0), stop=(dt == NDT - 1 and not nbias))
        if nbias:
            lastmm = pe.h.matmul(bank[:, tt * 8:(tt + 1) * 8], ones1[:, :], b_row[:, :],
                                 start=False, stop=True)
    pe.inc(lastmm)
    p.wait(dve, pe, pe.n)
    dve.h.tensor_copy(logit[:, 0:ntt * 8], bank[:, 0:ntt * 8])
    banks.consumed(bi, dve)
    for tt in range(ntt):
        l8 = logit[:, tt * 8:(tt + 1) * 8]
        g8 = gts[:, tt * 8:(tt + 1) * 8]
        p.wait(dve, act, act.n)
        dve.h.reduce_max(m1[:, tt:tt + 1], l8, axis=AX)
        dve.h.reduce_max(nm1[:, tt:tt + 1], l8, axis=AX, negate=True)
        dve.h.tensor_scalar(tg[:, :], l8, m1[:, tt:tt + 1], None, ALU.is_ge)
        dve.h.scalar_tensor_tensor(tg[:, :], tg[:, :], -1e30, l8, ALU.mult, ALU.add)
        dve.h.reduce_max(m2[:, tt:tt + 1], tg[:, :], axis=AX)
        dve.h.tensor_scalar(g8, l8, m2[:, tt:tt + 1], None, ALU.is_ge)
        p.wait(act, dve, dve.n)
        act.h.activation(tg[:, :], l8, AF.Exp, bias=nm1[:, tt:tt + 1])
        p.wait(dve, act, act.n)
        dve.h.tensor_tensor(g8, g8, tg[:, :], ALU.mult)
        dve.h.reduce_sum(zs[:, tt:tt + 1], g8, axis=AX)
        dve.h.reciprocal(zs[:, tt:tt + 1], zs[:, tt:tt + 1])
        dve.h.tensor_scalar_mul(g8, g8, zs[:, tt:tt + 1])


def wrow_wbc(p, banks, gts, e, id32, ones1, wrow, wbc, HTl=HT):
    pe, dve = p.pe, p.dve
    ntt = HTl // 128
    p.wait(pe, dve, dve.n)
    wr_banks = []
    for c in range(ntt // 4):
        bi, bank = banks.get()
        for j in range(4):
            tt = c * 4 + j
            last = pe.h.matmul(bank[0:1, j * 128:(j + 1) * 128],
                               gts[:, tt * 8 + e: tt * 8 + e + 1], id32[:, :],
                               start=True, stop=True)
        wr_banks.append((bi, bank, c))
    pe.inc(last)
    p.wait(dve, pe, pe.n)
    for bi, bank, c in wr_banks:
        dve.h.tensor_copy(wrow[0:1, c * 512:(c + 1) * 512], bank[0:1, 0:512])
    for bi, bank, c in wr_banks:
        banks.consumed(bi, dve)
    p.wait(pe, dve, dve.n)
    wb_banks = []
    for c in range(HTl // 512):
        bi, bank = banks.get()
        last = pe.h.matmul(bank[:, :], ones1[:, :], wrow[0:1, c * 512:(c + 1) * 512],
                           start=True, stop=True)
        wb_banks.append((bi, bank, c))
    pe.inc(last)
    p.wait(dve, pe, pe.n)
    for bi, bank, c in wb_banks:
        dve.h.tensor_copy(wbc[:, c * 512:(c + 1) * 512], bank[:, :])
    for bi, bank, c in wb_banks:
        banks.consumed(bi, dve)


def emit_k2(p, IO, ctx_loader=None):
    pe, act, dve, qs, qg = p.pe, p.act, p.dve, p.qs, p.qg
    ctx_d = IO.get("ctx"); x0_d = IO["x0"]; ow_d = IO["ow"]; ob_d = IO["ob_c"]
    ln2w_d = IO["ln2w_c"]; ln2b_d = IO["ln2b_c"]; f1_d = IO["f1w"]; f1b_d = IO["f1b_c"]
    f2_d = IO["f2w"]; f2b_d = IO["f2b_c"]; l1s_d = IO["l1s_c"]; l1h_d = IO["l1h_c"]
    x1_d = IO["x1"]; h1_d = IO["h1"]

    FC = 1024
    NFC = 4096 // FC
    NFT = FC // 128

    acc = p.sb("acc", [128, NDT * HT], F32)
    c16 = p.sb("c16", [128, NDT * HT], BF16)
    w_a = p.sb("w_a", [128, NDT * 2048], BF16)
    hbuf = p.sb("hbuf", [128, NFT * HT], BF16)
    ob_s = p.sb("ob_s", [128, NDT], F32)
    ln2w_s = p.sb("ln2w_s", [128, NDT], F32)
    ln2b_s = p.sb("ln2b_s", [128, NDT], F32)
    f1b_s = p.sb("f1b_s", [128, 32], F32)
    f2b_s = p.sb("f2b_s", [128, NDT], F32)
    l1s_s = p.sb("l1s_s", [128, NDT], F32)
    l1h_s = p.sb("l1h_s", [128, NDT], F32)
    onesc = p.sb("onesc", [128, 1], BF16)
    ones1f = p.sb("ones1f", [1, 128], F32)
    rows_s = p.sb("rows_s", [1, 1024], F32)
    rows_q = p.sb("rows_q", [1, 1024], F32)
    tmp_row = p.sb("tmp_row", [1, 1024], F32)
    sq_sc = p.sb("sq_sc", [128, NDT * 512], BF16)
    x3c = p.sb("x3c", [128, 4 * 512], F32)
    tsc = p.sb("tsc", [128, 512], F32)

    banks = Pool8(p)

    fm_load(p, qs, acc, x0_d, D, HT)
    if ctx_loader is None:
        qg.inc(qg.h.dma_start(out=sb3(c16, NDT, HT), in_=fm3(ctx_d)))
    else:
        ctx_loader(p, c16)
    qg.inc(qg.h.dma_start(out=sb3(w_a, NDT, 1024), in_=fm3(ow_d[:, 0:1024])))
    for s, dd in [(ob_s, ob_d), (ln2w_s, ln2w_d), (ln2b_s, ln2b_d), (f1b_s, f1b_d),
                  (f2b_s, f2b_d), (l1s_s, l1s_d), (l1h_s, l1h_d)]:
        qs.inc(qs.h.dma_start(out=s[:, :], in_=dd))
    dve.h.memset(onesc[:, :], 1.0)
    dve.h.memset(ones1f[:, :], 1.0)

    p.wait(pe, qg, qg.n)
    p.wait(pe, qs, qs.n)
    p.wait(dve, qs, qs.n)
    p.wait(act, qs, qs.n)
    for mt in range(NDT):
        for tb in range(4):
            bi, bank = banks.get()
            for dt in range(NDT):
                last = pe.h.matmul(bank[:, :],
                                   w_a[:, dt * 1024 + mt * 128: dt * 1024 + (mt + 1) * 128],
                                   c16[:, dt * HT + tb * 512: dt * HT + (tb + 1) * 512],
                                   start=(dt == 0), stop=(dt == NDT - 1))
            pe.inc(last)
            p.wait(dve, pe, pe.n)
            sl = acc[:, mt * HT + tb * 512: mt * HT + (tb + 1) * 512]
            dve.h.scalar_tensor_tensor(sl, bank[:, :], ob_s[:, mt:mt + 1], sl,
                                       ALU.add, ALU.add)
            banks.consumed(bi, dve)
    p.wait(act, dve, dve.n)
    for hf in range(2):
        t0 = hf * 1024
        row_stats(p, banks, acc, sq_sc, onesc, rows_s, rows_q, 1024, t0=t0)
        p.wait(dve, pe, pe.n)
        ln_finalize_rows(p, rows_s, rows_q, tmp_row, 1024, float(D), 1e-5)
        for c in range(2):
            c0 = t0 + c * 512
            r0 = c * 512
            p.wait(pe, dve, dve.n)
            bi1, bankR = banks.get()
            pe.h.matmul(bankR[:, :], ones1f[:, :], rows_q[0:1, r0:r0 + 512], start=True, stop=True)
            bi2, bankM = banks.get()
            last = pe.h.matmul(bankM[:, :], ones1f[:, :], rows_s[0:1, r0:r0 + 512], start=True, stop=True)
            pe.inc(last)
            p.wait(dve, pe, pe.n)
            for dt in range(NDT):
                sl = acc[:, dt * HT + c0: dt * HT + c0 + 512]
                dve.h.tensor_tensor(tsc[:, :], sl, bankR[:, :], ALU.mult)
                dve.h.tensor_tensor(tsc[:, :], tsc[:, :], bankM[:, :], ALU.add)
                dve.h.tensor_scalar(c16[:, dt * HT + c0: dt * HT + c0 + 512],
                                    tsc[:, :], ln2w_s[:, dt:dt + 1], ln2b_s[:, dt:dt + 1],
                                    ALU.mult, ALU.add)
            banks.consumed(bi1, dve)
            banks.consumed(bi2, dve)
        p.wait(act, dve, dve.n)

    p.wait(pe, dve, dve.n)
    for fc in range(NFC):
        p.wait(qg, pe, pe.n)
        p.wait(qg, act, act.n)
        qg.inc(qg.h.dma_start(out=sb3(w_a, NDT, FC),
                              in_=fm3(f1_d[:, fc * FC:(fc + 1) * FC])))
        p.wait(pe, qg, qg.n)
        for ft in range(NFT):
            for tb in range(4):
                bi, bank = banks.get()
                for dt in range(NDT):
                    last = pe.h.matmul(bank[:, :],
                                w_a[:, dt * FC + ft * 128: dt * FC + (ft + 1) * 128],
                                c16[:, dt * HT + tb * 512: dt * HT + (tb + 1) * 512],
                                start=(dt == 0), stop=(dt == NDT - 1))
                pe.inc(last)
                p.wait(act, pe, pe.n)
                act.h.activation(hbuf[:, ft * HT + tb * 512: ft * HT + (tb + 1) * 512],
                                 bank[:, :], AF.Gelu,
                                 bias=f1b_s[:, fc * NFT + ft: fc * NFT + ft + 1])
                banks.consumed(bi, act)
        p.wait(qg, pe, pe.n)
        qg.inc(qg.h.dma_start(out=sb3(w_a, NFT, D),
                              in_=fm3(f2_d[fc * FC:(fc + 1) * FC, :])))
        p.wait(pe, qg, qg.n)
        p.wait(pe, act, act.n)
        for mt in range(NDT):
            for tb in range(4):
                bi, bank = banks.get()
                for ft in range(NFT):
                    last = pe.h.matmul(bank[:, :],
                                w_a[:, ft * D + mt * 128: ft * D + (mt + 1) * 128],
                                hbuf[:, ft * HT + tb * 512: ft * HT + (tb + 1) * 512],
                                start=(ft == 0), stop=(ft == NFT - 1))
                pe.inc(last)
                p.wait(dve, pe, pe.n)
                sl = acc[:, mt * HT + tb * 512: mt * HT + (tb + 1) * 512]
                if fc == NFC - 1:
                    dve.h.scalar_tensor_tensor(sl, bank[:, :], f2b_s[:, mt:mt + 1], sl,
                                               ALU.add, ALU.add)
                else:
                    dve.h.tensor_tensor(sl, sl, bank[:, :], ALU.add)
                banks.consumed(bi, dve)

    add_dram_fm(p, banks, acc, x0_d, x3c)
    plain_ln_to_acc(p, banks, acc, sq_sc, onesc, ones1f, rows_s, rows_q, tmp_row,
                    tsc, 1e-5)
    p.wait(qs, dve, dve.n)
    fm_store(p, qs, x1_d, acc, D, HT)
    def emit_h1(dt, c0, tsc_ap, sl):
        p.dve.h.tensor_scalar(c16[:, dt * HT + c0: dt * HT + c0 + 512],
                              tsc_ap[:, :], l1s_s[:, dt:dt + 1], l1h_s[:, dt:dt + 1],
                              ALU.mult, ALU.add)
    plain_ln_to_acc(p, banks, acc, sq_sc, onesc, ones1f, rows_s, rows_q, tmp_row,
                    tsc, 1e-5, per_dt=emit_h1)
    p.wait(qs, dve, dve.n)
    qs.inc(qs.h.dma_start(out=fm3(h1_d), in_=sb3(c16, NDT, HT)))


def build_k2():
    p = Prog()
    IO = {n: p.din(n, s) for n, s in [("ctx", [D, HT]), ("x0", [D, HT]), ("ow", [D, D]),
          ("ob_c", [128, NDT]), ("ln2w_c", [128, NDT]), ("ln2b_c", [128, NDT]),
          ("f1w", [D, 4096]), ("f1b_c", [128, 32]), ("f2w", [4096, D]),
          ("f2b_c", [128, NDT]), ("l1s_c", [128, NDT]), ("l1h_c", [128, NDT])]}
    IO["x1"] = p.dout("x1", [D, HT])
    IO["h1"] = p.nc.dram_tensor("h1", [D, HT], BF16, kind="ExternalOutput").ap()
    p.push(); emit_k2(p, IO); p.pop()
    return p.finish()


def k2_inputs(inputs, attn_full, x0_full):
    c = np.ascontiguousarray
    col = lambda v: c(v.reshape(-1, 128).T)
    base = dict(
        ow=c(inputs["L0_attn_o_w"]), ob_c=col(inputs["L0_attn_o_b"]),
        ln2w_c=col(inputs["L0_attn_ln2_w"]), ln2b_c=col(inputs["L0_attn_ln2_b"]),
        f1w=c(inputs["L0_attn_f1_w"]), f1b_c=col(inputs["L0_attn_f1_b"]),
        f2w=c(inputs["L0_attn_f2_w"]), f2b_c=col(inputs["L0_attn_f2_b"]),
        l1s_c=col(inputs["L0_ffn_ln1_scale"]), l1h_c=col(inputs["L0_ffn_ln1_shift"]))
    maps = []
    for core in range(8):
        b, h = core // 2, core % 2
        m = dict(base)
        m["ctx"] = c(attn_full[b, h * HT:(h + 1) * HT, :].T)
        m["x0"] = c(x0_full[b, h * HT:(h + 1) * HT, :].T)
        maps.append(m)
    return maps


def emit_k4(p, IO, ctx_loader=None):
    pe, act, dve, qs, qg = p.pe, p.act, p.dve, p.qs, p.qg
    ctx_d = IO.get("ctx"); x1_d = IO["x1"]; wo_d = IO["wo"]; bo_d = IO["bo_c"]
    l2s_d = IO["l2s_c"]; l2h_d = IO["l2h_c"]; gw_d = IO["gate_w"]
    w1_d = IO["w1"]; w2_d = IO["w2"]; w3_d = IO["w3"]; id_d = IO["ident"]
    x2_d = IO["x2"]

    HC = 512
    NHC = 2048 // HC
    NHTl = HC // 128

    acc = p.sb("acc", [128, NDT * HT], F32)
    c16 = p.sb("c16", [128, NDT * HT], BF16)
    w1c = p.sb("w1c", [128, NDT * HC], BF16)
    w2c = p.sb("w2c", [128, NDT * HC], BF16)
    w3c = p.sb("w3c", [128, NHTl * D], BF16)
    hS = p.sb("hS", [128, NHTl * HT], BF16)
    gate_s = p.sb("gate_s", [128, NDT * 8], BF16)
    id32 = p.sb("id32", [128, 128], F32)
    bo_s = p.sb("bo_s", [128, NDT], F32)
    l2s_s = p.sb("l2s_s", [128, NDT], F32)
    l2h_s = p.sb("l2h_s", [128, NDT], F32)
    gts = p.sb("gts", [128, NTT * 8], F32)
    logit = p.sb("logit", [128, NTT * 8], F32)
    wrow = p.sb("wrow", [1, HT], BF16)
    wbc = p.sb("wbc", [128, HT], BF16)
    ones1 = p.sb("ones1", [1, 128], BF16)
    ones1f = p.sb("ones1f", [1, 128], F32)
    onesc = p.sb("onesc", [128, 1], BF16)
    rows_s = p.sb("rows_s", [1, 1024], F32)
    rows_q = p.sb("rows_q", [1, 1024], F32)
    tmp_row = p.sb("tmp_row", [1, 1024], F32)
    sq_sc = p.sb("sq_sc", [128, NDT * 512], BF16)
    x3c = p.sb("x3c", [128, 4 * 512], F32)
    tsc = p.sb("tsc", [128, 512], F32)
    sA = p.sb("sA", [128, 512], BF16)
    m1 = p.sb("m1", [128, NTT], F32)
    nm1 = p.sb("nm1", [128, NTT], F32)
    m2 = p.sb("m2", [128, NTT], F32)
    zs = p.sb("zs", [128, NTT], F32)
    tg = p.sb("tg", [128, 8], F32)

    banks = Pool8(p)

    fm_load(p, qs, acc, x1_d, D, HT)
    if ctx_loader is None:
        qg.inc(qg.h.dma_start(out=sb3(c16, NDT, HT), in_=fm3(ctx_d)))
    else:
        ctx_loader(p, c16)

    qg.inc(qg.h.dma_start(out=sb3(gate_s, NDT, 8), in_=fm3(gw_d)))
    qs.inc(qs.h.dma_start(out=id32[:, :], in_=id_d))
    for s, dd in [(bo_s, bo_d), (l2s_s, l2s_d), (l2h_s, l2h_d)]:
        qs.inc(qs.h.dma_start(out=s[:, :], in_=dd))
    dve.h.memset(ones1[:, :], 1.0)
    dve.h.memset(ones1f[:, :], 1.0)
    dve.h.memset(onesc[:, :], 1.0)

    p.wait(pe, qg, qg.n)
    p.wait(pe, qs, qs.n)
    p.wait(dve, qs, qs.n)
    p.wait(act, qs, qs.n)
    p.wait(pe, dve, dve.n)
    for mh in range(2):
        p.wait(qg, pe, pe.n)
        qg.inc(qg.h.dma_start(out=sb3(w1c, NDT, 512),
                              in_=fm3(wo_d[:, mh * 512:(mh + 1) * 512])))
        p.wait(pe, qg, qg.n)
        for mtl in range(4):
            mt = mh * 4 + mtl
            for tb in range(4):
                bi, bank = banks.get()
                for dt in range(NDT):
                    last = pe.h.matmul(bank[:, :],
                                       w1c[:, dt * 512 + mtl * 128: dt * 512 + (mtl + 1) * 128],
                                       c16[:, dt * HT + tb * 512: dt * HT + (tb + 1) * 512],
                                       start=(dt == 0), stop=(dt == NDT - 1))
                pe.inc(last)
                p.wait(dve, pe, pe.n)
                sl = acc[:, mt * HT + tb * 512: mt * HT + (tb + 1) * 512]
                dve.h.scalar_tensor_tensor(sl, bank[:, :], bo_s[:, mt:mt + 1], sl,
                                           ALU.add, ALU.add)
                banks.consumed(bi, dve)

    p.wait(act, dve, dve.n)
    def emit_h2(dt, c0, tsc_ap, sl):
        p.dve.h.tensor_scalar(c16[:, dt * HT + c0: dt * HT + c0 + 512],
                              tsc_ap[:, :], l2s_s[:, dt:dt + 1], l2h_s[:, dt:dt + 1],
                              ALU.mult, ALU.add)
    plain_ln_to_acc(p, banks, acc, sq_sc, onesc, ones1f, rows_s, rows_q, tmp_row,
                    tsc, 1e-5, per_dt=emit_h2)

    p.wait(pe, dve, dve.n)
    topk2_gates(p, banks, c16, gate_s, None, gts, logit,
                (ones1, m1, nm1, m2, zs, tg), nbias=False)

    for e in range(8):
        wrow_wbc(p, banks, gts, e, id32, ones1, wrow, wbc)
        for hc in range(NHC):
            p.wait(qg, pe, pe.n)
            p.wait(qg, dve, dve.n)
            qg.inc(qg.h.dma_start(out=sb3(w1c, NDT, HC),
                                  in_=fm3(w1_d[e, :, hc * HC:(hc + 1) * HC])))
            qg.inc(qg.h.dma_start(out=sb3(w2c, NDT, HC),
                                  in_=fm3(w2_d[e, :, hc * HC:(hc + 1) * HC])))
            qg.inc(qg.h.dma_start(out=sb3(w3c, NHTl, D),
                                  in_=fm3(w3_d[e, hc * HC:(hc + 1) * HC, :])))
            p.wait(pe, qg, qg.n)
            p.wait(pe, dve, dve.n)
            for ht in range(NHTl):
                for tb in range(4):
                    bia, banka = banks.get()
                    for dt in range(NDT):
                        lasta = pe.h.matmul(banka[:, :],
                                    w1c[:, dt * HC + ht * 128: dt * HC + (ht + 1) * 128],
                                    c16[:, dt * HT + tb * 512: dt * HT + (tb + 1) * 512],
                                    start=(dt == 0), stop=(dt == NDT - 1))
                    bib, bankb = banks.get()
                    for dt in range(NDT):
                        lastb = pe.h.matmul(bankb[:, :],
                                    w2c[:, dt * HC + ht * 128: dt * HC + (ht + 1) * 128],
                                    c16[:, dt * HT + tb * 512: dt * HT + (tb + 1) * 512],
                                    start=(dt == 0), stop=(dt == NDT - 1))
                    pe.inc(lasta, 1)
                    pe.inc(lastb, 1)
                    p.wait(act, pe, pe.n)
                    p.wait(act, dve, dve.n)
                    act.h.activation(sA[:, :], banka[:, :], AF.Silu)
                    banks.consumed(bia, act)
                    hsl = hS[:, ht * HT + tb * 512: ht * HT + (tb + 1) * 512]
                    p.wait(dve, act, act.n)
                    dve.h.tensor_tensor(hsl, bankb[:, :], wbc[:, tb * 512:(tb + 1) * 512],
                                        ALU.mult)
                    dve.h.tensor_tensor(hsl, hsl, sA[:, :], ALU.mult)
                    banks.consumed(bib, dve)
            p.wait(pe, dve, dve.n)
            for mt in range(NDT):
                for tb in range(4):
                    bi, bank = banks.get()
                    for ht in range(NHTl):
                        last = pe.h.matmul(bank[:, :],
                                    w3c[:, ht * D + mt * 128: ht * D + (mt + 1) * 128],
                                    hS[:, ht * HT + tb * 512: ht * HT + (tb + 1) * 512],
                                    start=(ht == 0), stop=(ht == NHTl - 1))
                    pe.inc(last)
                    p.wait(dve, pe, pe.n)
                    sl = acc[:, mt * HT + tb * 512: mt * HT + (tb + 1) * 512]
                    dve.h.tensor_tensor(sl, sl, bank[:, :], ALU.add)
                    banks.consumed(bi, dve)

    add_dram_fm(p, banks, acc, x1_d, x3c)
    p.wait(act, dve, dve.n)
    plain_ln_to_acc(p, banks, acc, sq_sc, onesc, ones1f, rows_s, rows_q, tmp_row,
                    tsc, 1e-5)
    p.wait(qs, dve, dve.n)
    fm_store(p, qs, x2_d, acc, D, HT)


def build_k4():
    p = Prog()
    IO = {n: p.din(n, s) for n, s in [("ctx", [D, HT]), ("x1", [D, HT]), ("wo", [D, D]),
          ("bo_c", [128, NDT]), ("l2s_c", [128, NDT]), ("l2h_c", [128, NDT]),
          ("gate_w", [D, 8]), ("w1", [8, D, 2048]), ("w2", [8, D, 2048]),
          ("w3", [8, 2048, D]), ("ident", [128, 128])]}
    IO["x2"] = p.dout("x2", [D, HT])
    p.push(); emit_k4(p, IO); p.pop()
    return p.finish()


def k4_inputs(inputs, ctx_full, x1_full):
    c = np.ascontiguousarray
    col = lambda v: c(v.reshape(-1, 128).T)
    base = dict(
        wo=c(inputs["L0_ffn_wo"]), bo_c=col(inputs["L0_ffn_bo"]),
        l2s_c=col(inputs["L0_ffn_ln2_scale"]), l2h_c=col(inputs["L0_ffn_ln2_shift"]),
        gate_w=c(inputs["L0_ffn_gate_w"]),
        w1=c(inputs["L0_ffn_w1"]), w2=c(inputs["L0_ffn_w2"]), w3=c(inputs["L0_ffn_w3"]),
        ident=np.eye(128, dtype=np.float32))
    maps = []
    for core in range(8):
        b, h = core // 2, core % 2
        m = dict(base)
        m["ctx"] = c(ctx_full[b, h * HT:(h + 1) * HT, :].T)
        m["x1"] = c(x1_full[b, h * HT:(h + 1) * HT, :].T)
        maps.append(m)
    return maps


def emit_k5(p, IO, fused=False):
    pe, act, dve, qs, qg = p.pe, p.act, p.dve, p.qs, p.qg
    xm_d = IO["xmy"]; wq_d = IO["wq"]; wk_d = IO["wk"]; wv_d = IO["wv"]
    bq_d = IO["bq_c"]; bk_d = IO["bk_c"]; bv_d = IO["bv_r"]
    wp_d = IO["wp"]; bp_d = IO["bp_c"]; x3_d = IO["x3"]
    xf_d = IO.get("xfull")
    kq_dram = IO["k_sc"]
    q_dram = IO["q_sc"]

    V_s = p.sb("V_s", [128, 32 * 1024], BF16)
    PT = p.sb("PT", [128, 32 * 512], BF16)
    wst = p.sb("wst", [128, NDT * 1024], BF16)
    xc = p.sb("xc", [128, NDT * 512], BF16)
    kc = p.sb("kc", [128, NDT * 512], BF16)
    sqs = p.sb("sqs", [128, 512], BF16)
    ybuf = p.sb("ybuf", [128, NDT * 512], BF16)
    xab = p.sb("xab", [128, NDT * 512], F32)
    rzb = p.sb("rzb", [128, 512], F32)
    bv_s = p.sb("bv_s", [1, D], BF16)
    bq_s = p.sb("bq_s", [128, NDT], F32)
    bq32 = p.sb("bq32", [128, NDT], F32)
    bk_s = p.sb("bk_s", [128, NDT], F32)
    bp_s = p.sb("bp_s", [128, NDT], F32)
    ones1 = p.sb("ones1", [1, 128], BF16)
    ones1f = p.sb("ones1f", [1, 128], F32)
    onesc = p.sb("onesc", [128, 1], BF16)
    nrm = p.sb("nrm", [1, 512], F32)
    kmax = p.sb("kmax", [1, 4], F32)
    qmax = p.sb("qmax", [1, 4], F32)
    mcol = p.sb("mcol", [128, 1], F32)
    km2 = p.sb("km2", [1, 8], F32)
    rows_s = p.sb("rows_s", [1, 512], F32)
    rows_q = p.sb("rows_q", [1, 512], F32)
    tmp_row = p.sb("tmp_row", [1, 512], F32)
    sq_sc = p.sb("sq_sc", [128, NDT * 512], BF16)
    tsc = p.sb("tsc", [128, 512], F32)
    x3cb = p.sb("x3cb", [128, NDT * 512], F32)

    banks = Pool8(p, n=7)
    zps = p.ps("zps", [1, 512])

    dve.h.memset(ones1[:, :], 1.0)
    dve.h.memset(ones1f[:, :], 1.0)
    dve.h.memset(onesc[:, :], 1.0)
    dve.h.memset(kmax[:, :], 0.0)
    dve.h.memset(qmax[:, :], 0.0)
    for s, dd in [(bq_s, bq_d), (bk_s, bk_d), (bp_s, bp_d)]:
        qs.inc(qs.h.dma_start(out=s[:, :], in_=dd))
    qg.inc(qg.h.dma_start(out=bv_s[:, :], in_=bv_d))
    p.wait(act, qs, qs.n)
    act.h.activation(bq32[:, :], bq_s[:, :], AF.Identity, scale=1.0 / 32.0)

    kv_src = xm_d if fused else xf_d
    n_kv = 4 if fused else 8
    p.wait(pe, dve, dve.n)
    p.wait(pe, qs, qs.n)
    qg.inc(qg.h.dma_start(out=sb3(wst, NDT, 1024), in_=fm3(wk_d)))
    for tc in range(n_kv):
        t0 = tc * 512
        p.wait(qg, pe, pe.n)
        qg.inc(qg.h.dma_start(out=sb3(xc, NDT, 512), in_=fm3(kv_src[:, t0:t0 + 512])))
        p.wait(pe, qg, qg.n)
        p.wait(pe, act, act.n)
        p.wait(act, qs, qs.n)
        for mt in range(NDT):
            bi, bank = banks.get()
            for dt in range(NDT):
                last = pe.h.matmul(bank[:, :],
                                   wst[:, dt * 1024 + mt * 128: dt * 1024 + (mt + 1) * 128],
                                   xc[:, dt * 512:(dt + 1) * 512],
                                   start=(dt == 0), stop=(dt == NDT - 1))
            pe.inc(last)
            p.wait(act, pe, pe.n)
            act.h.activation(kc[:, mt * 512:(mt + 1) * 512], bank[:, :], AF.Identity,
                             bias=bk_s[:, mt:mt + 1])
            banks.consumed(bi, act)
        p.wait(act, act, act.n)
        for mt in range(NDT):
            p.wait(act, pe, pe.n)
            act.h.activation(sqs[:, :], kc[:, mt * 512:(mt + 1) * 512], AF.Square)
            p.wait(pe, act, act.n)
            bi, bank = banks.get()
            last = pe.h.matmul(bank[0:1, 0:512], onesc[:, 0:1], sqs[:, :],
                               start=(mt == 0), stop=(mt == NDT - 1),
                               skip_group_check=True)
            pe.inc(last)
            if mt == NDT - 1:
                p.wait(dve, pe, pe.n)
                dve.h.reduce_max(kmax[0:1, 1:2], bank[0:1, 0:512], axis=AX)
                dve.h.tensor_tensor(kmax[0:1, 0:1], kmax[0:1, 0:1], kmax[0:1, 1:2], ALU.max)
                banks.consumed(bi, dve)
            else:
                banks.i -= 1
        p.wait(qs, act, act.n)
        qs.inc(qs.h.dma_start(out=fm3(kq_dram[:, t0:t0 + 512]), in_=sb3(kc, NDT, 512)))
    p.wait(qg, pe, pe.n)
    qg.inc(qg.h.dma_start(out=sb3(wst, NDT, 1024), in_=fm3(wv_d)))
    for tc in range(n_kv):
        t0 = tc * 512
        p.wait(qg, pe, pe.n)
        qg.inc(qg.h.dma_start(out=sb3(xc, NDT, 512), in_=fm3(kv_src[:, t0:t0 + 512])))
        p.wait(pe, qg, qg.n)
        for ttl in range(4):
            tt = tc * 4 + ttl
            for dh in range(2):
                bi, bank = banks.get()
                for dt in range(NDT):
                    pe.h.matmul(bank[:, :],
                                xc[:, dt * 512 + ttl * 128: dt * 512 + (ttl + 1) * 128],
                                wst[:, dt * 1024 + dh * 512: dt * 1024 + (dh + 1) * 512],
                                start=(dt == 0), stop=False)
                last = pe.h.matmul(bank[:, :], ones1[:, :],
                                   bv_s[0:1, dh * 512:(dh + 1) * 512],
                                   start=False, stop=True)
                pe.inc(last)
                p.wait(dve, pe, pe.n)
                dve.h.tensor_copy(V_s[:, tt * 1024 + dh * 512: tt * 1024 + (dh + 1) * 512],
                                  bank[:, :])
                banks.consumed(bi, dve)
        if fused:
            p.wait(qs, dve, dve.n)
            qs.inc(qs.h.dma_start(
                out=IO["v_bounce"][t0:t0 + 512, :],
                in_=V_s[:, tc * 4 * 1024:(tc + 1) * 4 * 1024]
                    .rearrange("p (tt v) -> p tt v", v=1024)))

    if fused:
        p.wait(qs, dve, dve.n)
        qs.inc(qs.h.dma_start(out=IO["kmax_b"], in_=kmax[0:1, :]))
        IO["mid_cb"](p)

    p.wait(qg, pe, pe.n)
    qg.inc(qg.h.dma_start(out=sb3(wst, NDT, 1024), in_=fm3(wq_d)))
    for tc in range(4):
        t0 = tc * 512
        p.wait(qg, pe, pe.n)
        qg.inc(qg.h.dma_start(out=sb3(xc, NDT, 512), in_=fm3(xm_d[:, t0:t0 + 512])))
        p.wait(pe, qg, qg.n)
        p.wait(pe, act, act.n)
        p.wait(act, qs, qs.n)
        for mt in range(NDT):
            bi, bank = banks.get()
            for dt in range(NDT):
                last = pe.h.matmul(bank[:, :],
                                   wst[:, dt * 1024 + mt * 128: dt * 1024 + (mt + 1) * 128],
                                   xc[:, dt * 512:(dt + 1) * 512],
                                   start=(dt == 0), stop=(dt == NDT - 1))
            pe.inc(last)
            p.wait(act, pe, pe.n)
            act.h.activation(kc[:, mt * 512:(mt + 1) * 512], bank[:, :], AF.Identity,
                             scale=1.0 / 32.0, bias=bq32[:, mt:mt + 1])
            banks.consumed(bi, act)
        p.wait(act, act, act.n)
        for mt in range(NDT):
            p.wait(act, pe, pe.n)
            act.h.activation(sqs[:, :], kc[:, mt * 512:(mt + 1) * 512], AF.Square)
            p.wait(pe, act, act.n)
            bi, bank = banks.get()
            last = pe.h.matmul(bank[0:1, 0:512], onesc[:, 0:1], sqs[:, :],
                               start=(mt == 0), stop=(mt == NDT - 1),
                               skip_group_check=True)
            pe.inc(last)
            if mt == NDT - 1:
                p.wait(dve, pe, pe.n)
                dve.h.reduce_max(qmax[0:1, 1:2], bank[0:1, 0:512], axis=AX)
                dve.h.tensor_tensor(qmax[0:1, 0:1], qmax[0:1, 0:1], qmax[0:1, 1:2], ALU.max)
                banks.consumed(bi, dve)
            else:
                banks.i -= 1
        p.wait(qs, act, act.n)
        qs.inc(qs.h.dma_start(out=fm3(q_dram[:, t0:t0 + 512]), in_=sb3(kc, NDT, 512)))

    if fused:
        p.wait(qs, p.cc)
        qs.inc(qs.h.dma_start(out=km2[0:1, :], in_=IO["ag_kmax"].rearrange("a b -> (a b)")))
        p.wait(dve, qs, qs.n)
        dve.h.tensor_tensor(kmax[0:1, 0:1], km2[0:1, 0:1], km2[0:1, 4:5], ALU.max)
    p.wait(act, dve, dve.n)
    act.h.activation(qmax[0:1, 2:3], qmax[0:1, 0:1], AF.Sqrt)
    act.h.activation(kmax[0:1, 2:3], kmax[0:1, 0:1], AF.Sqrt)
    p.wait(dve, act, act.n)
    dve.h.tensor_tensor(qmax[0:1, 3:4], qmax[0:1, 2:3], kmax[0:1, 2:3], ALU.mult)
    dve.h.tensor_scalar_mul(qmax[0:1, 3:4], qmax[0:1, 3:4], -1.0)
    p.wait(pe, dve, dve.n)
    bi, bank = banks.get()
    last = pe.h.matmul(bank[0:128, 0:1], ones1f[:, :], qmax[0:1, 3:4], start=True, stop=True)
    pe.inc(last)
    p.wait(dve, pe, pe.n)
    dve.h.tensor_copy(mcol[:, :], bank[0:128, 0:1])
    banks.consumed(bi, dve)

    if fused:
        p.wait(qs, p.cc)
        p.wait(qs, pe, pe.n)
        p.wait(qs, dve, dve.n)
        qs.inc(qs.h.dma_start(
            out=V_s[:, :].rearrange("p (tt v) -> p tt v", v=1024),
            in_=IO["ag_v"].rearrange("(tt p) v -> p tt v", p=128)))
    p.wait(qg, pe, pe.n)
    qg.inc(qg.h.dma_start(out=sb3(wst, NDT, 1024), in_=fm3(wp_d)))
    for qb in range(4):
        q0 = qb * 512
        p.wait(qs, pe, pe.n)
        qs.inc(qs.h.dma_start(out=sb3(xc, NDT, 512), in_=fm3(q_dram[:, q0:q0 + 512])))
        for kc4 in range(8):
            k0 = kc4 * 512
            p.wait(qs, pe, pe.n)
            p.wait(qs, act, act.n)
            if fused:
                blk = k0 // HT
                kk0 = k0 % HT
                qs.inc(qs.h.dma_start(out=sb3(kc, NDT, 512),
                                      in_=fm3(IO["ag_k"][blk * D:(blk + 1) * D, kk0:kk0 + 512])))
            else:
                qs.inc(qs.h.dma_start(out=sb3(kc, NDT, 512), in_=fm3(kq_dram[:, k0:k0 + 512])))
            p.wait(pe, qs, qs.n)
            p.wait(pe, dve, dve.n)
            for ktl in range(4):
                kt = kc4 * 4 + ktl
                bi, bank = banks.get()
                for dt in range(NDT):
                    last = pe.h.matmul(bank[:, :],
                                kc[:, dt * 512 + ktl * 128: dt * 512 + (ktl + 1) * 128],
                                xc[:, dt * 512:(dt + 1) * 512],
                                start=(dt == 0), stop=(dt == NDT - 1))
                pe.inc(last)
                p.wait(act, pe, pe.n)
                act.h.activation(PT[:, kt * 512:(kt + 1) * 512], bank[:, :], AF.Exp,
                                 bias=mcol[:, 0:1])
                banks.consumed(bi, act)
                p.wait(pe, act, act.n)
                last = pe.h.matmul(zps[0:1, :], onesc[:, 0:1], PT[:, kt * 512:(kt + 1) * 512],
                                   start=(kt == 0), stop=(kt == 31))
                if kt == 31:
                    pe.inc(last)
        p.wait(dve, pe, pe.n)
        dve.h.reciprocal(nrm[0:1, :], zps[0:1, :])
        p.wait(pe, dve, dve.n)
        bi, bank = banks.get()
        last = pe.h.matmul(bank[:, :], ones1f[:, :], nrm[0:1, :], start=True, stop=True)
        pe.inc(last)
        p.wait(dve, pe, pe.n)
        dve.h.tensor_copy(rzb[:, :], bank[:, :])
        banks.consumed(bi, dve)
        for dvt in range(NDT):
            bi, bank = banks.get()
            for kt in range(32):
                last = pe.h.matmul(bank[:, :],
                                   V_s[:, kt * 1024 + dvt * 128: kt * 1024 + (dvt + 1) * 128],
                                   PT[:, kt * 512:(kt + 1) * 512],
                                   start=(kt == 0), stop=(kt == 31))
            pe.inc(last)
            p.wait(dve, pe, pe.n)
            dve.h.tensor_tensor(ybuf[:, dvt * 512:(dvt + 1) * 512], bank[:, :],
                                rzb[:, :], ALU.mult)
            banks.consumed(bi, dve)
        p.wait(qs, dve, dve.n)
        qs.inc(qs.h.dma_start(out=sb3(x3cb, NDT, 512), in_=fm3(xm_d[:, q0:q0 + 512])))
        p.wait(pe, dve, dve.n)
        p.wait(pe, qg, qg.n)
        for mt in range(NDT):
            bi, bank = banks.get()
            for dt in range(NDT):
                last = pe.h.matmul(bank[:, :],
                                   wst[:, dt * 1024 + mt * 128: dt * 1024 + (mt + 1) * 128],
                                   ybuf[:, dt * 512:(dt + 1) * 512],
                                   start=(dt == 0), stop=(dt == NDT - 1))
            pe.inc(last)
            p.wait(dve, pe, pe.n)
            p.wait(dve, qs, qs.n)
            dve.h.scalar_tensor_tensor(xab[:, mt * 512:(mt + 1) * 512], bank[:, :],
                                       bp_s[:, mt:mt + 1],
                                       x3cb[:, mt * 512:(mt + 1) * 512],
                                       ALU.add, ALU.add)
            banks.consumed(bi, dve)
        p.wait(act, dve, dve.n)
        row_stats(p, banks, xab, sq_sc, onesc, rows_s, rows_q, 512, HTl=512)
        p.wait(dve, pe, pe.n)
        ln_finalize_rows(p, rows_s, rows_q, tmp_row, 512, float(D), 1e-5)
        p.wait(pe, dve, dve.n)
        bi1, bankR = banks.get()
        pe.h.matmul(bankR[:, :], ones1f[:, :], rows_q[0:1, 0:512], start=True, stop=True)
        bi2, bankM = banks.get()
        last = pe.h.matmul(bankM[:, :], ones1f[:, :], rows_s[0:1, 0:512], start=True, stop=True)
        pe.inc(last)
        p.wait(dve, pe, pe.n)
        for dt in range(NDT):
            sl = xab[:, dt * 512:(dt + 1) * 512]
            dve.h.tensor_tensor(tsc[:, :], sl, bankR[:, :], ALU.mult)
            dve.h.tensor_tensor(sl, tsc[:, :], bankM[:, :], ALU.add)
        banks.consumed(bi1, dve)
        banks.consumed(bi2, dve)
        p.wait(qs, dve, dve.n)
        qs.inc(qs.h.dma_start(out=fm3(x3_d[:, q0:q0 + 512]), in_=sb3(xab, NDT, 512)))


def build_k5():
    p = Prog()
    IO = {n: p.din(n, s) for n, s in [("xfull", [D, TFULL]), ("xmy", [D, HT]),
          ("wq", [D, D]), ("wk", [D, D]), ("wv", [D, D]), ("bq_c", [128, NDT]),
          ("bk_c", [128, NDT]), ("bv_r", [1, D]), ("wp", [D, D]), ("bp_c", [128, NDT])]}
    IO["x3"] = p.dout("x3", [D, HT])
    IO["k_sc"] = p.dint("k_sc", [D, TFULL], BF16).ap()
    IO["q_sc"] = p.dint("q_sc", [D, HT], BF16).ap()
    p.push(); emit_k5(p, IO); p.pop()
    return p.finish()


def k5_inputs(inputs, x2_full):
    c = np.ascontiguousarray
    col = lambda v: c(v.reshape(-1, 128).T)
    wqkv = inputs["L1_attn_w_qkv"]
    bqkv = inputs["L1_attn_b_qkv"]
    base = dict(
        wq=c(wqkv[:, 0:1024]), wk=c(wqkv[:, 1024:2048]), wv=c(wqkv[:, 2048:3072]),
        bq_c=col(bqkv[0:1024]), bk_c=col(bqkv[1024:2048]), bv_r=c(bqkv[2048:3072].reshape(1, -1)),
        wp=c(inputs["L1_attn_w_proj"]), bp_c=col(inputs["L1_attn_b_proj"]))
    maps = []
    for core in range(8):
        b, h = core // 2, core % 2
        m = dict(base)
        m["xfull"] = c(x2_full[b].T)
        m["xmy"] = c(x2_full[b, h * HT:(h + 1) * HT, :].T)
        maps.append(m)
    return maps


def emit_k3(p, IO, h1_loader=None, ctx_bf16=False):
    pe, act, dve, qs, qg = p.pe, p.act, p.dve, p.qs, p.qg
    h1_d = IO.get("h1f"); wq_d = IO["wq_my"]; wk_d = IO["wk_my"]; wv_d = IO["wv_my"]
    triu_d = IO["triu"]; ctx_d = IO["ctxo"]

    qf = p.sb("qf", [128, 4 * TFULL], BF16)
    kf = p.sb("kf", [128, 4 * TFULL], BF16)
    V_s = p.sb("V_s", [128, 32 * 512], BF16)
    PT = p.sb("PT", [128, 32 * 512], BF16)
    wq16 = p.sb("wq16", [128, NDT * 512], BF16)
    wk16 = p.sb("wk16", [128, NDT * 512], BF16)
    wv16 = p.sb("wv16", [128, NDT * 512], BF16)
    hc = p.sb("hc", [128, NDT * 512], BF16)
    sqs = p.sb("sqs", [128, 512], BF16)
    triu = p.sb("triu_s", [128, 128], BF16)
    qmax8 = p.sb("qmax8", [1, 32], F32)
    kmax8 = p.sb("kmax8", [1, 32], F32)
    nm8 = p.sb("nm8", [1, 8], F32)
    mcols = p.sb("mcols", [128, 8], F32)
    nrm = p.sb("nrm", [1, 512], F32)
    rzb = p.sb("rzb", [128, 512], F32)
    cstage = p.sb("cstage", [128, 512], BF16 if ctx_bf16 else F32)
    ones1f = p.sb("ones1f", [1, 128], F32)
    onesc = p.sb("onesc", [128, 1], BF16)

    banks = Pool8(p, n=7)
    zps = p.ps("zps", [1, 512])

    dve.h.memset(ones1f[:, :], 1.0)
    dve.h.memset(onesc[:, :], 1.0)
    dve.h.memset(qmax8[:, :], 0.0)
    dve.h.memset(kmax8[:, :], 0.0)
    qs.inc(qs.h.dma_start(out=triu[:, :], in_=triu_d))
    qg.inc(qg.h.dma_start(out=sb3(wq16, NDT, 512), in_=fm3(wq_d)))
    qg.inc(qg.h.dma_start(out=sb3(wk16, NDT, 512), in_=fm3(wk_d)))
    qg.inc(qg.h.dma_start(out=sb3(wv16, NDT, 512), in_=fm3(wv_d)))

    p.wait(pe, dve, dve.n)
    p.wait(pe, qg, qg.n)
    for tc in range(8):
        t0 = tc * 512
        p.wait(qs, pe, pe.n)
        if h1_loader is None:
            qs.inc(qs.h.dma_start(out=sb3(hc, NDT, 512), in_=fm3(h1_d[:, t0:t0 + 512])))
        else:
            h1_loader(p, hc, t0)
        p.wait(pe, qs, qs.n)
        for which, wbuf, dst, scale, mx in (
                (0, wq16, qf, 0.125, qmax8), (1, wk16, kf, 1.0, kmax8)):
            for hp in range(4):
                bi, bank = banks.get()
                for dt in range(NDT):
                    last = pe.h.matmul(bank[:, :],
                                       wbuf[:, dt * 512 + hp * 128: dt * 512 + (hp + 1) * 128],
                                       hc[:, dt * 512:(dt + 1) * 512],
                                       start=(dt == 0), stop=(dt == NDT - 1))
                pe.inc(last)
                p.wait(act, pe, pe.n)
                if scale == 1.0:
                    act.h.activation(dst[:, hp * TFULL + t0: hp * TFULL + t0 + 512],
                                     bank[:, :], AF.Copy)
                else:
                    act.h.activation(dst[:, hp * TFULL + t0: hp * TFULL + t0 + 512],
                                     bank[:, :], AF.Identity, scale=scale)
                banks.consumed(bi, act)
                p.wait(act, pe, pe.n)
                act.h.activation(sqs[:, :], dst[:, hp * TFULL + t0: hp * TFULL + t0 + 512],
                                 AF.Square)
                p.wait(pe, act, act.n)
                for lane in range(2):
                    h8 = hp * 2 + lane
                    bi2, bank2 = banks.get()
                    last = pe.h.matmul(bank2[0:1, 0:512], onesc[lane * 64:lane * 64 + 64, 0:1],
                                       sqs[lane * 64:lane * 64 + 64, :],
                                       start=True, stop=True,
                                       tile_position=(lane * 64, 0))
                    pe.inc(last)
                    p.wait(dve, pe, pe.n)
                    dve.h.reduce_max(mx[0:1, 8 + h8:9 + h8], bank2[0:1, 0:512], axis=AX)
                    dve.h.tensor_tensor(mx[0:1, h8:h8 + 1], mx[0:1, h8:h8 + 1],
                                        mx[0:1, 8 + h8:9 + h8], ALU.max)
                    banks.consumed(bi2, dve)
        p.wait(pe, dve, dve.n)
        for ttl in range(4):
            tt = tc * 4 + ttl
            bi, bank = banks.get()
            for dt in range(NDT):
                last = pe.h.matmul(bank[:, :],
                                   hc[:, dt * 512 + ttl * 128: dt * 512 + (ttl + 1) * 128],
                                   wv16[:, dt * 512:(dt + 1) * 512],
                                   start=(dt == 0), stop=(dt == NDT - 1))
            pe.inc(last)
            p.wait(dve, pe, pe.n)
            dve.h.tensor_copy(V_s[:, tt * 512:(tt + 1) * 512], bank[:, :])
            banks.consumed(bi, dve)

    p.wait(act, dve, dve.n)
    act.h.activation(qmax8[0:1, 16:24], qmax8[0:1, 0:8], AF.Sqrt)
    act.h.activation(kmax8[0:1, 16:24], kmax8[0:1, 0:8], AF.Sqrt)
    p.wait(dve, act, act.n)
    dve.h.tensor_tensor(nm8[0:1, :], qmax8[0:1, 16:24], kmax8[0:1, 16:24], ALU.mult)
    dve.h.tensor_scalar_mul(nm8[0:1, :], nm8[0:1, :], -0.125)
    p.wait(pe, dve, dve.n)
    bi, bank = banks.get()
    last = pe.h.matmul(bank[0:128, 0:8], ones1f[:, :], nm8[0:1, :], start=True, stop=True)
    pe.inc(last)
    p.wait(dve, pe, pe.n)
    dve.h.tensor_copy(mcols[:, :], bank[0:128, 0:8])
    banks.consumed(bi, dve)

    for h8 in range(8):
        hp, lane = h8 // 2, h8 % 2
        b0 = lane * 64
        for qb in range(8):
            q0 = qb * 512
            nkt = 4 * qb + 4
            p.wait(pe, dve, dve.n)
            for kt in range(nkt):
                bi, bank = banks.get()
                last = pe.h.matmul(bank[:, :],
                                   kf[b0:b0 + 64, hp * TFULL + kt * 128: hp * TFULL + (kt + 1) * 128],
                                   qf[b0:b0 + 64, hp * TFULL + q0: hp * TFULL + q0 + 512],
                                   start=True, stop=True, tile_position=(b0, 0))
                pe.inc(last)
                p.wait(act, pe, pe.n)
                act.h.activation(PT[:, kt * 512:(kt + 1) * 512], bank[:, :], AF.Exp,
                                 bias=mcols[:, h8:h8 + 1])
                banks.consumed(bi, act)
                jj = kt - 4 * qb
                if jj >= 0:
                    p.wait(dve, act, act.n)
                    if jj > 0:
                        dve.h.memset(PT[:, kt * 512: kt * 512 + jj * 128], 0.0)
                    dve.h.tensor_tensor(PT[:, kt * 512 + jj * 128: kt * 512 + (jj + 1) * 128],
                                        PT[:, kt * 512 + jj * 128: kt * 512 + (jj + 1) * 128],
                                        triu[:, :], ALU.mult)
                    p.wait(pe, dve, dve.n)
                else:
                    p.wait(pe, act, act.n)
                last = pe.h.matmul(zps[0:1, :], onesc[:, 0:1], PT[:, kt * 512:(kt + 1) * 512],
                                   start=(kt == 0), stop=(kt == nkt - 1))
                if kt == nkt - 1:
                    pe.inc(last)
            bi, bank = banks.get()
            for kt in range(nkt):
                last = pe.h.matmul(bank[0:64, :],
                                   V_s[:, kt * 512 + h8 * 64: kt * 512 + (h8 + 1) * 64],
                                   PT[:, kt * 512:(kt + 1) * 512],
                                   start=(kt == 0), stop=(kt == nkt - 1))
            pe.inc(last)
            p.wait(dve, pe, pe.n)
            dve.h.reciprocal(nrm[0:1, :], zps[0:1, :])
            p.wait(pe, dve, dve.n)
            bi2, bank2 = banks.get()
            last = pe.h.matmul(bank2[0:64, :], ones1f[0:1, 0:64], nrm[0:1, :],
                               start=True, stop=True)
            pe.inc(last)
            p.wait(dve, pe, pe.n)
            dve.h.tensor_copy(rzb[0:64, :], bank2[0:64, :])
            p.wait(dve, qs, qs.n)
            dve.h.tensor_tensor(cstage[0:64, :], bank[0:64, :], rzb[0:64, :], ALU.mult)
            banks.consumed(bi, dve)
            banks.consumed(bi2, dve)
            p.wait(qs, dve, dve.n)
            qs.inc(qs.h.dma_start(out=ctx_d[h8 * 64:(h8 + 1) * 64, q0:q0 + 512],
                                  in_=cstage[0:64, :]))


def build_k3():
    p = Prog()
    IO = {}
    IO["h1f"] = p.nc.dram_tensor("h1f", [D, TFULL], BF16, kind="ExternalInput").ap()
    IO["wq_my"] = p.din("wq_my", [D, 512])
    IO["wk_my"] = p.din("wk_my", [D, 512])
    IO["wv_my"] = p.din("wv_my", [D, 512])
    IO["triu"] = p.nc.dram_tensor("triu", [128, 128], BF16, kind="ExternalInput").ap()
    IO["ctxo"] = p.dout("ctxo", [512, TFULL])
    p.push(); emit_k3(p, IO); p.pop()
    return p.finish()


def k3_inputs(inputs, h1_full_bf16, triu):
    c = np.ascontiguousarray
    maps = []
    for core in range(8):
        b, r = core // 2, core % 2
        hs = r * 512
        m = dict(
            h1f=c(h1_full_bf16[b].T),
            wq_my=c(inputs["L0_ffn_wq"][:, hs:hs + 512]),
            wk_my=c(inputs["L0_ffn_wk"][:, hs:hs + 512]),
            wv_my=c(inputs["L0_ffn_wv"][:, hs:hs + 512]),
            triu=triu)
        maps.append(m)
    return maps


def emit_k1(p, IO, attn_bf16=False):
    pe, act, dve, qs, qg = p.pe, p.act, p.dve, p.qs, p.qg
    x0_d = IO["x0f"]; l1w_d = IO["ln1w_c"]; l1b_d = IO["ln1b_c"]
    wq_d = IO["wq_my"]; wk_d = IO["wk_my"]; wv_d = IO["wv_my"]; dec_d = IO["dec_my"]
    qb_d = IO["qb_c"]; kb_d = IO["kb_c"]; vb_d = IO["vb_r"]; db_d = IO["db_r"]
    ui_d = IO["u_incl"]; ur_d = IO["u_rvex"]; id_d = IO["identb"]
    attn_d = IO["attn"]
    outf_dram = IO["outf"]

    qf = p.sb("qf", [128, 4 * TFULL], BF16)
    kf = p.sb("kf", [128, 4 * TFULL], BF16)
    KT = p.sb("KT", [128, 32 * 512], BF16)
    V_s = p.sb("V_s", [128, 32 * 520], BF16)
    wq16 = p.sb("wq16", [128, NDT * 512], BF16)
    wk16 = p.sb("wk16", [128, NDT * 512], BF16)
    wv16 = p.sb("wv16", [128, NDT * 512], BF16)
    dec16 = p.sb("dec16", [128, NDT * 8], BF16)
    xc32 = p.sb("xc32", [128, NDT * 512], F32)
    xnc = p.sb("xnc", [128, NDT * 512], BF16)
    sq_sc = p.sb("sq_sc", [128, NDT * 512], BF16)
    rows_s = p.sb("rows_s", [1, 512], F32)
    rows_q = p.sb("rows_q", [1, 512], F32)
    tmp_row = p.sb("tmp_row", [1, 512], F32)
    tsc = p.sb("tsc", [128, 512], F32)
    phsc = p.sb("phsc", [128, 512], BF16)
    phsc2 = p.sb("phsc2", [128, 512], BF16)
    ll_tm = p.sb("ll_tm", [128, 32 * 8], F32)
    c_tm = p.sb("c_tm", [128, 8], F32)
    e_tm = p.sb("e_tm", [128, 8], F32)
    cl_s = p.sb("cl_s", [128, 8], F32)
    nll = p.sb("nll", [128, 8], F32)
    nllb = p.sb("nllb", [128, 128], F32)
    ones2d = p.sb("ones2d", [128, 128], F32)
    lamr = p.sb("lamr", [1, 8], F32)
    lamc = p.sb("lamc", [64, 8], F32)
    Sst = p.sb("Sst", [64, 8 * 65], F32)
    S16 = p.sb("S16", [128, 65], BF16)
    Ve = p.sb("Ve", [128, 65], BF16)
    dabs = p.sb("dabs", [128, 128], F32)
    Dm = p.sb("Dm", [128, 128], BF16)
    StD = p.sb("StD", [128, 128], BF16)
    outb = p.sb("outb", [128, 520], F32)
    astage = p.sb("astage", [128, 512], BF16 if attn_bf16 else F32)
    l1w_s = p.sb("l1w_s", [128, NDT], F32)
    l1b_s = p.sb("l1b_s", [128, NDT], F32)
    qb_s = p.sb("qb_s", [128, 4], F32)
    nqb_s = p.sb("nqb_s", [128, 4], F32)
    kb_s = p.sb("kb_s", [128, 4], F32)
    nkb_s = p.sb("nkb_s", [128, 4], F32)
    vb_s = p.sb("vb_s", [1, 512], BF16)
    db_s = p.sb("db_s", [1, 8], BF16)
    ui_s = p.sb("ui_s", [128, 128], F32)
    ur_s = p.sb("ur_s", [128, 128], F32)
    idb_s = p.sb("idb_s", [128, 128], BF16)
    ones1f = p.sb("ones1f", [1, 128], F32)
    onescf = p.sb("onescf", [128, 1], F32)
    onesc = p.sb("onesc", [128, 1], BF16)
    ones1b = p.sb("ones1b", [1, 128], BF16)

    banks = Pool8(p, n=7)
    tps = p.ps("tps", [128, 128], BF16)

    for s, dd in [(l1w_s, l1w_d), (l1b_s, l1b_d), (qb_s, qb_d), (kb_s, kb_d),
                  (ui_s, ui_d), (ur_s, ur_d)]:
        qs.inc(qs.h.dma_start(out=s[:, :], in_=dd))
    qs.inc(qs.h.dma_start(out=idb_s[:, :], in_=id_d))
    for s, dd in [(wq16, wq_d), (wk16, wk_d), (wv16, wv_d)]:
        qg.inc(qg.h.dma_start(out=sb3(s, NDT, 512), in_=fm3(dd)))
    qg.inc(qg.h.dma_start(out=sb3(dec16, NDT, 8), in_=fm3(dec_d)))
    qg.inc(qg.h.dma_start(out=vb_s[:, :], in_=vb_d))
    qg.inc(qg.h.dma_start(out=db_s[:, :], in_=db_d))
    dve.h.memset(ones1f[:, :], 1.0)
    dve.h.memset(ones1b[:, :], 1.0)
    dve.h.memset(onescf[:, :], 1.0)
    dve.h.memset(onesc[:, :], 1.0)
    dve.h.memset(ones2d[:, :], 1.0)
    dve.h.memset(Sst[:, :], 0.0)
    dve.h.memset(V_s[:, :], 1.0)
    p.wait(act, qs, qs.n)
    act.h.activation(nqb_s[:, :], qb_s[:, :], AF.Identity, scale=-1.0)
    act.h.activation(nkb_s[:, :], kb_s[:, :], AF.Identity, scale=-1.0)

    p.wait(pe, dve, dve.n)
    p.wait(pe, qg, qg.n)
    p.wait(pe, qs, qs.n)
    for tc in range(8):
        t0 = tc * 512
        p.wait(qs, dve, dve.n)
        qs.inc(qs.h.dma_start(out=sb3(xc32, NDT, 512), in_=fm3(x0_d[:, t0:t0 + 512])))
        p.wait(act, qs, qs.n)
        p.wait(dve, qs, qs.n)
        row_stats(p, banks, xc32, sq_sc, onesc, rows_s, rows_q, 512, HTl=512)
        p.wait(dve, pe, pe.n)
        ln_finalize_rows(p, rows_s, rows_q, tmp_row, 512, float(D), 1e-5)
        p.wait(pe, dve, dve.n)
        bi1, bankR = banks.get()
        pe.h.matmul(bankR[:, :], ones1f[:, :], rows_q[0:1, 0:512], start=True, stop=True)
        bi2, bankM = banks.get()
        last = pe.h.matmul(bankM[:, :], ones1f[:, :], rows_s[0:1, 0:512], start=True, stop=True)
        pe.inc(last)
        p.wait(dve, pe, pe.n)
        for dt in range(NDT):
            sl = xc32[:, dt * 512:(dt + 1) * 512]
            dve.h.tensor_tensor(tsc[:, :], sl, bankR[:, :], ALU.mult)
            dve.h.tensor_tensor(tsc[:, :], tsc[:, :], bankM[:, :], ALU.add)
            dve.h.tensor_scalar(xnc[:, dt * 512:(dt + 1) * 512], tsc[:, :],
                                l1w_s[:, dt:dt + 1], l1b_s[:, dt:dt + 1],
                                ALU.mult, ALU.add)
        banks.consumed(bi1, dve)
        banks.consumed(bi2, dve)
        p.wait(pe, dve, dve.n)
        for wbuf, dst, bcol, nbcol in ((wq16, qf, qb_s, nqb_s), (wk16, kf, kb_s, nkb_s)):
            for hp in range(4):
                bi, bank = banks.get()
                for dt in range(NDT):
                    last = pe.h.matmul(bank[:, :],
                                       wbuf[:, dt * 512 + hp * 128: dt * 512 + (hp + 1) * 128],
                                       xnc[:, dt * 512:(dt + 1) * 512],
                                       start=(dt == 0), stop=(dt == NDT - 1))
                pe.inc(last)
                p.wait(act, pe, pe.n)
                p.wait(act, dve, dve.n)
                act.h.activation(phsc[:, :], bank[:, :], AF.Relu, bias=bcol[:, hp:hp + 1])
                act.h.activation(phsc2[:, :], bank[:, :], AF.Relu, scale=-1.0,
                                 bias=nbcol[:, hp:hp + 1])
                act.h.activation(phsc2[:, :], phsc2[:, :], AF.Exp, scale=-1.0)
                banks.consumed(bi, act)
                p.wait(dve, act, act.n)
                dve.h.tensor_tensor(dst[:, hp * TFULL + t0: hp * TFULL + t0 + 512],
                                    phsc[:, :], phsc2[:, :], ALU.add)
        p.wait(pe, dve, dve.n)
        for hp in range(4):
            for ttl in range(4):
                tt = tc * 4 + ttl
                p.wait(pe, dve, dve.n)
                last = pe.h.transpose(tps[:, :],
                                      kf[:, hp * TFULL + t0 + ttl * 128: hp * TFULL + t0 + (ttl + 1) * 128],
                                      idb_s[:, :])
                pe.inc(last)
                p.wait(dve, pe, pe.n)
                dve.h.tensor_copy(KT[:, tt * 512 + hp * 128: tt * 512 + (hp + 1) * 128],
                                  tps[:, :])
        p.wait(pe, dve, dve.n)
        for ttl in range(4):
            tt = tc * 4 + ttl
            bi, bank = banks.get()
            for dt in range(NDT):
                pe.h.matmul(bank[:, :],
                            xnc[:, dt * 512 + ttl * 128: dt * 512 + (ttl + 1) * 128],
                            wv16[:, dt * 512:(dt + 1) * 512],
                            start=(dt == 0), stop=False)
            last = pe.h.matmul(bank[:, :], ones1b[:, :],
                               vb_s[:, :], start=False, stop=True)
            pe.inc(last)
            p.wait(dve, pe, pe.n)
            dve.h.tensor_copy(
                V_s[:, tt * 520: tt * 520 + 520].rearrange("p (h v) -> p h v", v=65)[:, :, 0:64],
                bank[:, :].rearrange("p (h v) -> p h v", v=64))
            banks.consumed(bi, dve)
            bi, bank = banks.get()
            for dt in range(NDT):
                pe.h.matmul(bank[:, 0:8],
                            xnc[:, dt * 512 + ttl * 128: dt * 512 + (ttl + 1) * 128],
                            dec16[:, dt * 8:(dt + 1) * 8],
                            start=(dt == 0), stop=False)
            last = pe.h.matmul(bank[:, 0:8], ones1b[:, :],
                               db_s[:, :], start=False, stop=True)
            pe.inc(last)
            p.wait(act, pe, pe.n)
            p.wait(act, dve, dve.n)
            act.h.activation(tsc[:, 0:8], bank[:, 0:8], AF.Sigmoid)
            banks.consumed(bi, act)
            p.wait(dve, act, act.n)
            dve.h.tensor_scalar_max(ll_tm[:, tt * 8:(tt + 1) * 8], tsc[:, 0:8], 1e-8)

    p.wait(act, dve, dve.n)
    act.h.activation(ll_tm[:, :], ll_tm[:, :], AF.Ln)

    EPS = 1e-6
    for sweep in range(2):
        fwd = (sweep == 0)
        order = range(32) if fwd else range(31, -1, -1)
        p.wait(dve, act, act.n)
        dve.h.memset(Sst[:, :], 0.0)
        for tt in order:
            p.wait(pe, dve, dve.n)
            bi1, cps = banks.get()
            pe.h.matmul(cps[:, 0:8], ui_s[:, :], ll_tm[:, tt * 8:(tt + 1) * 8],
                        start=True, stop=True)
            bi2, eps2 = banks.get()
            pe.h.matmul(eps2[:, 0:8], ur_s[:, :], ll_tm[:, tt * 8:(tt + 1) * 8],
                        start=True, stop=True)
            bi3, lps = banks.get()
            last = pe.h.matmul(lps[0:1, 0:8], onescf[:, 0:1], ll_tm[:, tt * 8:(tt + 1) * 8],
                               start=True, stop=True)
            pe.inc(last)
            p.wait(act, pe, pe.n)
            act.h.activation(c_tm[:, :], cps[:, 0:8], AF.Exp)
            act.h.activation(e_tm[:, :], eps2[:, 0:8], AF.Exp)
            act.h.activation(lamr[:, :], lps[0:1, 0:8], AF.Exp)
            p.wait(dve, act, act.n)
            dve.h.tensor_copy(cl_s[:, :], cps[:, 0:8])
            dve.h.tensor_scalar_mul(nll[:, :], ll_tm[:, tt * 8:(tt + 1) * 8], -1.0)
            banks.consumed(bi1, dve)
            banks.consumed(bi2, dve)
            banks.consumed(bi3, act)
            p.wait(pe, dve, dve.n)
            bi4, lbc = banks.get()
            last = pe.h.matmul(lbc[0:64, 0:8], ones1f[0:1, 0:64], lamr[:, :],
                               start=True, stop=True)
            pe.inc(last)
            p.wait(dve, pe, pe.n)
            dve.h.tensor_copy(lamc[:, :], lbc[0:64, 0:8])
            banks.consumed(bi4, dve)
            if fwd:
                p.wait(dve, qs, qs.n)
                dve.h.memset(outb[:, :], 0.0)
            else:
                qs.inc(qs.h.dma_start(out=outb[:, :],
                                      in_=outf_dram.ap()[tt * 128:(tt + 1) * 128, :]))
                p.wait(dve, qs, qs.n)
            for h8 in range(8):
                hp, lane = h8 // 2, h8 % 2
                b0 = lane * 64
                S_h = Sst[:, h8 * 65:(h8 + 1) * 65]
                p.wait(dve, act, act.n)
                dve.h.tensor_copy(S16[b0:b0 + 64, :], S_h)
                p.wait(pe, dve, dve.n)
                bi5, ops = banks.get()
                last = pe.h.matmul(ops[:, 0:65],
                                   qf[b0:b0 + 64, hp * TFULL + tt * 128: hp * TFULL + (tt + 1) * 128],
                                   S16[b0:b0 + 64, :], start=True, stop=True,
                                   tile_position=(b0, 0))
                pe.inc(last)
                p.wait(dve, pe, pe.n)
                scol = c_tm[:, h8:h8 + 1] if fwd else e_tm[:, h8:h8 + 1]
                dve.h.tensor_scalar(tsc[:, 0:65], ops[:, 0:65], scol, None, ALU.mult)
                dve.h.tensor_tensor(outb[:, h8 * 65:(h8 + 1) * 65],
                                    outb[:, h8 * 65:(h8 + 1) * 65], tsc[:, 0:65], ALU.add)
                banks.consumed(bi5, dve)
                if fwd:
                    p.wait(pe, dve, dve.n)
                    bi6, stp = banks.get()
                    pe.h.matmul(stp[:, 0:128],
                                kf[b0:b0 + 64, hp * TFULL + tt * 128: hp * TFULL + (tt + 1) * 128],
                                qf[b0:b0 + 64, hp * TFULL + tt * 128: hp * TFULL + (tt + 1) * 128],
                                start=True, stop=True, tile_position=(b0, 0))
                    bi7, dps = banks.get()
                    pe.h.matmul(dps[:, 0:128], ui_s[:, :],
                                ll_tm[:, tt * 8 + h8: tt * 8 + h8 + 1].broadcast_to([128, 128]),
                                start=True, stop=False)
                    last = pe.h.matmul(dps[:, 0:128],
                                       nll[:, h8:h8 + 1].broadcast_to([128, 128]),
                                       ui_s[:, :], start=False, stop=True)
                    pe.inc(last)
                    p.wait(act, pe, pe.n)
                    act.h.activation(dabs[:, :], dps[:, 0:128], AF.Abs)
                    act.h.activation(Dm[:, :], dabs[:, :], AF.Exp, scale=-1.0)
                    p.wait(dve, act, act.n)
                    dve.h.tensor_tensor(StD[:, :], stp[:, 0:128], Dm[:, :], ALU.mult)
                    banks.consumed(bi6, dve)
                    banks.consumed(bi7, act)
                    p.wait(pe, dve, dve.n)
                    bi8, o2 = banks.get()
                    last = pe.h.matmul(o2[:, 0:65], StD[:, :],
                                       V_s[:, tt * 520 + h8 * 65: tt * 520 + (h8 + 1) * 65],
                                       start=True, stop=True)
                    pe.inc(last)
                    p.wait(dve, pe, pe.n)
                    dve.h.tensor_tensor(outb[:, h8 * 65:(h8 + 1) * 65],
                                        outb[:, h8 * 65:(h8 + 1) * 65], o2[:, 0:65], ALU.add)
                    banks.consumed(bi8, dve)
                ucol = e_tm[:, h8:h8 + 1] if fwd else c_tm[:, h8:h8 + 1]
                dve.h.tensor_scalar(Ve[:, :],
                                    V_s[:, tt * 520 + h8 * 65: tt * 520 + (h8 + 1) * 65],
                                    ucol, None, ALU.mult)
                p.wait(pe, dve, dve.n)
                bi9, ups = banks.get()
                last = pe.h.matmul(ups[0:64, 0:65],
                                   KT[:, tt * 512 + hp * 128 + b0: tt * 512 + hp * 128 + b0 + 64],
                                   Ve[:, :], start=True, stop=True)
                pe.inc(last)
                p.wait(dve, pe, pe.n)
                dve.h.tensor_scalar(S_h, S_h, lamc[:, h8:h8 + 1], None, ALU.mult)
                dve.h.tensor_tensor(S_h, S_h, ups[0:64, 0:65], ALU.add)
                banks.consumed(bi9, dve)
            if fwd:
                p.wait(qs, dve, dve.n)
                qs.inc(qs.h.dma_start(out=outf_dram.ap()[tt * 128:(tt + 1) * 128, :],
                                      in_=outb[:, :]))
            else:
                p.wait(dve, qs, qs.n)
                for h8 in range(8):
                    dve.h.tensor_scalar_max(tsc[:, 0:1], outb[:, h8 * 65 + 64: h8 * 65 + 65], EPS)
                    dve.h.reciprocal(tsc[:, 0:1], tsc[:, 0:1])
                    dve.h.tensor_scalar(astage[:, h8 * 64:(h8 + 1) * 64],
                                        outb[:, h8 * 65: h8 * 65 + 64],
                                        tsc[:, 0:1], None, ALU.mult)
                p.wait(qs, dve, dve.n)
                qs.inc(qs.h.dma_start(out=attn_d[tt * 128:(tt + 1) * 128, :],
                                      in_=astage[:, :]))


def build_k1():
    p = Prog()
    IO = {n: p.din(n, s) for n, s in [("x0f", [D, TFULL]), ("ln1w_c", [128, NDT]),
          ("ln1b_c", [128, NDT]), ("wq_my", [D, 512]), ("wk_my", [D, 512]),
          ("wv_my", [D, 512]), ("dec_my", [D, 8]), ("qb_c", [128, 4]),
          ("kb_c", [128, 4]), ("vb_r", [1, 512]), ("db_r", [1, 8]),
          ("u_incl", [128, 128]), ("u_rvex", [128, 128])]}
    IO["identb"] = p.nc.dram_tensor("identb", [128, 128], BF16, kind="ExternalInput").ap()
    IO["attn"] = p.dout("attn", [TFULL, 512])
    IO["outf"] = p.dint("outf", [TFULL, 520])
    p.push(); emit_k1(p, IO); p.pop()
    return p.finish()


def k1_inputs(inputs):
    c = np.ascontiguousarray
    col = lambda v: c(v.reshape(-1, 128).T)
    t = np.arange(128)
    u_incl = (t[:, None] <= t[None, :]).astype(np.float32)
    u_rvex = (t[:, None] > t[None, :]).astype(np.float32)
    import ml_dtypes
    identb = np.eye(128, dtype=ml_dtypes.bfloat16)
    maps = []
    for core in range(8):
        b, r = core // 2, core % 2
        hs = r * 512
        m = dict(
            x0f=c(np.asarray(inputs["L0_attn_x"])[b].T),
            ln1w_c=col(inputs["L0_attn_ln1_w"]), ln1b_c=col(inputs["L0_attn_ln1_b"]),
            wq_my=c(inputs["L0_attn_q_w"][:, hs:hs + 512]),
            wk_my=c(inputs["L0_attn_k_w"][:, hs:hs + 512]),
            wv_my=c(inputs["L0_attn_v_w"][:, hs:hs + 512]),
            dec_my=c(inputs["L0_attn_dec_w"][:, r * 8:(r + 1) * 8]),
            qb_c=c(inputs["L0_attn_q_b"][hs:hs + 512].reshape(4, 128).T),
            kb_c=c(inputs["L0_attn_k_b"][hs:hs + 512].reshape(4, 128).T),
            vb_r=c(inputs["L0_attn_v_b"][hs:hs + 512].reshape(1, 512)),
            db_r=c(inputs["L0_attn_dec_b"][r * 8:(r + 1) * 8].reshape(1, 8)),
            u_incl=u_incl, u_rvex=u_rvex, identb=identb)
        maps.append(m)
    return maps


GROUPS = [[0, 1], [2, 3], [4, 5], [6, 7]]


def _ag(p, in_h, out_h):
    g = p.nc.gpsimd
    g.wait_ge(p.qs.sem, p.qs.n)
    g.wait_ge(p.qg.sem, p.qg.n)
    ins = g.collective_compute(
        "AllGather", mybir.AluOpType.bypass,
        replica_groups=GROUPS,
        ins=[in_h.ap().opt()],
        outs=[out_h.ap().opt()],
    )
    ins.then_inc(p.cc.sem, 1)
    p.cc.n += 1


def build_fused(dbg=False):
    p = Prog()
    nc = p.nc
    if dbg:
        _dint = p.dint
        def dint_dbg(name, shape, dt=F32):
            if name in ("x1_i", "x2_i", "x3_i", "ctx2_pre", "ctx4_pre"):
                class H:
                    def __init__(self, ap): self._ap = ap
                    def ap(self): return self._ap
                return H(p.nc.dram_tensor(name, list(shape), dt, kind="ExternalOutput").ap())
            return _dint(name, shape, dt)
        p.dint = dint_dbg
    p.cc = Eng(p, nc.gpsimd, "cc")

    def dins(pairs):
        return {n: p.din(n, s) for n, s in pairs}

    IO1 = dins([("x0f", [D, TFULL]), ("ln1w_c", [128, NDT]), ("ln1b_c", [128, NDT]),
                ("wq_my", [D, 512]), ("wk_my", [D, 512]), ("wv_my", [D, 512]),
                ("dec_my", [D, 8]), ("qb_c", [128, 4]), ("kb_c", [128, 4]),
                ("vb_r", [1, 512]), ("db_r", [1, 8]), ("u_incl", [128, 128]),
                ("u_rvex", [128, 128])])
    IO1["identb"] = nc.dram_tensor("identb", [128, 128], BF16, kind="ExternalInput").ap()
    IO1["outf"] = p.dint("outf", [TFULL, 520])
    attn_b = p.dint("attn_b", [TFULL, 512], BF16)
    IO1["attn"] = attn_b.ap()

    IO2 = dins([("x0my", [D, HT]), ("ow", [D, D]), ("ob_c", [128, NDT]),
                ("ln2w_c", [128, NDT]), ("ln2b_c", [128, NDT]), ("f1w", [D, 4096]),
                ("f1b_c", [128, 32]), ("f2w", [4096, D]), ("f2b_c", [128, NDT]),
                ("l1s_c", [128, NDT]), ("l1h_c", [128, NDT])])
    IO2["x0"] = IO2.pop("x0my")
    x1_i = p.dint("x1_i", [D, HT])
    h1_i = p.dint("h1_i", [D, HT], BF16)
    IO2["x1"] = x1_i.ap()
    IO2["h1"] = h1_i.ap()

    IO3 = {"wq_my": p.din("wq3_my", [D, 512]), "wk_my": p.din("wk3_my", [D, 512]),
           "wv_my": p.din("wv3_my", [D, 512])}
    IO3["triu"] = nc.dram_tensor("triu", [128, 128], BF16, kind="ExternalInput").ap()
    ctx3_i = p.dint("ctx3_i", [512, TFULL], BF16)
    IO3["ctxo"] = ctx3_i.ap()

    IO4 = {"wo": p.din("wo4", [D, D]), "bo_c": p.din("bo4_c", [128, NDT]),
           "l2s_c": p.din("l2s_c", [128, NDT]), "l2h_c": p.din("l2h_c", [128, NDT]),
           "gate_w": p.din("gate_w", [D, 8]), "w1": p.din("w1", [8, D, 2048]),
           "w2": p.din("w2", [8, D, 2048]), "w3": p.din("w3", [8, 2048, D]),
           "ident": p.din("ident", [128, 128])}
    IO4["x1"] = x1_i.ap()
    x2_i = p.dint("x2_i", [D, HT])
    IO4["x2"] = x2_i.ap()

    IO5 = {"wq": p.din("wq5", [D, D]), "wk": p.din("wk5", [D, D]),
           "wv": p.din("wv5", [D, D]), "bq_c": p.din("bq5_c", [128, NDT]),
           "bk_c": p.din("bk5_c", [128, NDT]), "bv_r": p.din("bv5_r", [1, D]),
           "wp": p.din("wp5", [D, D]), "bp_c": p.din("bp5_c", [128, NDT])}
    IO5["xmy"] = x2_i.ap()
    k_b = p.dint("k_b", [D, HT], BF16)
    v_b = p.dint("v_b", [HT, D], BF16)
    kmax_b = p.dint("kmax_b", [1, 4])
    ag_k = p.dint("ag_k", [2 * D, HT], BF16)
    ag_v = p.dint("ag_v", [TFULL, D], BF16)
    ag_km = p.dint("ag_km", [2, 4])
    IO5["k_sc"] = k_b.ap()
    IO5["v_bounce"] = v_b.ap()
    IO5["kmax_b"] = kmax_b.ap()
    IO5["ag_k"] = ag_k.ap()
    IO5["ag_v"] = ag_v.ap()
    IO5["ag_kmax"] = ag_km.ap()
    IO5["q_sc"] = p.dint("q_sc", [D, HT], BF16).ap()
    x3_i = p.dint("x3_i", [D, HT])
    IO5["x3"] = x3_i.ap()

    def mid_cb(p):
        _ag(p, k_b, ag_k)
        _ag(p, v_b, ag_v)
        _ag(p, kmax_b, ag_km)
    IO5["mid_cb"] = mid_cb

    IO6 = dins([("wr_t", [D, 8]), ("br_r", [1, 8]), ("w1t", [8, D, 4096]),
                ("b1c", [8, 128, 32]), ("w2t", [8, 4096, D]), ("b2r", [8, 1, D]),
                ("lnw_c", [128, 8]), ("lnb_c", [128, 8])])
    IO6["ident"] = IO4["ident"]
    IO6["x"] = x3_i.ap()
    IO6["y"] = p.dout("y", [D, HT])

    myt0_d = p.din("my_t0", [1, 1], I32)
    ag1 = p.dint("ag1", [2 * TFULL, 512], BF16)
    ctx2_pre = p.dint("ctx2_pre", [2 * HT, 512], BF16)
    ag2 = p.dint("ag2", [2 * D, HT], BF16)
    ag3 = p.dint("ag3", [2 * 512, TFULL], BF16)
    ctx4_pre = p.dint("ctx4_pre", [D, HT], BF16)

    myt0_s = p.sb("myt0_s", [1, 1], I32)
    p.qg.inc(p.qg.h.dma_start(out=myt0_s[:, :], in_=myt0_d))

    p.push(); emit_k1(p, IO1, attn_bf16=True); p.pop()
    p.barrier()
    _ag(p, attn_b, ag1)
    g = nc.gpsimd
    g.wait_ge(p.cc.sem, p.cc.n)
    g.wait_ge(p.qg.sem, p.qg.n)
    off = g.value_load(myt0_s[0:1, 0:1])
    for r in range(2):
        p.qg.h.dma_start(out=ctx2_pre.ap()[r * HT:(r + 1) * HT, :],
                         in_=ag1.ap()[r * TFULL:(r + 1) * TFULL][bass.ds(off, HT), :])

    def loader2(p, c16):
        p.wait(p.qs, p.qg, p.qg.n)
        for r in range(2):
            for cs in range(4):
                dt = r * 4 + cs
                p.qs.h.dma_start(
                    out=c16[:, dt * HT:(dt + 1) * HT],
                    in_=ctx2_pre.ap()[r * HT:(r + 1) * HT, cs * 128:(cs + 1) * 128],
                    transpose=True)

    p.push(); emit_k2(p, IO2, ctx_loader=loader2); p.pop()
    p.barrier()
    _ag(p, h1_i, ag2)

    def loader3(p, hc, t0):
        p.wait(p.qs, p.cc)
        blk = t0 // HT
        c0 = t0 % HT
        p.qs.h.dma_start(out=sb3(hc, NDT, 512),
                         in_=fm3(ag2.ap()[blk * D:(blk + 1) * D, c0:c0 + 512]))

    p.push(); emit_k3(p, IO3, h1_loader=loader3, ctx_bf16=True); p.pop()
    p.barrier()
    _ag(p, ctx3_i, ag3)
    g.wait_ge(p.cc.sem, p.cc.n)
    off2 = g.value_load(myt0_s[0:1, 0:1])
    p.qg.h.dma_start(out=ctx4_pre.ap(), in_=ag3.ap()[:, bass.ds(off2, HT)])

    def loader4(p, c16):
        p.wait(p.qg, p.qg, p.qg.n)
        p.qg.h.dma_start(out=sb3(c16, NDT, HT), in_=fm3(ctx4_pre.ap()))

    p.push(); emit_k4(p, IO4, ctx_loader=loader4); p.pop()
    p.barrier()

    p.push(); emit_k5(p, IO5, fused=True); p.pop()
    p.barrier()

    p.push(); emit_k6(p, IO6); p.pop()
    if dbg:
        p.barrier()
        for nm, h, shape, dt in [("dbg_attn", attn_b, [TFULL, 512], BF16),
                                 ("dbg_h1", h1_i, [D, HT], BF16),
                                 ("dbg_ctx3", ctx3_i, [512, TFULL], BF16)]:
            dd = nc.dram_tensor(nm, shape, dt, kind="ExternalOutput").ap()
            p.qs.h.dma_start(out=dd, in_=h.ap())
    return p.finish()


def fused_inputs(inputs):
    import ml_dtypes
    c = np.ascontiguousarray
    col = lambda v: c(v.reshape(-1, 128).T)
    x0 = np.asarray(inputs["L0_attn_x"])
    t = np.arange(128)
    u_incl = (t[:, None] <= t[None, :]).astype(np.float32)
    u_rvex = (t[:, None] > t[None, :]).astype(np.float32)
    identb = np.eye(128, dtype=ml_dtypes.bfloat16)
    triu = np.triu(np.ones((128, 128))).astype(ml_dtypes.bfloat16)
    wqkv = inputs["L1_attn_w_qkv"]
    bqkv = inputs["L1_attn_b_qkv"]
    base = dict(
        ln1w_c=col(inputs["L0_attn_ln1_w"]), ln1b_c=col(inputs["L0_attn_ln1_b"]),
        u_incl=u_incl, u_rvex=u_rvex, identb=identb, triu=triu,
        ow=c(inputs["L0_attn_o_w"]), ob_c=col(inputs["L0_attn_o_b"]),
        ln2w_c=col(inputs["L0_attn_ln2_w"]), ln2b_c=col(inputs["L0_attn_ln2_b"]),
        f1w=c(inputs["L0_attn_f1_w"]), f1b_c=col(inputs["L0_attn_f1_b"]),
        f2w=c(inputs["L0_attn_f2_w"]), f2b_c=col(inputs["L0_attn_f2_b"]),
        l1s_c=col(inputs["L0_ffn_ln1_scale"]), l1h_c=col(inputs["L0_ffn_ln1_shift"]),
        wo4=c(inputs["L0_ffn_wo"]), bo4_c=col(inputs["L0_ffn_bo"]),
        l2s_c=col(inputs["L0_ffn_ln2_scale"]), l2h_c=col(inputs["L0_ffn_ln2_shift"]),
        gate_w=c(inputs["L0_ffn_gate_w"]),
        w1=c(inputs["L0_ffn_w1"]), w2=c(inputs["L0_ffn_w2"]), w3=c(inputs["L0_ffn_w3"]),
        ident=np.eye(128, dtype=np.float32),
        wq5=c(wqkv[:, 0:1024]), wk5=c(wqkv[:, 1024:2048]), wv5=c(wqkv[:, 2048:3072]),
        bq5_c=col(bqkv[0:1024]), bk5_c=col(bqkv[1024:2048]),
        bv5_r=c(bqkv[2048:3072].reshape(1, -1)),
        wp5=c(inputs["L1_attn_w_proj"]), bp5_c=col(inputs["L1_attn_b_proj"]),
        wr_t=c(inputs["L1_ffn_Wr"].T), br_r=c(inputs["L1_ffn_br"].reshape(1, 8)),
        w1t=c(inputs["L1_ffn_W1"].transpose(0, 2, 1)),
        b1c=c(inputs["L1_ffn_b1"].reshape(8, 32, 128).transpose(0, 2, 1)),
        w2t=c(inputs["L1_ffn_W2"].transpose(0, 2, 1)),
        b2r=c(inputs["L1_ffn_b2"].reshape(8, 1, 1024)),
        lnw_c=c(inputs["L1_ffn_ln_w"].reshape(8, 128).T),
        lnb_c=c(inputs["L1_ffn_ln_b"].reshape(8, 128).T))
    maps = []
    for core in range(8):
        b, r = core // 2, core % 2
        hs = r * 512
        m = dict(base)
        m["x0f"] = c(x0[b].T)
        m["x0my"] = c(x0[b, r * HT:(r + 1) * HT, :].T)
        m["wq_my"] = c(inputs["L0_attn_q_w"][:, hs:hs + 512])
        m["wk_my"] = c(inputs["L0_attn_k_w"][:, hs:hs + 512])
        m["wv_my"] = c(inputs["L0_attn_v_w"][:, hs:hs + 512])
        m["dec_my"] = c(inputs["L0_attn_dec_w"][:, r * 8:(r + 1) * 8])
        m["qb_c"] = c(inputs["L0_attn_q_b"][hs:hs + 512].reshape(4, 128).T)
        m["kb_c"] = c(inputs["L0_attn_k_b"][hs:hs + 512].reshape(4, 128).T)
        m["vb_r"] = c(inputs["L0_attn_v_b"][hs:hs + 512].reshape(1, 512))
        m["db_r"] = c(inputs["L0_attn_dec_b"][r * 8:(r + 1) * 8].reshape(1, 8))
        m["wq3_my"] = c(inputs["L0_ffn_wq"][:, hs:hs + 512])
        m["wk3_my"] = c(inputs["L0_ffn_wk"][:, hs:hs + 512])
        m["wv3_my"] = c(inputs["L0_ffn_wv"][:, hs:hs + 512])
        m["my_t0"] = np.array([[r * HT]], dtype=np.int32)
        maps.append(m)
    return maps


def kernel_fused(inputs, dbg=False, res_out=None):
    nc = build_fused(dbg=dbg)
    res = run_bass_kernel_spmd(nc, fused_inputs(inputs), core_ids=list(range(8)))
    if res_out is not None:
        res_out.append(res)
    out = np.zeros((4, TFULL, D), dtype=np.float32)
    for core in range(8):
        b, h = core // 2, core % 2
        out[b, h * HT:(h + 1) * HT, :] = res.results[core]["y"].T
    return out


def kernel(**inputs):
    inputs = {k: np.asarray(v) for k, v in inputs.items()}
    return kernel_fused(inputs)


# version 03 (final)
# speedup vs baseline: 1.0817x; 1.0817x over previous
import contextlib
import numpy as np
import concourse.bass as bass
import concourse.mybir as mybir
from concourse.bass_utils import run_bass_kernel_spmd

F32 = mybir.dt.float32
BF16 = mybir.dt.bfloat16
I32 = mybir.dt.int32
AF = mybir.ActivationFunctionType
ALU = mybir.AluOpType
AX = mybir.AxisListType.X

K1_STAGES = {"inter", "intra", "update"}
K1_SWEEPS = 2
K1_NTT = 32
D = 1024
NDT = 8
TFULL = 4096
HT = 2048
NTT = HT // 128


class _AutoProxy:

    def __init__(self, eng):
        object.__setattr__(self, "_e", eng)

    def __getattr__(self, name):
        e = self._e
        f = getattr(e.raw, name)
        if name in ("wait_ge",) or not callable(f):
            return f

        def wrapped(*a, **k):
            if e.n > 0:
                e.raw.wait_ge(e.sem, e.n)
            r = f(*a, **k)
            if hasattr(r, "then_inc"):
                r.then_inc(e.sem, 1)
                e.n += 1
            return r
        return wrapped


class _DmaProxy:

    def __init__(self, eng):
        object.__setattr__(self, "_e", eng)

    def __getattr__(self, name):
        e = self._e
        f = getattr(e.raw, name)
        if name != "dma_start" or not callable(f):
            return f

        def wrapped(*a, **k):
            if e.n > 0:
                e.raw.wait_ge(e.sem, e.n)
            r = f(*a, **k)
            r.then_inc(e.sem, 16)
            e.n += 16
            return r
        return wrapped


class Eng:
    def __init__(self, prog, handle, name, step=1, auto=False, dma=False):
        self.p = prog
        self.raw = handle
        self.name = name
        self.step = step
        self.auto = auto
        self.sem = prog.es.enter_context(prog.nc.semaphore("sem_" + name))
        self.n = 0
        self._waited = {}
        if auto:
            self.h = _AutoProxy(self)
        elif dma:
            self.h = _DmaProxy(self)
        else:
            self.h = handle

    def inc(self, instr, k=None):
        if self.auto or isinstance(self.h, _DmaProxy):
            return self.n
        k = self.step if k is None else k
        instr.then_inc(self.sem, k)
        self.n += k
        return self.n


class Prog:
    def __init__(self):
        self.nc = bass.Bass("TRN2", target_bir_lowering=False)
        self.es = contextlib.ExitStack()
        nc = self.nc
        self.pe = Eng(self, nc.tensor, "pe")
        self.act = Eng(self, nc.scalar, "act", auto=True)
        self.dve = Eng(self, nc.vector, "dve", auto=True)
        self.qs = Eng(self, nc.sync, "qs", step=16, dma=True)
        self.qg = Eng(self, nc.gpsimd, "qg", step=16, dma=True)
        self.engines = [self.pe, self.act, self.dve, self.qs, self.qg]

    def wait(self, consumer, producer, n=None):
        tgt = producer.n if n is None else n
        if tgt <= 0:
            return
        key = consumer.name
        last = producer._waited.get(key, -1)
        if tgt <= last:
            return
        consumer.h.wait_ge(producer.sem, tgt)
        producer._waited[key] = tgt

    def push(self):
        if not hasattr(self, "_stacks"):
            self._stacks = []
        self._stacks.append(contextlib.ExitStack())

    def pop(self):
        self._stacks.pop().close()

    def _stack(self):
        st = getattr(self, "_stacks", None)
        return st[-1] if st else self.es

    def barrier(self):
        for c in self.engines:
            for pr in self.engines:
                if c is not pr:
                    self.wait(c, pr)

    def sb(self, name, shape, dt):
        name = f"{name}_{self._uid()}"
        return self._stack().enter_context(self.nc.sbuf_tensor(name, list(shape), dt))

    def ps(self, name, shape, dt=F32):
        name = f"{name}_{self._uid()}"
        return self._stack().enter_context(self.nc.psum_tensor(name, list(shape), dt))

    def _uid(self):
        self._n = getattr(self, "_n", 0) + 1
        return self._n

    def din(self, name, shape, dt=F32):
        return self.nc.dram_tensor(name, list(shape), dt, kind="ExternalInput").ap()

    def dout(self, name, shape, dt=F32):
        return self.nc.dram_tensor(name, list(shape), dt, kind="ExternalOutput").ap()

    def dint(self, name, shape, dt=F32):
        return self.nc.dram_tensor(name, list(shape), dt)

    def finish(self):
        for e in self.engines[1:]:
            self.wait(self.engines[0], e)
        self.es.close()
        return self.nc


class Pool8:

    def __init__(self, p, n=8):
        self.p = p
        self.t = [p.ps(f"bank{i}", [128, 512]) for i in range(n)]
        self.guard = [None] * n
        self.i = 0

    def get(self):
        i = self.i % len(self.t)
        self.i += 1
        g = self.guard[i]
        if g is not None:
            self.p.wait(self.p.pe, g[0], g[1])
        return i, self.t[i]

    def consumed(self, i, eng):
        self.guard[i] = (eng, eng.n)


def fm3(dram_ap, p_=128):
    return dram_ap.rearrange("(dt p) t -> p dt t", p=p_)


def sb3(sbuf, ndt, T):
    return sbuf[:, 0:ndt * T].rearrange("p (dt t) -> p dt t", t=T)


def fm_load(p, q, sbuf, dram_ap, Dfull, T, cast=False):
    nd = Dfull // 128
    q.inc(q.h.dma_start(out=sb3(sbuf, nd, T), in_=fm3(dram_ap)))


def fm_store(p, q, dram_ap, sbuf, Dfull, T):
    nd = Dfull // 128
    q.inc(q.h.dma_start(out=fm3(dram_ap), in_=sb3(sbuf, nd, T)))


def row_stats(p, banks, src32, sq_sc, onesc, rows_s, rows_q, T, ndt=NDT, HTl=HT, t0=0, sqw=512, cw0=512):
    pe, act, dve = p.pe, p.act, p.dve
    nch = (T + cw0 - 1) // cw0
    for c in range(nch):
        c0 = c * cw0
        cw = min(cw0, T - c0)
        p.wait(p.act, pe, pe.n)
        for dt in range(ndt):
            last = act.h.activation(sq_sc[:, dt * sqw: dt * sqw + cw],
                                    src32[:, dt * HTl + t0 + c0: dt * HTl + t0 + c0 + cw], AF.Copy)
        act.inc(last)
        p.wait(pe, act, act.n)
        bi1, b1 = banks.get()
        for dt in range(ndt):
            last = pe.h.matmul(b1[0:1, 0:cw], onesc[:, 0:1],
                               sq_sc[:, dt * sqw: dt * sqw + cw],
                               start=(dt == 0), stop=(dt == ndt - 1))
        pe.inc(last)
        p.wait(act, pe, pe.n)
        for dt in range(ndt):
            last = act.h.activation(sq_sc[:, dt * sqw: dt * sqw + cw],
                                    sq_sc[:, dt * sqw: dt * sqw + cw], AF.Square)
        act.inc(last)
        p.wait(pe, act, act.n)
        bi2, b2 = banks.get()
        for dt in range(ndt):
            last = pe.h.matmul(b2[0:1, 0:cw], onesc[:, 0:1],
                               sq_sc[:, dt * sqw: dt * sqw + cw],
                               start=(dt == 0), stop=(dt == ndt - 1))
        pe.inc(last)
        p.wait(dve, pe, pe.n)
        dve.h.tensor_copy(rows_s[0:1, c0:c0 + cw], b1[0:1, 0:cw])
        dve.inc(dve.h.tensor_copy(rows_q[0:1, c0:c0 + cw], b2[0:1, 0:cw]))
        banks.consumed(bi1, dve)
        banks.consumed(bi2, dve)


def ln_finalize_rows(p, rows_s, rows_q, tmp_row, T, nd, eps):
    act, dve = p.act, p.dve
    dve.h.tensor_scalar_mul(rows_s[0:1, 0:T], rows_s[0:1, 0:T], 1.0 / nd)
    dve.h.tensor_scalar_mul(rows_q[0:1, 0:T], rows_q[0:1, 0:T], 1.0 / nd)
    dve.h.tensor_tensor(tmp_row[0:1, 0:T], rows_s[0:1, 0:T], rows_s[0:1, 0:T], ALU.mult)
    dve.inc(dve.h.tensor_tensor(rows_q[0:1, 0:T], rows_q[0:1, 0:T], tmp_row[0:1, 0:T], ALU.subtract))
    dve.inc(dve.h.tensor_scalar_add(rows_q[0:1, 0:T], rows_q[0:1, 0:T], float(eps)))
    p.wait(act, dve, dve.n)
    act.inc(act.h.activation(rows_q[0:1, 0:T], rows_q[0:1, 0:T], AF.Sqrt))
    p.wait(dve, act, act.n)
    dve.h.reciprocal(rows_q[0:1, 0:T], rows_q[0:1, 0:T])
    dve.h.tensor_tensor(rows_s[0:1, 0:T], rows_s[0:1, 0:T], rows_q[0:1, 0:T], ALU.mult)
    dve.inc(dve.h.tensor_scalar_mul(rows_s[0:1, 0:T], rows_s[0:1, 0:T], -1.0))


def emit_k6(p, IO):
    pe, act, dve, qs, qg = p.pe, p.act, p.dve, p.qs, p.qg
    x_d = IO["x"]; wr_d = IO["wr_t"]; br_d = IO["br_r"]; w1_d = IO["w1t"]
    b1_d = IO["b1c"]; w2_d = IO["w2t"]; b2_d = IO["b2r"]; lnw_d = IO["lnw_c"]
    lnb_d = IO["lnb_c"]; id_d = IO["ident"]; y_d = IO["y"]

    FC = 1024
    NFC = 4096 // FC
    NFT = FC // 128

    acc = p.sb("acc", [128, NDT * HT], F32)
    x16 = p.sb("x16", [128, NDT * HT], BF16)
    w1c = p.sb("w1c", [128, NDT * FC], BF16)
    w2c = p.sb("w2c", [128, NFT * D], BF16)
    hbuf = p.sb("hbuf", [128, NFT * HT], BF16)
    wr_s = p.sb("wr_s", [128, NDT * 8], BF16)
    br_s = p.sb("br_s", [1, 8], BF16)
    id32 = p.sb("id32", [128, 128], F32)
    b1_s = p.sb("b1_s", [128, 32], F32)
    b2_s = p.sb("b2_s", [1, D], BF16)
    lnw_s = p.sb("lnw_s", [128, 8], F32)
    lnb_s = p.sb("lnb_s", [128, 8], F32)
    gts = p.sb("gts", [128, NTT * 8], F32)
    logit = p.sb("logit", [128, NTT * 8], F32)
    wrow = p.sb("wrow", [1, HT], BF16)
    wbc = p.sb("wbc", [128, HT], BF16)
    ones1 = p.sb("ones1", [1, 128], BF16)
    ones1f = p.sb("ones1f", [1, 128], F32)
    onesc = p.sb("onesc", [128, 1], BF16)
    rows_s = p.sb("rows_s", [1, 1024], F32)
    rows_q = p.sb("rows_q", [1, 1024], F32)
    tmp_row = p.sb("tmp_row", [1, 1024], F32)
    sq_sc = p.sb("sq_sc", [128, NDT * 256], BF16)
    x3c = p.sb("x3c", [128, 4 * 512], F32)
    tsc = p.sb("tsc", [128, 512], F32)
    tsc2 = p.sb("tsc2", [128, 512], F32)

    banks = Pool8(p)

    fm_load(p, qs, acc, x_d, D, HT)
    qg.inc(qg.h.dma_start(out=sb3(x16, NDT, HT), in_=fm3(x_d)))
    qg.inc(qg.h.dma_start(out=sb3(wr_s, NDT, 8), in_=fm3(wr_d)))
    qg.inc(qg.h.dma_start(out=br_s[:, :], in_=br_d))
    qs.inc(qs.h.dma_start(out=id32[:, :], in_=id_d))
    qs.inc(qs.h.dma_start(out=lnw_s[:, :], in_=lnw_d))
    qs.inc(qs.h.dma_start(out=lnb_s[:, :], in_=lnb_d))
    dve.h.memset(ones1[:, :], 1.0)
    dve.h.memset(ones1f[:, :], 1.0)
    dve.inc(dve.h.memset(onesc[:, :], 1.0))

    p.wait(pe, qg, qg.n)
    p.wait(pe, qs, qs.n)
    p.wait(dve, qs, qs.n)
    p.wait(act, qs, qs.n)
    p.wait(pe, dve, dve.n)
    bi, bank = banks.get()
    for tt in range(NTT):
        for dt in range(NDT):
            pe.h.matmul(bank[:, tt * 8:(tt + 1) * 8],
                        x16[:, dt * HT + tt * 128: dt * HT + (tt + 1) * 128],
                        wr_s[:, dt * 8:(dt + 1) * 8],
                        start=(dt == 0), stop=False)
        last = pe.h.matmul(bank[:, tt * 8:(tt + 1) * 8], ones1[:, :], br_s[:, :],
                           start=False, stop=True)
    pe.inc(last)
    p.wait(dve, pe, pe.n)
    dve.inc(dve.h.tensor_copy(logit[:, :], bank[:, 0:NTT * 8]))
    banks.consumed(bi, dve)

    m1 = p.sb("m1", [128, NTT], F32)
    nm1 = p.sb("nm1", [128, NTT], F32)
    m2 = p.sb("m2", [128, NTT], F32)
    zs = p.sb("zs", [128, NTT], F32)
    tg = p.sb("tg", [128, 8], F32)
    for tt in range(NTT):
        l8 = logit[:, tt * 8:(tt + 1) * 8]
        g8 = gts[:, tt * 8:(tt + 1) * 8]
        p.wait(dve, act, act.n)
        dve.h.reduce_max(m1[:, tt:tt + 1], l8, axis=AX)
        dve.h.reduce_max(nm1[:, tt:tt + 1], l8, axis=AX, negate=True)
        dve.h.tensor_scalar(tg[:, :], l8, m1[:, tt:tt + 1], None, ALU.is_ge)
        dve.h.scalar_tensor_tensor(tg[:, :], tg[:, :], -1e30, l8, ALU.mult, ALU.add)
        dve.h.reduce_max(m2[:, tt:tt + 1], tg[:, :], axis=AX)
        dve.inc(dve.h.tensor_scalar(g8, l8, m2[:, tt:tt + 1], None, ALU.is_ge))
        p.wait(act, dve, dve.n)
        act.inc(act.h.activation(tg[:, :], l8, AF.Exp, bias=nm1[:, tt:tt + 1]))
        p.wait(dve, act, act.n)
        dve.h.tensor_tensor(g8, g8, tg[:, :], ALU.mult)
        dve.h.reduce_sum(zs[:, tt:tt + 1], g8, axis=AX)
        dve.h.reciprocal(zs[:, tt:tt + 1], zs[:, tt:tt + 1])
        dve.inc(dve.h.tensor_scalar_mul(g8, g8, zs[:, tt:tt + 1]))

    st6 = {"w1g": None, "w2g": None, "w1n": 0, "w2n": 0}

    def k6_load_w1(idx):
        e, fc = idx // NFC, idx % NFC
        if st6["w1g"] is not None:
            p.wait(qg, st6["w1g"][0], st6["w1g"][1])
        qg.inc(qg.h.dma_start(
            out=sb3(w1c, NDT, FC),
            in_=fm3(w1_d[e, :, fc * FC:(fc + 1) * FC])))
        st6["w1n"] = qg.n

    def k6_load_w2(idx):
        e, fc = idx // NFC, idx % NFC
        if st6["w2g"] is not None:
            p.wait(qg, st6["w2g"][0], st6["w2g"][1])
        qg.inc(qg.h.dma_start(
            out=sb3(w2c, NFT, D),
            in_=fm3(w2_d[e, fc * FC:(fc + 1) * FC, :])))
        st6["w2n"] = qg.n

    k6_load_w1(0)
    for e in range(8):
        p.wait(pe, dve, dve.n)
        wr_banks = []
        for c in range(4):
            bi, bank = banks.get()
            for j in range(4):
                tt = c * 4 + j
                last = pe.h.matmul(bank[0:1, j * 128:(j + 1) * 128],
                                   gts[:, tt * 8 + e: tt * 8 + e + 1], id32[:, :],
                                   start=True, stop=True)
            wr_banks.append((bi, bank, c))
        pe.inc(last)
        p.wait(dve, pe, pe.n)
        for bi, bank, c in wr_banks:
            last = dve.h.tensor_copy(wrow[0:1, c * 512:(c + 1) * 512], bank[0:1, 0:512])
        dve.inc(last)
        for bi, bank, c in wr_banks:
            banks.consumed(bi, dve)
        p.wait(pe, dve, dve.n)
        wb_banks = []
        for c in range(4):
            bi, bank = banks.get()
            last = pe.h.matmul(bank[:, :], ones1[:, :], wrow[0:1, c * 512:(c + 1) * 512],
                               start=True, stop=True)
            wb_banks.append((bi, bank, c))
        pe.inc(last)
        p.wait(dve, pe, pe.n)
        for bi, bank, c in wb_banks:
            last = dve.h.tensor_copy(wbc[:, c * 512:(c + 1) * 512], bank[:, :])
        dve.inc(last)
        for bi, bank, c in wb_banks:
            banks.consumed(bi, dve)

        for fc in range(NFC):
            idx = e * NFC + fc
            p.wait(qg, act, act.n)
            qg.inc(qg.h.dma_start(out=b1_s[:, 0:NFT],
                                  in_=b1_d[e, :, fc * NFT:(fc + 1) * NFT]))
            if fc == NFC - 1:
                qg.inc(qg.h.dma_start(out=b2_s[:, :], in_=b2_d[e, 0:1, :]))
            b1n = qg.n
            k6_load_w2(idx)
            p.wait(pe, qg, st6["w1n"])
            p.wait(act, qg, b1n)
            p.wait(pe, dve, dve.n)
            for ft in range(NFT):
                for tb in range(4):
                    bi, bank = banks.get()
                    for dt in range(NDT):
                        last = pe.h.matmul(bank[:, :],
                                    w1c[:, dt * FC + ft * 128: dt * FC + (ft + 1) * 128],
                                    x16[:, dt * HT + tb * 512: dt * HT + (tb + 1) * 512],
                                    start=(dt == 0), stop=(dt == NDT - 1))
                    pe.inc(last)
                    p.wait(act, pe, pe.n)
                    act.inc(act.h.activation(
                        hbuf[:, ft * HT + tb * 512: ft * HT + (tb + 1) * 512],
                        bank[:, :], AF.Gelu, bias=b1_s[:, ft:ft + 1]))
                    banks.consumed(bi, act)
            st6["w1g"] = (pe, pe.n)
            if idx + 1 < 8 * NFC:
                k6_load_w1(idx + 1)
            p.wait(dve, act, act.n)
            for ft in range(NFT):
                last = dve.h.tensor_tensor(hbuf[:, ft * HT:(ft + 1) * HT],
                                    hbuf[:, ft * HT:(ft + 1) * HT],
                                    wbc[:, :], ALU.mult)
            dve.inc(last)
            p.wait(pe, dve, dve.n)
            p.wait(pe, qg, st6["w2n"])
            for mt in range(NDT):
                for tb in range(4):
                    bi, bank = banks.get()
                    last_real = (fc != NFC - 1)
                    for ft in range(NFT):
                        last = pe.h.matmul(bank[:, :],
                                    w2c[:, ft * D + mt * 128: ft * D + (mt + 1) * 128],
                                    hbuf[:, ft * HT + tb * 512: ft * HT + (tb + 1) * 512],
                                    start=(ft == 0),
                                    stop=(last_real and ft == NFT - 1))
                    if fc == NFC - 1:
                        last = pe.h.matmul(bank[:, :], b2_s[0:1, mt * 128:(mt + 1) * 128],
                                    wrow[0:1, tb * 512:(tb + 1) * 512],
                                    start=False, stop=True)
                    pe.inc(last)
                    p.wait(dve, pe, pe.n)
                    sl = acc[:, mt * HT + tb * 512: mt * HT + (tb + 1) * 512]
                    dve.inc(dve.h.tensor_tensor(sl, sl, bank[:, :], ALU.add))
                    banks.consumed(bi, dve)
            st6["w2g"] = (pe, pe.n)

    for ln_i in range(2):
        for hf in range(2):
            t0 = hf * 1024
            p.wait(act, dve, dve.n)
            row_stats(p, banks, acc, sq_sc, onesc, rows_s, rows_q, 1024, t0=t0, sqw=256, cw0=256)
            p.wait(dve, pe, pe.n)
            eps = 1e-12 if ln_i == 0 else 1e-5
            ln_finalize_rows(p, rows_s, rows_q, tmp_row, 1024, float(D), eps)
            for c in range(2):
                c0 = t0 + c * 512
                r0 = c * 512
                p.wait(pe, dve, dve.n)
                bi1, bankR = banks.get()
                pe.h.matmul(bankR[:, :], ones1f[:, :], rows_q[0:1, r0:r0 + 512], start=True, stop=True)
                bi2, bankM = banks.get()
                last = pe.h.matmul(bankM[:, :], ones1f[:, :], rows_s[0:1, r0:r0 + 512], start=True, stop=True)
                pe.inc(last)
                p.wait(dve, pe, pe.n)
                if ln_i == 0:
                    p.wait(dve, qs, qs.n)
                for dt in range(NDT):
                    if ln_i == 0 and dt % 4 == 0:
                        p.wait(qs, dve, dve.n)
                        qs.inc(qs.h.dma_start(
                            out=sb3(x3c, 4, 512),
                            in_=fm3(x_d[dt * 128:(dt + 4) * 128, c0:c0 + 512])))
                        p.wait(dve, qs, qs.n)
                    sl = acc[:, dt * HT + c0: dt * HT + c0 + 512]
                    dve.h.tensor_tensor(tsc[:, :], sl, bankR[:, :], ALU.mult)
                    dve.inc(dve.h.tensor_tensor(tsc[:, :], tsc[:, :], bankM[:, :], ALU.add))
                    if ln_i == 0:
                        dve.h.tensor_scalar(tsc2[:, :], tsc[:, :],
                                            lnw_s[:, dt:dt + 1], lnb_s[:, dt:dt + 1],
                                            ALU.mult, ALU.add)
                        dve.inc(dve.h.tensor_tensor(sl, tsc2[:, :],
                                                    x3c[:, (dt % 4) * 512:(dt % 4 + 1) * 512], ALU.add))
                    else:
                        dve.inc(dve.h.tensor_copy(sl, tsc[:, :]))
                banks.consumed(bi1, dve)
                banks.consumed(bi2, dve)

    p.wait(qs, dve, dve.n)
    fm_store(p, qs, y_d, acc, D, HT)


def build_k6():
    p = Prog()
    IO = {n: p.din(n, s) for n, s in [("x", [D, HT]), ("wr_t", [D, 8]), ("br_r", [1, 8]),
          ("w1t", [8, D, 4096]), ("b1c", [8, 128, 32]), ("w2t", [8, 4096, D]),
          ("b2r", [8, 1, D]), ("lnw_c", [128, 8]), ("lnb_c", [128, 8]),
          ("ident", [128, 128])]}
    IO["y"] = p.dout("y", [D, HT])
    p.push(); emit_k6(p, IO); p.pop()
    return p.finish()



def k6_inputs(inputs, x3_full):
    c = np.ascontiguousarray
    wr_t = c(inputs["L1_ffn_Wr"].T)
    br_r = c(inputs["L1_ffn_br"].reshape(1, 8))
    w1t = c(inputs["L1_ffn_W1"].transpose(0, 2, 1))
    b1c = c(inputs["L1_ffn_b1"].reshape(8, 32, 128).transpose(0, 2, 1))
    w2t = c(inputs["L1_ffn_W2"].transpose(0, 2, 1))
    b2r = c(inputs["L1_ffn_b2"].reshape(8, 1, 1024))
    lnw_c = c(inputs["L1_ffn_ln_w"].reshape(8, 128).T)
    lnb_c = c(inputs["L1_ffn_ln_b"].reshape(8, 128).T)
    ident = np.eye(128, dtype=np.float32)
    maps = []
    for core in range(8):
        b, h = core // 2, core % 2
        x = c(x3_full[b, h * HT:(h + 1) * HT, :].T)
        maps.append(dict(x=x, wr_t=wr_t, br_r=br_r, w1t=w1t, b1c=b1c,
                         w2t=w2t, b2r=b2r, lnw_c=lnw_c, lnb_c=lnb_c, ident=ident))
    return maps


def k6_collect(res):
    out = np.zeros((4, TFULL, D), dtype=np.float32)
    for core in range(8):
        b, h = core // 2, core % 2
        out[b, h * HT:(h + 1) * HT, :] = res.results[core]["y"].T
    return out


def plain_ln_to_acc(p, banks, acc, sq_sc, onesc, ones1f, rows_s, rows_q, tmp_row,
                    tsc, eps, per_dt=None, x_extra=None, x3c=None, HTl=HT):
    pe, act, dve, qs = p.pe, p.act, p.dve, p.qs
    for hf in range(HTl // 1024):
        t0 = hf * 1024
        p.wait(act, dve, dve.n)
        row_stats(p, banks, acc, sq_sc, onesc, rows_s, rows_q, 1024, t0=t0, HTl=HTl)
        p.wait(dve, pe, pe.n)
        ln_finalize_rows(p, rows_s, rows_q, tmp_row, 1024, float(D), eps)
        for c in range(2):
            c0 = t0 + c * 512
            r0 = c * 512
            p.wait(pe, dve, dve.n)
            bi1, bankR = banks.get()
            pe.h.matmul(bankR[:, :], ones1f[:, :], rows_q[0:1, r0:r0 + 512], start=True, stop=True)
            bi2, bankM = banks.get()
            last = pe.h.matmul(bankM[:, :], ones1f[:, :], rows_s[0:1, r0:r0 + 512], start=True, stop=True)
            pe.inc(last)
            p.wait(dve, pe, pe.n)
            for dt in range(NDT):
                sl = acc[:, dt * HTl + c0: dt * HTl + c0 + 512]
                dve.h.tensor_tensor(tsc[:, :], sl, bankR[:, :], ALU.mult)
                dve.h.tensor_tensor(tsc[:, :], tsc[:, :], bankM[:, :], ALU.add)
                if per_dt is None:
                    dve.h.tensor_copy(sl, tsc[:, :])
                else:
                    per_dt(dt, c0, tsc, sl)
            banks.consumed(bi1, dve)
            banks.consumed(bi2, dve)


def add_dram_fm(p, banks, acc, dram_ap, x3c, HTl=HT):
    dve, qs = p.dve, p.qs
    for c in range(HTl // 512):
        c0 = c * 512
        for half in range(2):
            dt0 = half * 4
            p.wait(qs, dve, dve.n)
            qs.inc(qs.h.dma_start(
                out=sb3(x3c, 4, 512),
                in_=fm3(dram_ap[dt0 * 128:(dt0 + 4) * 128, c0:c0 + 512])))
            p.wait(dve, qs, qs.n)
            for dt in range(dt0, dt0 + 4):
                sl = acc[:, dt * HTl + c0: dt * HTl + c0 + 512]
                dve.h.tensor_tensor(sl, sl, x3c[:, (dt % 4) * 512:(dt % 4 + 1) * 512], ALU.add)


def topk2_gates(p, banks, x16, w_s, b_row, gts, logit, scratch, HTl=HT, nbias=True):
    pe, act, dve = p.pe, p.act, p.dve
    ones1, m1, nm1, m2, zs, tg = scratch
    ntt = HTl // 128
    bi, bank = banks.get()
    for tt in range(ntt):
        lastmm = None
        for dt in range(NDT):
            lastmm = pe.h.matmul(bank[:, tt * 8:(tt + 1) * 8],
                        x16[:, dt * HTl + tt * 128: dt * HTl + (tt + 1) * 128],
                        w_s[:, dt * 8:(dt + 1) * 8],
                        start=(dt == 0), stop=(dt == NDT - 1 and not nbias))
        if nbias:
            lastmm = pe.h.matmul(bank[:, tt * 8:(tt + 1) * 8], ones1[:, :], b_row[:, :],
                                 start=False, stop=True)
    pe.inc(lastmm)
    p.wait(dve, pe, pe.n)
    dve.h.tensor_copy(logit[:, 0:ntt * 8], bank[:, 0:ntt * 8])
    banks.consumed(bi, dve)
    for tt in range(ntt):
        l8 = logit[:, tt * 8:(tt + 1) * 8]
        g8 = gts[:, tt * 8:(tt + 1) * 8]
        p.wait(dve, act, act.n)
        dve.h.reduce_max(m1[:, tt:tt + 1], l8, axis=AX)
        dve.h.reduce_max(nm1[:, tt:tt + 1], l8, axis=AX, negate=True)
        dve.h.tensor_scalar(tg[:, :], l8, m1[:, tt:tt + 1], None, ALU.is_ge)
        dve.h.scalar_tensor_tensor(tg[:, :], tg[:, :], -1e30, l8, ALU.mult, ALU.add)
        dve.h.reduce_max(m2[:, tt:tt + 1], tg[:, :], axis=AX)
        dve.h.tensor_scalar(g8, l8, m2[:, tt:tt + 1], None, ALU.is_ge)
        p.wait(act, dve, dve.n)
        act.h.activation(tg[:, :], l8, AF.Exp, bias=nm1[:, tt:tt + 1])
        p.wait(dve, act, act.n)
        dve.h.tensor_tensor(g8, g8, tg[:, :], ALU.mult)
        dve.h.reduce_sum(zs[:, tt:tt + 1], g8, axis=AX)
        dve.h.reciprocal(zs[:, tt:tt + 1], zs[:, tt:tt + 1])
        dve.h.tensor_scalar_mul(g8, g8, zs[:, tt:tt + 1])


def wrow_wbc(p, banks, gts, e, id32, ones1, wrow, wbc, HTl=HT):
    pe, dve = p.pe, p.dve
    ntt = HTl // 128
    p.wait(pe, dve, dve.n)
    wr_banks = []
    for c in range(ntt // 4):
        bi, bank = banks.get()
        for j in range(4):
            tt = c * 4 + j
            last = pe.h.matmul(bank[0:1, j * 128:(j + 1) * 128],
                               gts[:, tt * 8 + e: tt * 8 + e + 1], id32[:, :],
                               start=True, stop=True)
        wr_banks.append((bi, bank, c))
    pe.inc(last)
    p.wait(dve, pe, pe.n)
    for bi, bank, c in wr_banks:
        dve.h.tensor_copy(wrow[0:1, c * 512:(c + 1) * 512], bank[0:1, 0:512])
    for bi, bank, c in wr_banks:
        banks.consumed(bi, dve)
    p.wait(pe, dve, dve.n)
    wb_banks = []
    for c in range(HTl // 512):
        bi, bank = banks.get()
        last = pe.h.matmul(bank[:, :], ones1[:, :], wrow[0:1, c * 512:(c + 1) * 512],
                           start=True, stop=True)
        wb_banks.append((bi, bank, c))
    pe.inc(last)
    p.wait(dve, pe, pe.n)
    for bi, bank, c in wb_banks:
        dve.h.tensor_copy(wbc[:, c * 512:(c + 1) * 512], bank[:, :])
    for bi, bank, c in wb_banks:
        banks.consumed(bi, dve)


def emit_k2(p, IO, ctx_loader=None):
    pe, act, dve, qs, qg = p.pe, p.act, p.dve, p.qs, p.qg
    ctx_d = IO.get("ctx"); x0_d = IO["x0"]; ow_d = IO["ow"]; ob_d = IO["ob_c"]
    ln2w_d = IO["ln2w_c"]; ln2b_d = IO["ln2b_c"]; f1_d = IO["f1w"]; f1b_d = IO["f1b_c"]
    f2_d = IO["f2w"]; f2b_d = IO["f2b_c"]; l1s_d = IO["l1s_c"]; l1h_d = IO["l1h_c"]
    x1_d = IO["x1"]; h1_d = IO["h1"]

    FC = 1024
    NFC = 4096 // FC
    NFT = FC // 128

    acc = p.sb("acc", [128, NDT * HT], F32)
    c16 = p.sb("c16", [128, NDT * HT], BF16)
    w_a = p.sb("w_a", [128, NDT * 1024], BF16)
    w_b = p.sb("w_b", [128, NDT * 1024], BF16)
    hbuf = p.sb("hbuf", [128, NFT * HT], BF16)
    ob_s = p.sb("ob_s", [128, NDT], F32)
    ln2w_s = p.sb("ln2w_s", [128, NDT], F32)
    ln2b_s = p.sb("ln2b_s", [128, NDT], F32)
    f1b_s = p.sb("f1b_s", [128, 32], F32)
    f2b_s = p.sb("f2b_s", [128, NDT], F32)
    l1s_s = p.sb("l1s_s", [128, NDT], F32)
    l1h_s = p.sb("l1h_s", [128, NDT], F32)
    onesc = p.sb("onesc", [128, 1], BF16)
    ones1f = p.sb("ones1f", [1, 128], F32)
    rows_s = p.sb("rows_s", [1, 1024], F32)
    rows_q = p.sb("rows_q", [1, 1024], F32)
    tmp_row = p.sb("tmp_row", [1, 1024], F32)
    sq_sc = p.sb("sq_sc", [128, NDT * 512], BF16)
    x3c = p.sb("x3c", [128, 4 * 512], F32)
    tsc = p.sb("tsc", [128, 512], F32)

    banks = Pool8(p)

    fm_load(p, qs, acc, x0_d, D, HT)
    if ctx_loader is None:
        qg.inc(qg.h.dma_start(out=sb3(c16, NDT, HT), in_=fm3(ctx_d)))
    else:
        ctx_loader(p, c16)
    qg.inc(qg.h.dma_start(out=sb3(w_a, NDT, 1024), in_=fm3(ow_d[:, 0:1024])))
    for s, dd in [(ob_s, ob_d), (ln2w_s, ln2w_d), (ln2b_s, ln2b_d), (f1b_s, f1b_d),
                  (f2b_s, f2b_d), (l1s_s, l1s_d), (l1h_s, l1h_d)]:
        qs.inc(qs.h.dma_start(out=s[:, :], in_=dd))
    dve.h.memset(onesc[:, :], 1.0)
    dve.h.memset(ones1f[:, :], 1.0)

    p.wait(pe, qg, qg.n)
    p.wait(pe, qs, qs.n)
    p.wait(dve, qs, qs.n)
    p.wait(act, qs, qs.n)
    for mt in range(NDT):
        for tb in range(4):
            bi, bank = banks.get()
            for dt in range(NDT):
                last = pe.h.matmul(bank[:, :],
                                   w_a[:, dt * 1024 + mt * 128: dt * 1024 + (mt + 1) * 128],
                                   c16[:, dt * HT + tb * 512: dt * HT + (tb + 1) * 512],
                                   start=(dt == 0), stop=(dt == NDT - 1))
            pe.inc(last)
            p.wait(dve, pe, pe.n)
            sl = acc[:, mt * HT + tb * 512: mt * HT + (tb + 1) * 512]
            dve.h.scalar_tensor_tensor(sl, bank[:, :], ob_s[:, mt:mt + 1], sl,
                                       ALU.add, ALU.add)
            banks.consumed(bi, dve)
    p.wait(act, dve, dve.n)
    for hf in range(2):
        t0 = hf * 1024
        row_stats(p, banks, acc, sq_sc, onesc, rows_s, rows_q, 1024, t0=t0)
        p.wait(dve, pe, pe.n)
        ln_finalize_rows(p, rows_s, rows_q, tmp_row, 1024, float(D), 1e-5)
        for c in range(2):
            c0 = t0 + c * 512
            r0 = c * 512
            p.wait(pe, dve, dve.n)
            bi1, bankR = banks.get()
            pe.h.matmul(bankR[:, :], ones1f[:, :], rows_q[0:1, r0:r0 + 512], start=True, stop=True)
            bi2, bankM = banks.get()
            last = pe.h.matmul(bankM[:, :], ones1f[:, :], rows_s[0:1, r0:r0 + 512], start=True, stop=True)
            pe.inc(last)
            p.wait(dve, pe, pe.n)
            for dt in range(NDT):
                sl = acc[:, dt * HT + c0: dt * HT + c0 + 512]
                dve.h.tensor_tensor(tsc[:, :], sl, bankR[:, :], ALU.mult)
                dve.h.tensor_tensor(tsc[:, :], tsc[:, :], bankM[:, :], ALU.add)
                dve.h.tensor_scalar(c16[:, dt * HT + c0: dt * HT + c0 + 512],
                                    tsc[:, :], ln2w_s[:, dt:dt + 1], ln2b_s[:, dt:dt + 1],
                                    ALU.mult, ALU.add)
            banks.consumed(bi1, dve)
            banks.consumed(bi2, dve)
        p.wait(act, dve, dve.n)

    p.wait(pe, dve, dve.n)
    st2 = {"f1g": (pe, pe.n), "f2g": None, "f1n": 0, "f2n": 0}

    def k2_load_f1(fc2):
        p.wait(qg, st2["f1g"][0], st2["f1g"][1])
        qg.inc(qg.h.dma_start(out=sb3(w_a, NDT, FC),
                              in_=fm3(f1_d[:, fc2 * FC:(fc2 + 1) * FC])))
        st2["f1n"] = qg.n

    def k2_load_f2(fc2):
        if st2["f2g"] is not None:
            p.wait(qg, st2["f2g"][0], st2["f2g"][1])
        qg.inc(qg.h.dma_start(out=sb3(w_b, NFT, D),
                              in_=fm3(f2_d[fc2 * FC:(fc2 + 1) * FC, :])))
        st2["f2n"] = qg.n

    k2_load_f1(0)
    for fc in range(NFC):
        k2_load_f2(fc)
        p.wait(pe, qg, st2["f1n"])
        for ft in range(NFT):
            for tb in range(4):
                bi, bank = banks.get()
                for dt in range(NDT):
                    last = pe.h.matmul(bank[:, :],
                                w_a[:, dt * FC + ft * 128: dt * FC + (ft + 1) * 128],
                                c16[:, dt * HT + tb * 512: dt * HT + (tb + 1) * 512],
                                start=(dt == 0), stop=(dt == NDT - 1))
                pe.inc(last)
                p.wait(act, pe, pe.n)
                act.h.activation(hbuf[:, ft * HT + tb * 512: ft * HT + (tb + 1) * 512],
                                 bank[:, :], AF.Gelu,
                                 bias=f1b_s[:, fc * NFT + ft: fc * NFT + ft + 1])
                banks.consumed(bi, act)
        st2["f1g"] = (pe, pe.n)
        if fc + 1 < NFC:
            k2_load_f1(fc + 1)
        p.wait(pe, qg, st2["f2n"])
        p.wait(pe, act, act.n)
        for mt in range(NDT):
            for tb in range(4):
                bi, bank = banks.get()
                for ft in range(NFT):
                    last = pe.h.matmul(bank[:, :],
                                w_b[:, ft * D + mt * 128: ft * D + (mt + 1) * 128],
                                hbuf[:, ft * HT + tb * 512: ft * HT + (tb + 1) * 512],
                                start=(ft == 0), stop=(ft == NFT - 1))
                pe.inc(last)
                p.wait(dve, pe, pe.n)
                sl = acc[:, mt * HT + tb * 512: mt * HT + (tb + 1) * 512]
                if fc == NFC - 1:
                    dve.h.scalar_tensor_tensor(sl, bank[:, :], f2b_s[:, mt:mt + 1], sl,
                                               ALU.add, ALU.add)
                else:
                    dve.h.tensor_tensor(sl, sl, bank[:, :], ALU.add)
                banks.consumed(bi, dve)
        st2["f2g"] = (pe, pe.n)

    add_dram_fm(p, banks, acc, x0_d, x3c)
    plain_ln_to_acc(p, banks, acc, sq_sc, onesc, ones1f, rows_s, rows_q, tmp_row,
                    tsc, 1e-5)
    p.wait(qs, dve, dve.n)
    fm_store(p, qs, x1_d, acc, D, HT)
    def emit_h1(dt, c0, tsc_ap, sl):
        p.dve.h.tensor_scalar(c16[:, dt * HT + c0: dt * HT + c0 + 512],
                              tsc_ap[:, :], l1s_s[:, dt:dt + 1], l1h_s[:, dt:dt + 1],
                              ALU.mult, ALU.add)
    plain_ln_to_acc(p, banks, acc, sq_sc, onesc, ones1f, rows_s, rows_q, tmp_row,
                    tsc, 1e-5, per_dt=emit_h1)
    p.wait(qs, dve, dve.n)
    qs.inc(qs.h.dma_start(out=fm3(h1_d), in_=sb3(c16, NDT, HT)))


def build_k2():
    p = Prog()
    IO = {n: p.din(n, s) for n, s in [("ctx", [D, HT]), ("x0", [D, HT]), ("ow", [D, D]),
          ("ob_c", [128, NDT]), ("ln2w_c", [128, NDT]), ("ln2b_c", [128, NDT]),
          ("f1w", [D, 4096]), ("f1b_c", [128, 32]), ("f2w", [4096, D]),
          ("f2b_c", [128, NDT]), ("l1s_c", [128, NDT]), ("l1h_c", [128, NDT])]}
    IO["x1"] = p.dout("x1", [D, HT])
    IO["h1"] = p.nc.dram_tensor("h1", [D, HT], BF16, kind="ExternalOutput").ap()
    p.push(); emit_k2(p, IO); p.pop()
    return p.finish()


def k2_inputs(inputs, attn_full, x0_full):
    c = np.ascontiguousarray
    col = lambda v: c(v.reshape(-1, 128).T)
    base = dict(
        ow=c(inputs["L0_attn_o_w"]), ob_c=col(inputs["L0_attn_o_b"]),
        ln2w_c=col(inputs["L0_attn_ln2_w"]), ln2b_c=col(inputs["L0_attn_ln2_b"]),
        f1w=c(inputs["L0_attn_f1_w"]), f1b_c=col(inputs["L0_attn_f1_b"]),
        f2w=c(inputs["L0_attn_f2_w"]), f2b_c=col(inputs["L0_attn_f2_b"]),
        l1s_c=col(inputs["L0_ffn_ln1_scale"]), l1h_c=col(inputs["L0_ffn_ln1_shift"]))
    maps = []
    for core in range(8):
        b, h = core // 2, core % 2
        m = dict(base)
        m["ctx"] = c(attn_full[b, h * HT:(h + 1) * HT, :].T)
        m["x0"] = c(x0_full[b, h * HT:(h + 1) * HT, :].T)
        maps.append(m)
    return maps


def emit_k4(p, IO, ctx_loader=None):
    pe, act, dve, qs, qg = p.pe, p.act, p.dve, p.qs, p.qg
    ctx_d = IO.get("ctx"); x1_d = IO["x1"]; wo_d = IO["wo"]; bo_d = IO["bo_c"]
    l2s_d = IO["l2s_c"]; l2h_d = IO["l2h_c"]; gw_d = IO["gate_w"]
    w1_d = IO["w1"]; w2_d = IO["w2"]; w3_d = IO["w3"]; id_d = IO["ident"]
    x2_d = IO["x2"]

    HC = 512
    NHC = 2048 // HC
    NHTl = HC // 128

    acc = p.sb("acc", [128, NDT * HT], F32)
    c16 = p.sb("c16", [128, NDT * HT], BF16)
    w1c = p.sb("w1c", [128, NDT * HC], BF16)
    w2c = p.sb("w2c", [128, NDT * HC], BF16)
    w3c = p.sb("w3c", [128, NHTl * D], BF16)
    hS = p.sb("hS", [128, NHTl * HT], BF16)
    gate_s = p.sb("gate_s", [128, NDT * 8], BF16)
    id32 = p.sb("id32", [128, 128], F32)
    bo_s = p.sb("bo_s", [128, NDT], F32)
    l2s_s = p.sb("l2s_s", [128, NDT], F32)
    l2h_s = p.sb("l2h_s", [128, NDT], F32)
    gts = p.sb("gts", [128, NTT * 8], F32)
    logit = p.sb("logit", [128, NTT * 8], F32)
    wrow = p.sb("wrow", [1, HT], BF16)
    wbc = p.sb("wbc", [128, HT], BF16)
    ones1 = p.sb("ones1", [1, 128], BF16)
    ones1f = p.sb("ones1f", [1, 128], F32)
    onesc = p.sb("onesc", [128, 1], BF16)
    rows_s = p.sb("rows_s", [1, 1024], F32)
    rows_q = p.sb("rows_q", [1, 1024], F32)
    tmp_row = p.sb("tmp_row", [1, 1024], F32)
    sq_sc = p.sb("sq_sc", [128, NDT * 512], BF16)
    x3c = p.sb("x3c", [128, 4 * 512], F32)
    tsc = p.sb("tsc", [128, 512], F32)
    sA = p.sb("sA", [128, 512], BF16)
    m1 = p.sb("m1", [128, NTT], F32)
    nm1 = p.sb("nm1", [128, NTT], F32)
    m2 = p.sb("m2", [128, NTT], F32)
    zs = p.sb("zs", [128, NTT], F32)
    tg = p.sb("tg", [128, 8], F32)

    banks = Pool8(p)

    fm_load(p, qs, acc, x1_d, D, HT)
    if ctx_loader is None:
        qg.inc(qg.h.dma_start(out=sb3(c16, NDT, HT), in_=fm3(ctx_d)))
    else:
        ctx_loader(p, c16)

    qg.inc(qg.h.dma_start(out=sb3(gate_s, NDT, 8), in_=fm3(gw_d)))
    qs.inc(qs.h.dma_start(out=id32[:, :], in_=id_d))
    for s, dd in [(bo_s, bo_d), (l2s_s, l2s_d), (l2h_s, l2h_d)]:
        qs.inc(qs.h.dma_start(out=s[:, :], in_=dd))
    dve.h.memset(ones1[:, :], 1.0)
    dve.h.memset(ones1f[:, :], 1.0)
    dve.h.memset(onesc[:, :], 1.0)

    p.wait(pe, qg, qg.n)
    p.wait(pe, qs, qs.n)
    p.wait(dve, qs, qs.n)
    p.wait(act, qs, qs.n)
    p.wait(pe, dve, dve.n)
    for mh in range(2):
        p.wait(qg, pe, pe.n)
        qg.inc(qg.h.dma_start(out=sb3(w1c, NDT, 512),
                              in_=fm3(wo_d[:, mh * 512:(mh + 1) * 512])))
        p.wait(pe, qg, qg.n)
        for mtl in range(4):
            mt = mh * 4 + mtl
            for tb in range(4):
                bi, bank = banks.get()
                for dt in range(NDT):
                    last = pe.h.matmul(bank[:, :],
                                       w1c[:, dt * 512 + mtl * 128: dt * 512 + (mtl + 1) * 128],
                                       c16[:, dt * HT + tb * 512: dt * HT + (tb + 1) * 512],
                                       start=(dt == 0), stop=(dt == NDT - 1))
                pe.inc(last)
                p.wait(dve, pe, pe.n)
                sl = acc[:, mt * HT + tb * 512: mt * HT + (tb + 1) * 512]
                dve.h.scalar_tensor_tensor(sl, bank[:, :], bo_s[:, mt:mt + 1], sl,
                                           ALU.add, ALU.add)
                banks.consumed(bi, dve)

    p.wait(act, dve, dve.n)
    def emit_h2(dt, c0, tsc_ap, sl):
        p.dve.h.tensor_scalar(c16[:, dt * HT + c0: dt * HT + c0 + 512],
                              tsc_ap[:, :], l2s_s[:, dt:dt + 1], l2h_s[:, dt:dt + 1],
                              ALU.mult, ALU.add)
    plain_ln_to_acc(p, banks, acc, sq_sc, onesc, ones1f, rows_s, rows_q, tmp_row,
                    tsc, 1e-5, per_dt=emit_h2)

    p.wait(pe, dve, dve.n)
    topk2_gates(p, banks, c16, gate_s, None, gts, logit,
                (ones1, m1, nm1, m2, zs, tg), nbias=False)

    st4 = {"w12g": None, "w3g": None, "w12n": 0, "w3n": 0}

    def k4_load_w12(idx):
        e2, hc2 = idx // NHC, idx % NHC
        if st4["w12g"] is not None:
            p.wait(qg, st4["w12g"][0], st4["w12g"][1])
        qg.inc(qg.h.dma_start(out=sb3(w1c, NDT, HC),
                              in_=fm3(w1_d[e2, :, hc2 * HC:(hc2 + 1) * HC])))
        qg.inc(qg.h.dma_start(out=sb3(w2c, NDT, HC),
                              in_=fm3(w2_d[e2, :, hc2 * HC:(hc2 + 1) * HC])))
        st4["w12n"] = qg.n

    def k4_load_w3(idx):
        e2, hc2 = idx // NHC, idx % NHC
        if st4["w3g"] is not None:
            p.wait(qg, st4["w3g"][0], st4["w3g"][1])
        qg.inc(qg.h.dma_start(out=sb3(w3c, NHTl, D),
                              in_=fm3(w3_d[e2, hc2 * HC:(hc2 + 1) * HC, :])))
        st4["w3n"] = qg.n

    st4["w12g"] = (pe, pe.n)
    k4_load_w12(0)
    for e in range(8):
        wrow_wbc(p, banks, gts, e, id32, ones1, wrow, wbc)
        for hc in range(NHC):
            idx = e * NHC + hc
            k4_load_w3(idx)
            p.wait(pe, qg, st4["w12n"])
            p.wait(pe, dve, dve.n)
            for ht in range(NHTl):
                for tb in range(4):
                    bia, banka = banks.get()
                    for dt in range(NDT):
                        lasta = pe.h.matmul(banka[:, :],
                                    w1c[:, dt * HC + ht * 128: dt * HC + (ht + 1) * 128],
                                    c16[:, dt * HT + tb * 512: dt * HT + (tb + 1) * 512],
                                    start=(dt == 0), stop=(dt == NDT - 1))
                    bib, bankb = banks.get()
                    for dt in range(NDT):
                        lastb = pe.h.matmul(bankb[:, :],
                                    w2c[:, dt * HC + ht * 128: dt * HC + (ht + 1) * 128],
                                    c16[:, dt * HT + tb * 512: dt * HT + (tb + 1) * 512],
                                    start=(dt == 0), stop=(dt == NDT - 1))
                    pe.inc(lasta, 1)
                    pe.inc(lastb, 1)
                    p.wait(act, pe, pe.n)
                    p.wait(act, dve, dve.n)
                    act.h.activation(sA[:, :], banka[:, :], AF.Silu)
                    banks.consumed(bia, act)
                    hsl = hS[:, ht * HT + tb * 512: ht * HT + (tb + 1) * 512]
                    p.wait(dve, act, act.n)
                    dve.h.tensor_tensor(hsl, bankb[:, :], wbc[:, tb * 512:(tb + 1) * 512],
                                        ALU.mult)
                    dve.h.tensor_tensor(hsl, hsl, sA[:, :], ALU.mult)
                    banks.consumed(bib, dve)
            st4["w12g"] = (pe, pe.n)
            if idx + 1 < 8 * NHC:
                k4_load_w12(idx + 1)
            p.wait(pe, dve, dve.n)
            p.wait(pe, qg, st4["w3n"])
            for mt in range(NDT):
                for tb in range(4):
                    bi, bank = banks.get()
                    for ht in range(NHTl):
                        last = pe.h.matmul(bank[:, :],
                                    w3c[:, ht * D + mt * 128: ht * D + (mt + 1) * 128],
                                    hS[:, ht * HT + tb * 512: ht * HT + (tb + 1) * 512],
                                    start=(ht == 0), stop=(ht == NHTl - 1))
                    pe.inc(last)
                    p.wait(dve, pe, pe.n)
                    sl = acc[:, mt * HT + tb * 512: mt * HT + (tb + 1) * 512]
                    dve.h.tensor_tensor(sl, sl, bank[:, :], ALU.add)
                    banks.consumed(bi, dve)
            st4["w3g"] = (pe, pe.n)

    add_dram_fm(p, banks, acc, x1_d, x3c)
    p.wait(act, dve, dve.n)
    plain_ln_to_acc(p, banks, acc, sq_sc, onesc, ones1f, rows_s, rows_q, tmp_row,
                    tsc, 1e-5)
    p.wait(qs, dve, dve.n)
    fm_store(p, qs, x2_d, acc, D, HT)


def build_k4():
    p = Prog()
    IO = {n: p.din(n, s) for n, s in [("ctx", [D, HT]), ("x1", [D, HT]), ("wo", [D, D]),
          ("bo_c", [128, NDT]), ("l2s_c", [128, NDT]), ("l2h_c", [128, NDT]),
          ("gate_w", [D, 8]), ("w1", [8, D, 2048]), ("w2", [8, D, 2048]),
          ("w3", [8, 2048, D]), ("ident", [128, 128])]}
    IO["x2"] = p.dout("x2", [D, HT])
    p.push(); emit_k4(p, IO); p.pop()
    return p.finish()


def k4_inputs(inputs, ctx_full, x1_full):
    c = np.ascontiguousarray
    col = lambda v: c(v.reshape(-1, 128).T)
    base = dict(
        wo=c(inputs["L0_ffn_wo"]), bo_c=col(inputs["L0_ffn_bo"]),
        l2s_c=col(inputs["L0_ffn_ln2_scale"]), l2h_c=col(inputs["L0_ffn_ln2_shift"]),
        gate_w=c(inputs["L0_ffn_gate_w"]),
        w1=c(inputs["L0_ffn_w1"]), w2=c(inputs["L0_ffn_w2"]), w3=c(inputs["L0_ffn_w3"]),
        ident=np.eye(128, dtype=np.float32))
    maps = []
    for core in range(8):
        b, h = core // 2, core % 2
        m = dict(base)
        m["ctx"] = c(ctx_full[b, h * HT:(h + 1) * HT, :].T)
        m["x1"] = c(x1_full[b, h * HT:(h + 1) * HT, :].T)
        maps.append(m)
    return maps


def emit_k5(p, IO, fused=False):
    pe, act, dve, qs, qg = p.pe, p.act, p.dve, p.qs, p.qg
    xm_d = IO["xmy"]; wq_d = IO["wq"]; wk_d = IO["wk"]; wv_d = IO["wv"]
    bq_d = IO["bq_c"]; bk_d = IO["bk_c"]; bv_d = IO["bv_r"]
    wp_d = IO["wp"]; bp_d = IO["bp_c"]; x3_d = IO["x3"]
    xf_d = IO.get("xfull")
    kq_dram = IO["k_sc"]
    q_dram = IO["q_sc"]

    V_s = p.sb("V_s", [128, 32 * 1024], BF16)
    PT = p.sb("PT", [128, 32 * 512], BF16)
    wst = p.sb("wst", [128, NDT * 1024], BF16)
    xc = p.sb("xc", [128, NDT * 512], BF16)
    kc = p.sb("kc", [128, NDT * 512], BF16)
    kc2 = p.sb("kc2", [128, NDT * 512], BF16)
    sqs = p.sb("sqs", [128, 512], BF16)
    ybuf = p.sb("ybuf", [128, NDT * 512], BF16)
    xab = p.sb("xab", [128, NDT * 512], F32)
    rzb = p.sb("rzb", [128, 512], F32)
    bv_s = p.sb("bv_s", [1, D], BF16)
    bq_s = p.sb("bq_s", [128, NDT], F32)
    bq32 = p.sb("bq32", [128, NDT], F32)
    bk_s = p.sb("bk_s", [128, NDT], F32)
    bp_s = p.sb("bp_s", [128, NDT], F32)
    ones1 = p.sb("ones1", [1, 128], BF16)
    ones1f = p.sb("ones1f", [1, 128], F32)
    onesc = p.sb("onesc", [128, 1], BF16)
    nrm = p.sb("nrm", [1, 512], F32)
    kmax = p.sb("kmax", [1, 4], F32)
    qmax = p.sb("qmax", [1, 4], F32)
    mcol = p.sb("mcol", [128, 1], F32)
    km2 = p.sb("km2", [1, 8], F32)
    rows_s = p.sb("rows_s", [1, 512], F32)
    rows_q = p.sb("rows_q", [1, 512], F32)
    tmp_row = p.sb("tmp_row", [1, 512], F32)
    sq_sc = p.sb("sq_sc", [128, NDT * 512], BF16)
    tsc = p.sb("tsc", [128, 512], F32)
    x3cb = p.sb("x3cb", [128, NDT * 512], F32)

    banks = Pool8(p, n=7)
    zps = p.ps("zps", [1, 512])

    dve.h.memset(ones1[:, :], 1.0)
    dve.h.memset(ones1f[:, :], 1.0)
    dve.h.memset(onesc[:, :], 1.0)
    dve.h.memset(kmax[:, :], 0.0)
    dve.h.memset(qmax[:, :], 0.0)
    for s, dd in [(bq_s, bq_d), (bk_s, bk_d), (bp_s, bp_d)]:
        qs.inc(qs.h.dma_start(out=s[:, :], in_=dd))
    qg.inc(qg.h.dma_start(out=bv_s[:, :], in_=bv_d))
    p.wait(act, qs, qs.n)
    act.h.activation(bq32[:, :], bq_s[:, :], AF.Identity, scale=1.0 / 32.0)

    kv_src = xm_d if fused else xf_d
    n_kv = 4 if fused else 8
    p.wait(pe, dve, dve.n)
    p.wait(pe, qs, qs.n)
    qg.inc(qg.h.dma_start(out=sb3(wst, NDT, 1024), in_=fm3(wk_d)))
    for tc in range(n_kv):
        t0 = tc * 512
        p.wait(qg, pe, pe.n)
        qg.inc(qg.h.dma_start(out=sb3(xc, NDT, 512), in_=fm3(kv_src[:, t0:t0 + 512])))
        p.wait(pe, qg, qg.n)
        p.wait(pe, act, act.n)
        p.wait(act, qs, qs.n)
        for mt in range(NDT):
            bi, bank = banks.get()
            for dt in range(NDT):
                last = pe.h.matmul(bank[:, :],
                                   wst[:, dt * 1024 + mt * 128: dt * 1024 + (mt + 1) * 128],
                                   xc[:, dt * 512:(dt + 1) * 512],
                                   start=(dt == 0), stop=(dt == NDT - 1))
            pe.inc(last)
            p.wait(act, pe, pe.n)
            act.h.activation(kc[:, mt * 512:(mt + 1) * 512], bank[:, :], AF.Identity,
                             bias=bk_s[:, mt:mt + 1])
            banks.consumed(bi, act)
        p.wait(act, act, act.n)
        for mt in range(NDT):
            p.wait(act, pe, pe.n)
            act.h.activation(sqs[:, :], kc[:, mt * 512:(mt + 1) * 512], AF.Square)
            p.wait(pe, act, act.n)
            bi, bank = banks.get()
            last = pe.h.matmul(bank[0:1, 0:512], onesc[:, 0:1], sqs[:, :],
                               start=(mt == 0), stop=(mt == NDT - 1),
                               skip_group_check=True)
            pe.inc(last)
            if mt == NDT - 1:
                p.wait(dve, pe, pe.n)
                dve.h.reduce_max(kmax[0:1, 1:2], bank[0:1, 0:512], axis=AX)
                dve.h.tensor_tensor(kmax[0:1, 0:1], kmax[0:1, 0:1], kmax[0:1, 1:2], ALU.max)
                banks.consumed(bi, dve)
            else:
                banks.i -= 1
        p.wait(qs, act, act.n)
        qs.inc(qs.h.dma_start(out=fm3(kq_dram[:, t0:t0 + 512]), in_=sb3(kc, NDT, 512)))
    p.wait(qg, pe, pe.n)
    qg.inc(qg.h.dma_start(out=sb3(wst, NDT, 1024), in_=fm3(wv_d)))
    for tc in range(n_kv):
        t0 = tc * 512
        p.wait(qg, pe, pe.n)
        qg.inc(qg.h.dma_start(out=sb3(xc, NDT, 512), in_=fm3(kv_src[:, t0:t0 + 512])))
        p.wait(pe, qg, qg.n)
        for ttl in range(4):
            tt = tc * 4 + ttl
            for dh in range(2):
                bi, bank = banks.get()
                for dt in range(NDT):
                    pe.h.matmul(bank[:, :],
                                xc[:, dt * 512 + ttl * 128: dt * 512 + (ttl + 1) * 128],
                                wst[:, dt * 1024 + dh * 512: dt * 1024 + (dh + 1) * 512],
                                start=(dt == 0), stop=False)
                last = pe.h.matmul(bank[:, :], ones1[:, :],
                                   bv_s[0:1, dh * 512:(dh + 1) * 512],
                                   start=False, stop=True)
                pe.inc(last)
                p.wait(dve, pe, pe.n)
                dve.h.tensor_copy(V_s[:, tt * 1024 + dh * 512: tt * 1024 + (dh + 1) * 512],
                                  bank[:, :])
                banks.consumed(bi, dve)
        if fused:
            p.wait(qs, dve, dve.n)
            qs.inc(qs.h.dma_start(
                out=IO["v_bounce"][t0:t0 + 512, :],
                in_=V_s[:, tc * 4 * 1024:(tc + 1) * 4 * 1024]
                    .rearrange("p (tt v) -> p tt v", v=1024)))

    if fused:
        p.wait(qs, dve, dve.n)
        qs.inc(qs.h.dma_start(out=IO["kmax_b"], in_=kmax[0:1, :]))
        IO["mid_cb"](p)

    p.wait(qg, pe, pe.n)
    qg.inc(qg.h.dma_start(out=sb3(wst, NDT, 1024), in_=fm3(wq_d)))
    for tc in range(4):
        t0 = tc * 512
        p.wait(qg, pe, pe.n)
        qg.inc(qg.h.dma_start(out=sb3(xc, NDT, 512), in_=fm3(xm_d[:, t0:t0 + 512])))
        p.wait(pe, qg, qg.n)
        p.wait(pe, act, act.n)
        p.wait(act, qs, qs.n)
        for mt in range(NDT):
            bi, bank = banks.get()
            for dt in range(NDT):
                last = pe.h.matmul(bank[:, :],
                                   wst[:, dt * 1024 + mt * 128: dt * 1024 + (mt + 1) * 128],
                                   xc[:, dt * 512:(dt + 1) * 512],
                                   start=(dt == 0), stop=(dt == NDT - 1))
            pe.inc(last)
            p.wait(act, pe, pe.n)
            act.h.activation(kc[:, mt * 512:(mt + 1) * 512], bank[:, :], AF.Identity,
                             scale=1.0 / 32.0, bias=bq32[:, mt:mt + 1])
            banks.consumed(bi, act)
        p.wait(act, act, act.n)
        for mt in range(NDT):
            p.wait(act, pe, pe.n)
            act.h.activation(sqs[:, :], kc[:, mt * 512:(mt + 1) * 512], AF.Square)
            p.wait(pe, act, act.n)
            bi, bank = banks.get()
            last = pe.h.matmul(bank[0:1, 0:512], onesc[:, 0:1], sqs[:, :],
                               start=(mt == 0), stop=(mt == NDT - 1),
                               skip_group_check=True)
            pe.inc(last)
            if mt == NDT - 1:
                p.wait(dve, pe, pe.n)
                dve.h.reduce_max(qmax[0:1, 1:2], bank[0:1, 0:512], axis=AX)
                dve.h.tensor_tensor(qmax[0:1, 0:1], qmax[0:1, 0:1], qmax[0:1, 1:2], ALU.max)
                banks.consumed(bi, dve)
            else:
                banks.i -= 1
        p.wait(qs, act, act.n)
        qs.inc(qs.h.dma_start(out=fm3(q_dram[:, t0:t0 + 512]), in_=sb3(kc, NDT, 512)))

    if fused:
        p.wait(qs, p.cc)
        qs.inc(qs.h.dma_start(out=km2[0:1, :], in_=IO["ag_kmax"].rearrange("a b -> (a b)")))
        p.wait(dve, qs, qs.n)
        dve.h.tensor_tensor(kmax[0:1, 0:1], km2[0:1, 0:1], km2[0:1, 4:5], ALU.max)
    p.wait(act, dve, dve.n)
    act.h.activation(qmax[0:1, 2:3], qmax[0:1, 0:1], AF.Sqrt)
    act.h.activation(kmax[0:1, 2:3], kmax[0:1, 0:1], AF.Sqrt)
    p.wait(dve, act, act.n)
    dve.h.tensor_tensor(qmax[0:1, 3:4], qmax[0:1, 2:3], kmax[0:1, 2:3], ALU.mult)
    dve.h.tensor_scalar_mul(qmax[0:1, 3:4], qmax[0:1, 3:4], -1.0)
    p.wait(pe, dve, dve.n)
    bi, bank = banks.get()
    last = pe.h.matmul(bank[0:128, 0:1], ones1f[:, :], qmax[0:1, 3:4], start=True, stop=True)
    pe.inc(last)
    p.wait(dve, pe, pe.n)
    dve.h.tensor_copy(mcol[:, :], bank[0:128, 0:1])
    banks.consumed(bi, dve)

    if fused:
        p.wait(qs, p.cc)
        p.wait(qs, pe, pe.n)
        p.wait(qs, dve, dve.n)
        qs.inc(qs.h.dma_start(
            out=V_s[:, :].rearrange("p (tt v) -> p tt v", v=1024),
            in_=IO["ag_v"].rearrange("(tt p) v -> p tt v", p=128)))
    p.wait(qg, pe, pe.n)
    qg.inc(qg.h.dma_start(out=sb3(wst, NDT, 1024), in_=fm3(wp_d)))
    for qb in range(4):
        q0 = qb * 512
        p.wait(qs, pe, pe.n)
        qs.inc(qs.h.dma_start(out=sb3(xc, NDT, 512), in_=fm3(q_dram[:, q0:q0 + 512])))
        kbufs = [kc, kc2]
        k_guard = [None, None]
        k_load = [0, 0]

        def k5_load_k(kc4l):
            par2 = kc4l % 2
            if k_guard[par2] is not None:
                p.wait(qs, k_guard[par2][0], k_guard[par2][1])
            k0l = kc4l * 512
            if fused:
                blk = k0l // HT
                kk0 = k0l % HT
                qs.inc(qs.h.dma_start(out=sb3(kbufs[par2], NDT, 512),
                                      in_=fm3(IO["ag_k"][blk * D:(blk + 1) * D, kk0:kk0 + 512])))
            else:
                qs.inc(qs.h.dma_start(out=sb3(kbufs[par2], NDT, 512),
                                      in_=fm3(kq_dram[:, k0l:k0l + 512])))
            k_load[par2] = qs.n

        k5_load_k(0)
        for kc4 in range(8):
            par = kc4 % 2
            if kc4 + 1 < 8:
                k5_load_k(kc4 + 1)
            p.wait(pe, qs, k_load[par])
            p.wait(pe, dve, dve.n)
            for ktl in range(4):
                kt = kc4 * 4 + ktl
                bi, bank = banks.get()
                for dt in range(NDT):
                    last = pe.h.matmul(bank[:, :],
                                kbufs[par][:, dt * 512 + ktl * 128: dt * 512 + (ktl + 1) * 128],
                                xc[:, dt * 512:(dt + 1) * 512],
                                start=(dt == 0), stop=(dt == NDT - 1))
                pe.inc(last)
                p.wait(act, pe, pe.n)
                act.h.activation(PT[:, kt * 512:(kt + 1) * 512], bank[:, :], AF.Exp,
                                 bias=mcol[:, 0:1])
                banks.consumed(bi, act)
            k_guard[par] = (pe, pe.n)
        p.wait(pe, act, act.n)
        p.wait(pe, dve, dve.n)
        for kt in range(32):
            last = pe.h.matmul(zps[0:1, :], onesc[:, 0:1], PT[:, kt * 512:(kt + 1) * 512],
                               start=(kt == 0), stop=(kt == 31))
        pe.inc(last)
        p.wait(dve, pe, pe.n)
        dve.h.reciprocal(nrm[0:1, :], zps[0:1, :])
        p.wait(pe, dve, dve.n)
        bi, bank = banks.get()
        last = pe.h.matmul(bank[:, :], ones1f[:, :], nrm[0:1, :], start=True, stop=True)
        pe.inc(last)
        p.wait(dve, pe, pe.n)
        dve.h.tensor_copy(rzb[:, :], bank[:, :])
        banks.consumed(bi, dve)
        for dvt in range(NDT):
            bi, bank = banks.get()
            for kt in range(32):
                last = pe.h.matmul(bank[:, :],
                                   V_s[:, kt * 1024 + dvt * 128: kt * 1024 + (dvt + 1) * 128],
                                   PT[:, kt * 512:(kt + 1) * 512],
                                   start=(kt == 0), stop=(kt == 31))
            pe.inc(last)
            p.wait(dve, pe, pe.n)
            dve.h.tensor_tensor(ybuf[:, dvt * 512:(dvt + 1) * 512], bank[:, :],
                                rzb[:, :], ALU.mult)
            banks.consumed(bi, dve)
        p.wait(qs, dve, dve.n)
        qs.inc(qs.h.dma_start(out=sb3(x3cb, NDT, 512), in_=fm3(xm_d[:, q0:q0 + 512])))
        p.wait(pe, dve, dve.n)
        p.wait(pe, qg, qg.n)
        for mt in range(NDT):
            bi, bank = banks.get()
            for dt in range(NDT):
                last = pe.h.matmul(bank[:, :],
                                   wst[:, dt * 1024 + mt * 128: dt * 1024 + (mt + 1) * 128],
                                   ybuf[:, dt * 512:(dt + 1) * 512],
                                   start=(dt == 0), stop=(dt == NDT - 1))
            pe.inc(last)
            p.wait(dve, pe, pe.n)
            p.wait(dve, qs, qs.n)
            dve.h.scalar_tensor_tensor(xab[:, mt * 512:(mt + 1) * 512], bank[:, :],
                                       bp_s[:, mt:mt + 1],
                                       x3cb[:, mt * 512:(mt + 1) * 512],
                                       ALU.add, ALU.add)
            banks.consumed(bi, dve)
        p.wait(act, dve, dve.n)
        row_stats(p, banks, xab, sq_sc, onesc, rows_s, rows_q, 512, HTl=512)
        p.wait(dve, pe, pe.n)
        ln_finalize_rows(p, rows_s, rows_q, tmp_row, 512, float(D), 1e-5)
        p.wait(pe, dve, dve.n)
        bi1, bankR = banks.get()
        pe.h.matmul(bankR[:, :], ones1f[:, :], rows_q[0:1, 0:512], start=True, stop=True)
        bi2, bankM = banks.get()
        last = pe.h.matmul(bankM[:, :], ones1f[:, :], rows_s[0:1, 0:512], start=True, stop=True)
        pe.inc(last)
        p.wait(dve, pe, pe.n)
        for dt in range(NDT):
            sl = xab[:, dt * 512:(dt + 1) * 512]
            dve.h.tensor_tensor(tsc[:, :], sl, bankR[:, :], ALU.mult)
            dve.h.tensor_tensor(sl, tsc[:, :], bankM[:, :], ALU.add)
        banks.consumed(bi1, dve)
        banks.consumed(bi2, dve)
        p.wait(qs, dve, dve.n)
        qs.inc(qs.h.dma_start(out=fm3(x3_d[:, q0:q0 + 512]), in_=sb3(xab, NDT, 512)))


def build_k5():
    p = Prog()
    IO = {n: p.din(n, s) for n, s in [("xfull", [D, TFULL]), ("xmy", [D, HT]),
          ("wq", [D, D]), ("wk", [D, D]), ("wv", [D, D]), ("bq_c", [128, NDT]),
          ("bk_c", [128, NDT]), ("bv_r", [1, D]), ("wp", [D, D]), ("bp_c", [128, NDT])]}
    IO["x3"] = p.dout("x3", [D, HT])
    IO["k_sc"] = p.dint("k_sc", [D, TFULL], BF16).ap()
    IO["q_sc"] = p.dint("q_sc", [D, HT], BF16).ap()
    p.push(); emit_k5(p, IO); p.pop()
    return p.finish()


def k5_inputs(inputs, x2_full):
    c = np.ascontiguousarray
    col = lambda v: c(v.reshape(-1, 128).T)
    wqkv = inputs["L1_attn_w_qkv"]
    bqkv = inputs["L1_attn_b_qkv"]
    base = dict(
        wq=c(wqkv[:, 0:1024]), wk=c(wqkv[:, 1024:2048]), wv=c(wqkv[:, 2048:3072]),
        bq_c=col(bqkv[0:1024]), bk_c=col(bqkv[1024:2048]), bv_r=c(bqkv[2048:3072].reshape(1, -1)),
        wp=c(inputs["L1_attn_w_proj"]), bp_c=col(inputs["L1_attn_b_proj"]))
    maps = []
    for core in range(8):
        b, h = core // 2, core % 2
        m = dict(base)
        m["xfull"] = c(x2_full[b].T)
        m["xmy"] = c(x2_full[b, h * HT:(h + 1) * HT, :].T)
        maps.append(m)
    return maps


def emit_k3(p, IO, h1_loader=None, ctx_bf16=False):
    pe, act, dve, qs, qg = p.pe, p.act, p.dve, p.qs, p.qg
    h1_d = IO.get("h1f"); wq_d = IO["wq_my"]; wk_d = IO["wk_my"]; wv_d = IO["wv_my"]
    triu_d = IO["triu"]; ctx_d = IO["ctxo"]

    qf = p.sb("qf", [128, 4 * TFULL], BF16)
    kf = p.sb("kf", [128, 4 * TFULL], BF16)
    V_s = p.sb("V_s", [128, 32 * 520], BF16)
    PT = p.sb("PT", [128, 32 * 512], BF16)
    wq16 = p.sb("wq16", [128, NDT * 512], BF16)
    wk16 = p.sb("wk16", [128, NDT * 512], BF16)
    wv16 = p.sb("wv16", [128, NDT * 512], BF16)
    hc = p.sb("hc", [128, NDT * 512], BF16)
    sqs = p.sb("sqs", [128, 512], BF16)
    triu = p.sb("triu_s", [128, 128], BF16)
    qmax8 = p.sb("qmax8", [1, 32], F32)
    kmax8 = p.sb("kmax8", [1, 32], F32)
    nm8 = p.sb("nm8", [1, 8], F32)
    mcols = p.sb("mcols", [128, 8], F32)
    nrm = p.sb("nrm", [1, 512], F32)
    rzb = p.sb("rzb", [128, 512], F32)
    cstage = p.sb("cstage", [128, 512], BF16 if ctx_bf16 else F32)
    ones1f = p.sb("ones1f", [1, 128], F32)
    onesc = p.sb("onesc", [128, 1], BF16)

    banks = Pool8(p, n=8)

    dve.h.memset(ones1f[:, :], 1.0)
    dve.h.memset(onesc[:, :], 1.0)
    dve.h.memset(qmax8[:, :], 0.0)
    dve.h.memset(kmax8[:, :], 0.0)
    dve.h.memset(V_s[:, :], 1.0)
    qs.inc(qs.h.dma_start(out=triu[:, :], in_=triu_d))
    qg.inc(qg.h.dma_start(out=sb3(wq16, NDT, 512), in_=fm3(wq_d)))
    qg.inc(qg.h.dma_start(out=sb3(wk16, NDT, 512), in_=fm3(wk_d)))
    qg.inc(qg.h.dma_start(out=sb3(wv16, NDT, 512), in_=fm3(wv_d)))

    p.wait(pe, dve, dve.n)
    p.wait(pe, qg, qg.n)
    for tc in range(8):
        t0 = tc * 512
        p.wait(qs, pe, pe.n)
        if h1_loader is None:
            qs.inc(qs.h.dma_start(out=sb3(hc, NDT, 512), in_=fm3(h1_d[:, t0:t0 + 512])))
        else:
            h1_loader(p, hc, t0)
        p.wait(pe, qs, qs.n)
        for which, wbuf, dst, scale, mx in (
                (0, wq16, qf, 0.125, qmax8), (1, wk16, kf, 1.0, kmax8)):
            for hp in range(4):
                bi, bank = banks.get()
                for dt in range(NDT):
                    last = pe.h.matmul(bank[:, :],
                                       wbuf[:, dt * 512 + hp * 128: dt * 512 + (hp + 1) * 128],
                                       hc[:, dt * 512:(dt + 1) * 512],
                                       start=(dt == 0), stop=(dt == NDT - 1))
                pe.inc(last)
                p.wait(act, pe, pe.n)
                if scale == 1.0:
                    act.h.activation(dst[:, hp * TFULL + t0: hp * TFULL + t0 + 512],
                                     bank[:, :], AF.Copy)
                else:
                    act.h.activation(dst[:, hp * TFULL + t0: hp * TFULL + t0 + 512],
                                     bank[:, :], AF.Identity, scale=scale)
                banks.consumed(bi, act)
                p.wait(act, pe, pe.n)
                act.h.activation(sqs[:, :], dst[:, hp * TFULL + t0: hp * TFULL + t0 + 512],
                                 AF.Square)
                p.wait(pe, act, act.n)
                for lane in range(2):
                    h8 = hp * 2 + lane
                    bi2, bank2 = banks.get()
                    last = pe.h.matmul(bank2[0:1, 0:512], onesc[lane * 64:lane * 64 + 64, 0:1],
                                       sqs[lane * 64:lane * 64 + 64, :],
                                       start=True, stop=True,
                                       tile_position=(lane * 64, 0))
                    pe.inc(last)
                    p.wait(dve, pe, pe.n)
                    dve.h.reduce_max(mx[0:1, 8 + h8:9 + h8], bank2[0:1, 0:512], axis=AX)
                    dve.h.tensor_tensor(mx[0:1, h8:h8 + 1], mx[0:1, h8:h8 + 1],
                                        mx[0:1, 8 + h8:9 + h8], ALU.max)
                    banks.consumed(bi2, dve)
        p.wait(pe, dve, dve.n)
        for ttl in range(4):
            tt = tc * 4 + ttl
            bi, bank = banks.get()
            for dt in range(NDT):
                last = pe.h.matmul(bank[:, :],
                                   hc[:, dt * 512 + ttl * 128: dt * 512 + (ttl + 1) * 128],
                                   wv16[:, dt * 512:(dt + 1) * 512],
                                   start=(dt == 0), stop=(dt == NDT - 1))
            pe.inc(last)
            p.wait(dve, pe, pe.n)
            dve.h.tensor_copy(
                V_s[:, tt * 520: tt * 520 + 520].rearrange("p (h v) -> p h v", v=65)[:, :, 0:64],
                bank[:, :].rearrange("p (h v) -> p h v", v=64))
            banks.consumed(bi, dve)

    p.wait(act, dve, dve.n)
    act.h.activation(qmax8[0:1, 16:24], qmax8[0:1, 0:8], AF.Sqrt)
    act.h.activation(kmax8[0:1, 16:24], kmax8[0:1, 0:8], AF.Sqrt)
    p.wait(dve, act, act.n)
    dve.h.tensor_tensor(nm8[0:1, :], qmax8[0:1, 16:24], kmax8[0:1, 16:24], ALU.mult)
    dve.h.tensor_scalar_mul(nm8[0:1, :], nm8[0:1, :], -0.125)
    p.wait(pe, dve, dve.n)
    bi, bank = banks.get()
    last = pe.h.matmul(bank[0:128, 0:8], ones1f[:, :], nm8[0:1, :], start=True, stop=True)
    pe.inc(last)
    p.wait(dve, pe, pe.n)
    dve.h.tensor_copy(mcols[:, :], bank[0:128, 0:8])
    banks.consumed(bi, dve)

    for h8 in range(8):
        hp, lane = h8 // 2, h8 % 2
        b0 = lane * 64
        for qb in range(8):
            q0 = qb * 512
            nkt = 4 * qb + 4
            p.wait(pe, dve, dve.n)
            for kt in range(nkt):
                bi, bank = banks.get()
                last = pe.h.matmul(bank[:, :],
                                   kf[b0:b0 + 64, hp * TFULL + kt * 128: hp * TFULL + (kt + 1) * 128],
                                   qf[b0:b0 + 64, hp * TFULL + q0: hp * TFULL + q0 + 512],
                                   start=True, stop=True, tile_position=(b0, 0))
                pe.inc(last)
                p.wait(act, pe, pe.n)
                act.h.activation(PT[:, kt * 512:(kt + 1) * 512], bank[:, :], AF.Exp,
                                 bias=mcols[:, h8:h8 + 1])
                banks.consumed(bi, act)
                jj = kt - 4 * qb
                if jj >= 0:
                    p.wait(dve, act, act.n)
                    if jj > 0:
                        dve.h.memset(PT[:, kt * 512: kt * 512 + jj * 128], 0.0)
                    dve.h.tensor_tensor(PT[:, kt * 512 + jj * 128: kt * 512 + (jj + 1) * 128],
                                        PT[:, kt * 512 + jj * 128: kt * 512 + (jj + 1) * 128],
                                        triu[:, :], ALU.mult)
            p.wait(pe, act, act.n)
            p.wait(pe, dve, dve.n)
            bi, bank = banks.get()
            for kt in range(nkt):
                last = pe.h.matmul(bank[0:65, :],
                                   V_s[:, kt * 520 + h8 * 65: kt * 520 + (h8 + 1) * 65],
                                   PT[:, kt * 512:(kt + 1) * 512],
                                   start=(kt == 0), stop=(kt == nkt - 1))
            pe.inc(last)
            p.wait(dve, pe, pe.n)
            dve.h.reciprocal(nrm[0:1, :], bank[64:65, :])
            p.wait(pe, dve, dve.n)
            bi2, bank2 = banks.get()
            last = pe.h.matmul(bank2[0:64, :], ones1f[0:1, 0:64], nrm[0:1, :],
                               start=True, stop=True)
            pe.inc(last)
            p.wait(dve, pe, pe.n)
            dve.h.tensor_copy(rzb[0:64, :], bank2[0:64, :])
            p.wait(dve, qs, qs.n)
            dve.h.tensor_tensor(cstage[0:64, :], bank[0:64, :], rzb[0:64, :], ALU.mult)
            banks.consumed(bi, dve)
            banks.consumed(bi2, dve)
            p.wait(qs, dve, dve.n)
            qs.inc(qs.h.dma_start(out=ctx_d[h8 * 64:(h8 + 1) * 64, q0:q0 + 512],
                                  in_=cstage[0:64, :]))


def build_k3():
    p = Prog()
    IO = {}
    IO["h1f"] = p.nc.dram_tensor("h1f", [D, TFULL], BF16, kind="ExternalInput").ap()
    IO["wq_my"] = p.din("wq_my", [D, 512])
    IO["wk_my"] = p.din("wk_my", [D, 512])
    IO["wv_my"] = p.din("wv_my", [D, 512])
    IO["triu"] = p.nc.dram_tensor("triu", [128, 128], BF16, kind="ExternalInput").ap()
    IO["ctxo"] = p.dout("ctxo", [512, TFULL])
    p.push(); emit_k3(p, IO); p.pop()
    return p.finish()


def k3_inputs(inputs, h1_full_bf16, triu):
    c = np.ascontiguousarray
    maps = []
    for core in range(8):
        b, r = core // 2, core % 2
        hs = r * 512
        m = dict(
            h1f=c(h1_full_bf16[b].T),
            wq_my=c(inputs["L0_ffn_wq"][:, hs:hs + 512]),
            wk_my=c(inputs["L0_ffn_wk"][:, hs:hs + 512]),
            wv_my=c(inputs["L0_ffn_wv"][:, hs:hs + 512]),
            triu=triu)
        maps.append(m)
    return maps


def emit_k1(p, IO, attn_bf16=False):
    pe, act, dve, qs, qg = p.pe, p.act, p.dve, p.qs, p.qg
    x0_d = IO["x0f"]; l1w_d = IO["ln1w_c"]; l1b_d = IO["ln1b_c"]
    wq_d = IO["wq_my"]; wk_d = IO["wk_my"]; wv_d = IO["wv_my"]; dec_d = IO["dec_my"]
    qb_d = IO["qb_c"]; kb_d = IO["kb_c"]; vb_d = IO["vb_r"]; db_d = IO["db_r"]
    ui_d = IO["u_incl"]; ur_d = IO["u_rvex"]; id_d = IO["identb"]
    attn_d = IO["attn"]
    outf_dram = IO["outf"]

    qf = p.sb("qf", [128, 4 * TFULL], BF16)
    kf = p.sb("kf", [128, 4 * TFULL], BF16)
    KT = p.sb("KT", [128, 32 * 512], BF16)
    V_s = p.sb("V_s", [128, 32 * 520], BF16)
    wq16 = p.sb("wq16", [128, NDT * 512], BF16)
    wk16 = p.sb("wk16", [128, NDT * 512], BF16)
    wv16 = p.sb("wv16", [128, NDT * 512], BF16)
    dec16 = p.sb("dec16", [128, NDT * 8], BF16)
    xc32 = p.sb("xc32", [128, NDT * 512], F32)
    xnc = p.sb("xnc", [128, NDT * 512], BF16)
    sq_sc = p.sb("sq_sc", [128, NDT * 256], BF16)
    rows_s = p.sb("rows_s", [1, 512], F32)
    rows_q = p.sb("rows_q", [1, 512], F32)
    tmp_row = p.sb("tmp_row", [1, 512], F32)
    tsc = p.sb("tsc", [128, 512], F32)
    phsc = p.sb("phsc", [128, 512], BF16)
    phsc2 = p.sb("phsc2", [128, 512], BF16)
    ll_tm = p.sb("ll_tm", [128, 32 * 8], F32)
    c_tm = p.sb("c_tm", [128, 8], F32)
    e_tm = p.sb("e_tm", [128, 8], F32)
    nll = p.sb("nll", [128, 8], F32)
    lamr = p.sb("lamr", [1, 8], F32)
    lamc = p.sb("lamc", [64, 8], F32)
    Sst = p.sb("Sst", [64, 8 * 65], F32)
    outb = p.sb("outb", [128, 520], F32)
    astage = p.sb("astage", [128, 512], BF16 if attn_bf16 else F32)
    l1w_s = p.sb("l1w_s", [128, NDT], F32)
    l1b_s = p.sb("l1b_s", [128, NDT], F32)
    qb_s = p.sb("qb_s", [128, 4], F32)
    nqb_s = p.sb("nqb_s", [128, 4], F32)
    kb_s = p.sb("kb_s", [128, 4], F32)
    nkb_s = p.sb("nkb_s", [128, 4], F32)
    vb_s = p.sb("vb_s", [1, 512], BF16)
    db_s = p.sb("db_s", [1, 8], BF16)
    ui_s = p.sb("ui_s", [128, 128], F32)
    ur_s = p.sb("ur_s", [128, 128], F32)
    idb_s = p.sb("idb_s", [128, 128], BF16)
    ones1f = p.sb("ones1f", [1, 128], F32)
    onescf = p.sb("onescf", [128, 1], F32)
    onesc = p.sb("onesc", [128, 1], BF16)
    ones1b = p.sb("ones1b", [1, 128], BF16)

    banks = Pool8(p, n=8)

    for s, dd in [(l1w_s, l1w_d), (l1b_s, l1b_d), (qb_s, qb_d), (kb_s, kb_d),
                  (ui_s, ui_d), (ur_s, ur_d)]:
        qs.inc(qs.h.dma_start(out=s[:, :], in_=dd))
    qs.inc(qs.h.dma_start(out=idb_s[:, :], in_=id_d))
    for s, dd in [(wq16, wq_d), (wk16, wk_d), (wv16, wv_d)]:
        qg.inc(qg.h.dma_start(out=sb3(s, NDT, 512), in_=fm3(dd)))
    qg.inc(qg.h.dma_start(out=sb3(dec16, NDT, 8), in_=fm3(dec_d)))
    qg.inc(qg.h.dma_start(out=vb_s[:, :], in_=vb_d))
    qg.inc(qg.h.dma_start(out=db_s[:, :], in_=db_d))
    dve.h.memset(ones1f[:, :], 1.0)
    dve.h.memset(ones1b[:, :], 1.0)
    dve.h.memset(onescf[:, :], 1.0)
    dve.h.memset(onesc[:, :], 1.0)
    dve.h.memset(Sst[:, :], 0.0)
    dve.h.memset(V_s[:, :], 1.0)
    p.wait(act, qs, qs.n)
    act.h.activation(nqb_s[:, :], qb_s[:, :], AF.Identity, scale=-1.0)
    act.h.activation(nkb_s[:, :], kb_s[:, :], AF.Identity, scale=-1.0)

    p.wait(pe, dve, dve.n)
    p.wait(pe, qg, qg.n)
    p.wait(pe, qs, qs.n)
    for tc in range(8):
        t0 = tc * 512
        p.wait(qs, dve, dve.n)
        qs.inc(qs.h.dma_start(out=sb3(xc32, NDT, 512), in_=fm3(x0_d[:, t0:t0 + 512])))
        p.wait(act, qs, qs.n)
        p.wait(dve, qs, qs.n)
        row_stats(p, banks, xc32, sq_sc, onesc, rows_s, rows_q, 512, HTl=512, sqw=256, cw0=256)
        p.wait(dve, pe, pe.n)
        ln_finalize_rows(p, rows_s, rows_q, tmp_row, 512, float(D), 1e-5)
        p.wait(pe, dve, dve.n)
        bi1, bankR = banks.get()
        pe.h.matmul(bankR[:, :], ones1f[:, :], rows_q[0:1, 0:512], start=True, stop=True)
        bi2, bankM = banks.get()
        last = pe.h.matmul(bankM[:, :], ones1f[:, :], rows_s[0:1, 0:512], start=True, stop=True)
        pe.inc(last)
        p.wait(dve, pe, pe.n)
        for dt in range(NDT):
            sl = xc32[:, dt * 512:(dt + 1) * 512]
            dve.h.tensor_tensor(tsc[:, :], sl, bankR[:, :], ALU.mult)
            dve.h.tensor_tensor(tsc[:, :], tsc[:, :], bankM[:, :], ALU.add)
            dve.h.tensor_scalar(xnc[:, dt * 512:(dt + 1) * 512], tsc[:, :],
                                l1w_s[:, dt:dt + 1], l1b_s[:, dt:dt + 1],
                                ALU.mult, ALU.add)
        banks.consumed(bi1, dve)
        banks.consumed(bi2, dve)
        p.wait(pe, dve, dve.n)
        for wbuf, dst, bcol, nbcol in ((wq16, qf, qb_s, nqb_s), (wk16, kf, kb_s, nkb_s)):
            for hp in range(4):
                bi, bank = banks.get()
                for dt in range(NDT):
                    last = pe.h.matmul(bank[:, :],
                                       wbuf[:, dt * 512 + hp * 128: dt * 512 + (hp + 1) * 128],
                                       xnc[:, dt * 512:(dt + 1) * 512],
                                       start=(dt == 0), stop=(dt == NDT - 1))
                pe.inc(last)
                p.wait(act, pe, pe.n)
                p.wait(act, dve, dve.n)
                act.h.activation(phsc[:, :], bank[:, :], AF.Relu, bias=bcol[:, hp:hp + 1])
                act.h.activation(phsc2[:, :], bank[:, :], AF.Relu, scale=-1.0,
                                 bias=nbcol[:, hp:hp + 1])
                act.h.activation(phsc2[:, :], phsc2[:, :], AF.Exp, scale=-1.0)
                banks.consumed(bi, act)
                p.wait(dve, act, act.n)
                dve.h.tensor_tensor(dst[:, hp * TFULL + t0: hp * TFULL + t0 + 512],
                                    phsc[:, :], phsc2[:, :], ALU.add)
        p.wait(pe, dve, dve.n)
        for hp in range(4):
            for ttl in range(4):
                tt = tc * 4 + ttl
                bi, bank = banks.get()
                tview = bank[:, 0:64].bitcast(BF16)
                last = pe.h.transpose(tview,
                                      kf[:, hp * TFULL + t0 + ttl * 128: hp * TFULL + t0 + (ttl + 1) * 128],
                                      idb_s[:, :])
                pe.inc(last)
                p.wait(dve, pe, pe.n)
                dve.h.tensor_copy(KT[:, tt * 512 + hp * 128: tt * 512 + (hp + 1) * 128],
                                  tview)
                banks.consumed(bi, dve)
        p.wait(pe, dve, dve.n)
        for ttl in range(4):
            tt = tc * 4 + ttl
            bi, bank = banks.get()
            for dt in range(NDT):
                pe.h.matmul(bank[:, :],
                            xnc[:, dt * 512 + ttl * 128: dt * 512 + (ttl + 1) * 128],
                            wv16[:, dt * 512:(dt + 1) * 512],
                            start=(dt == 0), stop=False)
            last = pe.h.matmul(bank[:, :], ones1b[:, :],
                               vb_s[:, :], start=False, stop=True)
            pe.inc(last)
            p.wait(dve, pe, pe.n)
            dve.h.tensor_copy(
                V_s[:, tt * 520: tt * 520 + 520].rearrange("p (h v) -> p h v", v=65)[:, :, 0:64],
                bank[:, :].rearrange("p (h v) -> p h v", v=64))
            banks.consumed(bi, dve)
            bi, bank = banks.get()
            for dt in range(NDT):
                pe.h.matmul(bank[:, 0:8],
                            xnc[:, dt * 512 + ttl * 128: dt * 512 + (ttl + 1) * 128],
                            dec16[:, dt * 8:(dt + 1) * 8],
                            start=(dt == 0), stop=False)
            last = pe.h.matmul(bank[:, 0:8], ones1b[:, :],
                               db_s[:, :], start=False, stop=True)
            pe.inc(last)
            p.wait(act, pe, pe.n)
            p.wait(act, dve, dve.n)
            act.h.activation(tsc[:, 0:8], bank[:, 0:8], AF.Sigmoid)
            banks.consumed(bi, act)
            p.wait(dve, act, act.n)
            dve.h.tensor_scalar_max(ll_tm[:, tt * 8:(tt + 1) * 8], tsc[:, 0:8], 1e-8)

    p.wait(act, dve, dve.n)
    act.h.activation(ll_tm[:, :], ll_tm[:, :], AF.Ln)

    EPS = 1e-6
    S16a = p.sb("S16a", [128, 8 * 65], BF16)
    Ve8 = p.sb("Ve8", [128, 8 * 65], BF16)
    Dm8 = p.sb("Dm8", [128, 8 * 128], BF16)
    StD8 = p.sb("StD8", [128, 8 * 128], BF16)
    for sweep in range(K1_SWEEPS):
        fwd = (sweep == 0)
        order = list(range(32))[:K1_NTT] if fwd else list(range(31, -1, -1))[:K1_NTT]
        p.wait(dve, act, act.n)
        dve.h.memset(Sst[:, :], 0.0)
        for tt in order:
          if True:
            lls = ll_tm[:, tt * 8:(tt + 1) * 8]
            p.wait(pe, dve, dve.n)
            biA, bA = banks.get()
            pe.h.matmul(bA[:, 0:8], ui_s[:, :], lls, start=True, stop=True)
            biA2, bA2 = banks.get()
            pe.h.matmul(bA2[:, 0:8], ur_s[:, :], lls, start=True, stop=True)
            biA3, bA3 = banks.get()
            last = pe.h.matmul(bA3[0:1, 0:8], onescf[:, 0:1], lls, start=True, stop=True)
            pe.inc(last)
            p.wait(act, pe, pe.n)
            act.h.activation(c_tm[:, :], bA[:, 0:8], AF.Exp)
            act.h.activation(e_tm[:, :], bA2[:, 0:8], AF.Exp)
            act.h.activation(lamr[:, :], bA3[0:1, 0:8], AF.Exp)
            p.wait(dve, act, act.n)
            dve.h.tensor_scalar_mul(nll[:, :], lls, -1.0)
            banks.consumed(biA, act)
            banks.consumed(biA2, act)
            banks.consumed(biA3, act)
            p.wait(pe, act, act.n)
            biL, bL = banks.get()
            last = pe.h.matmul(bL[0:64, 0:8], ones1f[0:1, 0:64], lamr[:, :],
                               start=True, stop=True)
            pe.inc(last)
            p.wait(dve, pe, pe.n)
            dve.h.tensor_copy(lamc[:, :], bL[0:64, 0:8])
            banks.consumed(biL, dve)
          if True:
            if fwd:
                p.wait(dve, qs, qs.n)
                dve.h.memset(outb[:, :], 0.0)
            else:
                qs.inc(qs.h.dma_start(out=outb[:, :],
                                      in_=outf_dram.ap()[tt * 128:(tt + 1) * 128, :]))
                p.wait(dve, qs, qs.n)
            for h8 in (range(8) if "inter" in K1_STAGES else []):
                b0 = (h8 % 2) * 64
                dve.h.tensor_copy(S16a[b0:b0 + 64, h8 * 65:(h8 + 1) * 65],
                                  Sst[:, h8 * 65:(h8 + 1) * 65])
            p.wait(pe, dve, dve.n)
            ibks = []
            for h8 in (range(8) if "inter" in K1_STAGES else []):
                hp, lane = h8 // 2, h8 % 2
                b0 = lane * 64
                bi5, bk = banks.get()
                last = pe.h.matmul(bk[:, 0:65],
                                   qf[b0:b0 + 64, hp * TFULL + tt * 128: hp * TFULL + (tt + 1) * 128],
                                   S16a[b0:b0 + 64, h8 * 65:(h8 + 1) * 65],
                                   start=True, stop=True, tile_position=(b0, 0))
                ibks.append((bi5, bk))
            if "inter" in K1_STAGES:
                pe.inc(last)
            p.wait(dve, pe, pe.n)
            for h8 in (range(8) if "inter" in K1_STAGES else []):
                bi5, bk = ibks[h8]
                scol = c_tm[:, h8:h8 + 1] if fwd else e_tm[:, h8:h8 + 1]
                dve.h.tensor_scalar(tsc[:, 0:65], bk[:, 0:65], scol, None, ALU.mult)
                dve.h.tensor_tensor(outb[:, h8 * 65:(h8 + 1) * 65],
                                    outb[:, h8 * 65:(h8 + 1) * 65], tsc[:, 0:65], ALU.add)
                banks.consumed(bi5, dve)
            if fwd and "intra" in K1_STAGES:
                p.wait(pe, dve, dve.n)
                for grp in range(2):
                    hs = range(grp * 4, grp * 4 + 4)
                    sbks = []
                    for h8 in hs:
                        hp, lane = h8 // 2, h8 % 2
                        b0 = lane * 64
                        biS, bk = banks.get()
                        sl = qf[b0:b0 + 64, hp * TFULL + tt * 128: hp * TFULL + (tt + 1) * 128]
                        kl = kf[b0:b0 + 64, hp * TFULL + tt * 128: hp * TFULL + (tt + 1) * 128]
                        pe.h.matmul(bk[:, 0:128], kl, sl,
                                    start=True, stop=True, tile_position=(b0, 0))
                        sbks.append((biS, bk))
                    dbks = []
                    for h8 in hs:
                        biDl, bk = banks.get()
                        pe.h.matmul(bk[:, 0:128], ui_s[:, :],
                                    ll_tm[:, tt * 8 + h8: tt * 8 + h8 + 1].broadcast_to([128, 128]),
                                    start=True, stop=False)
                        last = pe.h.matmul(bk[:, 0:128],
                                           nll[:, h8:h8 + 1].broadcast_to([128, 128]),
                                           ui_s[:, :], start=False, stop=True)
                        dbks.append((biDl, bk))
                    pe.inc(last)
                    p.wait(act, pe, pe.n)
                    for j, h8 in enumerate(hs):
                        biDl, bk = dbks[j]
                        act.h.activation(StD8[:, h8 * 128:(h8 + 1) * 128], bk[:, 0:128], AF.Abs)
                        act.h.activation(Dm8[:, h8 * 128:(h8 + 1) * 128],
                                         StD8[:, h8 * 128:(h8 + 1) * 128], AF.Exp, scale=-1.0)
                        banks.consumed(biDl, act)
                    p.wait(dve, act, act.n)
                    for j, h8 in enumerate(hs):
                        biS, bk = sbks[j]
                        dve.h.tensor_tensor(StD8[:, h8 * 128:(h8 + 1) * 128],
                                            bk[:, 0:128],
                                            Dm8[:, h8 * 128:(h8 + 1) * 128], ALU.mult)
                        banks.consumed(biS, dve)
                p.wait(pe, dve, dve.n)
                obks = []
                for h8 in range(8):
                    bi8, bk = banks.get()
                    last = pe.h.matmul(bk[:, 0:65], StD8[:, h8 * 128:(h8 + 1) * 128],
                                       V_s[:, tt * 520 + h8 * 65: tt * 520 + (h8 + 1) * 65],
                                       start=True, stop=True)
                    obks.append((bi8, bk))
                pe.inc(last)
                p.wait(dve, pe, pe.n)
                for h8 in range(8):
                    bi8, bk = obks[h8]
                    dve.h.tensor_tensor(outb[:, h8 * 65:(h8 + 1) * 65],
                                        outb[:, h8 * 65:(h8 + 1) * 65], bk[:, 0:65], ALU.add)
                    banks.consumed(bi8, dve)
            for h8 in (range(8) if "update" in K1_STAGES else []):
                ucol = e_tm[:, h8:h8 + 1] if fwd else c_tm[:, h8:h8 + 1]
                dve.h.tensor_scalar(Ve8[:, h8 * 65:(h8 + 1) * 65],
                                    V_s[:, tt * 520 + h8 * 65: tt * 520 + (h8 + 1) * 65],
                                    ucol, None, ALU.mult)
            p.wait(pe, dve, dve.n)
            ubks = []
            for h8 in (range(8) if "update" in K1_STAGES else []):
                hp, lane = h8 // 2, h8 % 2
                b0 = lane * 64
                bi9, bk = banks.get()
                last = pe.h.matmul(bk[0:64, 0:65],
                                   KT[:, tt * 512 + hp * 128 + b0: tt * 512 + hp * 128 + b0 + 64],
                                   Ve8[:, h8 * 65:(h8 + 1) * 65], start=True, stop=True)
                ubks.append((bi9, bk))
            if "update" in K1_STAGES:
                pe.inc(last)
            p.wait(dve, pe, pe.n)
            for h8 in (range(8) if "update" in K1_STAGES else []):
                bi9, bk = ubks[h8]
                S_h = Sst[:, h8 * 65:(h8 + 1) * 65]
                dve.h.tensor_scalar(S_h, S_h, lamc[:, h8:h8 + 1], None, ALU.mult)
                dve.h.tensor_tensor(S_h, S_h, bk[0:64, 0:65], ALU.add)
                banks.consumed(bi9, dve)
            if fwd:
                p.wait(qs, dve, dve.n)
                qs.inc(qs.h.dma_start(out=outf_dram.ap()[tt * 128:(tt + 1) * 128, :],
                                      in_=outb[:, :]))
            else:
                p.wait(dve, qs, qs.n)
                for h8 in range(8):
                    dve.h.tensor_scalar_max(tsc[:, h8:h8 + 1],
                                            outb[:, h8 * 65 + 64: h8 * 65 + 65], EPS)
                    dve.h.reciprocal(tsc[:, h8:h8 + 1], tsc[:, h8:h8 + 1])
                    dve.h.tensor_scalar(astage[:, h8 * 64:(h8 + 1) * 64],
                                        outb[:, h8 * 65: h8 * 65 + 64],
                                        tsc[:, h8:h8 + 1], None, ALU.mult)
                p.wait(qs, dve, dve.n)
                qs.inc(qs.h.dma_start(out=attn_d[tt * 128:(tt + 1) * 128, :],
                                      in_=astage[:, :]))


def build_k1():
    p = Prog()
    IO = {n: p.din(n, s) for n, s in [("x0f", [D, TFULL]), ("ln1w_c", [128, NDT]),
          ("ln1b_c", [128, NDT]), ("wq_my", [D, 512]), ("wk_my", [D, 512]),
          ("wv_my", [D, 512]), ("dec_my", [D, 8]), ("qb_c", [128, 4]),
          ("kb_c", [128, 4]), ("vb_r", [1, 512]), ("db_r", [1, 8]),
          ("u_incl", [128, 128]), ("u_rvex", [128, 128])]}
    IO["identb"] = p.nc.dram_tensor("identb", [128, 128], BF16, kind="ExternalInput").ap()
    IO["attn"] = p.dout("attn", [TFULL, 512])
    IO["outf"] = p.dint("outf", [TFULL, 520])
    p.push(); emit_k1(p, IO); p.pop()
    return p.finish()


def k1_inputs(inputs):
    c = np.ascontiguousarray
    col = lambda v: c(v.reshape(-1, 128).T)
    t = np.arange(128)
    u_incl = (t[:, None] <= t[None, :]).astype(np.float32)
    u_rvex = (t[:, None] > t[None, :]).astype(np.float32)
    import ml_dtypes
    identb = np.eye(128, dtype=ml_dtypes.bfloat16)
    maps = []
    for core in range(8):
        b, r = core // 2, core % 2
        hs = r * 512
        m = dict(
            x0f=c(np.asarray(inputs["L0_attn_x"])[b].T),
            ln1w_c=col(inputs["L0_attn_ln1_w"]), ln1b_c=col(inputs["L0_attn_ln1_b"]),
            wq_my=c(inputs["L0_attn_q_w"][:, hs:hs + 512]),
            wk_my=c(inputs["L0_attn_k_w"][:, hs:hs + 512]),
            wv_my=c(inputs["L0_attn_v_w"][:, hs:hs + 512]),
            dec_my=c(inputs["L0_attn_dec_w"][:, r * 8:(r + 1) * 8]),
            qb_c=c(inputs["L0_attn_q_b"][hs:hs + 512].reshape(4, 128).T),
            kb_c=c(inputs["L0_attn_k_b"][hs:hs + 512].reshape(4, 128).T),
            vb_r=c(inputs["L0_attn_v_b"][hs:hs + 512].reshape(1, 512)),
            db_r=c(inputs["L0_attn_dec_b"][r * 8:(r + 1) * 8].reshape(1, 8)),
            u_incl=u_incl, u_rvex=u_rvex, identb=identb)
        maps.append(m)
    return maps


GROUPS = [[0, 1], [2, 3], [4, 5], [6, 7]]


def _ag(p, in_h, out_h):
    g = p.nc.gpsimd
    g.wait_ge(p.qs.sem, p.qs.n)
    g.wait_ge(p.qg.sem, p.qg.n)
    ins = g.collective_compute(
        "AllGather", mybir.AluOpType.bypass,
        replica_groups=GROUPS,
        ins=[in_h.ap().opt()],
        outs=[out_h.ap().opt()],
    )
    ins.then_inc(p.cc.sem, 1)
    p.cc.n += 1


def build_fused(dbg=False):
    p = Prog()
    nc = p.nc
    if dbg:
        _dint = p.dint
        def dint_dbg(name, shape, dt=F32):
            if name in ("x1_i", "x2_i", "x3_i", "ctx2_pre", "ctx4_pre"):
                class H:
                    def __init__(self, ap): self._ap = ap
                    def ap(self): return self._ap
                return H(p.nc.dram_tensor(name, list(shape), dt, kind="ExternalOutput").ap())
            return _dint(name, shape, dt)
        p.dint = dint_dbg
    p.cc = Eng(p, nc.gpsimd, "cc")

    def dins(pairs):
        return {n: p.din(n, s) for n, s in pairs}

    IO1 = dins([("x0f", [D, TFULL]), ("ln1w_c", [128, NDT]), ("ln1b_c", [128, NDT]),
                ("wq_my", [D, 512]), ("wk_my", [D, 512]), ("wv_my", [D, 512]),
                ("dec_my", [D, 8]), ("qb_c", [128, 4]), ("kb_c", [128, 4]),
                ("vb_r", [1, 512]), ("db_r", [1, 8]), ("u_incl", [128, 128]),
                ("u_rvex", [128, 128])])
    IO1["identb"] = nc.dram_tensor("identb", [128, 128], BF16, kind="ExternalInput").ap()
    IO1["outf"] = p.dint("outf", [TFULL, 520])
    attn_b = p.dint("attn_b", [TFULL, 512], BF16)
    IO1["attn"] = attn_b.ap()

    IO2 = dins([("x0my", [D, HT]), ("ow", [D, D]), ("ob_c", [128, NDT]),
                ("ln2w_c", [128, NDT]), ("ln2b_c", [128, NDT]), ("f1w", [D, 4096]),
                ("f1b_c", [128, 32]), ("f2w", [4096, D]), ("f2b_c", [128, NDT]),
                ("l1s_c", [128, NDT]), ("l1h_c", [128, NDT])])
    IO2["x0"] = IO2.pop("x0my")
    x1_i = p.dint("x1_i", [D, HT])
    h1_i = p.dint("h1_i", [D, HT], BF16)
    IO2["x1"] = x1_i.ap()
    IO2["h1"] = h1_i.ap()

    IO3 = {"wq_my": p.din("wq3_my", [D, 512]), "wk_my": p.din("wk3_my", [D, 512]),
           "wv_my": p.din("wv3_my", [D, 512])}
    IO3["triu"] = nc.dram_tensor("triu", [128, 128], BF16, kind="ExternalInput").ap()
    ctx3_i = p.dint("ctx3_i", [512, TFULL], BF16)
    IO3["ctxo"] = ctx3_i.ap()

    IO4 = {"wo": p.din("wo4", [D, D]), "bo_c": p.din("bo4_c", [128, NDT]),
           "l2s_c": p.din("l2s_c", [128, NDT]), "l2h_c": p.din("l2h_c", [128, NDT]),
           "gate_w": p.din("gate_w", [D, 8]), "w1": p.din("w1", [8, D, 2048]),
           "w2": p.din("w2", [8, D, 2048]), "w3": p.din("w3", [8, 2048, D]),
           "ident": p.din("ident", [128, 128])}
    IO4["x1"] = x1_i.ap()
    x2_i = p.dint("x2_i", [D, HT])
    IO4["x2"] = x2_i.ap()

    IO5 = {"wq": p.din("wq5", [D, D]), "wk": p.din("wk5", [D, D]),
           "wv": p.din("wv5", [D, D]), "bq_c": p.din("bq5_c", [128, NDT]),
           "bk_c": p.din("bk5_c", [128, NDT]), "bv_r": p.din("bv5_r", [1, D]),
           "wp": p.din("wp5", [D, D]), "bp_c": p.din("bp5_c", [128, NDT])}
    IO5["xmy"] = x2_i.ap()
    k_b = p.dint("k_b", [D, HT], BF16)
    v_b = p.dint("v_b", [HT, D], BF16)
    kmax_b = p.dint("kmax_b", [1, 4])
    ag_k = p.dint("ag_k", [2 * D, HT], BF16)
    ag_v = p.dint("ag_v", [TFULL, D], BF16)
    ag_km = p.dint("ag_km", [2, 4])
    IO5["k_sc"] = k_b.ap()
    IO5["v_bounce"] = v_b.ap()
    IO5["kmax_b"] = kmax_b.ap()
    IO5["ag_k"] = ag_k.ap()
    IO5["ag_v"] = ag_v.ap()
    IO5["ag_kmax"] = ag_km.ap()
    IO5["q_sc"] = p.dint("q_sc", [D, HT], BF16).ap()
    x3_i = p.dint("x3_i", [D, HT])
    IO5["x3"] = x3_i.ap()

    def mid_cb(p):
        _ag(p, k_b, ag_k)
        _ag(p, v_b, ag_v)
        _ag(p, kmax_b, ag_km)
    IO5["mid_cb"] = mid_cb

    IO6 = dins([("wr_t", [D, 8]), ("br_r", [1, 8]), ("w1t", [8, D, 4096]),
                ("b1c", [8, 128, 32]), ("w2t", [8, 4096, D]), ("b2r", [8, 1, D]),
                ("lnw_c", [128, 8]), ("lnb_c", [128, 8])])
    IO6["ident"] = IO4["ident"]
    IO6["x"] = x3_i.ap()
    IO6["y"] = p.dout("y", [D, HT])

    myt0_d = p.din("my_t0", [1, 1], I32)
    ag1 = p.dint("ag1", [2 * TFULL, 512], BF16)
    ctx2_pre = p.dint("ctx2_pre", [2 * HT, 512], BF16)
    ag2 = p.dint("ag2", [2 * D, HT], BF16)
    ag3 = p.dint("ag3", [2 * 512, TFULL], BF16)
    ctx4_pre = p.dint("ctx4_pre", [D, HT], BF16)

    myt0_s = p.sb("myt0_s", [1, 1], I32)
    p.qg.inc(p.qg.h.dma_start(out=myt0_s[:, :], in_=myt0_d))

    p.push(); emit_k1(p, IO1, attn_bf16=True); p.pop()
    p.barrier()
    _ag(p, attn_b, ag1)
    g = nc.gpsimd
    g.wait_ge(p.cc.sem, p.cc.n)
    g.wait_ge(p.qg.sem, p.qg.n)
    off = g.value_load(myt0_s[0:1, 0:1])
    for r in range(2):
        p.qg.h.dma_start(out=ctx2_pre.ap()[r * HT:(r + 1) * HT, :],
                         in_=ag1.ap()[r * TFULL:(r + 1) * TFULL][bass.ds(off, HT), :])

    def loader2(p, c16):
        p.wait(p.qs, p.qg, p.qg.n)
        for r in range(2):
            for cs in range(4):
                dt = r * 4 + cs
                p.qs.h.dma_start(
                    out=c16[:, dt * HT:(dt + 1) * HT],
                    in_=ctx2_pre.ap()[r * HT:(r + 1) * HT, cs * 128:(cs + 1) * 128],
                    transpose=True)

    p.push(); emit_k2(p, IO2, ctx_loader=loader2); p.pop()
    p.barrier()
    _ag(p, h1_i, ag2)

    def loader3(p, hc, t0):
        p.wait(p.qs, p.cc)
        blk = t0 // HT
        c0 = t0 % HT
        p.qs.h.dma_start(out=sb3(hc, NDT, 512),
                         in_=fm3(ag2.ap()[blk * D:(blk + 1) * D, c0:c0 + 512]))

    p.push(); emit_k3(p, IO3, h1_loader=loader3, ctx_bf16=True); p.pop()
    p.barrier()
    _ag(p, ctx3_i, ag3)
    g.wait_ge(p.cc.sem, p.cc.n)
    off2 = g.value_load(myt0_s[0:1, 0:1])
    p.qg.h.dma_start(out=ctx4_pre.ap(), in_=ag3.ap()[:, bass.ds(off2, HT)])

    def loader4(p, c16):
        p.wait(p.qg, p.qg, p.qg.n)
        p.qg.h.dma_start(out=sb3(c16, NDT, HT), in_=fm3(ctx4_pre.ap()))

    p.push(); emit_k4(p, IO4, ctx_loader=loader4); p.pop()
    p.barrier()

    p.push(); emit_k5(p, IO5, fused=True); p.pop()
    p.barrier()

    p.push(); emit_k6(p, IO6); p.pop()
    if dbg:
        p.barrier()
        for nm, h, shape, dt in [("dbg_attn", attn_b, [TFULL, 512], BF16),
                                 ("dbg_h1", h1_i, [D, HT], BF16),
                                 ("dbg_ctx3", ctx3_i, [512, TFULL], BF16)]:
            dd = nc.dram_tensor(nm, shape, dt, kind="ExternalOutput").ap()
            p.qs.h.dma_start(out=dd, in_=h.ap())
    return p.finish()


def fused_inputs(inputs):
    import ml_dtypes
    c = np.ascontiguousarray
    col = lambda v: c(v.reshape(-1, 128).T)
    x0 = np.asarray(inputs["L0_attn_x"])
    t = np.arange(128)
    u_incl = (t[:, None] <= t[None, :]).astype(np.float32)
    u_rvex = (t[:, None] > t[None, :]).astype(np.float32)
    identb = np.eye(128, dtype=ml_dtypes.bfloat16)
    triu = np.triu(np.ones((128, 128))).astype(ml_dtypes.bfloat16)
    wqkv = inputs["L1_attn_w_qkv"]
    bqkv = inputs["L1_attn_b_qkv"]
    base = dict(
        ln1w_c=col(inputs["L0_attn_ln1_w"]), ln1b_c=col(inputs["L0_attn_ln1_b"]),
        u_incl=u_incl, u_rvex=u_rvex, identb=identb, triu=triu,
        ow=c(inputs["L0_attn_o_w"]), ob_c=col(inputs["L0_attn_o_b"]),
        ln2w_c=col(inputs["L0_attn_ln2_w"]), ln2b_c=col(inputs["L0_attn_ln2_b"]),
        f1w=c(inputs["L0_attn_f1_w"]), f1b_c=col(inputs["L0_attn_f1_b"]),
        f2w=c(inputs["L0_attn_f2_w"]), f2b_c=col(inputs["L0_attn_f2_b"]),
        l1s_c=col(inputs["L0_ffn_ln1_scale"]), l1h_c=col(inputs["L0_ffn_ln1_shift"]),
        wo4=c(inputs["L0_ffn_wo"]), bo4_c=col(inputs["L0_ffn_bo"]),
        l2s_c=col(inputs["L0_ffn_ln2_scale"]), l2h_c=col(inputs["L0_ffn_ln2_shift"]),
        gate_w=c(inputs["L0_ffn_gate_w"]),
        w1=c(inputs["L0_ffn_w1"]), w2=c(inputs["L0_ffn_w2"]), w3=c(inputs["L0_ffn_w3"]),
        ident=np.eye(128, dtype=np.float32),
        wq5=c(wqkv[:, 0:1024]), wk5=c(wqkv[:, 1024:2048]), wv5=c(wqkv[:, 2048:3072]),
        bq5_c=col(bqkv[0:1024]), bk5_c=col(bqkv[1024:2048]),
        bv5_r=c(bqkv[2048:3072].reshape(1, -1)),
        wp5=c(inputs["L1_attn_w_proj"]), bp5_c=col(inputs["L1_attn_b_proj"]),
        wr_t=c(inputs["L1_ffn_Wr"].T), br_r=c(inputs["L1_ffn_br"].reshape(1, 8)),
        w1t=c(inputs["L1_ffn_W1"].transpose(0, 2, 1)),
        b1c=c(inputs["L1_ffn_b1"].reshape(8, 32, 128).transpose(0, 2, 1)),
        w2t=c(inputs["L1_ffn_W2"].transpose(0, 2, 1)),
        b2r=c(inputs["L1_ffn_b2"].reshape(8, 1, 1024)),
        lnw_c=c(inputs["L1_ffn_ln_w"].reshape(8, 128).T),
        lnb_c=c(inputs["L1_ffn_ln_b"].reshape(8, 128).T))
    maps = []
    for core in range(8):
        b, r = core // 2, core % 2
        hs = r * 512
        m = dict(base)
        m["x0f"] = c(x0[b].T)
        m["x0my"] = c(x0[b, r * HT:(r + 1) * HT, :].T)
        m["wq_my"] = c(inputs["L0_attn_q_w"][:, hs:hs + 512])
        m["wk_my"] = c(inputs["L0_attn_k_w"][:, hs:hs + 512])
        m["wv_my"] = c(inputs["L0_attn_v_w"][:, hs:hs + 512])
        m["dec_my"] = c(inputs["L0_attn_dec_w"][:, r * 8:(r + 1) * 8])
        m["qb_c"] = c(inputs["L0_attn_q_b"][hs:hs + 512].reshape(4, 128).T)
        m["kb_c"] = c(inputs["L0_attn_k_b"][hs:hs + 512].reshape(4, 128).T)
        m["vb_r"] = c(inputs["L0_attn_v_b"][hs:hs + 512].reshape(1, 512))
        m["db_r"] = c(inputs["L0_attn_dec_b"][r * 8:(r + 1) * 8].reshape(1, 8))
        m["wq3_my"] = c(inputs["L0_ffn_wq"][:, hs:hs + 512])
        m["wk3_my"] = c(inputs["L0_ffn_wk"][:, hs:hs + 512])
        m["wv3_my"] = c(inputs["L0_ffn_wv"][:, hs:hs + 512])
        m["my_t0"] = np.array([[r * HT]], dtype=np.int32)
        maps.append(m)
    return maps


def kernel_fused(inputs, dbg=False, res_out=None):
    nc = build_fused(dbg=dbg)
    res = run_bass_kernel_spmd(nc, fused_inputs(inputs), core_ids=list(range(8)))
    if res_out is not None:
        res_out.append(res)
    out = np.zeros((4, TFULL, D), dtype=np.float32)
    for core in range(8):
        b, h = core // 2, core % 2
        out[b, h * HT:(h + 1) * HT, :] = res.results[core]["y"].T
    return out


def kernel(**inputs):
    inputs = {k: np.asarray(v) for k, v in inputs.items()}
    return kernel_fused(inputs)


# version 04
# speedup vs baseline: 1.2147x; 1.1230x over previous
import contextlib
import numpy as np
import concourse.bass as bass
import concourse.mybir as mybir
from concourse.bass_utils import run_bass_kernel_spmd

F32 = mybir.dt.float32
BF16 = mybir.dt.bfloat16
I32 = mybir.dt.int32
AF = mybir.ActivationFunctionType
ALU = mybir.AluOpType
AX = mybir.AxisListType.X

K1_STAGES = {"inter", "intra", "update"}
K1_SWEEPS = 2
K1_NTT = 32
D = 1024
NDT = 8
TFULL = 4096
HT = 2048
NTT = HT // 128


class _AutoProxy:

    def __init__(self, eng):
        object.__setattr__(self, "_e", eng)

    def __getattr__(self, name):
        e = self._e
        f = getattr(e.raw, name)
        if name in ("wait_ge",) or not callable(f):
            return f

        def wrapped(*a, **k):
            if e.n > 0:
                e.raw.wait_ge(e.sem, e.n)
            r = f(*a, **k)
            if hasattr(r, "then_inc"):
                r.then_inc(e.sem, 1)
                e.n += 1
            return r
        return wrapped


class _DmaProxy:

    def __init__(self, eng):
        object.__setattr__(self, "_e", eng)

    def __getattr__(self, name):
        e = self._e
        f = getattr(e.raw, name)
        if name != "dma_start" or not callable(f):
            return f

        def wrapped(*a, **k):
            if e.n > 0:
                e.raw.wait_ge(e.sem, e.n)
            r = f(*a, **k)
            r.then_inc(e.sem, 16)
            e.n += 16
            return r
        return wrapped


class Eng:
    def __init__(self, prog, handle, name, step=1, auto=False, dma=False):
        self.p = prog
        self.raw = handle
        self.name = name
        self.step = step
        self.auto = auto
        self.sem = prog.es.enter_context(prog.nc.semaphore("sem_" + name))
        self.n = 0
        self._waited = {}
        if auto:
            self.h = _AutoProxy(self)
        elif dma:
            self.h = _DmaProxy(self)
        else:
            self.h = handle

    def inc(self, instr, k=None):
        if self.auto or isinstance(self.h, _DmaProxy):
            return self.n
        k = self.step if k is None else k
        instr.then_inc(self.sem, k)
        self.n += k
        return self.n


class Prog:
    def __init__(self):
        self.nc = bass.Bass("TRN2", target_bir_lowering=False)
        self.es = contextlib.ExitStack()
        nc = self.nc
        self.pe = Eng(self, nc.tensor, "pe")
        self.act = Eng(self, nc.scalar, "act", auto=True)
        self.dve = Eng(self, nc.vector, "dve", auto=True)
        self.qs = Eng(self, nc.sync, "qs", step=16, dma=True)
        self.qg = Eng(self, nc.gpsimd, "qg", step=16, dma=True)
        self.engines = [self.pe, self.act, self.dve, self.qs, self.qg]

    def wait(self, consumer, producer, n=None):
        tgt = producer.n if n is None else n
        if tgt <= 0:
            return
        key = consumer.name
        last = producer._waited.get(key, -1)
        if tgt <= last:
            return
        consumer.h.wait_ge(producer.sem, tgt)
        producer._waited[key] = tgt

    def push(self):
        if not hasattr(self, "_stacks"):
            self._stacks = []
        self._stacks.append(contextlib.ExitStack())

    def pop(self):
        self._stacks.pop().close()

    def _stack(self):
        st = getattr(self, "_stacks", None)
        return st[-1] if st else self.es

    def barrier(self):
        for c in self.engines:
            for pr in self.engines:
                if c is not pr:
                    self.wait(c, pr)

    def sb(self, name, shape, dt):
        name = f"{name}_{self._uid()}"
        return self._stack().enter_context(self.nc.sbuf_tensor(name, list(shape), dt))

    def ps(self, name, shape, dt=F32):
        name = f"{name}_{self._uid()}"
        return self._stack().enter_context(self.nc.psum_tensor(name, list(shape), dt))

    def _uid(self):
        self._n = getattr(self, "_n", 0) + 1
        return self._n

    def din(self, name, shape, dt=F32):
        return self.nc.dram_tensor(name, list(shape), dt, kind="ExternalInput").ap()

    def dout(self, name, shape, dt=F32):
        return self.nc.dram_tensor(name, list(shape), dt, kind="ExternalOutput").ap()

    def dint(self, name, shape, dt=F32):
        return self.nc.dram_tensor(name, list(shape), dt)

    def finish(self):
        for e in self.engines[1:]:
            self.wait(self.engines[0], e)
        self.es.close()
        return self.nc


class Pool8:

    def __init__(self, p, n=8):
        self.p = p
        self.t = [p.ps(f"bank{i}", [128, 512]) for i in range(n)]
        self.guard = [None] * n
        self.i = 0

    def get(self):
        i = self.i % len(self.t)
        self.i += 1
        g = self.guard[i]
        if g is not None:
            self.p.wait(self.p.pe, g[0], g[1])
        return i, self.t[i]

    def consumed(self, i, eng):
        self.guard[i] = (eng, eng.n)


def fm3(dram_ap, p_=128):
    return dram_ap.rearrange("(dt p) t -> p dt t", p=p_)


def sb3(sbuf, ndt, T):
    return sbuf[:, 0:ndt * T].rearrange("p (dt t) -> p dt t", t=T)


def fm_load(p, q, sbuf, dram_ap, Dfull, T, cast=False):
    nd = Dfull // 128
    q.inc(q.h.dma_start(out=sb3(sbuf, nd, T), in_=fm3(dram_ap)))


def fm_store(p, q, dram_ap, sbuf, Dfull, T):
    nd = Dfull // 128
    q.inc(q.h.dma_start(out=fm3(dram_ap), in_=sb3(sbuf, nd, T)))


def row_stats(p, banks, src32, sq_sc, onesc, rows_s, rows_q, T, ndt=NDT, HTl=HT, t0=0, sqw=512, cw0=512):
    pe, act, dve = p.pe, p.act, p.dve
    nch = (T + cw0 - 1) // cw0
    for c in range(nch):
        c0 = c * cw0
        cw = min(cw0, T - c0)
        p.wait(p.act, pe, pe.n)
        for dt in range(ndt):
            last = act.h.activation(sq_sc[:, dt * sqw: dt * sqw + cw],
                                    src32[:, dt * HTl + t0 + c0: dt * HTl + t0 + c0 + cw], AF.Copy)
        act.inc(last)
        p.wait(pe, act, act.n)
        bi1, b1 = banks.get()
        for dt in range(ndt):
            last = pe.h.matmul(b1[0:1, 0:cw], onesc[:, 0:1],
                               sq_sc[:, dt * sqw: dt * sqw + cw],
                               start=(dt == 0), stop=(dt == ndt - 1))
        pe.inc(last)
        p.wait(act, pe, pe.n)
        for dt in range(ndt):
            last = act.h.activation(sq_sc[:, dt * sqw: dt * sqw + cw],
                                    sq_sc[:, dt * sqw: dt * sqw + cw], AF.Square)
        act.inc(last)
        p.wait(pe, act, act.n)
        bi2, b2 = banks.get()
        for dt in range(ndt):
            last = pe.h.matmul(b2[0:1, 0:cw], onesc[:, 0:1],
                               sq_sc[:, dt * sqw: dt * sqw + cw],
                               start=(dt == 0), stop=(dt == ndt - 1))
        pe.inc(last)
        p.wait(dve, pe, pe.n)
        dve.h.tensor_copy(rows_s[0:1, c0:c0 + cw], b1[0:1, 0:cw])
        dve.inc(dve.h.tensor_copy(rows_q[0:1, c0:c0 + cw], b2[0:1, 0:cw]))
        banks.consumed(bi1, dve)
        banks.consumed(bi2, dve)


def ln_finalize_rows(p, rows_s, rows_q, tmp_row, T, nd, eps):
    act, dve = p.act, p.dve
    dve.h.tensor_scalar_mul(rows_s[0:1, 0:T], rows_s[0:1, 0:T], 1.0 / nd)
    dve.h.tensor_scalar_mul(rows_q[0:1, 0:T], rows_q[0:1, 0:T], 1.0 / nd)
    dve.h.tensor_tensor(tmp_row[0:1, 0:T], rows_s[0:1, 0:T], rows_s[0:1, 0:T], ALU.mult)
    dve.inc(dve.h.tensor_tensor(rows_q[0:1, 0:T], rows_q[0:1, 0:T], tmp_row[0:1, 0:T], ALU.subtract))
    dve.inc(dve.h.tensor_scalar_add(rows_q[0:1, 0:T], rows_q[0:1, 0:T], float(eps)))
    p.wait(act, dve, dve.n)
    act.inc(act.h.activation(rows_q[0:1, 0:T], rows_q[0:1, 0:T], AF.Sqrt))
    p.wait(dve, act, act.n)
    dve.h.reciprocal(rows_q[0:1, 0:T], rows_q[0:1, 0:T])
    dve.h.tensor_tensor(rows_s[0:1, 0:T], rows_s[0:1, 0:T], rows_q[0:1, 0:T], ALU.mult)
    dve.inc(dve.h.tensor_scalar_mul(rows_s[0:1, 0:T], rows_s[0:1, 0:T], -1.0))


def emit_k6(p, IO):
    pe, act, dve, qs, qg = p.pe, p.act, p.dve, p.qs, p.qg
    x_d = IO["x"]; wr_d = IO["wr_t"]; br_d = IO["br_r"]; w1_d = IO["w1t"]
    b1_d = IO["b1c"]; w2_d = IO["w2t"]; b2_d = IO["b2r"]; lnw_d = IO["lnw_c"]
    lnb_d = IO["lnb_c"]; id_d = IO["ident"]; y_d = IO["y"]

    FC = 1024
    NFC = 4096 // FC
    NFT = FC // 128

    acc = p.sb("acc", [128, NDT * HT], F32)
    x16 = p.sb("x16", [128, NDT * HT], BF16)
    w1c = p.sb("w1c", [128, NDT * FC], BF16)
    w2c = p.sb("w2c", [128, NFT * D], BF16)
    hbuf = p.sb("hbuf", [128, NFT * HT], BF16)
    wr_s = p.sb("wr_s", [128, NDT * 8], BF16)
    br_s = p.sb("br_s", [1, 8], BF16)
    id32 = p.sb("id32", [128, 128], F32)
    b1_s = p.sb("b1_s", [128, 32], F32)
    b2_s = p.sb("b2_s", [1, D], BF16)
    lnw_s = p.sb("lnw_s", [128, 8], F32)
    lnb_s = p.sb("lnb_s", [128, 8], F32)
    gts = p.sb("gts", [128, NTT * 8], F32)
    logit = p.sb("logit", [128, NTT * 8], F32)
    wrow = p.sb("wrow", [1, HT], BF16)
    wbc = p.sb("wbc", [128, HT], BF16)
    ones1 = p.sb("ones1", [1, 128], BF16)
    ones1f = p.sb("ones1f", [1, 128], F32)
    onesc = p.sb("onesc", [128, 1], BF16)
    rows_s = p.sb("rows_s", [1, 1024], F32)
    rows_q = p.sb("rows_q", [1, 1024], F32)
    tmp_row = p.sb("tmp_row", [1, 1024], F32)
    sq_sc = p.sb("sq_sc", [128, NDT * 256], BF16)
    x3c = p.sb("x3c", [128, 4 * 512], F32)
    tsc = p.sb("tsc", [128, 512], F32)
    tsc2 = p.sb("tsc2", [128, 512], F32)

    banks = Pool8(p)

    fm_load(p, qs, acc, x_d, D, HT)
    qg.inc(qg.h.dma_start(out=sb3(x16, NDT, HT), in_=fm3(x_d)))
    qg.inc(qg.h.dma_start(out=sb3(wr_s, NDT, 8), in_=fm3(wr_d)))
    qg.inc(qg.h.dma_start(out=br_s[:, :], in_=br_d))
    qs.inc(qs.h.dma_start(out=id32[:, :], in_=id_d))
    qs.inc(qs.h.dma_start(out=lnw_s[:, :], in_=lnw_d))
    qs.inc(qs.h.dma_start(out=lnb_s[:, :], in_=lnb_d))
    dve.h.memset(ones1[:, :], 1.0)
    dve.h.memset(ones1f[:, :], 1.0)
    dve.inc(dve.h.memset(onesc[:, :], 1.0))

    p.wait(pe, qg, qg.n)
    p.wait(pe, qs, qs.n)
    p.wait(dve, qs, qs.n)
    p.wait(act, qs, qs.n)
    p.wait(pe, dve, dve.n)
    bi, bank = banks.get()
    for tt in range(NTT):
        for dt in range(NDT):
            pe.h.matmul(bank[:, tt * 8:(tt + 1) * 8],
                        x16[:, dt * HT + tt * 128: dt * HT + (tt + 1) * 128],
                        wr_s[:, dt * 8:(dt + 1) * 8],
                        start=(dt == 0), stop=False)
        last = pe.h.matmul(bank[:, tt * 8:(tt + 1) * 8], ones1[:, :], br_s[:, :],
                           start=False, stop=True)
    pe.inc(last)
    p.wait(dve, pe, pe.n)
    dve.inc(dve.h.tensor_copy(logit[:, :], bank[:, 0:NTT * 8]))
    banks.consumed(bi, dve)

    m1 = p.sb("m1", [128, NTT], F32)
    nm1 = p.sb("nm1", [128, NTT], F32)
    m2 = p.sb("m2", [128, NTT], F32)
    zs = p.sb("zs", [128, NTT], F32)
    tg = p.sb("tg", [128, 8], F32)
    for tt in range(NTT):
        l8 = logit[:, tt * 8:(tt + 1) * 8]
        g8 = gts[:, tt * 8:(tt + 1) * 8]
        p.wait(dve, act, act.n)
        dve.h.reduce_max(m1[:, tt:tt + 1], l8, axis=AX)
        dve.h.reduce_max(nm1[:, tt:tt + 1], l8, axis=AX, negate=True)
        dve.h.tensor_scalar(tg[:, :], l8, m1[:, tt:tt + 1], None, ALU.is_ge)
        dve.h.scalar_tensor_tensor(tg[:, :], tg[:, :], -1e30, l8, ALU.mult, ALU.add)
        dve.h.reduce_max(m2[:, tt:tt + 1], tg[:, :], axis=AX)
        dve.inc(dve.h.tensor_scalar(g8, l8, m2[:, tt:tt + 1], None, ALU.is_ge))
        p.wait(act, dve, dve.n)
        act.inc(act.h.activation(tg[:, :], l8, AF.Exp, bias=nm1[:, tt:tt + 1]))
        p.wait(dve, act, act.n)
        dve.h.tensor_tensor(g8, g8, tg[:, :], ALU.mult)
        dve.h.reduce_sum(zs[:, tt:tt + 1], g8, axis=AX)
        dve.h.reciprocal(zs[:, tt:tt + 1], zs[:, tt:tt + 1])
        dve.inc(dve.h.tensor_scalar_mul(g8, g8, zs[:, tt:tt + 1]))

    st6 = {"w1g": None, "w2g": None, "w1n": 0, "w2n": 0}

    def k6_load_w1(idx):
        e, fc = idx // NFC, idx % NFC
        if st6["w1g"] is not None:
            p.wait(qg, st6["w1g"][0], st6["w1g"][1])
        qg.inc(qg.h.dma_start(
            out=sb3(w1c, NDT, FC),
            in_=fm3(w1_d[e, :, fc * FC:(fc + 1) * FC])))
        st6["w1n"] = qg.n

    def k6_load_w2(idx):
        e, fc = idx // NFC, idx % NFC
        if st6["w2g"] is not None:
            p.wait(qg, st6["w2g"][0], st6["w2g"][1])
        qg.inc(qg.h.dma_start(
            out=sb3(w2c, NFT, D),
            in_=fm3(w2_d[e, fc * FC:(fc + 1) * FC, :])))
        st6["w2n"] = qg.n

    k6_load_w1(0)
    for e in range(8):
        p.wait(pe, dve, dve.n)
        wr_banks = []
        for c in range(4):
            bi, bank = banks.get()
            for j in range(4):
                tt = c * 4 + j
                last = pe.h.matmul(bank[0:1, j * 128:(j + 1) * 128],
                                   gts[:, tt * 8 + e: tt * 8 + e + 1], id32[:, :],
                                   start=True, stop=True)
            wr_banks.append((bi, bank, c))
        pe.inc(last)
        p.wait(dve, pe, pe.n)
        for bi, bank, c in wr_banks:
            last = dve.h.tensor_copy(wrow[0:1, c * 512:(c + 1) * 512], bank[0:1, 0:512])
        dve.inc(last)
        for bi, bank, c in wr_banks:
            banks.consumed(bi, dve)
        p.wait(pe, dve, dve.n)
        wb_banks = []
        for c in range(4):
            bi, bank = banks.get()
            last = pe.h.matmul(bank[:, :], ones1[:, :], wrow[0:1, c * 512:(c + 1) * 512],
                               start=True, stop=True)
            wb_banks.append((bi, bank, c))
        pe.inc(last)
        p.wait(dve, pe, pe.n)
        for bi, bank, c in wb_banks:
            last = dve.h.tensor_copy(wbc[:, c * 512:(c + 1) * 512], bank[:, :])
        dve.inc(last)
        for bi, bank, c in wb_banks:
            banks.consumed(bi, dve)

        for fc in range(NFC):
            idx = e * NFC + fc
            p.wait(qg, act, act.n)
            qg.inc(qg.h.dma_start(out=b1_s[:, 0:NFT],
                                  in_=b1_d[e, :, fc * NFT:(fc + 1) * NFT]))
            if fc == NFC - 1:
                qg.inc(qg.h.dma_start(out=b2_s[:, :], in_=b2_d[e, 0:1, :]))
            b1n = qg.n
            k6_load_w2(idx)
            p.wait(pe, qg, st6["w1n"])
            p.wait(act, qg, b1n)
            p.wait(pe, dve, dve.n)
            for ft in range(NFT):
                for tb in range(4):
                    bi, bank = banks.get()
                    for dt in range(NDT):
                        last = pe.h.matmul(bank[:, :],
                                    w1c[:, dt * FC + ft * 128: dt * FC + (ft + 1) * 128],
                                    x16[:, dt * HT + tb * 512: dt * HT + (tb + 1) * 512],
                                    start=(dt == 0), stop=(dt == NDT - 1))
                    pe.inc(last)
                    p.wait(act, pe, pe.n)
                    act.inc(act.h.activation(
                        hbuf[:, ft * HT + tb * 512: ft * HT + (tb + 1) * 512],
                        bank[:, :], AF.Gelu, bias=b1_s[:, ft:ft + 1]))
                    banks.consumed(bi, act)
            st6["w1g"] = (pe, pe.n)
            if idx + 1 < 8 * NFC:
                k6_load_w1(idx + 1)
            p.wait(dve, act, act.n)
            for ft in range(NFT):
                last = dve.h.tensor_tensor(hbuf[:, ft * HT:(ft + 1) * HT],
                                    hbuf[:, ft * HT:(ft + 1) * HT],
                                    wbc[:, :], ALU.mult)
            dve.inc(last)
            p.wait(pe, dve, dve.n)
            p.wait(pe, qg, st6["w2n"])
            for mt in range(NDT):
                for tb in range(4):
                    bi, bank = banks.get()
                    last_real = (fc != NFC - 1)
                    for ft in range(NFT):
                        last = pe.h.matmul(bank[:, :],
                                    w2c[:, ft * D + mt * 128: ft * D + (mt + 1) * 128],
                                    hbuf[:, ft * HT + tb * 512: ft * HT + (tb + 1) * 512],
                                    start=(ft == 0),
                                    stop=(last_real and ft == NFT - 1))
                    if fc == NFC - 1:
                        last = pe.h.matmul(bank[:, :], b2_s[0:1, mt * 128:(mt + 1) * 128],
                                    wrow[0:1, tb * 512:(tb + 1) * 512],
                                    start=False, stop=True)
                    pe.inc(last)
                    p.wait(dve, pe, pe.n)
                    sl = acc[:, mt * HT + tb * 512: mt * HT + (tb + 1) * 512]
                    dve.inc(dve.h.tensor_tensor(sl, sl, bank[:, :], ALU.add))
                    banks.consumed(bi, dve)
            st6["w2g"] = (pe, pe.n)

    for ln_i in range(2):
        for hf in range(2):
            t0 = hf * 1024
            p.wait(act, dve, dve.n)
            row_stats(p, banks, acc, sq_sc, onesc, rows_s, rows_q, 1024, t0=t0, sqw=256, cw0=256)
            p.wait(dve, pe, pe.n)
            eps = 1e-12 if ln_i == 0 else 1e-5
            ln_finalize_rows(p, rows_s, rows_q, tmp_row, 1024, float(D), eps)
            for c in range(2):
                c0 = t0 + c * 512
                r0 = c * 512
                p.wait(pe, dve, dve.n)
                bi1, bankR = banks.get()
                pe.h.matmul(bankR[:, :], ones1f[:, :], rows_q[0:1, r0:r0 + 512], start=True, stop=True)
                bi2, bankM = banks.get()
                last = pe.h.matmul(bankM[:, :], ones1f[:, :], rows_s[0:1, r0:r0 + 512], start=True, stop=True)
                pe.inc(last)
                p.wait(dve, pe, pe.n)
                if ln_i == 0:
                    p.wait(dve, qs, qs.n)
                for dt in range(NDT):
                    if ln_i == 0 and dt % 4 == 0:
                        p.wait(qs, dve, dve.n)
                        qs.inc(qs.h.dma_start(
                            out=sb3(x3c, 4, 512),
                            in_=fm3(x_d[dt * 128:(dt + 4) * 128, c0:c0 + 512])))
                        p.wait(dve, qs, qs.n)
                    sl = acc[:, dt * HT + c0: dt * HT + c0 + 512]
                    dve.h.tensor_tensor(tsc[:, :], sl, bankR[:, :], ALU.mult)
                    dve.inc(dve.h.tensor_tensor(tsc[:, :], tsc[:, :], bankM[:, :], ALU.add))
                    if ln_i == 0:
                        dve.h.tensor_scalar(tsc2[:, :], tsc[:, :],
                                            lnw_s[:, dt:dt + 1], lnb_s[:, dt:dt + 1],
                                            ALU.mult, ALU.add)
                        dve.inc(dve.h.tensor_tensor(sl, tsc2[:, :],
                                                    x3c[:, (dt % 4) * 512:(dt % 4 + 1) * 512], ALU.add))
                    else:
                        dve.inc(dve.h.tensor_copy(sl, tsc[:, :]))
                banks.consumed(bi1, dve)
                banks.consumed(bi2, dve)

    p.wait(qs, dve, dve.n)
    fm_store(p, qs, y_d, acc, D, HT)


def build_k6():
    p = Prog()
    IO = {n: p.din(n, s) for n, s in [("x", [D, HT]), ("wr_t", [D, 8]), ("br_r", [1, 8]),
          ("w1t", [8, D, 4096]), ("b1c", [8, 128, 32]), ("w2t", [8, 4096, D]),
          ("b2r", [8, 1, D]), ("lnw_c", [128, 8]), ("lnb_c", [128, 8]),
          ("ident", [128, 128])]}
    IO["y"] = p.dout("y", [D, HT])
    p.push(); emit_k6(p, IO); p.pop()
    return p.finish()



def k6_inputs(inputs, x3_full):
    c = np.ascontiguousarray
    wr_t = c(inputs["L1_ffn_Wr"].T)
    br_r = c(inputs["L1_ffn_br"].reshape(1, 8))
    w1t = c(inputs["L1_ffn_W1"].transpose(0, 2, 1))
    b1c = c(inputs["L1_ffn_b1"].reshape(8, 32, 128).transpose(0, 2, 1))
    w2t = c(inputs["L1_ffn_W2"].transpose(0, 2, 1))
    b2r = c(inputs["L1_ffn_b2"].reshape(8, 1, 1024))
    lnw_c = c(inputs["L1_ffn_ln_w"].reshape(8, 128).T)
    lnb_c = c(inputs["L1_ffn_ln_b"].reshape(8, 128).T)
    ident = np.eye(128, dtype=np.float32)
    maps = []
    for core in range(8):
        b, h = core // 2, core % 2
        x = c(x3_full[b, h * HT:(h + 1) * HT, :].T)
        maps.append(dict(x=x, wr_t=wr_t, br_r=br_r, w1t=w1t, b1c=b1c,
                         w2t=w2t, b2r=b2r, lnw_c=lnw_c, lnb_c=lnb_c, ident=ident))
    return maps


def k6_collect(res):
    out = np.zeros((4, TFULL, D), dtype=np.float32)
    for core in range(8):
        b, h = core // 2, core % 2
        out[b, h * HT:(h + 1) * HT, :] = res.results[core]["y"].T
    return out


def plain_ln_to_acc(p, banks, acc, sq_sc, onesc, ones1f, rows_s, rows_q, tmp_row,
                    tsc, eps, per_dt=None, x_extra=None, x3c=None, HTl=HT):
    pe, act, dve, qs = p.pe, p.act, p.dve, p.qs
    for hf in range(HTl // 1024):
        t0 = hf * 1024
        p.wait(act, dve, dve.n)
        row_stats(p, banks, acc, sq_sc, onesc, rows_s, rows_q, 1024, t0=t0, HTl=HTl)
        p.wait(dve, pe, pe.n)
        ln_finalize_rows(p, rows_s, rows_q, tmp_row, 1024, float(D), eps)
        for c in range(2):
            c0 = t0 + c * 512
            r0 = c * 512
            p.wait(pe, dve, dve.n)
            bi1, bankR = banks.get()
            pe.h.matmul(bankR[:, :], ones1f[:, :], rows_q[0:1, r0:r0 + 512], start=True, stop=True)
            bi2, bankM = banks.get()
            last = pe.h.matmul(bankM[:, :], ones1f[:, :], rows_s[0:1, r0:r0 + 512], start=True, stop=True)
            pe.inc(last)
            p.wait(dve, pe, pe.n)
            for dt in range(NDT):
                sl = acc[:, dt * HTl + c0: dt * HTl + c0 + 512]
                dve.h.tensor_tensor(tsc[:, :], sl, bankR[:, :], ALU.mult)
                dve.h.tensor_tensor(tsc[:, :], tsc[:, :], bankM[:, :], ALU.add)
                if per_dt is None:
                    dve.h.tensor_copy(sl, tsc[:, :])
                else:
                    per_dt(dt, c0, tsc, sl)
            banks.consumed(bi1, dve)
            banks.consumed(bi2, dve)


def add_dram_fm(p, banks, acc, dram_ap, x3c, HTl=HT):
    dve, qs = p.dve, p.qs
    for c in range(HTl // 512):
        c0 = c * 512
        for half in range(2):
            dt0 = half * 4
            p.wait(qs, dve, dve.n)
            qs.inc(qs.h.dma_start(
                out=sb3(x3c, 4, 512),
                in_=fm3(dram_ap[dt0 * 128:(dt0 + 4) * 128, c0:c0 + 512])))
            p.wait(dve, qs, qs.n)
            for dt in range(dt0, dt0 + 4):
                sl = acc[:, dt * HTl + c0: dt * HTl + c0 + 512]
                dve.h.tensor_tensor(sl, sl, x3c[:, (dt % 4) * 512:(dt % 4 + 1) * 512], ALU.add)


def topk2_gates(p, banks, x16, w_s, b_row, gts, logit, scratch, HTl=HT, nbias=True):
    pe, act, dve = p.pe, p.act, p.dve
    ones1, m1, nm1, m2, zs, tg = scratch
    ntt = HTl // 128
    bi, bank = banks.get()
    for tt in range(ntt):
        lastmm = None
        for dt in range(NDT):
            lastmm = pe.h.matmul(bank[:, tt * 8:(tt + 1) * 8],
                        x16[:, dt * HTl + tt * 128: dt * HTl + (tt + 1) * 128],
                        w_s[:, dt * 8:(dt + 1) * 8],
                        start=(dt == 0), stop=(dt == NDT - 1 and not nbias))
        if nbias:
            lastmm = pe.h.matmul(bank[:, tt * 8:(tt + 1) * 8], ones1[:, :], b_row[:, :],
                                 start=False, stop=True)
    pe.inc(lastmm)
    p.wait(dve, pe, pe.n)
    dve.h.tensor_copy(logit[:, 0:ntt * 8], bank[:, 0:ntt * 8])
    banks.consumed(bi, dve)
    for tt in range(ntt):
        l8 = logit[:, tt * 8:(tt + 1) * 8]
        g8 = gts[:, tt * 8:(tt + 1) * 8]
        p.wait(dve, act, act.n)
        dve.h.reduce_max(m1[:, tt:tt + 1], l8, axis=AX)
        dve.h.reduce_max(nm1[:, tt:tt + 1], l8, axis=AX, negate=True)
        dve.h.tensor_scalar(tg[:, :], l8, m1[:, tt:tt + 1], None, ALU.is_ge)
        dve.h.scalar_tensor_tensor(tg[:, :], tg[:, :], -1e30, l8, ALU.mult, ALU.add)
        dve.h.reduce_max(m2[:, tt:tt + 1], tg[:, :], axis=AX)
        dve.h.tensor_scalar(g8, l8, m2[:, tt:tt + 1], None, ALU.is_ge)
        p.wait(act, dve, dve.n)
        act.h.activation(tg[:, :], l8, AF.Exp, bias=nm1[:, tt:tt + 1])
        p.wait(dve, act, act.n)
        dve.h.tensor_tensor(g8, g8, tg[:, :], ALU.mult)
        dve.h.reduce_sum(zs[:, tt:tt + 1], g8, axis=AX)
        dve.h.reciprocal(zs[:, tt:tt + 1], zs[:, tt:tt + 1])
        dve.h.tensor_scalar_mul(g8, g8, zs[:, tt:tt + 1])


def wrow_wbc(p, banks, gts, e, id32, ones1, wrow, wbc, HTl=HT):
    pe, dve = p.pe, p.dve
    ntt = HTl // 128
    p.wait(pe, dve, dve.n)
    wr_banks = []
    for c in range(ntt // 4):
        bi, bank = banks.get()
        for j in range(4):
            tt = c * 4 + j
            last = pe.h.matmul(bank[0:1, j * 128:(j + 1) * 128],
                               gts[:, tt * 8 + e: tt * 8 + e + 1], id32[:, :],
                               start=True, stop=True)
        wr_banks.append((bi, bank, c))
    pe.inc(last)
    p.wait(dve, pe, pe.n)
    for bi, bank, c in wr_banks:
        dve.h.tensor_copy(wrow[0:1, c * 512:(c + 1) * 512], bank[0:1, 0:512])
    for bi, bank, c in wr_banks:
        banks.consumed(bi, dve)
    p.wait(pe, dve, dve.n)
    wb_banks = []
    for c in range(HTl // 512):
        bi, bank = banks.get()
        last = pe.h.matmul(bank[:, :], ones1[:, :], wrow[0:1, c * 512:(c + 1) * 512],
                           start=True, stop=True)
        wb_banks.append((bi, bank, c))
    pe.inc(last)
    p.wait(dve, pe, pe.n)
    for bi, bank, c in wb_banks:
        dve.h.tensor_copy(wbc[:, c * 512:(c + 1) * 512], bank[:, :])
    for bi, bank, c in wb_banks:
        banks.consumed(bi, dve)


def emit_k2(p, IO, ctx_loader=None):
    pe, act, dve, qs, qg = p.pe, p.act, p.dve, p.qs, p.qg
    ctx_d = IO.get("ctx"); x0_d = IO["x0"]; ow_d = IO["ow"]; ob_d = IO["ob_c"]
    ln2w_d = IO["ln2w_c"]; ln2b_d = IO["ln2b_c"]; f1_d = IO["f1w"]; f1b_d = IO["f1b_c"]
    f2_d = IO["f2w"]; f2b_d = IO["f2b_c"]; l1s_d = IO["l1s_c"]; l1h_d = IO["l1h_c"]
    x1_d = IO["x1"]; h1_d = IO["h1"]

    FC = 1024
    NFC = 4096 // FC
    NFT = FC // 128

    acc = p.sb("acc", [128, NDT * HT], F32)
    c16 = p.sb("c16", [128, NDT * HT], BF16)
    w_a = p.sb("w_a", [128, NDT * 1024], BF16)
    w_b = p.sb("w_b", [128, NDT * 1024], BF16)
    hbuf = p.sb("hbuf", [128, NFT * HT], BF16)
    ob_s = p.sb("ob_s", [128, NDT], F32)
    ln2w_s = p.sb("ln2w_s", [128, NDT], F32)
    ln2b_s = p.sb("ln2b_s", [128, NDT], F32)
    f1b_s = p.sb("f1b_s", [128, 32], F32)
    f2b_s = p.sb("f2b_s", [128, NDT], F32)
    l1s_s = p.sb("l1s_s", [128, NDT], F32)
    l1h_s = p.sb("l1h_s", [128, NDT], F32)
    onesc = p.sb("onesc", [128, 1], BF16)
    ones1f = p.sb("ones1f", [1, 128], F32)
    rows_s = p.sb("rows_s", [1, 1024], F32)
    rows_q = p.sb("rows_q", [1, 1024], F32)
    tmp_row = p.sb("tmp_row", [1, 1024], F32)
    sq_sc = p.sb("sq_sc", [128, NDT * 512], BF16)
    x3c = p.sb("x3c", [128, 4 * 512], F32)
    tsc = p.sb("tsc", [128, 512], F32)

    banks = Pool8(p)

    fm_load(p, qs, acc, x0_d, D, HT)
    if ctx_loader is None:
        qg.inc(qg.h.dma_start(out=sb3(c16, NDT, HT), in_=fm3(ctx_d)))
    else:
        ctx_loader(p, c16)
    qg.inc(qg.h.dma_start(out=sb3(w_a, NDT, 1024), in_=fm3(ow_d[:, 0:1024])))
    for s, dd in [(ob_s, ob_d), (ln2w_s, ln2w_d), (ln2b_s, ln2b_d), (f1b_s, f1b_d),
                  (f2b_s, f2b_d), (l1s_s, l1s_d), (l1h_s, l1h_d)]:
        qs.inc(qs.h.dma_start(out=s[:, :], in_=dd))
    dve.h.memset(onesc[:, :], 1.0)
    dve.h.memset(ones1f[:, :], 1.0)

    p.wait(pe, qg, qg.n)
    p.wait(pe, qs, qs.n)
    p.wait(dve, qs, qs.n)
    p.wait(act, qs, qs.n)
    for mt in range(NDT):
        for tb in range(4):
            bi, bank = banks.get()
            for dt in range(NDT):
                last = pe.h.matmul(bank[:, :],
                                   w_a[:, dt * 1024 + mt * 128: dt * 1024 + (mt + 1) * 128],
                                   c16[:, dt * HT + tb * 512: dt * HT + (tb + 1) * 512],
                                   start=(dt == 0), stop=(dt == NDT - 1))
            pe.inc(last)
            p.wait(dve, pe, pe.n)
            sl = acc[:, mt * HT + tb * 512: mt * HT + (tb + 1) * 512]
            dve.h.scalar_tensor_tensor(sl, bank[:, :], ob_s[:, mt:mt + 1], sl,
                                       ALU.add, ALU.add)
            banks.consumed(bi, dve)
    p.wait(act, dve, dve.n)
    for hf in range(2):
        t0 = hf * 1024
        row_stats(p, banks, acc, sq_sc, onesc, rows_s, rows_q, 1024, t0=t0)
        p.wait(dve, pe, pe.n)
        ln_finalize_rows(p, rows_s, rows_q, tmp_row, 1024, float(D), 1e-5)
        for c in range(2):
            c0 = t0 + c * 512
            r0 = c * 512
            p.wait(pe, dve, dve.n)
            bi1, bankR = banks.get()
            pe.h.matmul(bankR[:, :], ones1f[:, :], rows_q[0:1, r0:r0 + 512], start=True, stop=True)
            bi2, bankM = banks.get()
            last = pe.h.matmul(bankM[:, :], ones1f[:, :], rows_s[0:1, r0:r0 + 512], start=True, stop=True)
            pe.inc(last)
            p.wait(dve, pe, pe.n)
            for dt in range(NDT):
                sl = acc[:, dt * HT + c0: dt * HT + c0 + 512]
                dve.h.tensor_tensor(tsc[:, :], sl, bankR[:, :], ALU.mult)
                dve.h.tensor_tensor(tsc[:, :], tsc[:, :], bankM[:, :], ALU.add)
                dve.h.tensor_scalar(c16[:, dt * HT + c0: dt * HT + c0 + 512],
                                    tsc[:, :], ln2w_s[:, dt:dt + 1], ln2b_s[:, dt:dt + 1],
                                    ALU.mult, ALU.add)
            banks.consumed(bi1, dve)
            banks.consumed(bi2, dve)
        p.wait(act, dve, dve.n)

    p.wait(pe, dve, dve.n)
    st2 = {"f1g": (pe, pe.n), "f2g": None, "f1n": 0, "f2n": 0}

    def k2_load_f1(fc2):
        p.wait(qg, st2["f1g"][0], st2["f1g"][1])
        qg.inc(qg.h.dma_start(out=sb3(w_a, NDT, FC),
                              in_=fm3(f1_d[:, fc2 * FC:(fc2 + 1) * FC])))
        st2["f1n"] = qg.n

    def k2_load_f2(fc2):
        if st2["f2g"] is not None:
            p.wait(qg, st2["f2g"][0], st2["f2g"][1])
        qg.inc(qg.h.dma_start(out=sb3(w_b, NFT, D),
                              in_=fm3(f2_d[fc2 * FC:(fc2 + 1) * FC, :])))
        st2["f2n"] = qg.n

    k2_load_f1(0)
    for fc in range(NFC):
        k2_load_f2(fc)
        p.wait(pe, qg, st2["f1n"])
        for ft in range(NFT):
            for tb in range(4):
                bi, bank = banks.get()
                for dt in range(NDT):
                    last = pe.h.matmul(bank[:, :],
                                w_a[:, dt * FC + ft * 128: dt * FC + (ft + 1) * 128],
                                c16[:, dt * HT + tb * 512: dt * HT + (tb + 1) * 512],
                                start=(dt == 0), stop=(dt == NDT - 1))
                pe.inc(last)
                p.wait(act, pe, pe.n)
                act.h.activation(hbuf[:, ft * HT + tb * 512: ft * HT + (tb + 1) * 512],
                                 bank[:, :], AF.Gelu,
                                 bias=f1b_s[:, fc * NFT + ft: fc * NFT + ft + 1])
                banks.consumed(bi, act)
        st2["f1g"] = (pe, pe.n)
        if fc + 1 < NFC:
            k2_load_f1(fc + 1)
        p.wait(pe, qg, st2["f2n"])
        p.wait(pe, act, act.n)
        for mt in range(NDT):
            for tb in range(4):
                bi, bank = banks.get()
                for ft in range(NFT):
                    last = pe.h.matmul(bank[:, :],
                                w_b[:, ft * D + mt * 128: ft * D + (mt + 1) * 128],
                                hbuf[:, ft * HT + tb * 512: ft * HT + (tb + 1) * 512],
                                start=(ft == 0), stop=(ft == NFT - 1))
                pe.inc(last)
                p.wait(dve, pe, pe.n)
                sl = acc[:, mt * HT + tb * 512: mt * HT + (tb + 1) * 512]
                if fc == NFC - 1:
                    dve.h.scalar_tensor_tensor(sl, bank[:, :], f2b_s[:, mt:mt + 1], sl,
                                               ALU.add, ALU.add)
                else:
                    dve.h.tensor_tensor(sl, sl, bank[:, :], ALU.add)
                banks.consumed(bi, dve)
        st2["f2g"] = (pe, pe.n)

    add_dram_fm(p, banks, acc, x0_d, x3c)
    plain_ln_to_acc(p, banks, acc, sq_sc, onesc, ones1f, rows_s, rows_q, tmp_row,
                    tsc, 1e-5)
    p.wait(qs, dve, dve.n)
    fm_store(p, qs, x1_d, acc, D, HT)
    def emit_h1(dt, c0, tsc_ap, sl):
        p.dve.h.tensor_scalar(c16[:, dt * HT + c0: dt * HT + c0 + 512],
                              tsc_ap[:, :], l1s_s[:, dt:dt + 1], l1h_s[:, dt:dt + 1],
                              ALU.mult, ALU.add)
    plain_ln_to_acc(p, banks, acc, sq_sc, onesc, ones1f, rows_s, rows_q, tmp_row,
                    tsc, 1e-5, per_dt=emit_h1)
    p.wait(qs, dve, dve.n)
    qs.inc(qs.h.dma_start(out=fm3(h1_d), in_=sb3(c16, NDT, HT)))


def build_k2():
    p = Prog()
    IO = {n: p.din(n, s) for n, s in [("ctx", [D, HT]), ("x0", [D, HT]), ("ow", [D, D]),
          ("ob_c", [128, NDT]), ("ln2w_c", [128, NDT]), ("ln2b_c", [128, NDT]),
          ("f1w", [D, 4096]), ("f1b_c", [128, 32]), ("f2w", [4096, D]),
          ("f2b_c", [128, NDT]), ("l1s_c", [128, NDT]), ("l1h_c", [128, NDT])]}
    IO["x1"] = p.dout("x1", [D, HT])
    IO["h1"] = p.nc.dram_tensor("h1", [D, HT], BF16, kind="ExternalOutput").ap()
    p.push(); emit_k2(p, IO); p.pop()
    return p.finish()


def k2_inputs(inputs, attn_full, x0_full):
    c = np.ascontiguousarray
    col = lambda v: c(v.reshape(-1, 128).T)
    base = dict(
        ow=c(inputs["L0_attn_o_w"]), ob_c=col(inputs["L0_attn_o_b"]),
        ln2w_c=col(inputs["L0_attn_ln2_w"]), ln2b_c=col(inputs["L0_attn_ln2_b"]),
        f1w=c(inputs["L0_attn_f1_w"]), f1b_c=col(inputs["L0_attn_f1_b"]),
        f2w=c(inputs["L0_attn_f2_w"]), f2b_c=col(inputs["L0_attn_f2_b"]),
        l1s_c=col(inputs["L0_ffn_ln1_scale"]), l1h_c=col(inputs["L0_ffn_ln1_shift"]))
    maps = []
    for core in range(8):
        b, h = core // 2, core % 2
        m = dict(base)
        m["ctx"] = c(attn_full[b, h * HT:(h + 1) * HT, :].T)
        m["x0"] = c(x0_full[b, h * HT:(h + 1) * HT, :].T)
        maps.append(m)
    return maps


def emit_k4(p, IO, ctx_loader=None):
    pe, act, dve, qs, qg = p.pe, p.act, p.dve, p.qs, p.qg
    ctx_d = IO.get("ctx"); x1_d = IO["x1"]; wo_d = IO["wo"]; bo_d = IO["bo_c"]
    l2s_d = IO["l2s_c"]; l2h_d = IO["l2h_c"]; gw_d = IO["gate_w"]
    w1_d = IO["w1"]; w2_d = IO["w2"]; w3_d = IO["w3"]; id_d = IO["ident"]
    x2_d = IO["x2"]

    HC = 512
    NHC = 2048 // HC
    NHTl = HC // 128

    acc = p.sb("acc", [128, NDT * HT], F32)
    c16 = p.sb("c16", [128, NDT * HT], BF16)
    w1c = p.sb("w1c", [128, NDT * HC], BF16)
    w2c = p.sb("w2c", [128, NDT * HC], BF16)
    w3c = p.sb("w3c", [128, NHTl * D], BF16)
    hS = p.sb("hS", [128, NHTl * HT], BF16)
    gate_s = p.sb("gate_s", [128, NDT * 8], BF16)
    id32 = p.sb("id32", [128, 128], F32)
    bo_s = p.sb("bo_s", [128, NDT], F32)
    l2s_s = p.sb("l2s_s", [128, NDT], F32)
    l2h_s = p.sb("l2h_s", [128, NDT], F32)
    gts = p.sb("gts", [128, NTT * 8], F32)
    logit = p.sb("logit", [128, NTT * 8], F32)
    wrow = p.sb("wrow", [1, HT], BF16)
    wbc = p.sb("wbc", [128, HT], BF16)
    ones1 = p.sb("ones1", [1, 128], BF16)
    ones1f = p.sb("ones1f", [1, 128], F32)
    onesc = p.sb("onesc", [128, 1], BF16)
    rows_s = p.sb("rows_s", [1, 1024], F32)
    rows_q = p.sb("rows_q", [1, 1024], F32)
    tmp_row = p.sb("tmp_row", [1, 1024], F32)
    sq_sc = p.sb("sq_sc", [128, NDT * 512], BF16)
    x3c = p.sb("x3c", [128, 4 * 512], F32)
    tsc = p.sb("tsc", [128, 512], F32)
    sA = p.sb("sA", [128, 512], BF16)
    m1 = p.sb("m1", [128, NTT], F32)
    nm1 = p.sb("nm1", [128, NTT], F32)
    m2 = p.sb("m2", [128, NTT], F32)
    zs = p.sb("zs", [128, NTT], F32)
    tg = p.sb("tg", [128, 8], F32)

    banks = Pool8(p)

    fm_load(p, qs, acc, x1_d, D, HT)
    if ctx_loader is None:
        qg.inc(qg.h.dma_start(out=sb3(c16, NDT, HT), in_=fm3(ctx_d)))
    else:
        ctx_loader(p, c16)

    qg.inc(qg.h.dma_start(out=sb3(gate_s, NDT, 8), in_=fm3(gw_d)))
    qs.inc(qs.h.dma_start(out=id32[:, :], in_=id_d))
    for s, dd in [(bo_s, bo_d), (l2s_s, l2s_d), (l2h_s, l2h_d)]:
        qs.inc(qs.h.dma_start(out=s[:, :], in_=dd))
    dve.h.memset(ones1[:, :], 1.0)
    dve.h.memset(ones1f[:, :], 1.0)
    dve.h.memset(onesc[:, :], 1.0)

    p.wait(pe, qg, qg.n)
    p.wait(pe, qs, qs.n)
    p.wait(dve, qs, qs.n)
    p.wait(act, qs, qs.n)
    p.wait(pe, dve, dve.n)
    for mh in range(2):
        p.wait(qg, pe, pe.n)
        qg.inc(qg.h.dma_start(out=sb3(w1c, NDT, 512),
                              in_=fm3(wo_d[:, mh * 512:(mh + 1) * 512])))
        p.wait(pe, qg, qg.n)
        for mtl in range(4):
            mt = mh * 4 + mtl
            for tb in range(4):
                bi, bank = banks.get()
                for dt in range(NDT):
                    last = pe.h.matmul(bank[:, :],
                                       w1c[:, dt * 512 + mtl * 128: dt * 512 + (mtl + 1) * 128],
                                       c16[:, dt * HT + tb * 512: dt * HT + (tb + 1) * 512],
                                       start=(dt == 0), stop=(dt == NDT - 1))
                pe.inc(last)
                p.wait(dve, pe, pe.n)
                sl = acc[:, mt * HT + tb * 512: mt * HT + (tb + 1) * 512]
                dve.h.scalar_tensor_tensor(sl, bank[:, :], bo_s[:, mt:mt + 1], sl,
                                           ALU.add, ALU.add)
                banks.consumed(bi, dve)

    p.wait(act, dve, dve.n)
    def emit_h2(dt, c0, tsc_ap, sl):
        p.dve.h.tensor_scalar(c16[:, dt * HT + c0: dt * HT + c0 + 512],
                              tsc_ap[:, :], l2s_s[:, dt:dt + 1], l2h_s[:, dt:dt + 1],
                              ALU.mult, ALU.add)
    plain_ln_to_acc(p, banks, acc, sq_sc, onesc, ones1f, rows_s, rows_q, tmp_row,
                    tsc, 1e-5, per_dt=emit_h2)

    p.wait(pe, dve, dve.n)
    topk2_gates(p, banks, c16, gate_s, None, gts, logit,
                (ones1, m1, nm1, m2, zs, tg), nbias=False)

    st4 = {"w12g": None, "w3g": None, "w12n": 0, "w3n": 0}

    def k4_load_w12(idx):
        e2, hc2 = idx // NHC, idx % NHC
        if st4["w12g"] is not None:
            p.wait(qg, st4["w12g"][0], st4["w12g"][1])
        qg.inc(qg.h.dma_start(out=sb3(w1c, NDT, HC),
                              in_=fm3(w1_d[e2, :, hc2 * HC:(hc2 + 1) * HC])))
        qg.inc(qg.h.dma_start(out=sb3(w2c, NDT, HC),
                              in_=fm3(w2_d[e2, :, hc2 * HC:(hc2 + 1) * HC])))
        st4["w12n"] = qg.n

    def k4_load_w3(idx):
        e2, hc2 = idx // NHC, idx % NHC
        if st4["w3g"] is not None:
            p.wait(qg, st4["w3g"][0], st4["w3g"][1])
        qg.inc(qg.h.dma_start(out=sb3(w3c, NHTl, D),
                              in_=fm3(w3_d[e2, hc2 * HC:(hc2 + 1) * HC, :])))
        st4["w3n"] = qg.n

    st4["w12g"] = (pe, pe.n)
    k4_load_w12(0)
    for e in range(8):
        wrow_wbc(p, banks, gts, e, id32, ones1, wrow, wbc)
        for hc in range(NHC):
            idx = e * NHC + hc
            k4_load_w3(idx)
            p.wait(pe, qg, st4["w12n"])
            p.wait(pe, dve, dve.n)
            for ht in range(NHTl):
                for tb in range(4):
                    bia, banka = banks.get()
                    for dt in range(NDT):
                        lasta = pe.h.matmul(banka[:, :],
                                    w1c[:, dt * HC + ht * 128: dt * HC + (ht + 1) * 128],
                                    c16[:, dt * HT + tb * 512: dt * HT + (tb + 1) * 512],
                                    start=(dt == 0), stop=(dt == NDT - 1))
                    bib, bankb = banks.get()
                    for dt in range(NDT):
                        lastb = pe.h.matmul(bankb[:, :],
                                    w2c[:, dt * HC + ht * 128: dt * HC + (ht + 1) * 128],
                                    c16[:, dt * HT + tb * 512: dt * HT + (tb + 1) * 512],
                                    start=(dt == 0), stop=(dt == NDT - 1))
                    pe.inc(lasta, 1)
                    pe.inc(lastb, 1)
                    p.wait(act, pe, pe.n)
                    p.wait(act, dve, dve.n)
                    act.h.activation(sA[:, :], banka[:, :], AF.Silu)
                    banks.consumed(bia, act)
                    hsl = hS[:, ht * HT + tb * 512: ht * HT + (tb + 1) * 512]
                    p.wait(dve, act, act.n)
                    dve.h.tensor_tensor(hsl, bankb[:, :], wbc[:, tb * 512:(tb + 1) * 512],
                                        ALU.mult)
                    dve.h.tensor_tensor(hsl, hsl, sA[:, :], ALU.mult)
                    banks.consumed(bib, dve)
            st4["w12g"] = (pe, pe.n)
            if idx + 1 < 8 * NHC:
                k4_load_w12(idx + 1)
            p.wait(pe, dve, dve.n)
            p.wait(pe, qg, st4["w3n"])
            for mt in range(NDT):
                for tb in range(4):
                    bi, bank = banks.get()
                    for ht in range(NHTl):
                        last = pe.h.matmul(bank[:, :],
                                    w3c[:, ht * D + mt * 128: ht * D + (mt + 1) * 128],
                                    hS[:, ht * HT + tb * 512: ht * HT + (tb + 1) * 512],
                                    start=(ht == 0), stop=(ht == NHTl - 1))
                    pe.inc(last)
                    p.wait(dve, pe, pe.n)
                    sl = acc[:, mt * HT + tb * 512: mt * HT + (tb + 1) * 512]
                    dve.h.tensor_tensor(sl, sl, bank[:, :], ALU.add)
                    banks.consumed(bi, dve)
            st4["w3g"] = (pe, pe.n)

    add_dram_fm(p, banks, acc, x1_d, x3c)
    p.wait(act, dve, dve.n)
    plain_ln_to_acc(p, banks, acc, sq_sc, onesc, ones1f, rows_s, rows_q, tmp_row,
                    tsc, 1e-5)
    p.wait(qs, dve, dve.n)
    fm_store(p, qs, x2_d, acc, D, HT)


def build_k4():
    p = Prog()
    IO = {n: p.din(n, s) for n, s in [("ctx", [D, HT]), ("x1", [D, HT]), ("wo", [D, D]),
          ("bo_c", [128, NDT]), ("l2s_c", [128, NDT]), ("l2h_c", [128, NDT]),
          ("gate_w", [D, 8]), ("w1", [8, D, 2048]), ("w2", [8, D, 2048]),
          ("w3", [8, 2048, D]), ("ident", [128, 128])]}
    IO["x2"] = p.dout("x2", [D, HT])
    p.push(); emit_k4(p, IO); p.pop()
    return p.finish()


def k4_inputs(inputs, ctx_full, x1_full):
    c = np.ascontiguousarray
    col = lambda v: c(v.reshape(-1, 128).T)
    base = dict(
        wo=c(inputs["L0_ffn_wo"]), bo_c=col(inputs["L0_ffn_bo"]),
        l2s_c=col(inputs["L0_ffn_ln2_scale"]), l2h_c=col(inputs["L0_ffn_ln2_shift"]),
        gate_w=c(inputs["L0_ffn_gate_w"]),
        w1=c(inputs["L0_ffn_w1"]), w2=c(inputs["L0_ffn_w2"]), w3=c(inputs["L0_ffn_w3"]),
        ident=np.eye(128, dtype=np.float32))
    maps = []
    for core in range(8):
        b, h = core // 2, core % 2
        m = dict(base)
        m["ctx"] = c(ctx_full[b, h * HT:(h + 1) * HT, :].T)
        m["x1"] = c(x1_full[b, h * HT:(h + 1) * HT, :].T)
        maps.append(m)
    return maps


def emit_k5(p, IO, fused=False):
    pe, act, dve, qs, qg = p.pe, p.act, p.dve, p.qs, p.qg
    xm_d = IO["xmy"]; wq_d = IO["wq"]; wk_d = IO["wk"]; wv_d = IO["wv"]
    bq_d = IO["bq_c"]; bk_d = IO["bk_c"]; bv_d = IO["bv_r"]
    wp_d = IO["wp"]; bp_d = IO["bp_c"]; x3_d = IO["x3"]
    xf_d = IO.get("xfull")
    kq_dram = IO["k_sc"]
    q_dram = IO["q_sc"]

    V_s = p.sb("V_s", [128, 32 * 1024], BF16)
    PT = p.sb("PT", [128, 32 * 512], BF16)
    wst = p.sb("wst", [128, NDT * 1024], BF16)
    xc = p.sb("xc", [128, NDT * 512], BF16)
    kc = p.sb("kc", [128, NDT * 512], BF16)
    kc2 = p.sb("kc2", [128, NDT * 512], BF16)
    sqs = p.sb("sqs", [128, 512], BF16)
    ybuf = p.sb("ybuf", [128, NDT * 512], BF16)
    xab = p.sb("xab", [128, NDT * 512], F32)
    rzb = p.sb("rzb", [128, 512], F32)
    bv_s = p.sb("bv_s", [1, D], BF16)
    bq_s = p.sb("bq_s", [128, NDT], F32)
    bq32 = p.sb("bq32", [128, NDT], F32)
    bk_s = p.sb("bk_s", [128, NDT], F32)
    bp_s = p.sb("bp_s", [128, NDT], F32)
    ones1 = p.sb("ones1", [1, 128], BF16)
    ones1f = p.sb("ones1f", [1, 128], F32)
    onesc = p.sb("onesc", [128, 1], BF16)
    nrm = p.sb("nrm", [1, 512], F32)
    kmax = p.sb("kmax", [1, 4], F32)
    qmax = p.sb("qmax", [1, 4], F32)
    mcol = p.sb("mcol", [128, 1], F32)
    km2 = p.sb("km2", [1, 8], F32)
    rows_s = p.sb("rows_s", [1, 512], F32)
    rows_q = p.sb("rows_q", [1, 512], F32)
    tmp_row = p.sb("tmp_row", [1, 512], F32)
    sq_sc = p.sb("sq_sc", [128, NDT * 512], BF16)
    tsc = p.sb("tsc", [128, 512], F32)
    x3cb = p.sb("x3cb", [128, NDT * 512], F32)

    banks = Pool8(p, n=7)
    zps = p.ps("zps", [1, 512])

    dve.h.memset(ones1[:, :], 1.0)
    dve.h.memset(ones1f[:, :], 1.0)
    dve.h.memset(onesc[:, :], 1.0)
    dve.h.memset(kmax[:, :], 0.0)
    dve.h.memset(qmax[:, :], 0.0)
    for s, dd in [(bq_s, bq_d), (bk_s, bk_d), (bp_s, bp_d)]:
        qs.inc(qs.h.dma_start(out=s[:, :], in_=dd))
    qg.inc(qg.h.dma_start(out=bv_s[:, :], in_=bv_d))
    p.wait(act, qs, qs.n)
    act.h.activation(bq32[:, :], bq_s[:, :], AF.Identity, scale=1.0 / 32.0)

    kv_src = xm_d if fused else xf_d
    n_kv = 4 if fused else 8
    p.wait(pe, dve, dve.n)
    p.wait(pe, qs, qs.n)
    qg.inc(qg.h.dma_start(out=sb3(wst, NDT, 1024), in_=fm3(wk_d)))
    for tc in range(n_kv):
        t0 = tc * 512
        p.wait(qg, pe, pe.n)
        qg.inc(qg.h.dma_start(out=sb3(xc, NDT, 512), in_=fm3(kv_src[:, t0:t0 + 512])))
        p.wait(pe, qg, qg.n)
        p.wait(pe, act, act.n)
        p.wait(act, qs, qs.n)
        for mt in range(NDT):
            bi, bank = banks.get()
            for dt in range(NDT):
                last = pe.h.matmul(bank[:, :],
                                   wst[:, dt * 1024 + mt * 128: dt * 1024 + (mt + 1) * 128],
                                   xc[:, dt * 512:(dt + 1) * 512],
                                   start=(dt == 0), stop=(dt == NDT - 1))
            pe.inc(last)
            p.wait(act, pe, pe.n)
            act.h.activation(kc[:, mt * 512:(mt + 1) * 512], bank[:, :], AF.Identity,
                             bias=bk_s[:, mt:mt + 1])
            banks.consumed(bi, act)
        p.wait(act, act, act.n)
        for mt in range(NDT):
            p.wait(act, pe, pe.n)
            act.h.activation(sqs[:, :], kc[:, mt * 512:(mt + 1) * 512], AF.Square)
            p.wait(pe, act, act.n)
            bi, bank = banks.get()
            last = pe.h.matmul(bank[0:1, 0:512], onesc[:, 0:1], sqs[:, :],
                               start=(mt == 0), stop=(mt == NDT - 1),
                               skip_group_check=True)
            pe.inc(last)
            if mt == NDT - 1:
                p.wait(dve, pe, pe.n)
                dve.h.reduce_max(kmax[0:1, 1:2], bank[0:1, 0:512], axis=AX)
                dve.h.tensor_tensor(kmax[0:1, 0:1], kmax[0:1, 0:1], kmax[0:1, 1:2], ALU.max)
                banks.consumed(bi, dve)
            else:
                banks.i -= 1
        p.wait(qs, act, act.n)
        qs.inc(qs.h.dma_start(out=fm3(kq_dram[:, t0:t0 + 512]), in_=sb3(kc, NDT, 512)))
    p.wait(qg, pe, pe.n)
    qg.inc(qg.h.dma_start(out=sb3(wst, NDT, 1024), in_=fm3(wv_d)))
    for tc in range(n_kv):
        t0 = tc * 512
        p.wait(qg, pe, pe.n)
        qg.inc(qg.h.dma_start(out=sb3(xc, NDT, 512), in_=fm3(kv_src[:, t0:t0 + 512])))
        p.wait(pe, qg, qg.n)
        for ttl in range(4):
            tt = tc * 4 + ttl
            for dh in range(2):
                bi, bank = banks.get()
                for dt in range(NDT):
                    pe.h.matmul(bank[:, :],
                                xc[:, dt * 512 + ttl * 128: dt * 512 + (ttl + 1) * 128],
                                wst[:, dt * 1024 + dh * 512: dt * 1024 + (dh + 1) * 512],
                                start=(dt == 0), stop=False)
                last = pe.h.matmul(bank[:, :], ones1[:, :],
                                   bv_s[0:1, dh * 512:(dh + 1) * 512],
                                   start=False, stop=True)
                pe.inc(last)
                p.wait(dve, pe, pe.n)
                dve.h.tensor_copy(V_s[:, tt * 1024 + dh * 512: tt * 1024 + (dh + 1) * 512],
                                  bank[:, :])
                banks.consumed(bi, dve)
        if fused:
            p.wait(qs, dve, dve.n)
            qs.inc(qs.h.dma_start(
                out=IO["v_bounce"][t0:t0 + 512, :],
                in_=V_s[:, tc * 4 * 1024:(tc + 1) * 4 * 1024]
                    .rearrange("p (tt v) -> p tt v", v=1024)))

    if fused:
        p.wait(qs, dve, dve.n)
        qs.inc(qs.h.dma_start(out=IO["kmax_b"], in_=kmax[0:1, :]))
        IO["mid_cb"](p)

    p.wait(qg, pe, pe.n)
    qg.inc(qg.h.dma_start(out=sb3(wst, NDT, 1024), in_=fm3(wq_d)))
    for tc in range(4):
        t0 = tc * 512
        p.wait(qg, pe, pe.n)
        qg.inc(qg.h.dma_start(out=sb3(xc, NDT, 512), in_=fm3(xm_d[:, t0:t0 + 512])))
        p.wait(pe, qg, qg.n)
        p.wait(pe, act, act.n)
        p.wait(act, qs, qs.n)
        for mt in range(NDT):
            bi, bank = banks.get()
            for dt in range(NDT):
                last = pe.h.matmul(bank[:, :],
                                   wst[:, dt * 1024 + mt * 128: dt * 1024 + (mt + 1) * 128],
                                   xc[:, dt * 512:(dt + 1) * 512],
                                   start=(dt == 0), stop=(dt == NDT - 1))
            pe.inc(last)
            p.wait(act, pe, pe.n)
            act.h.activation(kc[:, mt * 512:(mt + 1) * 512], bank[:, :], AF.Identity,
                             scale=1.0 / 32.0, bias=bq32[:, mt:mt + 1])
            banks.consumed(bi, act)
        p.wait(act, act, act.n)
        for mt in range(NDT):
            p.wait(act, pe, pe.n)
            act.h.activation(sqs[:, :], kc[:, mt * 512:(mt + 1) * 512], AF.Square)
            p.wait(pe, act, act.n)
            bi, bank = banks.get()
            last = pe.h.matmul(bank[0:1, 0:512], onesc[:, 0:1], sqs[:, :],
                               start=(mt == 0), stop=(mt == NDT - 1),
                               skip_group_check=True)
            pe.inc(last)
            if mt == NDT - 1:
                p.wait(dve, pe, pe.n)
                dve.h.reduce_max(qmax[0:1, 1:2], bank[0:1, 0:512], axis=AX)
                dve.h.tensor_tensor(qmax[0:1, 0:1], qmax[0:1, 0:1], qmax[0:1, 1:2], ALU.max)
                banks.consumed(bi, dve)
            else:
                banks.i -= 1
        p.wait(qs, act, act.n)
        qs.inc(qs.h.dma_start(out=fm3(q_dram[:, t0:t0 + 512]), in_=sb3(kc, NDT, 512)))

    if fused:
        p.wait(qs, p.cc)
        qs.inc(qs.h.dma_start(out=km2[0:1, :], in_=IO["ag_kmax"].rearrange("a b -> (a b)")))
        p.wait(dve, qs, qs.n)
        dve.h.tensor_tensor(kmax[0:1, 0:1], km2[0:1, 0:1], km2[0:1, 4:5], ALU.max)
    p.wait(act, dve, dve.n)
    act.h.activation(qmax[0:1, 2:3], qmax[0:1, 0:1], AF.Sqrt)
    act.h.activation(kmax[0:1, 2:3], kmax[0:1, 0:1], AF.Sqrt)
    p.wait(dve, act, act.n)
    dve.h.tensor_tensor(qmax[0:1, 3:4], qmax[0:1, 2:3], kmax[0:1, 2:3], ALU.mult)
    dve.h.tensor_scalar_mul(qmax[0:1, 3:4], qmax[0:1, 3:4], -1.0)
    p.wait(pe, dve, dve.n)
    bi, bank = banks.get()
    last = pe.h.matmul(bank[0:128, 0:1], ones1f[:, :], qmax[0:1, 3:4], start=True, stop=True)
    pe.inc(last)
    p.wait(dve, pe, pe.n)
    dve.h.tensor_copy(mcol[:, :], bank[0:128, 0:1])
    banks.consumed(bi, dve)

    if fused:
        p.wait(qs, p.cc)
        p.wait(qs, pe, pe.n)
        p.wait(qs, dve, dve.n)
        qs.inc(qs.h.dma_start(
            out=V_s[:, :].rearrange("p (tt v) -> p tt v", v=1024),
            in_=IO["ag_v"].rearrange("(tt p) v -> p tt v", p=128)))
    p.wait(qg, pe, pe.n)
    qg.inc(qg.h.dma_start(out=sb3(wst, NDT, 1024), in_=fm3(wp_d)))
    for qb in range(4):
        q0 = qb * 512
        p.wait(qs, pe, pe.n)
        qs.inc(qs.h.dma_start(out=sb3(xc, NDT, 512), in_=fm3(q_dram[:, q0:q0 + 512])))
        kbufs = [kc, kc2]
        k_guard = [None, None]
        k_load = [0, 0]

        def k5_load_k(kc4l):
            par2 = kc4l % 2
            if k_guard[par2] is not None:
                p.wait(qs, k_guard[par2][0], k_guard[par2][1])
            k0l = kc4l * 512
            if fused:
                blk = k0l // HT
                kk0 = k0l % HT
                qs.inc(qs.h.dma_start(out=sb3(kbufs[par2], NDT, 512),
                                      in_=fm3(IO["ag_k"][blk * D:(blk + 1) * D, kk0:kk0 + 512])))
            else:
                qs.inc(qs.h.dma_start(out=sb3(kbufs[par2], NDT, 512),
                                      in_=fm3(kq_dram[:, k0l:k0l + 512])))
            k_load[par2] = qs.n

        k5_load_k(0)
        for kc4 in range(8):
            par = kc4 % 2
            if kc4 + 1 < 8:
                k5_load_k(kc4 + 1)
            p.wait(pe, qs, k_load[par])
            p.wait(pe, dve, dve.n)
            for ktl in range(4):
                kt = kc4 * 4 + ktl
                bi, bank = banks.get()
                for dt in range(NDT):
                    last = pe.h.matmul(bank[:, :],
                                kbufs[par][:, dt * 512 + ktl * 128: dt * 512 + (ktl + 1) * 128],
                                xc[:, dt * 512:(dt + 1) * 512],
                                start=(dt == 0), stop=(dt == NDT - 1))
                pe.inc(last)
                p.wait(act, pe, pe.n)
                act.h.activation(PT[:, kt * 512:(kt + 1) * 512], bank[:, :], AF.Exp,
                                 bias=mcol[:, 0:1])
                banks.consumed(bi, act)
            k_guard[par] = (pe, pe.n)
        p.wait(pe, act, act.n)
        p.wait(pe, dve, dve.n)
        for kt in range(32):
            last = pe.h.matmul(zps[0:1, :], onesc[:, 0:1], PT[:, kt * 512:(kt + 1) * 512],
                               start=(kt == 0), stop=(kt == 31))
        pe.inc(last)
        p.wait(dve, pe, pe.n)
        dve.h.reciprocal(nrm[0:1, :], zps[0:1, :])
        p.wait(pe, dve, dve.n)
        bi, bank = banks.get()
        last = pe.h.matmul(bank[:, :], ones1f[:, :], nrm[0:1, :], start=True, stop=True)
        pe.inc(last)
        p.wait(dve, pe, pe.n)
        dve.h.tensor_copy(rzb[:, :], bank[:, :])
        banks.consumed(bi, dve)
        for dvt in range(NDT):
            bi, bank = banks.get()
            for kt in range(32):
                last = pe.h.matmul(bank[:, :],
                                   V_s[:, kt * 1024 + dvt * 128: kt * 1024 + (dvt + 1) * 128],
                                   PT[:, kt * 512:(kt + 1) * 512],
                                   start=(kt == 0), stop=(kt == 31))
            pe.inc(last)
            p.wait(dve, pe, pe.n)
            dve.h.tensor_tensor(ybuf[:, dvt * 512:(dvt + 1) * 512], bank[:, :],
                                rzb[:, :], ALU.mult)
            banks.consumed(bi, dve)
        p.wait(qs, dve, dve.n)
        qs.inc(qs.h.dma_start(out=sb3(x3cb, NDT, 512), in_=fm3(xm_d[:, q0:q0 + 512])))
        p.wait(pe, dve, dve.n)
        p.wait(pe, qg, qg.n)
        for mt in range(NDT):
            bi, bank = banks.get()
            for dt in range(NDT):
                last = pe.h.matmul(bank[:, :],
                                   wst[:, dt * 1024 + mt * 128: dt * 1024 + (mt + 1) * 128],
                                   ybuf[:, dt * 512:(dt + 1) * 512],
                                   start=(dt == 0), stop=(dt == NDT - 1))
            pe.inc(last)
            p.wait(dve, pe, pe.n)
            p.wait(dve, qs, qs.n)
            dve.h.scalar_tensor_tensor(xab[:, mt * 512:(mt + 1) * 512], bank[:, :],
                                       bp_s[:, mt:mt + 1],
                                       x3cb[:, mt * 512:(mt + 1) * 512],
                                       ALU.add, ALU.add)
            banks.consumed(bi, dve)
        p.wait(act, dve, dve.n)
        row_stats(p, banks, xab, sq_sc, onesc, rows_s, rows_q, 512, HTl=512)
        p.wait(dve, pe, pe.n)
        ln_finalize_rows(p, rows_s, rows_q, tmp_row, 512, float(D), 1e-5)
        p.wait(pe, dve, dve.n)
        bi1, bankR = banks.get()
        pe.h.matmul(bankR[:, :], ones1f[:, :], rows_q[0:1, 0:512], start=True, stop=True)
        bi2, bankM = banks.get()
        last = pe.h.matmul(bankM[:, :], ones1f[:, :], rows_s[0:1, 0:512], start=True, stop=True)
        pe.inc(last)
        p.wait(dve, pe, pe.n)
        for dt in range(NDT):
            sl = xab[:, dt * 512:(dt + 1) * 512]
            dve.h.tensor_tensor(tsc[:, :], sl, bankR[:, :], ALU.mult)
            dve.h.tensor_tensor(sl, tsc[:, :], bankM[:, :], ALU.add)
        banks.consumed(bi1, dve)
        banks.consumed(bi2, dve)
        p.wait(qs, dve, dve.n)
        qs.inc(qs.h.dma_start(out=fm3(x3_d[:, q0:q0 + 512]), in_=sb3(xab, NDT, 512)))


def build_k5():
    p = Prog()
    IO = {n: p.din(n, s) for n, s in [("xfull", [D, TFULL]), ("xmy", [D, HT]),
          ("wq", [D, D]), ("wk", [D, D]), ("wv", [D, D]), ("bq_c", [128, NDT]),
          ("bk_c", [128, NDT]), ("bv_r", [1, D]), ("wp", [D, D]), ("bp_c", [128, NDT])]}
    IO["x3"] = p.dout("x3", [D, HT])
    IO["k_sc"] = p.dint("k_sc", [D, TFULL], BF16).ap()
    IO["q_sc"] = p.dint("q_sc", [D, HT], BF16).ap()
    p.push(); emit_k5(p, IO); p.pop()
    return p.finish()


def k5_inputs(inputs, x2_full):
    c = np.ascontiguousarray
    col = lambda v: c(v.reshape(-1, 128).T)
    wqkv = inputs["L1_attn_w_qkv"]
    bqkv = inputs["L1_attn_b_qkv"]
    base = dict(
        wq=c(wqkv[:, 0:1024]), wk=c(wqkv[:, 1024:2048]), wv=c(wqkv[:, 2048:3072]),
        bq_c=col(bqkv[0:1024]), bk_c=col(bqkv[1024:2048]), bv_r=c(bqkv[2048:3072].reshape(1, -1)),
        wp=c(inputs["L1_attn_w_proj"]), bp_c=col(inputs["L1_attn_b_proj"]))
    maps = []
    for core in range(8):
        b, h = core // 2, core % 2
        m = dict(base)
        m["xfull"] = c(x2_full[b].T)
        m["xmy"] = c(x2_full[b, h * HT:(h + 1) * HT, :].T)
        maps.append(m)
    return maps


def emit_k3(p, IO, h1_loader=None, ctx_bf16=False):
    pe, act, dve, qs, qg = p.pe, p.act, p.dve, p.qs, p.qg
    h1_d = IO.get("h1f"); wq_d = IO["wq_my"]; wk_d = IO["wk_my"]; wv_d = IO["wv_my"]
    triu_d = IO["triu"]; ctx_d = IO["ctxo"]

    qf = p.sb("qf", [128, 4 * TFULL], BF16)
    kf = p.sb("kf", [128, 4 * TFULL], BF16)
    V_s = p.sb("V_s", [128, 32 * 520], BF16)
    PT = p.sb("PT", [128, 32 * 512], BF16)
    wq16 = p.sb("wq16", [128, NDT * 512], BF16)
    wk16 = p.sb("wk16", [128, NDT * 512], BF16)
    wv16 = p.sb("wv16", [128, NDT * 512], BF16)
    hc = p.sb("hc", [128, NDT * 512], BF16)
    sqs = p.sb("sqs", [128, 512], BF16)
    triu = p.sb("triu_s", [128, 128], BF16)
    qmax8 = p.sb("qmax8", [1, 32], F32)
    kmax8 = p.sb("kmax8", [1, 32], F32)
    nm8 = p.sb("nm8", [1, 8], F32)
    mcols = p.sb("mcols", [128, 8], F32)
    nrm = p.sb("nrm", [1, 512], F32)
    rzb = p.sb("rzb", [128, 512], F32)
    cstage = p.sb("cstage", [128, 512], BF16 if ctx_bf16 else F32)
    ones1f = p.sb("ones1f", [1, 128], F32)
    onesc = p.sb("onesc", [128, 1], BF16)

    banks = Pool8(p, n=8)

    dve.h.memset(ones1f[:, :], 1.0)
    dve.h.memset(onesc[:, :], 1.0)
    dve.h.memset(qmax8[:, :], 0.0)
    dve.h.memset(kmax8[:, :], 0.0)
    dve.h.memset(V_s[:, :], 1.0)
    qs.inc(qs.h.dma_start(out=triu[:, :], in_=triu_d))
    qg.inc(qg.h.dma_start(out=sb3(wq16, NDT, 512), in_=fm3(wq_d)))
    qg.inc(qg.h.dma_start(out=sb3(wk16, NDT, 512), in_=fm3(wk_d)))
    qg.inc(qg.h.dma_start(out=sb3(wv16, NDT, 512), in_=fm3(wv_d)))

    p.wait(pe, dve, dve.n)
    p.wait(pe, qg, qg.n)
    for tc in range(8):
        t0 = tc * 512
        p.wait(qs, pe, pe.n)
        if h1_loader is None:
            qs.inc(qs.h.dma_start(out=sb3(hc, NDT, 512), in_=fm3(h1_d[:, t0:t0 + 512])))
        else:
            h1_loader(p, hc, t0)
        p.wait(pe, qs, qs.n)
        for which, wbuf, dst, scale, mx in (
                (0, wq16, qf, 0.125, qmax8), (1, wk16, kf, 1.0, kmax8)):
            for hp in range(4):
                bi, bank = banks.get()
                for dt in range(NDT):
                    last = pe.h.matmul(bank[:, :],
                                       wbuf[:, dt * 512 + hp * 128: dt * 512 + (hp + 1) * 128],
                                       hc[:, dt * 512:(dt + 1) * 512],
                                       start=(dt == 0), stop=(dt == NDT - 1))
                pe.inc(last)
                p.wait(act, pe, pe.n)
                if scale == 1.0:
                    act.h.activation(dst[:, hp * TFULL + t0: hp * TFULL + t0 + 512],
                                     bank[:, :], AF.Copy)
                else:
                    act.h.activation(dst[:, hp * TFULL + t0: hp * TFULL + t0 + 512],
                                     bank[:, :], AF.Identity, scale=scale)
                banks.consumed(bi, act)
                p.wait(act, pe, pe.n)
                act.h.activation(sqs[:, :], dst[:, hp * TFULL + t0: hp * TFULL + t0 + 512],
                                 AF.Square)
                p.wait(pe, act, act.n)
                for lane in range(2):
                    h8 = hp * 2 + lane
                    bi2, bank2 = banks.get()
                    last = pe.h.matmul(bank2[0:1, 0:512], onesc[lane * 64:lane * 64 + 64, 0:1],
                                       sqs[lane * 64:lane * 64 + 64, :],
                                       start=True, stop=True,
                                       tile_position=(lane * 64, 0))
                    pe.inc(last)
                    p.wait(dve, pe, pe.n)
                    dve.h.reduce_max(mx[0:1, 8 + h8:9 + h8], bank2[0:1, 0:512], axis=AX)
                    dve.h.tensor_tensor(mx[0:1, h8:h8 + 1], mx[0:1, h8:h8 + 1],
                                        mx[0:1, 8 + h8:9 + h8], ALU.max)
                    banks.consumed(bi2, dve)
        p.wait(pe, dve, dve.n)
        for ttl in range(4):
            tt = tc * 4 + ttl
            bi, bank = banks.get()
            for dt in range(NDT):
                last = pe.h.matmul(bank[:, :],
                                   hc[:, dt * 512 + ttl * 128: dt * 512 + (ttl + 1) * 128],
                                   wv16[:, dt * 512:(dt + 1) * 512],
                                   start=(dt == 0), stop=(dt == NDT - 1))
            pe.inc(last)
            p.wait(dve, pe, pe.n)
            dve.h.tensor_copy(
                V_s[:, tt * 520: tt * 520 + 520].rearrange("p (h v) -> p h v", v=65)[:, :, 0:64],
                bank[:, :].rearrange("p (h v) -> p h v", v=64))
            banks.consumed(bi, dve)

    p.wait(act, dve, dve.n)
    act.h.activation(qmax8[0:1, 16:24], qmax8[0:1, 0:8], AF.Sqrt)
    act.h.activation(kmax8[0:1, 16:24], kmax8[0:1, 0:8], AF.Sqrt)
    p.wait(dve, act, act.n)
    dve.h.tensor_tensor(nm8[0:1, :], qmax8[0:1, 16:24], kmax8[0:1, 16:24], ALU.mult)
    dve.h.tensor_scalar_mul(nm8[0:1, :], nm8[0:1, :], -0.125)
    p.wait(pe, dve, dve.n)
    bi, bank = banks.get()
    last = pe.h.matmul(bank[0:128, 0:8], ones1f[:, :], nm8[0:1, :], start=True, stop=True)
    pe.inc(last)
    p.wait(dve, pe, pe.n)
    dve.h.tensor_copy(mcols[:, :], bank[0:128, 0:8])
    banks.consumed(bi, dve)

    for h8 in range(8):
        hp, lane = h8 // 2, h8 % 2
        b0 = lane * 64
        for qb in range(8):
            q0 = qb * 512
            nkt = 4 * qb + 4
            p.wait(pe, dve, dve.n)
            for kt in range(nkt):
                bi, bank = banks.get()
                last = pe.h.matmul(bank[:, :],
                                   kf[b0:b0 + 64, hp * TFULL + kt * 128: hp * TFULL + (kt + 1) * 128],
                                   qf[b0:b0 + 64, hp * TFULL + q0: hp * TFULL + q0 + 512],
                                   start=True, stop=True, tile_position=(b0, 0))
                pe.inc(last)
                p.wait(act, pe, pe.n)
                act.h.activation(PT[:, kt * 512:(kt + 1) * 512], bank[:, :], AF.Exp,
                                 bias=mcols[:, h8:h8 + 1])
                banks.consumed(bi, act)
                jj = kt - 4 * qb
                if jj >= 0:
                    p.wait(dve, act, act.n)
                    if jj > 0:
                        dve.h.memset(PT[:, kt * 512: kt * 512 + jj * 128], 0.0)
                    dve.h.tensor_tensor(PT[:, kt * 512 + jj * 128: kt * 512 + (jj + 1) * 128],
                                        PT[:, kt * 512 + jj * 128: kt * 512 + (jj + 1) * 128],
                                        triu[:, :], ALU.mult)
            p.wait(pe, act, act.n)
            p.wait(pe, dve, dve.n)
            bi, bank = banks.get()
            for kt in range(nkt):
                last = pe.h.matmul(bank[0:65, :],
                                   V_s[:, kt * 520 + h8 * 65: kt * 520 + (h8 + 1) * 65],
                                   PT[:, kt * 512:(kt + 1) * 512],
                                   start=(kt == 0), stop=(kt == nkt - 1))
            pe.inc(last)
            p.wait(dve, pe, pe.n)
            dve.h.reciprocal(nrm[0:1, :], bank[64:65, :])
            p.wait(pe, dve, dve.n)
            bi2, bank2 = banks.get()
            last = pe.h.matmul(bank2[0:64, :], ones1f[0:1, 0:64], nrm[0:1, :],
                               start=True, stop=True)
            pe.inc(last)
            p.wait(dve, pe, pe.n)
            dve.h.tensor_copy(rzb[0:64, :], bank2[0:64, :])
            p.wait(dve, qs, qs.n)
            dve.h.tensor_tensor(cstage[0:64, :], bank[0:64, :], rzb[0:64, :], ALU.mult)
            banks.consumed(bi, dve)
            banks.consumed(bi2, dve)
            p.wait(qs, dve, dve.n)
            qs.inc(qs.h.dma_start(out=ctx_d[h8 * 64:(h8 + 1) * 64, q0:q0 + 512],
                                  in_=cstage[0:64, :]))


def build_k3():
    p = Prog()
    IO = {}
    IO["h1f"] = p.nc.dram_tensor("h1f", [D, TFULL], BF16, kind="ExternalInput").ap()
    IO["wq_my"] = p.din("wq_my", [D, 512])
    IO["wk_my"] = p.din("wk_my", [D, 512])
    IO["wv_my"] = p.din("wv_my", [D, 512])
    IO["triu"] = p.nc.dram_tensor("triu", [128, 128], BF16, kind="ExternalInput").ap()
    IO["ctxo"] = p.dout("ctxo", [512, TFULL])
    p.push(); emit_k3(p, IO); p.pop()
    return p.finish()


def k3_inputs(inputs, h1_full_bf16, triu):
    c = np.ascontiguousarray
    maps = []
    for core in range(8):
        b, r = core // 2, core % 2
        hs = r * 512
        m = dict(
            h1f=c(h1_full_bf16[b].T),
            wq_my=c(inputs["L0_ffn_wq"][:, hs:hs + 512]),
            wk_my=c(inputs["L0_ffn_wk"][:, hs:hs + 512]),
            wv_my=c(inputs["L0_ffn_wv"][:, hs:hs + 512]),
            triu=triu)
        maps.append(m)
    return maps


def emit_k1(p, IO, attn_bf16=False):
    pe, act, dve, qs, qg = p.pe, p.act, p.dve, p.qs, p.qg
    x0_d = IO["x0f"]; l1w_d = IO["ln1w_c"]; l1b_d = IO["ln1b_c"]
    wq_d = IO["wq_my"]; wk_d = IO["wk_my"]; wv_d = IO["wv_my"]; dec_d = IO["dec_my"]
    qb_d = IO["qb_c"]; kb_d = IO["kb_c"]; vb_d = IO["vb_r"]; db_d = IO["db_r"]
    ui_d = IO["u_incl"]; ur_d = IO["u_rvex"]; id_d = IO["identb"]
    attn_d = IO["attn"]
    outf_dram = IO["outf"]

    qf = p.sb("qf", [128, 4 * TFULL], BF16)
    kf = p.sb("kf", [128, 4 * TFULL], BF16)
    KT = p.sb("KT", [128, 32 * 512], BF16)
    V_s = p.sb("V_s", [128, 32 * 520], BF16)
    wq16 = p.sb("wq16", [128, NDT * 512], BF16)
    wk16 = p.sb("wk16", [128, NDT * 512], BF16)
    wv16 = p.sb("wv16", [128, NDT * 512], BF16)
    dec16 = p.sb("dec16", [128, NDT * 8], BF16)
    xc32 = p.sb("xc32", [128, NDT * 512], F32)
    xnc = p.sb("xnc", [128, NDT * 512], BF16)
    sq_sc = p.sb("sq_sc", [128, NDT * 256], BF16)
    rows_s = p.sb("rows_s", [1, 512], F32)
    rows_q = p.sb("rows_q", [1, 512], F32)
    tmp_row = p.sb("tmp_row", [1, 512], F32)
    tsc = p.sb("tsc", [128, 512], F32)
    ll_tm = p.sb("ll_tm", [128, 32 * 8], F32)
    Sst = p.sb("Sst", [64, 8 * 65], F32)
    outb = p.sb("outb", [128, 520], F32)
    astage = p.sb("astage", [128, 512], BF16 if attn_bf16 else F32)
    l1w_s = p.sb("l1w_s", [128, NDT], F32)
    l1b_s = p.sb("l1b_s", [128, NDT], F32)
    qb_s = p.sb("qb_s", [128, 4], F32)
    nqb_s = p.sb("nqb_s", [128, 4], F32)
    kb_s = p.sb("kb_s", [128, 4], F32)
    nkb_s = p.sb("nkb_s", [128, 4], F32)
    vb_s = p.sb("vb_s", [1, 512], BF16)
    db_s = p.sb("db_s", [1, 8], BF16)
    ui_s = p.sb("ui_s", [128, 128], F32)
    ur_s = p.sb("ur_s", [128, 128], F32)
    idb_s = p.sb("idb_s", [128, 128], BF16)
    ones1f = p.sb("ones1f", [1, 128], F32)
    onescf = p.sb("onescf", [128, 1], F32)
    onesc = p.sb("onesc", [128, 1], BF16)
    ones1b = p.sb("ones1b", [1, 128], BF16)

    banks = Pool8(p, n=8)

    for s, dd in [(l1w_s, l1w_d), (l1b_s, l1b_d), (qb_s, qb_d), (kb_s, kb_d),
                  (ui_s, ui_d), (ur_s, ur_d)]:
        qs.inc(qs.h.dma_start(out=s[:, :], in_=dd))
    qs.inc(qs.h.dma_start(out=idb_s[:, :], in_=id_d))
    for s, dd in [(wq16, wq_d), (wk16, wk_d), (wv16, wv_d)]:
        qg.inc(qg.h.dma_start(out=sb3(s, NDT, 512), in_=fm3(dd)))
    qg.inc(qg.h.dma_start(out=sb3(dec16, NDT, 8), in_=fm3(dec_d)))
    qg.inc(qg.h.dma_start(out=vb_s[:, :], in_=vb_d))
    qg.inc(qg.h.dma_start(out=db_s[:, :], in_=db_d))
    dve.h.memset(ones1f[:, :], 1.0)
    dve.h.memset(ones1b[:, :], 1.0)
    dve.h.memset(onescf[:, :], 1.0)
    dve.h.memset(onesc[:, :], 1.0)
    dve.h.memset(Sst[:, :], 0.0)
    dve.h.memset(V_s[:, :], 1.0)
    p.wait(act, qs, qs.n)
    act.h.activation(nqb_s[:, :], qb_s[:, :], AF.Identity, scale=-1.0)
    act.h.activation(nkb_s[:, :], kb_s[:, :], AF.Identity, scale=-1.0)

    p.wait(pe, dve, dve.n)
    p.wait(pe, qg, qg.n)
    p.wait(pe, qs, qs.n)
    for tc in range(8):
        t0 = tc * 512
        p.wait(qs, dve, dve.n)
        qs.inc(qs.h.dma_start(out=sb3(xc32, NDT, 512), in_=fm3(x0_d[:, t0:t0 + 512])))
        p.wait(act, qs, qs.n)
        p.wait(act, dve, dve.n)
        p.wait(dve, qs, qs.n)
        row_stats(p, banks, xc32, sq_sc, onesc, rows_s, rows_q, 512, HTl=512, sqw=256, cw0=256)
        p.wait(dve, pe, pe.n)
        ln_finalize_rows(p, rows_s, rows_q, tmp_row, 512, float(D), 1e-5)
        p.wait(pe, dve, dve.n)
        bi1, bankR = banks.get()
        pe.h.matmul(bankR[:, :], ones1f[:, :], rows_q[0:1, 0:512], start=True, stop=True)
        bi2, bankM = banks.get()
        last = pe.h.matmul(bankM[:, :], ones1f[:, :], rows_s[0:1, 0:512], start=True, stop=True)
        pe.inc(last)
        p.wait(dve, pe, pe.n)
        for dt in range(NDT):
            sl = xc32[:, dt * 512:(dt + 1) * 512]
            dve.h.tensor_tensor(tsc[:, :], sl, bankR[:, :], ALU.mult)
            dve.h.tensor_tensor(tsc[:, :], tsc[:, :], bankM[:, :], ALU.add)
            dve.h.tensor_scalar(xnc[:, dt * 512:(dt + 1) * 512], tsc[:, :],
                                l1w_s[:, dt:dt + 1], l1b_s[:, dt:dt + 1],
                                ALU.mult, ALU.add)
        banks.consumed(bi1, dve)
        banks.consumed(bi2, dve)
        p.wait(pe, dve, dve.n)
        for wbuf, dst, bcol, nbcol in ((wq16, qf, qb_s, nqb_s), (wk16, kf, kb_s, nkb_s)):
            for hp in range(4):
                bi, bank = banks.get()
                for dt in range(NDT):
                    last = pe.h.matmul(bank[:, :],
                                       wbuf[:, dt * 512 + hp * 128: dt * 512 + (hp + 1) * 128],
                                       xnc[:, dt * 512:(dt + 1) * 512],
                                       start=(dt == 0), stop=(dt == NDT - 1))
                pe.inc(last)
                p.wait(act, pe, pe.n)
                p.wait(act, dve, dve.n)
                phsc = sq_sc[:, 0:512]
                phsc2 = sq_sc[:, 512:1024]
                act.h.activation(phsc[:, :], bank[:, :], AF.Relu, bias=bcol[:, hp:hp + 1])
                act.h.activation(phsc2[:, :], bank[:, :], AF.Relu, scale=-1.0,
                                 bias=nbcol[:, hp:hp + 1])
                act.h.activation(phsc2[:, :], phsc2[:, :], AF.Exp, scale=-1.0)
                banks.consumed(bi, act)
                p.wait(dve, act, act.n)
                dve.h.tensor_tensor(dst[:, hp * TFULL + t0: hp * TFULL + t0 + 512],
                                    phsc[:, :], phsc2[:, :], ALU.add)
        p.wait(pe, dve, dve.n)
        for hp in range(4):
            for ttl in range(4):
                tt = tc * 4 + ttl
                bi, bank = banks.get()
                tview = bank[:, 0:64].bitcast(BF16)
                last = pe.h.transpose(tview,
                                      kf[:, hp * TFULL + t0 + ttl * 128: hp * TFULL + t0 + (ttl + 1) * 128],
                                      idb_s[:, :])
                pe.inc(last)
                p.wait(dve, pe, pe.n)
                dve.h.tensor_copy(KT[:, tt * 512 + hp * 128: tt * 512 + (hp + 1) * 128],
                                  tview)
                banks.consumed(bi, dve)
        p.wait(pe, dve, dve.n)
        for ttl in range(4):
            tt = tc * 4 + ttl
            bi, bank = banks.get()
            for dt in range(NDT):
                pe.h.matmul(bank[:, :],
                            xnc[:, dt * 512 + ttl * 128: dt * 512 + (ttl + 1) * 128],
                            wv16[:, dt * 512:(dt + 1) * 512],
                            start=(dt == 0), stop=False)
            last = pe.h.matmul(bank[:, :], ones1b[:, :],
                               vb_s[:, :], start=False, stop=True)
            pe.inc(last)
            p.wait(dve, pe, pe.n)
            dve.h.tensor_copy(
                V_s[:, tt * 520: tt * 520 + 520].rearrange("p (h v) -> p h v", v=65)[:, :, 0:64],
                bank[:, :].rearrange("p (h v) -> p h v", v=64))
            banks.consumed(bi, dve)
            bi, bank = banks.get()
            for dt in range(NDT):
                pe.h.matmul(bank[:, 0:8],
                            xnc[:, dt * 512 + ttl * 128: dt * 512 + (ttl + 1) * 128],
                            dec16[:, dt * 8:(dt + 1) * 8],
                            start=(dt == 0), stop=False)
            last = pe.h.matmul(bank[:, 0:8], ones1b[:, :],
                               db_s[:, :], start=False, stop=True)
            pe.inc(last)
            p.wait(act, pe, pe.n)
            p.wait(act, dve, dve.n)
            act.h.activation(tsc[:, 0:8], bank[:, 0:8], AF.Sigmoid)
            banks.consumed(bi, act)
            p.wait(dve, act, act.n)
            dve.h.tensor_scalar_max(ll_tm[:, tt * 8:(tt + 1) * 8], tsc[:, 0:8], 1e-8)

    p.wait(act, dve, dve.n)
    act.h.activation(ll_tm[:, :], ll_tm[:, :], AF.Ln)

    cA = p.sb("cA", [128, 32 * 8], F32)
    eA = p.sb("eA", [128, 32 * 8], F32)
    nllA = p.sb("nllA", [128, 32 * 8], F32)
    lamrA = p.sb("lamrA", [1, 32 * 8], F32)
    lamcA = p.sb("lamcA", [64, 32 * 8], F32)
    p.wait(pe, act, act.n)
    p.wait(pe, dve, dve.n)
    for tt in range(32):
        lls = ll_tm[:, tt * 8:(tt + 1) * 8]
        biA, bA = banks.get()
        pe.h.matmul(bA[:, 0:8], ui_s[:, :], lls, start=True, stop=True)
        pe.h.matmul(bA[:, 8:16], ur_s[:, :], lls, start=True, stop=True)
        last = pe.h.matmul(bA[0:1, 16:24], onescf[:, 0:1], lls, start=True, stop=True)
        pe.inc(last)
        p.wait(act, pe, pe.n)
        act.h.activation(cA[:, tt * 8:(tt + 1) * 8], bA[:, 0:8], AF.Exp)
        act.h.activation(eA[:, tt * 8:(tt + 1) * 8], bA[:, 8:16], AF.Exp)
        act.h.activation(lamrA[0:1, tt * 8:(tt + 1) * 8], bA[0:1, 16:24], AF.Exp)
        banks.consumed(biA, act)
    p.wait(dve, act, act.n)
    dve.h.tensor_scalar_mul(nllA[:, :], ll_tm[:, :], -1.0)
    p.wait(pe, act, act.n)
    biL, bL = banks.get()
    last = pe.h.matmul(bL[0:64, 0:256], ones1f[0:1, 0:64], lamrA[0:1, :],
                       start=True, stop=True)
    pe.inc(last)
    p.wait(dve, pe, pe.n)
    dve.h.tensor_copy(lamcA[:, :], bL[0:64, 0:256])
    banks.consumed(biL, dve)

    EPS = 1e-6
    S16a = p.sb("S16a", [128, 8 * 65], BF16)
    Ve8 = p.sb("Ve8", [128, 8 * 65], BF16)
    Dm8 = p.sb("Dm8", [128, 4 * 128], BF16)
    StD8 = p.sb("StD8", [128, 4 * 128], BF16)
    for sweep in range(K1_SWEEPS):
        fwd = (sweep == 0)
        order = list(range(32))[:K1_NTT] if fwd else list(range(31, -1, -1))[:K1_NTT]
        p.wait(dve, act, act.n)
        dve.h.memset(Sst[:, :], 0.0)
        for tt in order:
          if True:
            c_tm = cA[:, tt * 8:(tt + 1) * 8]
            e_tm = eA[:, tt * 8:(tt + 1) * 8]
            nll = nllA[:, tt * 8:(tt + 1) * 8]
            lamc = lamcA[:, tt * 8:(tt + 1) * 8]
          if True:
            if fwd:
                p.wait(dve, qs, qs.n)
                dve.h.memset(outb[:, :], 0.0)
            else:
                qs.inc(qs.h.dma_start(out=outb[:, :],
                                      in_=outf_dram.ap()[tt * 128:(tt + 1) * 128, :]))
                p.wait(dve, qs, qs.n)
            for h8 in (range(8) if "inter" in K1_STAGES else []):
                b0 = (h8 % 2) * 64
                dve.h.tensor_copy(S16a[b0:b0 + 64, h8 * 65:(h8 + 1) * 65],
                                  Sst[:, h8 * 65:(h8 + 1) * 65])
            p.wait(pe, dve, dve.n)
            ibks = []
            for h8 in (range(8) if "inter" in K1_STAGES else []):
                hp, lane = h8 // 2, h8 % 2
                b0 = lane * 64
                bi5, bk = banks.get()
                last = pe.h.matmul(bk[:, 0:65],
                                   qf[b0:b0 + 64, hp * TFULL + tt * 128: hp * TFULL + (tt + 1) * 128],
                                   S16a[b0:b0 + 64, h8 * 65:(h8 + 1) * 65],
                                   start=True, stop=True, tile_position=(b0, 0))
                ibks.append((bi5, bk))
            if "inter" in K1_STAGES:
                pe.inc(last)
            p.wait(dve, pe, pe.n)
            for h8 in (range(8) if "inter" in K1_STAGES else []):
                bi5, bk = ibks[h8]
                scol = c_tm[:, h8:h8 + 1] if fwd else e_tm[:, h8:h8 + 1]
                dve.h.tensor_scalar(tsc[:, 0:65], bk[:, 0:65], scol, None, ALU.mult)
                dve.h.tensor_tensor(outb[:, h8 * 65:(h8 + 1) * 65],
                                    outb[:, h8 * 65:(h8 + 1) * 65], tsc[:, 0:65], ALU.add)
                banks.consumed(bi5, dve)
            if fwd and "intra" in K1_STAGES:
                p.wait(pe, dve, dve.n)
                for grp in range(2):
                    hs = range(grp * 4, grp * 4 + 4)
                    sbks = []
                    for h8 in hs:
                        hp, lane = h8 // 2, h8 % 2
                        b0 = lane * 64
                        biS, bk = banks.get()
                        sl = qf[b0:b0 + 64, hp * TFULL + tt * 128: hp * TFULL + (tt + 1) * 128]
                        kl = kf[b0:b0 + 64, hp * TFULL + tt * 128: hp * TFULL + (tt + 1) * 128]
                        pe.h.matmul(bk[:, 0:128], kl, sl,
                                    start=True, stop=True, tile_position=(b0, 0))
                        sbks.append((biS, bk))
                    dbks = []
                    for h8 in hs:
                        biDl, bk = banks.get()
                        pe.h.matmul(bk[:, 0:128], ui_s[:, :],
                                    ll_tm[:, tt * 8 + h8: tt * 8 + h8 + 1].broadcast_to([128, 128]),
                                    start=True, stop=False)
                        last = pe.h.matmul(bk[:, 0:128],
                                           nll[:, h8:h8 + 1].broadcast_to([128, 128]),
                                           ui_s[:, :], start=False, stop=True)
                        dbks.append((biDl, bk))
                    pe.inc(last)
                    p.wait(act, pe, pe.n)
                    for j, h8 in enumerate(hs):
                        biDl, bk = dbks[j]
                        act.h.activation(StD8[:, j * 128:(j + 1) * 128], bk[:, 0:128], AF.Abs)
                        act.h.activation(Dm8[:, j * 128:(j + 1) * 128],
                                         StD8[:, j * 128:(j + 1) * 128], AF.Exp, scale=-1.0)
                        banks.consumed(biDl, act)
                    p.wait(dve, act, act.n)
                    for j, h8 in enumerate(hs):
                        biS, bk = sbks[j]
                        dve.h.tensor_tensor(StD8[:, j * 128:(j + 1) * 128],
                                            bk[:, 0:128],
                                            Dm8[:, j * 128:(j + 1) * 128], ALU.mult)
                        banks.consumed(biS, dve)
                    p.wait(pe, dve, dve.n)
                    obks = []
                    for j, h8 in enumerate(hs):
                        bi8, bk = banks.get()
                        last = pe.h.matmul(bk[:, 0:65], StD8[:, j * 128:(j + 1) * 128],
                                           V_s[:, tt * 520 + h8 * 65: tt * 520 + (h8 + 1) * 65],
                                           start=True, stop=True)
                        obks.append((bi8, bk))
                    pe.inc(last)
                    p.wait(dve, pe, pe.n)
                    for j, h8 in enumerate(hs):
                        bi8, bk = obks[j]
                        dve.h.tensor_tensor(outb[:, h8 * 65:(h8 + 1) * 65],
                                            outb[:, h8 * 65:(h8 + 1) * 65], bk[:, 0:65], ALU.add)
                        banks.consumed(bi8, dve)
            for h8 in (range(8) if "update" in K1_STAGES else []):
                ucol = e_tm[:, h8:h8 + 1] if fwd else c_tm[:, h8:h8 + 1]
                dve.h.tensor_scalar(Ve8[:, h8 * 65:(h8 + 1) * 65],
                                    V_s[:, tt * 520 + h8 * 65: tt * 520 + (h8 + 1) * 65],
                                    ucol, None, ALU.mult)
            p.wait(pe, dve, dve.n)
            ubks = []
            for h8 in (range(8) if "update" in K1_STAGES else []):
                hp, lane = h8 // 2, h8 % 2
                b0 = lane * 64
                bi9, bk = banks.get()
                last = pe.h.matmul(bk[0:64, 0:65],
                                   KT[:, tt * 512 + hp * 128 + b0: tt * 512 + hp * 128 + b0 + 64],
                                   Ve8[:, h8 * 65:(h8 + 1) * 65], start=True, stop=True)
                ubks.append((bi9, bk))
            if "update" in K1_STAGES:
                pe.inc(last)
            p.wait(dve, pe, pe.n)
            for h8 in (range(8) if "update" in K1_STAGES else []):
                bi9, bk = ubks[h8]
                S_h = Sst[:, h8 * 65:(h8 + 1) * 65]
                dve.h.tensor_scalar(S_h, S_h, lamc[:, h8:h8 + 1], None, ALU.mult)
                dve.h.tensor_tensor(S_h, S_h, bk[0:64, 0:65], ALU.add)
                banks.consumed(bi9, dve)
            if fwd:
                p.wait(qs, dve, dve.n)
                qs.inc(qs.h.dma_start(out=outf_dram.ap()[tt * 128:(tt + 1) * 128, :],
                                      in_=outb[:, :]))
            else:
                p.wait(dve, qs, qs.n)
                for h8 in range(8):
                    dve.h.tensor_scalar_max(tsc[:, h8:h8 + 1],
                                            outb[:, h8 * 65 + 64: h8 * 65 + 65], EPS)
                    dve.h.reciprocal(tsc[:, h8:h8 + 1], tsc[:, h8:h8 + 1])
                    dve.h.tensor_scalar(astage[:, h8 * 64:(h8 + 1) * 64],
                                        outb[:, h8 * 65: h8 * 65 + 64],
                                        tsc[:, h8:h8 + 1], None, ALU.mult)
                p.wait(qs, dve, dve.n)
                qs.inc(qs.h.dma_start(out=attn_d[tt * 128:(tt + 1) * 128, :],
                                      in_=astage[:, :]))


def build_k1():
    p = Prog()
    IO = {n: p.din(n, s) for n, s in [("x0f", [D, TFULL]), ("ln1w_c", [128, NDT]),
          ("ln1b_c", [128, NDT]), ("wq_my", [D, 512]), ("wk_my", [D, 512]),
          ("wv_my", [D, 512]), ("dec_my", [D, 8]), ("qb_c", [128, 4]),
          ("kb_c", [128, 4]), ("vb_r", [1, 512]), ("db_r", [1, 8]),
          ("u_incl", [128, 128]), ("u_rvex", [128, 128])]}
    IO["identb"] = p.nc.dram_tensor("identb", [128, 128], BF16, kind="ExternalInput").ap()
    IO["attn"] = p.nc.dram_tensor("attn", [TFULL, 512], BF16, kind="ExternalOutput").ap()
    IO["outf"] = p.dint("outf", [TFULL, 520])
    p.push(); emit_k1(p, IO, attn_bf16=True); p.pop()
    return p.finish()


def k1_inputs(inputs):
    c = np.ascontiguousarray
    col = lambda v: c(v.reshape(-1, 128).T)
    t = np.arange(128)
    u_incl = (t[:, None] <= t[None, :]).astype(np.float32)
    u_rvex = (t[:, None] > t[None, :]).astype(np.float32)
    import ml_dtypes
    identb = np.eye(128, dtype=ml_dtypes.bfloat16)
    maps = []
    for core in range(8):
        b, r = core // 2, core % 2
        hs = r * 512
        m = dict(
            x0f=c(np.asarray(inputs["L0_attn_x"])[b].T),
            ln1w_c=col(inputs["L0_attn_ln1_w"]), ln1b_c=col(inputs["L0_attn_ln1_b"]),
            wq_my=c(inputs["L0_attn_q_w"][:, hs:hs + 512]),
            wk_my=c(inputs["L0_attn_k_w"][:, hs:hs + 512]),
            wv_my=c(inputs["L0_attn_v_w"][:, hs:hs + 512]),
            dec_my=c(inputs["L0_attn_dec_w"][:, r * 8:(r + 1) * 8]),
            qb_c=c(inputs["L0_attn_q_b"][hs:hs + 512].reshape(4, 128).T),
            kb_c=c(inputs["L0_attn_k_b"][hs:hs + 512].reshape(4, 128).T),
            vb_r=c(inputs["L0_attn_v_b"][hs:hs + 512].reshape(1, 512)),
            db_r=c(inputs["L0_attn_dec_b"][r * 8:(r + 1) * 8].reshape(1, 8)),
            u_incl=u_incl, u_rvex=u_rvex, identb=identb)
        maps.append(m)
    return maps


GROUPS = [[0, 1], [2, 3], [4, 5], [6, 7]]


def _ag(p, in_h, out_h):
    g = p.nc.gpsimd
    g.wait_ge(p.qs.sem, p.qs.n)
    g.wait_ge(p.qg.sem, p.qg.n)
    ins = g.collective_compute(
        "AllGather", mybir.AluOpType.bypass,
        replica_groups=GROUPS,
        ins=[in_h.ap().opt()],
        outs=[out_h.ap().opt()],
    )
    ins.then_inc(p.cc.sem, 1)
    p.cc.n += 1


def build_fused(dbg=False):
    p = Prog()
    nc = p.nc
    if dbg:
        _dint = p.dint
        def dint_dbg(name, shape, dt=F32):
            if name in ("x1_i", "x2_i", "x3_i", "ctx2_pre", "ctx4_pre"):
                class H:
                    def __init__(self, ap): self._ap = ap
                    def ap(self): return self._ap
                return H(p.nc.dram_tensor(name, list(shape), dt, kind="ExternalOutput").ap())
            return _dint(name, shape, dt)
        p.dint = dint_dbg
    p.cc = Eng(p, nc.gpsimd, "cc")

    def dins(pairs):
        return {n: p.din(n, s) for n, s in pairs}

    IO1 = dins([("x0f", [D, TFULL]), ("ln1w_c", [128, NDT]), ("ln1b_c", [128, NDT]),
                ("wq_my", [D, 512]), ("wk_my", [D, 512]), ("wv_my", [D, 512]),
                ("dec_my", [D, 8]), ("qb_c", [128, 4]), ("kb_c", [128, 4]),
                ("vb_r", [1, 512]), ("db_r", [1, 8]), ("u_incl", [128, 128]),
                ("u_rvex", [128, 128])])
    IO1["identb"] = nc.dram_tensor("identb", [128, 128], BF16, kind="ExternalInput").ap()
    IO1["outf"] = p.dint("outf", [TFULL, 520])
    attn_b = p.dint("attn_b", [TFULL, 512], BF16)
    IO1["attn"] = attn_b.ap()

    IO2 = dins([("x0my", [D, HT]), ("ow", [D, D]), ("ob_c", [128, NDT]),
                ("ln2w_c", [128, NDT]), ("ln2b_c", [128, NDT]), ("f1w", [D, 4096]),
                ("f1b_c", [128, 32]), ("f2w", [4096, D]), ("f2b_c", [128, NDT]),
                ("l1s_c", [128, NDT]), ("l1h_c", [128, NDT])])
    IO2["x0"] = IO2.pop("x0my")
    x1_i = p.dint("x1_i", [D, HT])
    h1_i = p.dint("h1_i", [D, HT], BF16)
    IO2["x1"] = x1_i.ap()
    IO2["h1"] = h1_i.ap()

    IO3 = {"wq_my": p.din("wq3_my", [D, 512]), "wk_my": p.din("wk3_my", [D, 512]),
           "wv_my": p.din("wv3_my", [D, 512])}
    IO3["triu"] = nc.dram_tensor("triu", [128, 128], BF16, kind="ExternalInput").ap()
    ctx3_i = p.dint("ctx3_i", [512, TFULL], BF16)
    IO3["ctxo"] = ctx3_i.ap()

    IO4 = {"wo": p.din("wo4", [D, D]), "bo_c": p.din("bo4_c", [128, NDT]),
           "l2s_c": p.din("l2s_c", [128, NDT]), "l2h_c": p.din("l2h_c", [128, NDT]),
           "gate_w": p.din("gate_w", [D, 8]), "w1": p.din("w1", [8, D, 2048]),
           "w2": p.din("w2", [8, D, 2048]), "w3": p.din("w3", [8, 2048, D]),
           "ident": p.din("ident", [128, 128])}
    IO4["x1"] = x1_i.ap()
    x2_i = p.dint("x2_i", [D, HT])
    IO4["x2"] = x2_i.ap()

    IO5 = {"wq": p.din("wq5", [D, D]), "wk": p.din("wk5", [D, D]),
           "wv": p.din("wv5", [D, D]), "bq_c": p.din("bq5_c", [128, NDT]),
           "bk_c": p.din("bk5_c", [128, NDT]), "bv_r": p.din("bv5_r", [1, D]),
           "wp": p.din("wp5", [D, D]), "bp_c": p.din("bp5_c", [128, NDT])}
    IO5["xmy"] = x2_i.ap()
    k_b = p.dint("k_b", [D, HT], BF16)
    v_b = p.dint("v_b", [HT, D], BF16)
    kmax_b = p.dint("kmax_b", [1, 4])
    ag_k = p.dint("ag_k", [2 * D, HT], BF16)
    ag_v = p.dint("ag_v", [TFULL, D], BF16)
    ag_km = p.dint("ag_km", [2, 4])
    IO5["k_sc"] = k_b.ap()
    IO5["v_bounce"] = v_b.ap()
    IO5["kmax_b"] = kmax_b.ap()
    IO5["ag_k"] = ag_k.ap()
    IO5["ag_v"] = ag_v.ap()
    IO5["ag_kmax"] = ag_km.ap()
    IO5["q_sc"] = p.dint("q_sc", [D, HT], BF16).ap()
    x3_i = p.dint("x3_i", [D, HT])
    IO5["x3"] = x3_i.ap()

    def mid_cb(p):
        _ag(p, k_b, ag_k)
        _ag(p, v_b, ag_v)
        _ag(p, kmax_b, ag_km)
    IO5["mid_cb"] = mid_cb

    IO6 = dins([("wr_t", [D, 8]), ("br_r", [1, 8]), ("w1t", [8, D, 4096]),
                ("b1c", [8, 128, 32]), ("w2t", [8, 4096, D]), ("b2r", [8, 1, D]),
                ("lnw_c", [128, 8]), ("lnb_c", [128, 8])])
    IO6["ident"] = IO4["ident"]
    IO6["x"] = x3_i.ap()
    IO6["y"] = p.dout("y", [D, HT])

    myt0_d = p.din("my_t0", [1, 1], I32)
    ag1 = p.dint("ag1", [2 * TFULL, 512], BF16)
    ctx2_pre = p.dint("ctx2_pre", [2 * HT, 512], BF16)
    ag2 = p.dint("ag2", [2 * D, HT], BF16)
    ag3 = p.dint("ag3", [2 * 512, TFULL], BF16)
    ctx4_pre = p.dint("ctx4_pre", [D, HT], BF16)

    myt0_s = p.sb("myt0_s", [1, 1], I32)
    p.qg.inc(p.qg.h.dma_start(out=myt0_s[:, :], in_=myt0_d))

    p.push(); emit_k1(p, IO1, attn_bf16=True); p.pop()
    p.barrier()
    _ag(p, attn_b, ag1)
    g = nc.gpsimd
    g.wait_ge(p.cc.sem, p.cc.n)
    g.wait_ge(p.qg.sem, p.qg.n)
    off = g.value_load(myt0_s[0:1, 0:1])
    for r in range(2):
        p.qg.h.dma_start(out=ctx2_pre.ap()[r * HT:(r + 1) * HT, :],
                         in_=ag1.ap()[r * TFULL:(r + 1) * TFULL][bass.ds(off, HT), :])

    def loader2(p, c16):
        p.wait(p.qs, p.qg, p.qg.n)
        for r in range(2):
            for cs in range(4):
                dt = r * 4 + cs
                p.qs.h.dma_start(
                    out=c16[:, dt * HT:(dt + 1) * HT],
                    in_=ctx2_pre.ap()[r * HT:(r + 1) * HT, cs * 128:(cs + 1) * 128],
                    transpose=True)

    p.push(); emit_k2(p, IO2, ctx_loader=loader2); p.pop()
    p.barrier()
    _ag(p, h1_i, ag2)

    def loader3(p, hc, t0):
        p.wait(p.qs, p.cc)
        blk = t0 // HT
        c0 = t0 % HT
        p.qs.h.dma_start(out=sb3(hc, NDT, 512),
                         in_=fm3(ag2.ap()[blk * D:(blk + 1) * D, c0:c0 + 512]))

    p.push(); emit_k3(p, IO3, h1_loader=loader3, ctx_bf16=True); p.pop()
    p.barrier()
    _ag(p, ctx3_i, ag3)
    g.wait_ge(p.cc.sem, p.cc.n)
    off2 = g.value_load(myt0_s[0:1, 0:1])
    p.qg.h.dma_start(out=ctx4_pre.ap(), in_=ag3.ap()[:, bass.ds(off2, HT)])

    def loader4(p, c16):
        p.wait(p.qg, p.qg, p.qg.n)
        p.qg.h.dma_start(out=sb3(c16, NDT, HT), in_=fm3(ctx4_pre.ap()))

    p.push(); emit_k4(p, IO4, ctx_loader=loader4); p.pop()
    p.barrier()

    p.push(); emit_k5(p, IO5, fused=True); p.pop()
    p.barrier()

    p.push(); emit_k6(p, IO6); p.pop()
    if dbg:
        p.barrier()
        for nm, h, shape, dt in [("dbg_attn", attn_b, [TFULL, 512], BF16),
                                 ("dbg_h1", h1_i, [D, HT], BF16),
                                 ("dbg_ctx3", ctx3_i, [512, TFULL], BF16)]:
            dd = nc.dram_tensor(nm, shape, dt, kind="ExternalOutput").ap()
            p.qs.h.dma_start(out=dd, in_=h.ap())
    return p.finish()


def fused_inputs(inputs):
    import ml_dtypes
    c = np.ascontiguousarray
    col = lambda v: c(v.reshape(-1, 128).T)
    x0 = np.asarray(inputs["L0_attn_x"])
    t = np.arange(128)
    u_incl = (t[:, None] <= t[None, :]).astype(np.float32)
    u_rvex = (t[:, None] > t[None, :]).astype(np.float32)
    identb = np.eye(128, dtype=ml_dtypes.bfloat16)
    triu = np.triu(np.ones((128, 128))).astype(ml_dtypes.bfloat16)
    wqkv = inputs["L1_attn_w_qkv"]
    bqkv = inputs["L1_attn_b_qkv"]
    base = dict(
        ln1w_c=col(inputs["L0_attn_ln1_w"]), ln1b_c=col(inputs["L0_attn_ln1_b"]),
        u_incl=u_incl, u_rvex=u_rvex, identb=identb, triu=triu,
        ow=c(inputs["L0_attn_o_w"]), ob_c=col(inputs["L0_attn_o_b"]),
        ln2w_c=col(inputs["L0_attn_ln2_w"]), ln2b_c=col(inputs["L0_attn_ln2_b"]),
        f1w=c(inputs["L0_attn_f1_w"]), f1b_c=col(inputs["L0_attn_f1_b"]),
        f2w=c(inputs["L0_attn_f2_w"]), f2b_c=col(inputs["L0_attn_f2_b"]),
        l1s_c=col(inputs["L0_ffn_ln1_scale"]), l1h_c=col(inputs["L0_ffn_ln1_shift"]),
        wo4=c(inputs["L0_ffn_wo"]), bo4_c=col(inputs["L0_ffn_bo"]),
        l2s_c=col(inputs["L0_ffn_ln2_scale"]), l2h_c=col(inputs["L0_ffn_ln2_shift"]),
        gate_w=c(inputs["L0_ffn_gate_w"]),
        w1=c(inputs["L0_ffn_w1"]), w2=c(inputs["L0_ffn_w2"]), w3=c(inputs["L0_ffn_w3"]),
        ident=np.eye(128, dtype=np.float32),
        wq5=c(wqkv[:, 0:1024]), wk5=c(wqkv[:, 1024:2048]), wv5=c(wqkv[:, 2048:3072]),
        bq5_c=col(bqkv[0:1024]), bk5_c=col(bqkv[1024:2048]),
        bv5_r=c(bqkv[2048:3072].reshape(1, -1)),
        wp5=c(inputs["L1_attn_w_proj"]), bp5_c=col(inputs["L1_attn_b_proj"]),
        wr_t=c(inputs["L1_ffn_Wr"].T), br_r=c(inputs["L1_ffn_br"].reshape(1, 8)),
        w1t=c(inputs["L1_ffn_W1"].transpose(0, 2, 1)),
        b1c=c(inputs["L1_ffn_b1"].reshape(8, 32, 128).transpose(0, 2, 1)),
        w2t=c(inputs["L1_ffn_W2"].transpose(0, 2, 1)),
        b2r=c(inputs["L1_ffn_b2"].reshape(8, 1, 1024)),
        lnw_c=c(inputs["L1_ffn_ln_w"].reshape(8, 128).T),
        lnb_c=c(inputs["L1_ffn_ln_b"].reshape(8, 128).T))
    maps = []
    for core in range(8):
        b, r = core // 2, core % 2
        hs = r * 512
        m = dict(base)
        m["x0f"] = c(x0[b].T)
        m["x0my"] = c(x0[b, r * HT:(r + 1) * HT, :].T)
        m["wq_my"] = c(inputs["L0_attn_q_w"][:, hs:hs + 512])
        m["wk_my"] = c(inputs["L0_attn_k_w"][:, hs:hs + 512])
        m["wv_my"] = c(inputs["L0_attn_v_w"][:, hs:hs + 512])
        m["dec_my"] = c(inputs["L0_attn_dec_w"][:, r * 8:(r + 1) * 8])
        m["qb_c"] = c(inputs["L0_attn_q_b"][hs:hs + 512].reshape(4, 128).T)
        m["kb_c"] = c(inputs["L0_attn_k_b"][hs:hs + 512].reshape(4, 128).T)
        m["vb_r"] = c(inputs["L0_attn_v_b"][hs:hs + 512].reshape(1, 512))
        m["db_r"] = c(inputs["L0_attn_dec_b"][r * 8:(r + 1) * 8].reshape(1, 8))
        m["wq3_my"] = c(inputs["L0_ffn_wq"][:, hs:hs + 512])
        m["wk3_my"] = c(inputs["L0_ffn_wk"][:, hs:hs + 512])
        m["wv3_my"] = c(inputs["L0_ffn_wv"][:, hs:hs + 512])
        m["my_t0"] = np.array([[r * HT]], dtype=np.int32)
        maps.append(m)
    return maps


def kernel_fused(inputs, dbg=False, res_out=None):
    nc = build_fused(dbg=dbg)
    res = run_bass_kernel_spmd(nc, fused_inputs(inputs), core_ids=list(range(8)))
    if res_out is not None:
        res_out.append(res)
    out = np.zeros((4, TFULL, D), dtype=np.float32)
    for core in range(8):
        b, h = core // 2, core % 2
        out[b, h * HT:(h + 1) * HT, :] = res.results[core]["y"].T
    return out


def kernel(**inputs):
    inputs = {k: np.asarray(v) for k, v in inputs.items()}
    return kernel_fused(inputs)


# version 05
# speedup vs baseline: 1.2930x; 1.0644x over previous
import contextlib
import numpy as np
import concourse.bass as bass
import concourse.mybir as mybir
from concourse.bass_utils import run_bass_kernel_spmd

F32 = mybir.dt.float32
BF16 = mybir.dt.bfloat16
I32 = mybir.dt.int32
AF = mybir.ActivationFunctionType
ALU = mybir.AluOpType
AX = mybir.AxisListType.X

K1_STAGES = {"inter", "intra", "update"}
K1_SWEEPS = 2
K1_NTT = 32
D = 1024
NDT = 8
TFULL = 4096
HT = 2048
NTT = HT // 128


class _AutoProxy:

    def __init__(self, eng):
        object.__setattr__(self, "_e", eng)

    def __getattr__(self, name):
        e = self._e
        f = getattr(e.raw, name)
        if name in ("wait_ge",) or not callable(f):
            return f

        def wrapped(*a, **k):
            if e.n > 0:
                e.raw.wait_ge(e.sem, e.n)
            r = f(*a, **k)
            if hasattr(r, "then_inc"):
                r.then_inc(e.sem, 1)
                e.n += 1
            return r
        return wrapped


class _DmaProxy:

    def __init__(self, eng):
        object.__setattr__(self, "_e", eng)

    def __getattr__(self, name):
        e = self._e
        f = getattr(e.raw, name)
        if name != "dma_start" or not callable(f):
            return f

        def wrapped(*a, **k):
            if e.n > 0:
                e.raw.wait_ge(e.sem, e.n)
            r = f(*a, **k)
            r.then_inc(e.sem, 16)
            e.n += 16
            return r
        return wrapped


class Eng:
    def __init__(self, prog, handle, name, step=1, auto=False, dma=False):
        self.p = prog
        self.raw = handle
        self.name = name
        self.step = step
        self.auto = auto
        self.sem = prog.es.enter_context(prog.nc.semaphore("sem_" + name))
        self.n = 0
        self._waited = {}
        if auto:
            self.h = _AutoProxy(self)
        elif dma:
            self.h = _DmaProxy(self)
        else:
            self.h = handle

    def inc(self, instr, k=None):
        if self.auto or isinstance(self.h, _DmaProxy):
            return self.n
        k = self.step if k is None else k
        instr.then_inc(self.sem, k)
        self.n += k
        return self.n


class Prog:
    def __init__(self):
        self.nc = bass.Bass("TRN2", target_bir_lowering=False)
        self.es = contextlib.ExitStack()
        nc = self.nc
        self.pe = Eng(self, nc.tensor, "pe")
        self.act = Eng(self, nc.scalar, "act", auto=True)
        self.dve = Eng(self, nc.vector, "dve", auto=True)
        self.qs = Eng(self, nc.sync, "qs", step=16, dma=True)
        self.qg = Eng(self, nc.gpsimd, "qg", step=16, dma=True)
        self.engines = [self.pe, self.act, self.dve, self.qs, self.qg]

    def wait(self, consumer, producer, n=None):
        tgt = producer.n if n is None else n
        if tgt <= 0:
            return
        key = consumer.name
        last = producer._waited.get(key, -1)
        if tgt <= last:
            return
        consumer.h.wait_ge(producer.sem, tgt)
        producer._waited[key] = tgt

    def push(self):
        if not hasattr(self, "_stacks"):
            self._stacks = []
        self._stacks.append(contextlib.ExitStack())

    def pop(self):
        self._stacks.pop().close()

    def _stack(self):
        st = getattr(self, "_stacks", None)
        return st[-1] if st else self.es

    def barrier(self):
        for c in self.engines:
            for pr in self.engines:
                if c is not pr:
                    self.wait(c, pr)

    def sb(self, name, shape, dt):
        name = f"{name}_{self._uid()}"
        return self._stack().enter_context(self.nc.sbuf_tensor(name, list(shape), dt))

    def ps(self, name, shape, dt=F32):
        name = f"{name}_{self._uid()}"
        return self._stack().enter_context(self.nc.psum_tensor(name, list(shape), dt))

    def _uid(self):
        self._n = getattr(self, "_n", 0) + 1
        return self._n

    def din(self, name, shape, dt=F32):
        return self.nc.dram_tensor(name, list(shape), dt, kind="ExternalInput").ap()

    def dout(self, name, shape, dt=F32):
        return self.nc.dram_tensor(name, list(shape), dt, kind="ExternalOutput").ap()

    def dint(self, name, shape, dt=F32):
        return self.nc.dram_tensor(name, list(shape), dt)

    def finish(self):
        for e in self.engines[1:]:
            self.wait(self.engines[0], e)
        self.es.close()
        return self.nc


class Pool8:

    def __init__(self, p, n=8):
        self.p = p
        self.t = [p.ps(f"bank{i}", [128, 512]) for i in range(n)]
        self.guard = [None] * n
        self.i = 0

    def get(self):
        i = self.i % len(self.t)
        self.i += 1
        g = self.guard[i]
        if g is not None:
            self.p.wait(self.p.pe, g[0], g[1])
        return i, self.t[i]

    def consumed(self, i, eng):
        self.guard[i] = (eng, eng.n)


def fm3(dram_ap, p_=128):
    return dram_ap.rearrange("(dt p) t -> p dt t", p=p_)


def sb3(sbuf, ndt, T):
    return sbuf[:, 0:ndt * T].rearrange("p (dt t) -> p dt t", t=T)


def fm_load(p, q, sbuf, dram_ap, Dfull, T, cast=False):
    nd = Dfull // 128
    q.inc(q.h.dma_start(out=sb3(sbuf, nd, T), in_=fm3(dram_ap)))


def fm_store(p, q, dram_ap, sbuf, Dfull, T):
    nd = Dfull // 128
    q.inc(q.h.dma_start(out=fm3(dram_ap), in_=sb3(sbuf, nd, T)))


def row_stats(p, banks, src32, sq_sc, onesc, rows_s, rows_q, T, ndt=NDT, HTl=HT, t0=0, sqw=512, cw0=512):
    pe, act, dve = p.pe, p.act, p.dve
    nch = (T + cw0 - 1) // cw0
    for c in range(nch):
        c0 = c * cw0
        cw = min(cw0, T - c0)
        p.wait(p.act, pe, pe.n)
        for dt in range(ndt):
            last = act.h.activation(sq_sc[:, dt * sqw: dt * sqw + cw],
                                    src32[:, dt * HTl + t0 + c0: dt * HTl + t0 + c0 + cw], AF.Copy)
        act.inc(last)
        p.wait(pe, act, act.n)
        bi1, b1 = banks.get()
        for dt in range(ndt):
            last = pe.h.matmul(b1[0:1, 0:cw], onesc[:, 0:1],
                               sq_sc[:, dt * sqw: dt * sqw + cw],
                               start=(dt == 0), stop=(dt == ndt - 1))
        pe.inc(last)
        p.wait(act, pe, pe.n)
        for dt in range(ndt):
            last = act.h.activation(sq_sc[:, dt * sqw: dt * sqw + cw],
                                    sq_sc[:, dt * sqw: dt * sqw + cw], AF.Square)
        act.inc(last)
        p.wait(pe, act, act.n)
        bi2, b2 = banks.get()
        for dt in range(ndt):
            last = pe.h.matmul(b2[0:1, 0:cw], onesc[:, 0:1],
                               sq_sc[:, dt * sqw: dt * sqw + cw],
                               start=(dt == 0), stop=(dt == ndt - 1))
        pe.inc(last)
        p.wait(dve, pe, pe.n)
        dve.h.tensor_copy(rows_s[0:1, c0:c0 + cw], b1[0:1, 0:cw])
        dve.inc(dve.h.tensor_copy(rows_q[0:1, c0:c0 + cw], b2[0:1, 0:cw]))
        banks.consumed(bi1, dve)
        banks.consumed(bi2, dve)


def ln_finalize_rows(p, rows_s, rows_q, tmp_row, T, nd, eps):
    act, dve = p.act, p.dve
    dve.h.tensor_scalar_mul(rows_s[0:1, 0:T], rows_s[0:1, 0:T], 1.0 / nd)
    dve.h.tensor_scalar_mul(rows_q[0:1, 0:T], rows_q[0:1, 0:T], 1.0 / nd)
    dve.h.tensor_tensor(tmp_row[0:1, 0:T], rows_s[0:1, 0:T], rows_s[0:1, 0:T], ALU.mult)
    dve.inc(dve.h.tensor_tensor(rows_q[0:1, 0:T], rows_q[0:1, 0:T], tmp_row[0:1, 0:T], ALU.subtract))
    dve.inc(dve.h.tensor_scalar_add(rows_q[0:1, 0:T], rows_q[0:1, 0:T], float(eps)))
    p.wait(act, dve, dve.n)
    act.inc(act.h.activation(rows_q[0:1, 0:T], rows_q[0:1, 0:T], AF.Sqrt))
    p.wait(dve, act, act.n)
    dve.h.reciprocal(rows_q[0:1, 0:T], rows_q[0:1, 0:T])
    dve.h.tensor_tensor(rows_s[0:1, 0:T], rows_s[0:1, 0:T], rows_q[0:1, 0:T], ALU.mult)
    dve.inc(dve.h.tensor_scalar_mul(rows_s[0:1, 0:T], rows_s[0:1, 0:T], -1.0))


def emit_k6(p, IO):
    pe, act, dve, qs, qg = p.pe, p.act, p.dve, p.qs, p.qg
    x_d = IO["x"]; wr_d = IO["wr_t"]; br_d = IO["br_r"]; w1_d = IO["w1t"]
    b1_d = IO["b1c"]; w2_d = IO["w2t"]; b2_d = IO["b2r"]; lnw_d = IO["lnw_c"]
    lnb_d = IO["lnb_c"]; id_d = IO["ident"]; y_d = IO["y"]

    FC = 1024
    NFC = 4096 // FC
    NFT = FC // 128

    acc = p.sb("acc", [128, NDT * HT], F32)
    x16 = p.sb("x16", [128, NDT * HT], BF16)
    w1c = p.sb("w1c", [128, NDT * FC], BF16)
    w2c = p.sb("w2c", [128, NFT * D], BF16)
    hbuf = p.sb("hbuf", [128, NFT * HT], BF16)
    wr_s = p.sb("wr_s", [128, NDT * 8], BF16)
    br_s = p.sb("br_s", [1, 8], BF16)
    id32 = p.sb("id32", [128, 128], F32)
    b1_s = p.sb("b1_s", [128, 32], F32)
    b2_s = p.sb("b2_s", [1, D], BF16)
    lnw_s = p.sb("lnw_s", [128, 8], F32)
    lnb_s = p.sb("lnb_s", [128, 8], F32)
    gts = p.sb("gts", [128, NTT * 8], F32)
    logit = p.sb("logit", [128, NTT * 8], F32)
    wrow = p.sb("wrow", [1, HT], BF16)
    wbc = p.sb("wbc", [128, HT], BF16)
    ones1 = p.sb("ones1", [1, 128], BF16)
    ones1f = p.sb("ones1f", [1, 128], F32)
    onesc = p.sb("onesc", [128, 1], BF16)
    rows_s = p.sb("rows_s", [1, 1024], F32)
    rows_q = p.sb("rows_q", [1, 1024], F32)
    tmp_row = p.sb("tmp_row", [1, 1024], F32)
    sq_sc = p.sb("sq_sc", [128, NDT * 256], BF16)
    x3c = p.sb("x3c", [128, 4 * 512], F32)
    tsc = p.sb("tsc", [128, 512], F32)
    tsc2 = p.sb("tsc2", [128, 512], F32)

    banks = Pool8(p)

    fm_load(p, qs, acc, x_d, D, HT)
    qg.inc(qg.h.dma_start(out=sb3(x16, NDT, HT), in_=fm3(x_d)))
    qg.inc(qg.h.dma_start(out=sb3(wr_s, NDT, 8), in_=fm3(wr_d)))
    qg.inc(qg.h.dma_start(out=br_s[:, :], in_=br_d))
    qs.inc(qs.h.dma_start(out=id32[:, :], in_=id_d))
    qs.inc(qs.h.dma_start(out=lnw_s[:, :], in_=lnw_d))
    qs.inc(qs.h.dma_start(out=lnb_s[:, :], in_=lnb_d))
    dve.h.memset(ones1[:, :], 1.0)
    dve.h.memset(ones1f[:, :], 1.0)
    dve.inc(dve.h.memset(onesc[:, :], 1.0))

    p.wait(pe, qg, qg.n)
    p.wait(pe, qs, qs.n)
    p.wait(dve, qs, qs.n)
    p.wait(act, qs, qs.n)
    p.wait(pe, dve, dve.n)
    bi, bank = banks.get()
    for tt in range(NTT):
        for dt in range(NDT):
            pe.h.matmul(bank[:, tt * 8:(tt + 1) * 8],
                        x16[:, dt * HT + tt * 128: dt * HT + (tt + 1) * 128],
                        wr_s[:, dt * 8:(dt + 1) * 8],
                        start=(dt == 0), stop=False)
        last = pe.h.matmul(bank[:, tt * 8:(tt + 1) * 8], ones1[:, :], br_s[:, :],
                           start=False, stop=True)
    pe.inc(last)
    p.wait(dve, pe, pe.n)
    dve.inc(dve.h.tensor_copy(logit[:, :], bank[:, 0:NTT * 8]))
    banks.consumed(bi, dve)

    m1 = p.sb("m1", [128, NTT], F32)
    nm1 = p.sb("nm1", [128, NTT], F32)
    m2 = p.sb("m2", [128, NTT], F32)
    zs = p.sb("zs", [128, NTT], F32)
    tg = p.sb("tg", [128, 8], F32)
    for tt in range(NTT):
        l8 = logit[:, tt * 8:(tt + 1) * 8]
        g8 = gts[:, tt * 8:(tt + 1) * 8]
        p.wait(dve, act, act.n)
        dve.h.reduce_max(m1[:, tt:tt + 1], l8, axis=AX)
        dve.h.reduce_max(nm1[:, tt:tt + 1], l8, axis=AX, negate=True)
        dve.h.tensor_scalar(tg[:, :], l8, m1[:, tt:tt + 1], None, ALU.is_ge)
        dve.h.scalar_tensor_tensor(tg[:, :], tg[:, :], -1e30, l8, ALU.mult, ALU.add)
        dve.h.reduce_max(m2[:, tt:tt + 1], tg[:, :], axis=AX)
        dve.inc(dve.h.tensor_scalar(g8, l8, m2[:, tt:tt + 1], None, ALU.is_ge))
        p.wait(act, dve, dve.n)
        act.inc(act.h.activation(tg[:, :], l8, AF.Exp, bias=nm1[:, tt:tt + 1]))
        p.wait(dve, act, act.n)
        dve.h.tensor_tensor(g8, g8, tg[:, :], ALU.mult)
        dve.h.reduce_sum(zs[:, tt:tt + 1], g8, axis=AX)
        dve.h.reciprocal(zs[:, tt:tt + 1], zs[:, tt:tt + 1])
        dve.inc(dve.h.tensor_scalar_mul(g8, g8, zs[:, tt:tt + 1]))

    st6 = {"w1g": None, "w2g": None, "w1n": 0, "w2n": 0}

    def k6_load_w1(idx):
        e, fc = idx // NFC, idx % NFC
        if st6["w1g"] is not None:
            p.wait(qg, st6["w1g"][0], st6["w1g"][1])
        qg.inc(qg.h.dma_start(
            out=sb3(w1c, NDT, FC),
            in_=fm3(w1_d[e, :, fc * FC:(fc + 1) * FC])))
        st6["w1n"] = qg.n

    def k6_load_w2(idx):
        e, fc = idx // NFC, idx % NFC
        if st6["w2g"] is not None:
            p.wait(qg, st6["w2g"][0], st6["w2g"][1])
        qg.inc(qg.h.dma_start(
            out=sb3(w2c, NFT, D),
            in_=fm3(w2_d[e, fc * FC:(fc + 1) * FC, :])))
        st6["w2n"] = qg.n

    k6_load_w1(0)
    for e in range(8):
        p.wait(pe, dve, dve.n)
        wr_banks = []
        for c in range(4):
            bi, bank = banks.get()
            for j in range(4):
                tt = c * 4 + j
                last = pe.h.matmul(bank[0:1, j * 128:(j + 1) * 128],
                                   gts[:, tt * 8 + e: tt * 8 + e + 1], id32[:, :],
                                   start=True, stop=True)
            wr_banks.append((bi, bank, c))
        pe.inc(last)
        p.wait(dve, pe, pe.n)
        for bi, bank, c in wr_banks:
            last = dve.h.tensor_copy(wrow[0:1, c * 512:(c + 1) * 512], bank[0:1, 0:512])
        dve.inc(last)
        for bi, bank, c in wr_banks:
            banks.consumed(bi, dve)
        p.wait(pe, dve, dve.n)
        wb_banks = []
        for c in range(4):
            bi, bank = banks.get()
            last = pe.h.matmul(bank[:, :], ones1[:, :], wrow[0:1, c * 512:(c + 1) * 512],
                               start=True, stop=True)
            wb_banks.append((bi, bank, c))
        pe.inc(last)
        p.wait(dve, pe, pe.n)
        for bi, bank, c in wb_banks:
            last = dve.h.tensor_copy(wbc[:, c * 512:(c + 1) * 512], bank[:, :])
        dve.inc(last)
        for bi, bank, c in wb_banks:
            banks.consumed(bi, dve)

        for fc in range(NFC):
            idx = e * NFC + fc
            p.wait(qg, act, act.n)
            qg.inc(qg.h.dma_start(out=b1_s[:, 0:NFT],
                                  in_=b1_d[e, :, fc * NFT:(fc + 1) * NFT]))
            if fc == NFC - 1:
                qg.inc(qg.h.dma_start(out=b2_s[:, :], in_=b2_d[e, 0:1, :]))
            b1n = qg.n
            k6_load_w2(idx)
            p.wait(pe, qg, st6["w1n"])
            p.wait(act, qg, b1n)
            p.wait(pe, dve, dve.n)
            for ft in range(NFT):
                for tb in range(4):
                    bi, bank = banks.get()
                    for dt in range(NDT):
                        last = pe.h.matmul(bank[:, :],
                                    w1c[:, dt * FC + ft * 128: dt * FC + (ft + 1) * 128],
                                    x16[:, dt * HT + tb * 512: dt * HT + (tb + 1) * 512],
                                    start=(dt == 0), stop=(dt == NDT - 1))
                    pe.inc(last)
                    p.wait(act, pe, pe.n)
                    act.inc(act.h.activation(
                        hbuf[:, ft * HT + tb * 512: ft * HT + (tb + 1) * 512],
                        bank[:, :], AF.Gelu, bias=b1_s[:, ft:ft + 1]))
                    banks.consumed(bi, act)
            st6["w1g"] = (pe, pe.n)
            if idx + 1 < 8 * NFC:
                k6_load_w1(idx + 1)
            p.wait(dve, act, act.n)
            for ft in range(NFT):
                last = dve.h.tensor_tensor(hbuf[:, ft * HT:(ft + 1) * HT],
                                    hbuf[:, ft * HT:(ft + 1) * HT],
                                    wbc[:, :], ALU.mult)
            dve.inc(last)
            p.wait(pe, dve, dve.n)
            p.wait(pe, qg, st6["w2n"])
            for mt in range(NDT):
                for tb in range(4):
                    bi, bank = banks.get()
                    last_real = (fc != NFC - 1)
                    for ft in range(NFT):
                        last = pe.h.matmul(bank[:, :],
                                    w2c[:, ft * D + mt * 128: ft * D + (mt + 1) * 128],
                                    hbuf[:, ft * HT + tb * 512: ft * HT + (tb + 1) * 512],
                                    start=(ft == 0),
                                    stop=(last_real and ft == NFT - 1))
                    if fc == NFC - 1:
                        last = pe.h.matmul(bank[:, :], b2_s[0:1, mt * 128:(mt + 1) * 128],
                                    wrow[0:1, tb * 512:(tb + 1) * 512],
                                    start=False, stop=True)
                    pe.inc(last)
                    p.wait(dve, pe, pe.n)
                    sl = acc[:, mt * HT + tb * 512: mt * HT + (tb + 1) * 512]
                    dve.inc(dve.h.tensor_tensor(sl, sl, bank[:, :], ALU.add))
                    banks.consumed(bi, dve)
            st6["w2g"] = (pe, pe.n)

    for ln_i in range(2):
        for hf in range(2):
            t0 = hf * 1024
            p.wait(act, dve, dve.n)
            row_stats(p, banks, acc, sq_sc, onesc, rows_s, rows_q, 1024, t0=t0, sqw=256, cw0=256)
            p.wait(dve, pe, pe.n)
            eps = 1e-12 if ln_i == 0 else 1e-5
            ln_finalize_rows(p, rows_s, rows_q, tmp_row, 1024, float(D), eps)
            for c in range(2):
                c0 = t0 + c * 512
                r0 = c * 512
                p.wait(pe, dve, dve.n)
                bi1, bankR = banks.get()
                pe.h.matmul(bankR[:, :], ones1f[:, :], rows_q[0:1, r0:r0 + 512], start=True, stop=True)
                bi2, bankM = banks.get()
                last = pe.h.matmul(bankM[:, :], ones1f[:, :], rows_s[0:1, r0:r0 + 512], start=True, stop=True)
                pe.inc(last)
                p.wait(dve, pe, pe.n)
                if ln_i == 0:
                    p.wait(dve, qs, qs.n)
                for dt in range(NDT):
                    if ln_i == 0 and dt % 4 == 0:
                        p.wait(qs, dve, dve.n)
                        qs.inc(qs.h.dma_start(
                            out=sb3(x3c, 4, 512),
                            in_=fm3(x_d[dt * 128:(dt + 4) * 128, c0:c0 + 512])))
                        p.wait(dve, qs, qs.n)
                    sl = acc[:, dt * HT + c0: dt * HT + c0 + 512]
                    dve.h.tensor_tensor(tsc[:, :], sl, bankR[:, :], ALU.mult)
                    dve.inc(dve.h.tensor_tensor(tsc[:, :], tsc[:, :], bankM[:, :], ALU.add))
                    if ln_i == 0:
                        dve.h.tensor_scalar(tsc2[:, :], tsc[:, :],
                                            lnw_s[:, dt:dt + 1], lnb_s[:, dt:dt + 1],
                                            ALU.mult, ALU.add)
                        dve.inc(dve.h.tensor_tensor(sl, tsc2[:, :],
                                                    x3c[:, (dt % 4) * 512:(dt % 4 + 1) * 512], ALU.add))
                    else:
                        dve.inc(dve.h.tensor_copy(sl, tsc[:, :]))
                banks.consumed(bi1, dve)
                banks.consumed(bi2, dve)

    p.wait(qs, dve, dve.n)
    fm_store(p, qs, y_d, acc, D, HT)


def build_k6():
    p = Prog()
    IO = {n: p.din(n, s) for n, s in [("x", [D, HT]), ("wr_t", [D, 8]), ("br_r", [1, 8]),
          ("w1t", [8, D, 4096]), ("b1c", [8, 128, 32]), ("w2t", [8, 4096, D]),
          ("b2r", [8, 1, D]), ("lnw_c", [128, 8]), ("lnb_c", [128, 8]),
          ("ident", [128, 128])]}
    IO["y"] = p.dout("y", [D, HT])
    p.push(); emit_k6(p, IO); p.pop()
    return p.finish()



def k6_inputs(inputs, x3_full):
    c = np.ascontiguousarray
    wr_t = c(inputs["L1_ffn_Wr"].T)
    br_r = c(inputs["L1_ffn_br"].reshape(1, 8))
    w1t = c(inputs["L1_ffn_W1"].transpose(0, 2, 1))
    b1c = c(inputs["L1_ffn_b1"].reshape(8, 32, 128).transpose(0, 2, 1))
    w2t = c(inputs["L1_ffn_W2"].transpose(0, 2, 1))
    b2r = c(inputs["L1_ffn_b2"].reshape(8, 1, 1024))
    lnw_c = c(inputs["L1_ffn_ln_w"].reshape(8, 128).T)
    lnb_c = c(inputs["L1_ffn_ln_b"].reshape(8, 128).T)
    ident = np.eye(128, dtype=np.float32)
    maps = []
    for core in range(8):
        b, h = core // 2, core % 2
        x = c(x3_full[b, h * HT:(h + 1) * HT, :].T)
        maps.append(dict(x=x, wr_t=wr_t, br_r=br_r, w1t=w1t, b1c=b1c,
                         w2t=w2t, b2r=b2r, lnw_c=lnw_c, lnb_c=lnb_c, ident=ident))
    return maps


def k6_collect(res):
    out = np.zeros((4, TFULL, D), dtype=np.float32)
    for core in range(8):
        b, h = core // 2, core % 2
        out[b, h * HT:(h + 1) * HT, :] = res.results[core]["y"].T
    return out


def plain_ln_to_acc(p, banks, acc, sq_sc, onesc, ones1f, rows_s, rows_q, tmp_row,
                    tsc, eps, per_dt=None, x_extra=None, x3c=None, HTl=HT):
    pe, act, dve, qs = p.pe, p.act, p.dve, p.qs
    for hf in range(HTl // 1024):
        t0 = hf * 1024
        p.wait(act, dve, dve.n)
        row_stats(p, banks, acc, sq_sc, onesc, rows_s, rows_q, 1024, t0=t0, HTl=HTl)
        p.wait(dve, pe, pe.n)
        ln_finalize_rows(p, rows_s, rows_q, tmp_row, 1024, float(D), eps)
        for c in range(2):
            c0 = t0 + c * 512
            r0 = c * 512
            p.wait(pe, dve, dve.n)
            bi1, bankR = banks.get()
            pe.h.matmul(bankR[:, :], ones1f[:, :], rows_q[0:1, r0:r0 + 512], start=True, stop=True)
            bi2, bankM = banks.get()
            last = pe.h.matmul(bankM[:, :], ones1f[:, :], rows_s[0:1, r0:r0 + 512], start=True, stop=True)
            pe.inc(last)
            p.wait(dve, pe, pe.n)
            for dt in range(NDT):
                sl = acc[:, dt * HTl + c0: dt * HTl + c0 + 512]
                dve.h.tensor_tensor(tsc[:, :], sl, bankR[:, :], ALU.mult)
                dve.h.tensor_tensor(tsc[:, :], tsc[:, :], bankM[:, :], ALU.add)
                if per_dt is None:
                    dve.h.tensor_copy(sl, tsc[:, :])
                else:
                    per_dt(dt, c0, tsc, sl)
            banks.consumed(bi1, dve)
            banks.consumed(bi2, dve)


def add_dram_fm(p, banks, acc, dram_ap, x3c, HTl=HT):
    dve, qs = p.dve, p.qs
    for c in range(HTl // 512):
        c0 = c * 512
        for half in range(2):
            dt0 = half * 4
            p.wait(qs, dve, dve.n)
            qs.inc(qs.h.dma_start(
                out=sb3(x3c, 4, 512),
                in_=fm3(dram_ap[dt0 * 128:(dt0 + 4) * 128, c0:c0 + 512])))
            p.wait(dve, qs, qs.n)
            for dt in range(dt0, dt0 + 4):
                sl = acc[:, dt * HTl + c0: dt * HTl + c0 + 512]
                dve.h.tensor_tensor(sl, sl, x3c[:, (dt % 4) * 512:(dt % 4 + 1) * 512], ALU.add)


def topk2_gates(p, banks, x16, w_s, b_row, gts, logit, scratch, HTl=HT, nbias=True):
    pe, act, dve = p.pe, p.act, p.dve
    ones1, m1, nm1, m2, zs, tg = scratch
    ntt = HTl // 128
    bi, bank = banks.get()
    for tt in range(ntt):
        lastmm = None
        for dt in range(NDT):
            lastmm = pe.h.matmul(bank[:, tt * 8:(tt + 1) * 8],
                        x16[:, dt * HTl + tt * 128: dt * HTl + (tt + 1) * 128],
                        w_s[:, dt * 8:(dt + 1) * 8],
                        start=(dt == 0), stop=(dt == NDT - 1 and not nbias))
        if nbias:
            lastmm = pe.h.matmul(bank[:, tt * 8:(tt + 1) * 8], ones1[:, :], b_row[:, :],
                                 start=False, stop=True)
    pe.inc(lastmm)
    p.wait(dve, pe, pe.n)
    dve.h.tensor_copy(logit[:, 0:ntt * 8], bank[:, 0:ntt * 8])
    banks.consumed(bi, dve)
    for tt in range(ntt):
        l8 = logit[:, tt * 8:(tt + 1) * 8]
        g8 = gts[:, tt * 8:(tt + 1) * 8]
        p.wait(dve, act, act.n)
        dve.h.reduce_max(m1[:, tt:tt + 1], l8, axis=AX)
        dve.h.reduce_max(nm1[:, tt:tt + 1], l8, axis=AX, negate=True)
        dve.h.tensor_scalar(tg[:, :], l8, m1[:, tt:tt + 1], None, ALU.is_ge)
        dve.h.scalar_tensor_tensor(tg[:, :], tg[:, :], -1e30, l8, ALU.mult, ALU.add)
        dve.h.reduce_max(m2[:, tt:tt + 1], tg[:, :], axis=AX)
        dve.h.tensor_scalar(g8, l8, m2[:, tt:tt + 1], None, ALU.is_ge)
        p.wait(act, dve, dve.n)
        act.h.activation(tg[:, :], l8, AF.Exp, bias=nm1[:, tt:tt + 1])
        p.wait(dve, act, act.n)
        dve.h.tensor_tensor(g8, g8, tg[:, :], ALU.mult)
        dve.h.reduce_sum(zs[:, tt:tt + 1], g8, axis=AX)
        dve.h.reciprocal(zs[:, tt:tt + 1], zs[:, tt:tt + 1])
        dve.h.tensor_scalar_mul(g8, g8, zs[:, tt:tt + 1])


def wrow_wbc(p, banks, gts, e, id32, ones1, wrow, wbc, HTl=HT):
    pe, dve = p.pe, p.dve
    ntt = HTl // 128
    p.wait(pe, dve, dve.n)
    wr_banks = []
    for c in range(ntt // 4):
        bi, bank = banks.get()
        for j in range(4):
            tt = c * 4 + j
            last = pe.h.matmul(bank[0:1, j * 128:(j + 1) * 128],
                               gts[:, tt * 8 + e: tt * 8 + e + 1], id32[:, :],
                               start=True, stop=True)
        wr_banks.append((bi, bank, c))
    pe.inc(last)
    p.wait(dve, pe, pe.n)
    for bi, bank, c in wr_banks:
        dve.h.tensor_copy(wrow[0:1, c * 512:(c + 1) * 512], bank[0:1, 0:512])
    for bi, bank, c in wr_banks:
        banks.consumed(bi, dve)
    p.wait(pe, dve, dve.n)
    wb_banks = []
    for c in range(HTl // 512):
        bi, bank = banks.get()
        last = pe.h.matmul(bank[:, :], ones1[:, :], wrow[0:1, c * 512:(c + 1) * 512],
                           start=True, stop=True)
        wb_banks.append((bi, bank, c))
    pe.inc(last)
    p.wait(dve, pe, pe.n)
    for bi, bank, c in wb_banks:
        dve.h.tensor_copy(wbc[:, c * 512:(c + 1) * 512], bank[:, :])
    for bi, bank, c in wb_banks:
        banks.consumed(bi, dve)


def emit_k2(p, IO, ctx_loader=None):
    pe, act, dve, qs, qg = p.pe, p.act, p.dve, p.qs, p.qg
    ctx_d = IO.get("ctx"); x0_d = IO["x0"]; ow_d = IO["ow"]; ob_d = IO["ob_c"]
    ln2w_d = IO["ln2w_c"]; ln2b_d = IO["ln2b_c"]; f1_d = IO["f1w"]; f1b_d = IO["f1b_c"]
    f2_d = IO["f2w"]; f2b_d = IO["f2b_c"]; l1s_d = IO["l1s_c"]; l1h_d = IO["l1h_c"]
    x1_d = IO["x1"]; h1_d = IO["h1"]

    FC = 1024
    NFC = 4096 // FC
    NFT = FC // 128

    acc = p.sb("acc", [128, NDT * HT], F32)
    c16 = p.sb("c16", [128, NDT * HT], BF16)
    w_a = p.sb("w_a", [128, NDT * 1024], BF16)
    w_b = p.sb("w_b", [128, NDT * 1024], BF16)
    hbuf = p.sb("hbuf", [128, NFT * HT], BF16)
    ob_s = p.sb("ob_s", [128, NDT], F32)
    ln2w_s = p.sb("ln2w_s", [128, NDT], F32)
    ln2b_s = p.sb("ln2b_s", [128, NDT], F32)
    f1b_s = p.sb("f1b_s", [128, 32], F32)
    f2b_s = p.sb("f2b_s", [128, NDT], F32)
    l1s_s = p.sb("l1s_s", [128, NDT], F32)
    l1h_s = p.sb("l1h_s", [128, NDT], F32)
    onesc = p.sb("onesc", [128, 1], BF16)
    ones1f = p.sb("ones1f", [1, 128], F32)
    rows_s = p.sb("rows_s", [1, 1024], F32)
    rows_q = p.sb("rows_q", [1, 1024], F32)
    tmp_row = p.sb("tmp_row", [1, 1024], F32)
    sq_sc = p.sb("sq_sc", [128, NDT * 512], BF16)
    x3c = p.sb("x3c", [128, 4 * 512], F32)
    tsc = p.sb("tsc", [128, 512], F32)

    banks = Pool8(p)

    fm_load(p, qs, acc, x0_d, D, HT)
    if ctx_loader is None:
        qg.inc(qg.h.dma_start(out=sb3(c16, NDT, HT), in_=fm3(ctx_d)))
    else:
        ctx_loader(p, c16)
    qg.inc(qg.h.dma_start(out=sb3(w_a, NDT, 1024), in_=fm3(ow_d[:, 0:1024])))
    for s, dd in [(ob_s, ob_d), (ln2w_s, ln2w_d), (ln2b_s, ln2b_d), (f1b_s, f1b_d),
                  (f2b_s, f2b_d), (l1s_s, l1s_d), (l1h_s, l1h_d)]:
        qs.inc(qs.h.dma_start(out=s[:, :], in_=dd))
    dve.h.memset(onesc[:, :], 1.0)
    dve.h.memset(ones1f[:, :], 1.0)

    p.wait(pe, qg, qg.n)
    p.wait(pe, qs, qs.n)
    p.wait(dve, qs, qs.n)
    p.wait(act, qs, qs.n)
    for mt in range(NDT):
        for tb in range(4):
            bi, bank = banks.get()
            for dt in range(NDT):
                last = pe.h.matmul(bank[:, :],
                                   w_a[:, dt * 1024 + mt * 128: dt * 1024 + (mt + 1) * 128],
                                   c16[:, dt * HT + tb * 512: dt * HT + (tb + 1) * 512],
                                   start=(dt == 0), stop=(dt == NDT - 1))
            pe.inc(last)
            p.wait(dve, pe, pe.n)
            sl = acc[:, mt * HT + tb * 512: mt * HT + (tb + 1) * 512]
            dve.h.scalar_tensor_tensor(sl, bank[:, :], ob_s[:, mt:mt + 1], sl,
                                       ALU.add, ALU.add)
            banks.consumed(bi, dve)
    p.wait(act, dve, dve.n)
    for hf in range(2):
        t0 = hf * 1024
        row_stats(p, banks, acc, sq_sc, onesc, rows_s, rows_q, 1024, t0=t0)
        p.wait(dve, pe, pe.n)
        ln_finalize_rows(p, rows_s, rows_q, tmp_row, 1024, float(D), 1e-5)
        for c in range(2):
            c0 = t0 + c * 512
            r0 = c * 512
            p.wait(pe, dve, dve.n)
            bi1, bankR = banks.get()
            pe.h.matmul(bankR[:, :], ones1f[:, :], rows_q[0:1, r0:r0 + 512], start=True, stop=True)
            bi2, bankM = banks.get()
            last = pe.h.matmul(bankM[:, :], ones1f[:, :], rows_s[0:1, r0:r0 + 512], start=True, stop=True)
            pe.inc(last)
            p.wait(dve, pe, pe.n)
            for dt in range(NDT):
                sl = acc[:, dt * HT + c0: dt * HT + c0 + 512]
                dve.h.tensor_tensor(tsc[:, :], sl, bankR[:, :], ALU.mult)
                dve.h.tensor_tensor(tsc[:, :], tsc[:, :], bankM[:, :], ALU.add)
                dve.h.tensor_scalar(c16[:, dt * HT + c0: dt * HT + c0 + 512],
                                    tsc[:, :], ln2w_s[:, dt:dt + 1], ln2b_s[:, dt:dt + 1],
                                    ALU.mult, ALU.add)
            banks.consumed(bi1, dve)
            banks.consumed(bi2, dve)
        p.wait(act, dve, dve.n)

    p.wait(pe, dve, dve.n)
    st2 = {"f1g": (pe, pe.n), "f2g": None, "f1n": 0, "f2n": 0}

    def k2_load_f1(fc2):
        p.wait(qg, st2["f1g"][0], st2["f1g"][1])
        qg.inc(qg.h.dma_start(out=sb3(w_a, NDT, FC),
                              in_=fm3(f1_d[:, fc2 * FC:(fc2 + 1) * FC])))
        st2["f1n"] = qg.n

    def k2_load_f2(fc2):
        if st2["f2g"] is not None:
            p.wait(qg, st2["f2g"][0], st2["f2g"][1])
        qg.inc(qg.h.dma_start(out=sb3(w_b, NFT, D),
                              in_=fm3(f2_d[fc2 * FC:(fc2 + 1) * FC, :])))
        st2["f2n"] = qg.n

    k2_load_f1(0)
    for fc in range(NFC):
        k2_load_f2(fc)
        p.wait(pe, qg, st2["f1n"])
        for ft in range(NFT):
            for tb in range(4):
                bi, bank = banks.get()
                for dt in range(NDT):
                    last = pe.h.matmul(bank[:, :],
                                w_a[:, dt * FC + ft * 128: dt * FC + (ft + 1) * 128],
                                c16[:, dt * HT + tb * 512: dt * HT + (tb + 1) * 512],
                                start=(dt == 0), stop=(dt == NDT - 1))
                pe.inc(last)
                p.wait(act, pe, pe.n)
                act.h.activation(hbuf[:, ft * HT + tb * 512: ft * HT + (tb + 1) * 512],
                                 bank[:, :], AF.Gelu,
                                 bias=f1b_s[:, fc * NFT + ft: fc * NFT + ft + 1])
                banks.consumed(bi, act)
        st2["f1g"] = (pe, pe.n)
        if fc + 1 < NFC:
            k2_load_f1(fc + 1)
        p.wait(pe, qg, st2["f2n"])
        p.wait(pe, act, act.n)
        for mt in range(NDT):
            for tb in range(4):
                bi, bank = banks.get()
                for ft in range(NFT):
                    last = pe.h.matmul(bank[:, :],
                                w_b[:, ft * D + mt * 128: ft * D + (mt + 1) * 128],
                                hbuf[:, ft * HT + tb * 512: ft * HT + (tb + 1) * 512],
                                start=(ft == 0), stop=(ft == NFT - 1))
                pe.inc(last)
                p.wait(dve, pe, pe.n)
                sl = acc[:, mt * HT + tb * 512: mt * HT + (tb + 1) * 512]
                if fc == NFC - 1:
                    dve.h.scalar_tensor_tensor(sl, bank[:, :], f2b_s[:, mt:mt + 1], sl,
                                               ALU.add, ALU.add)
                else:
                    dve.h.tensor_tensor(sl, sl, bank[:, :], ALU.add)
                banks.consumed(bi, dve)
        st2["f2g"] = (pe, pe.n)

    add_dram_fm(p, banks, acc, x0_d, x3c)
    plain_ln_to_acc(p, banks, acc, sq_sc, onesc, ones1f, rows_s, rows_q, tmp_row,
                    tsc, 1e-5)
    p.wait(qs, dve, dve.n)
    fm_store(p, qs, x1_d, acc, D, HT)
    def emit_h1(dt, c0, tsc_ap, sl):
        p.dve.h.tensor_scalar(c16[:, dt * HT + c0: dt * HT + c0 + 512],
                              tsc_ap[:, :], l1s_s[:, dt:dt + 1], l1h_s[:, dt:dt + 1],
                              ALU.mult, ALU.add)
    plain_ln_to_acc(p, banks, acc, sq_sc, onesc, ones1f, rows_s, rows_q, tmp_row,
                    tsc, 1e-5, per_dt=emit_h1)
    p.wait(qs, dve, dve.n)
    qs.inc(qs.h.dma_start(out=fm3(h1_d), in_=sb3(c16, NDT, HT)))


def build_k2():
    p = Prog()
    IO = {n: p.din(n, s) for n, s in [("ctx", [D, HT]), ("x0", [D, HT]), ("ow", [D, D]),
          ("ob_c", [128, NDT]), ("ln2w_c", [128, NDT]), ("ln2b_c", [128, NDT]),
          ("f1w", [D, 4096]), ("f1b_c", [128, 32]), ("f2w", [4096, D]),
          ("f2b_c", [128, NDT]), ("l1s_c", [128, NDT]), ("l1h_c", [128, NDT])]}
    IO["x1"] = p.dout("x1", [D, HT])
    IO["h1"] = p.nc.dram_tensor("h1", [D, HT], BF16, kind="ExternalOutput").ap()
    p.push(); emit_k2(p, IO); p.pop()
    return p.finish()


def k2_inputs(inputs, attn_full, x0_full):
    c = np.ascontiguousarray
    col = lambda v: c(v.reshape(-1, 128).T)
    base = dict(
        ow=c(inputs["L0_attn_o_w"]), ob_c=col(inputs["L0_attn_o_b"]),
        ln2w_c=col(inputs["L0_attn_ln2_w"]), ln2b_c=col(inputs["L0_attn_ln2_b"]),
        f1w=c(inputs["L0_attn_f1_w"]), f1b_c=col(inputs["L0_attn_f1_b"]),
        f2w=c(inputs["L0_attn_f2_w"]), f2b_c=col(inputs["L0_attn_f2_b"]),
        l1s_c=col(inputs["L0_ffn_ln1_scale"]), l1h_c=col(inputs["L0_ffn_ln1_shift"]))
    maps = []
    for core in range(8):
        b, h = core // 2, core % 2
        m = dict(base)
        m["ctx"] = c(attn_full[b, h * HT:(h + 1) * HT, :].T)
        m["x0"] = c(x0_full[b, h * HT:(h + 1) * HT, :].T)
        maps.append(m)
    return maps


def emit_k4(p, IO, ctx_loader=None):
    pe, act, dve, qs, qg = p.pe, p.act, p.dve, p.qs, p.qg
    ctx_d = IO.get("ctx"); x1_d = IO["x1"]; wo_d = IO["wo"]; bo_d = IO["bo_c"]
    l2s_d = IO["l2s_c"]; l2h_d = IO["l2h_c"]; gw_d = IO["gate_w"]
    w1_d = IO["w1"]; w2_d = IO["w2"]; w3_d = IO["w3"]; id_d = IO["ident"]
    x2_d = IO["x2"]

    HC = 512
    NHC = 2048 // HC
    NHTl = HC // 128

    acc = p.sb("acc", [128, NDT * HT], F32)
    c16 = p.sb("c16", [128, NDT * HT], BF16)
    w1c = p.sb("w1c", [128, NDT * HC], BF16)
    w2c = p.sb("w2c", [128, NDT * HC], BF16)
    w3c = p.sb("w3c", [128, NHTl * D], BF16)
    hS = p.sb("hS", [128, NHTl * HT], BF16)
    gate_s = p.sb("gate_s", [128, NDT * 8], BF16)
    id32 = p.sb("id32", [128, 128], F32)
    bo_s = p.sb("bo_s", [128, NDT], F32)
    l2s_s = p.sb("l2s_s", [128, NDT], F32)
    l2h_s = p.sb("l2h_s", [128, NDT], F32)
    gts = p.sb("gts", [128, NTT * 8], F32)
    logit = p.sb("logit", [128, NTT * 8], F32)
    wrow = p.sb("wrow", [1, HT], BF16)
    wbc = p.sb("wbc", [128, HT], BF16)
    ones1 = p.sb("ones1", [1, 128], BF16)
    ones1f = p.sb("ones1f", [1, 128], F32)
    onesc = p.sb("onesc", [128, 1], BF16)
    rows_s = p.sb("rows_s", [1, 1024], F32)
    rows_q = p.sb("rows_q", [1, 1024], F32)
    tmp_row = p.sb("tmp_row", [1, 1024], F32)
    sq_sc = p.sb("sq_sc", [128, NDT * 512], BF16)
    x3c = p.sb("x3c", [128, 4 * 512], F32)
    tsc = p.sb("tsc", [128, 512], F32)
    sA = p.sb("sA", [128, 512], BF16)
    m1 = p.sb("m1", [128, NTT], F32)
    nm1 = p.sb("nm1", [128, NTT], F32)
    m2 = p.sb("m2", [128, NTT], F32)
    zs = p.sb("zs", [128, NTT], F32)
    tg = p.sb("tg", [128, 8], F32)

    banks = Pool8(p)

    fm_load(p, qs, acc, x1_d, D, HT)
    if ctx_loader is None:
        qg.inc(qg.h.dma_start(out=sb3(c16, NDT, HT), in_=fm3(ctx_d)))
    else:
        ctx_loader(p, c16)

    qg.inc(qg.h.dma_start(out=sb3(gate_s, NDT, 8), in_=fm3(gw_d)))
    qs.inc(qs.h.dma_start(out=id32[:, :], in_=id_d))
    for s, dd in [(bo_s, bo_d), (l2s_s, l2s_d), (l2h_s, l2h_d)]:
        qs.inc(qs.h.dma_start(out=s[:, :], in_=dd))
    dve.h.memset(ones1[:, :], 1.0)
    dve.h.memset(ones1f[:, :], 1.0)
    dve.h.memset(onesc[:, :], 1.0)

    p.wait(pe, qg, qg.n)
    p.wait(pe, qs, qs.n)
    p.wait(dve, qs, qs.n)
    p.wait(act, qs, qs.n)
    p.wait(pe, dve, dve.n)
    for mh in range(2):
        p.wait(qg, pe, pe.n)
        qg.inc(qg.h.dma_start(out=sb3(w1c, NDT, 512),
                              in_=fm3(wo_d[:, mh * 512:(mh + 1) * 512])))
        p.wait(pe, qg, qg.n)
        for mtl in range(4):
            mt = mh * 4 + mtl
            for tb in range(4):
                bi, bank = banks.get()
                for dt in range(NDT):
                    last = pe.h.matmul(bank[:, :],
                                       w1c[:, dt * 512 + mtl * 128: dt * 512 + (mtl + 1) * 128],
                                       c16[:, dt * HT + tb * 512: dt * HT + (tb + 1) * 512],
                                       start=(dt == 0), stop=(dt == NDT - 1))
                pe.inc(last)
                p.wait(dve, pe, pe.n)
                sl = acc[:, mt * HT + tb * 512: mt * HT + (tb + 1) * 512]
                dve.h.scalar_tensor_tensor(sl, bank[:, :], bo_s[:, mt:mt + 1], sl,
                                           ALU.add, ALU.add)
                banks.consumed(bi, dve)

    p.wait(act, dve, dve.n)
    def emit_h2(dt, c0, tsc_ap, sl):
        p.dve.h.tensor_scalar(c16[:, dt * HT + c0: dt * HT + c0 + 512],
                              tsc_ap[:, :], l2s_s[:, dt:dt + 1], l2h_s[:, dt:dt + 1],
                              ALU.mult, ALU.add)
    plain_ln_to_acc(p, banks, acc, sq_sc, onesc, ones1f, rows_s, rows_q, tmp_row,
                    tsc, 1e-5, per_dt=emit_h2)

    p.wait(pe, dve, dve.n)
    topk2_gates(p, banks, c16, gate_s, None, gts, logit,
                (ones1, m1, nm1, m2, zs, tg), nbias=False)

    st4 = {"w12g": None, "w3g": None, "w12n": 0, "w3n": 0}

    def k4_load_w12(idx):
        e2, hc2 = idx // NHC, idx % NHC
        if st4["w12g"] is not None:
            p.wait(qg, st4["w12g"][0], st4["w12g"][1])
        qg.inc(qg.h.dma_start(out=sb3(w1c, NDT, HC),
                              in_=fm3(w1_d[e2, :, hc2 * HC:(hc2 + 1) * HC])))
        qg.inc(qg.h.dma_start(out=sb3(w2c, NDT, HC),
                              in_=fm3(w2_d[e2, :, hc2 * HC:(hc2 + 1) * HC])))
        st4["w12n"] = qg.n

    def k4_load_w3(idx):
        e2, hc2 = idx // NHC, idx % NHC
        if st4["w3g"] is not None:
            p.wait(qg, st4["w3g"][0], st4["w3g"][1])
        qg.inc(qg.h.dma_start(out=sb3(w3c, NHTl, D),
                              in_=fm3(w3_d[e2, hc2 * HC:(hc2 + 1) * HC, :])))
        st4["w3n"] = qg.n

    st4["w12g"] = (pe, pe.n)
    k4_load_w12(0)
    for e in range(8):
        wrow_wbc(p, banks, gts, e, id32, ones1, wrow, wbc)
        for hc in range(NHC):
            idx = e * NHC + hc
            k4_load_w3(idx)
            p.wait(pe, qg, st4["w12n"])
            p.wait(pe, dve, dve.n)
            for ht in range(NHTl):
                for tb in range(4):
                    bia, banka = banks.get()
                    for dt in range(NDT):
                        lasta = pe.h.matmul(banka[:, :],
                                    w1c[:, dt * HC + ht * 128: dt * HC + (ht + 1) * 128],
                                    c16[:, dt * HT + tb * 512: dt * HT + (tb + 1) * 512],
                                    start=(dt == 0), stop=(dt == NDT - 1))
                    bib, bankb = banks.get()
                    for dt in range(NDT):
                        lastb = pe.h.matmul(bankb[:, :],
                                    w2c[:, dt * HC + ht * 128: dt * HC + (ht + 1) * 128],
                                    c16[:, dt * HT + tb * 512: dt * HT + (tb + 1) * 512],
                                    start=(dt == 0), stop=(dt == NDT - 1))
                    pe.inc(lasta, 1)
                    pe.inc(lastb, 1)
                    p.wait(act, pe, pe.n)
                    p.wait(act, dve, dve.n)
                    act.h.activation(sA[:, :], banka[:, :], AF.Silu)
                    banks.consumed(bia, act)
                    hsl = hS[:, ht * HT + tb * 512: ht * HT + (tb + 1) * 512]
                    p.wait(dve, act, act.n)
                    dve.h.tensor_tensor(hsl, bankb[:, :], wbc[:, tb * 512:(tb + 1) * 512],
                                        ALU.mult)
                    dve.h.tensor_tensor(hsl, hsl, sA[:, :], ALU.mult)
                    banks.consumed(bib, dve)
            st4["w12g"] = (pe, pe.n)
            if idx + 1 < 8 * NHC:
                k4_load_w12(idx + 1)
            p.wait(pe, dve, dve.n)
            p.wait(pe, qg, st4["w3n"])
            for mt in range(NDT):
                for tb in range(4):
                    bi, bank = banks.get()
                    for ht in range(NHTl):
                        last = pe.h.matmul(bank[:, :],
                                    w3c[:, ht * D + mt * 128: ht * D + (mt + 1) * 128],
                                    hS[:, ht * HT + tb * 512: ht * HT + (tb + 1) * 512],
                                    start=(ht == 0), stop=(ht == NHTl - 1))
                    pe.inc(last)
                    p.wait(dve, pe, pe.n)
                    sl = acc[:, mt * HT + tb * 512: mt * HT + (tb + 1) * 512]
                    dve.h.tensor_tensor(sl, sl, bank[:, :], ALU.add)
                    banks.consumed(bi, dve)
            st4["w3g"] = (pe, pe.n)

    add_dram_fm(p, banks, acc, x1_d, x3c)
    p.wait(act, dve, dve.n)
    plain_ln_to_acc(p, banks, acc, sq_sc, onesc, ones1f, rows_s, rows_q, tmp_row,
                    tsc, 1e-5)
    p.wait(qs, dve, dve.n)
    fm_store(p, qs, x2_d, acc, D, HT)


def build_k4():
    p = Prog()
    IO = {n: p.din(n, s) for n, s in [("ctx", [D, HT]), ("x1", [D, HT]), ("wo", [D, D]),
          ("bo_c", [128, NDT]), ("l2s_c", [128, NDT]), ("l2h_c", [128, NDT]),
          ("gate_w", [D, 8]), ("w1", [8, D, 2048]), ("w2", [8, D, 2048]),
          ("w3", [8, 2048, D]), ("ident", [128, 128])]}
    IO["x2"] = p.dout("x2", [D, HT])
    p.push(); emit_k4(p, IO); p.pop()
    return p.finish()


def k4_inputs(inputs, ctx_full, x1_full):
    c = np.ascontiguousarray
    col = lambda v: c(v.reshape(-1, 128).T)
    base = dict(
        wo=c(inputs["L0_ffn_wo"]), bo_c=col(inputs["L0_ffn_bo"]),
        l2s_c=col(inputs["L0_ffn_ln2_scale"]), l2h_c=col(inputs["L0_ffn_ln2_shift"]),
        gate_w=c(inputs["L0_ffn_gate_w"]),
        w1=c(inputs["L0_ffn_w1"]), w2=c(inputs["L0_ffn_w2"]), w3=c(inputs["L0_ffn_w3"]),
        ident=np.eye(128, dtype=np.float32))
    maps = []
    for core in range(8):
        b, h = core // 2, core % 2
        m = dict(base)
        m["ctx"] = c(ctx_full[b, h * HT:(h + 1) * HT, :].T)
        m["x1"] = c(x1_full[b, h * HT:(h + 1) * HT, :].T)
        maps.append(m)
    return maps


def emit_k5(p, IO, fused=False):
    pe, act, dve, qs, qg = p.pe, p.act, p.dve, p.qs, p.qg
    xm_d = IO["xmy"]; wq_d = IO["wq"]; wk_d = IO["wk"]; wv_d = IO["wv"]
    bq_d = IO["bq_c"]; bk_d = IO["bk_c"]; bv_d = IO["bv_r"]
    wp_d = IO["wp"]; bp_d = IO["bp_c"]; x3_d = IO["x3"]
    xf_d = IO.get("xfull")
    kq_dram = IO["k_sc"]
    q_dram = IO["q_sc"]

    V_s = p.sb("V_s", [128, 32 * 1024], BF16)
    PT = p.sb("PT", [128, 32 * 512], BF16)
    wst = p.sb("wst", [128, NDT * 1024], BF16)
    xc = p.sb("xc", [128, NDT * 512], BF16)
    kc = p.sb("kc", [128, NDT * 512], BF16)
    kc2 = p.sb("kc2", [128, NDT * 512], BF16)
    sqs = p.sb("sqs", [128, 512], BF16)
    ybuf = p.sb("ybuf", [128, NDT * 512], BF16)
    xab = p.sb("xab", [128, NDT * 512], F32)
    rzb = p.sb("rzb", [128, 512], F32)
    bv_s = p.sb("bv_s", [1, D], BF16)
    bq_s = p.sb("bq_s", [128, NDT], F32)
    bq32 = p.sb("bq32", [128, NDT], F32)
    bk_s = p.sb("bk_s", [128, NDT], F32)
    bp_s = p.sb("bp_s", [128, NDT], F32)
    ones1 = p.sb("ones1", [1, 128], BF16)
    ones1f = p.sb("ones1f", [1, 128], F32)
    onesc = p.sb("onesc", [128, 1], BF16)
    nrm = p.sb("nrm", [1, 512], F32)
    kmax = p.sb("kmax", [1, 4], F32)
    qmax = p.sb("qmax", [1, 4], F32)
    mcol = p.sb("mcol", [128, 1], F32)
    km2 = p.sb("km2", [1, 8], F32)
    rows_s = p.sb("rows_s", [1, 512], F32)
    rows_q = p.sb("rows_q", [1, 512], F32)
    tmp_row = p.sb("tmp_row", [1, 512], F32)
    sq_sc = p.sb("sq_sc", [128, NDT * 512], BF16)
    tsc = p.sb("tsc", [128, 512], F32)
    x3cb = p.sb("x3cb", [128, NDT * 512], F32)

    banks = Pool8(p, n=7)
    zps = p.ps("zps", [1, 512])

    dve.h.memset(ones1[:, :], 1.0)
    dve.h.memset(ones1f[:, :], 1.0)
    dve.h.memset(onesc[:, :], 1.0)
    dve.h.memset(kmax[:, :], 0.0)
    dve.h.memset(qmax[:, :], 0.0)
    for s, dd in [(bq_s, bq_d), (bk_s, bk_d), (bp_s, bp_d)]:
        qs.inc(qs.h.dma_start(out=s[:, :], in_=dd))
    qg.inc(qg.h.dma_start(out=bv_s[:, :], in_=bv_d))
    p.wait(act, qs, qs.n)
    act.h.activation(bq32[:, :], bq_s[:, :], AF.Identity, scale=1.0 / 32.0)

    kv_src = xm_d if fused else xf_d
    n_kv = 4 if fused else 8
    p.wait(pe, dve, dve.n)
    p.wait(pe, qs, qs.n)
    qg.inc(qg.h.dma_start(out=sb3(wst, NDT, 1024), in_=fm3(wk_d)))
    for tc in range(n_kv):
        t0 = tc * 512
        p.wait(qg, pe, pe.n)
        qg.inc(qg.h.dma_start(out=sb3(xc, NDT, 512), in_=fm3(kv_src[:, t0:t0 + 512])))
        p.wait(pe, qg, qg.n)
        p.wait(pe, act, act.n)
        p.wait(act, qs, qs.n)
        for mt in range(NDT):
            bi, bank = banks.get()
            for dt in range(NDT):
                last = pe.h.matmul(bank[:, :],
                                   wst[:, dt * 1024 + mt * 128: dt * 1024 + (mt + 1) * 128],
                                   xc[:, dt * 512:(dt + 1) * 512],
                                   start=(dt == 0), stop=(dt == NDT - 1))
            pe.inc(last)
            p.wait(act, pe, pe.n)
            act.h.activation(kc[:, mt * 512:(mt + 1) * 512], bank[:, :], AF.Identity,
                             bias=bk_s[:, mt:mt + 1])
            banks.consumed(bi, act)
        p.wait(act, act, act.n)
        for mt in range(NDT):
            p.wait(act, pe, pe.n)
            act.h.activation(sqs[:, :], kc[:, mt * 512:(mt + 1) * 512], AF.Square)
            p.wait(pe, act, act.n)
            bi, bank = banks.get()
            last = pe.h.matmul(bank[0:1, 0:512], onesc[:, 0:1], sqs[:, :],
                               start=(mt == 0), stop=(mt == NDT - 1),
                               skip_group_check=True)
            pe.inc(last)
            if mt == NDT - 1:
                p.wait(dve, pe, pe.n)
                dve.h.reduce_max(kmax[0:1, 1:2], bank[0:1, 0:512], axis=AX)
                dve.h.tensor_tensor(kmax[0:1, 0:1], kmax[0:1, 0:1], kmax[0:1, 1:2], ALU.max)
                banks.consumed(bi, dve)
            else:
                banks.i -= 1
        p.wait(qs, act, act.n)
        qs.inc(qs.h.dma_start(out=fm3(kq_dram[:, t0:t0 + 512]), in_=sb3(kc, NDT, 512)))
    p.wait(qg, pe, pe.n)
    qg.inc(qg.h.dma_start(out=sb3(wst, NDT, 1024), in_=fm3(wv_d)))
    for tc in range(n_kv):
        t0 = tc * 512
        p.wait(qg, pe, pe.n)
        qg.inc(qg.h.dma_start(out=sb3(xc, NDT, 512), in_=fm3(kv_src[:, t0:t0 + 512])))
        p.wait(pe, qg, qg.n)
        for ttl in range(4):
            tt = tc * 4 + ttl
            for dh in range(2):
                bi, bank = banks.get()
                for dt in range(NDT):
                    pe.h.matmul(bank[:, :],
                                xc[:, dt * 512 + ttl * 128: dt * 512 + (ttl + 1) * 128],
                                wst[:, dt * 1024 + dh * 512: dt * 1024 + (dh + 1) * 512],
                                start=(dt == 0), stop=False)
                last = pe.h.matmul(bank[:, :], ones1[:, :],
                                   bv_s[0:1, dh * 512:(dh + 1) * 512],
                                   start=False, stop=True)
                pe.inc(last)
                p.wait(dve, pe, pe.n)
                dve.h.tensor_copy(V_s[:, tt * 1024 + dh * 512: tt * 1024 + (dh + 1) * 512],
                                  bank[:, :])
                banks.consumed(bi, dve)
        if fused:
            p.wait(qs, dve, dve.n)
            qs.inc(qs.h.dma_start(
                out=IO["v_bounce"][t0:t0 + 512, :],
                in_=V_s[:, tc * 4 * 1024:(tc + 1) * 4 * 1024]
                    .rearrange("p (tt v) -> p tt v", v=1024)))

    if fused:
        p.wait(qs, dve, dve.n)
        qs.inc(qs.h.dma_start(out=IO["kmax_b"], in_=kmax[0:1, :]))
        IO["mid_cb"](p)

    p.wait(qg, pe, pe.n)
    qg.inc(qg.h.dma_start(out=sb3(wst, NDT, 1024), in_=fm3(wq_d)))
    for tc in range(4):
        t0 = tc * 512
        p.wait(qg, pe, pe.n)
        qg.inc(qg.h.dma_start(out=sb3(xc, NDT, 512), in_=fm3(xm_d[:, t0:t0 + 512])))
        p.wait(pe, qg, qg.n)
        p.wait(pe, act, act.n)
        p.wait(act, qs, qs.n)
        for mt in range(NDT):
            bi, bank = banks.get()
            for dt in range(NDT):
                last = pe.h.matmul(bank[:, :],
                                   wst[:, dt * 1024 + mt * 128: dt * 1024 + (mt + 1) * 128],
                                   xc[:, dt * 512:(dt + 1) * 512],
                                   start=(dt == 0), stop=(dt == NDT - 1))
            pe.inc(last)
            p.wait(act, pe, pe.n)
            act.h.activation(kc[:, mt * 512:(mt + 1) * 512], bank[:, :], AF.Identity,
                             scale=1.0 / 32.0, bias=bq32[:, mt:mt + 1])
            banks.consumed(bi, act)
        p.wait(act, act, act.n)
        for mt in range(NDT):
            p.wait(act, pe, pe.n)
            act.h.activation(sqs[:, :], kc[:, mt * 512:(mt + 1) * 512], AF.Square)
            p.wait(pe, act, act.n)
            bi, bank = banks.get()
            last = pe.h.matmul(bank[0:1, 0:512], onesc[:, 0:1], sqs[:, :],
                               start=(mt == 0), stop=(mt == NDT - 1),
                               skip_group_check=True)
            pe.inc(last)
            if mt == NDT - 1:
                p.wait(dve, pe, pe.n)
                dve.h.reduce_max(qmax[0:1, 1:2], bank[0:1, 0:512], axis=AX)
                dve.h.tensor_tensor(qmax[0:1, 0:1], qmax[0:1, 0:1], qmax[0:1, 1:2], ALU.max)
                banks.consumed(bi, dve)
            else:
                banks.i -= 1
        p.wait(qs, act, act.n)
        qs.inc(qs.h.dma_start(out=fm3(q_dram[:, t0:t0 + 512]), in_=sb3(kc, NDT, 512)))

    if fused:
        p.wait(qs, p.cc)
        qs.inc(qs.h.dma_start(out=km2[0:1, :], in_=IO["ag_kmax"].rearrange("a b -> (a b)")))
        p.wait(dve, qs, qs.n)
        dve.h.tensor_tensor(kmax[0:1, 0:1], km2[0:1, 0:1], km2[0:1, 4:5], ALU.max)
    p.wait(act, dve, dve.n)
    act.h.activation(qmax[0:1, 2:3], qmax[0:1, 0:1], AF.Sqrt)
    act.h.activation(kmax[0:1, 2:3], kmax[0:1, 0:1], AF.Sqrt)
    p.wait(dve, act, act.n)
    dve.h.tensor_tensor(qmax[0:1, 3:4], qmax[0:1, 2:3], kmax[0:1, 2:3], ALU.mult)
    dve.h.tensor_scalar_mul(qmax[0:1, 3:4], qmax[0:1, 3:4], -1.0)
    p.wait(pe, dve, dve.n)
    bi, bank = banks.get()
    last = pe.h.matmul(bank[0:128, 0:1], ones1f[:, :], qmax[0:1, 3:4], start=True, stop=True)
    pe.inc(last)
    p.wait(dve, pe, pe.n)
    dve.h.tensor_copy(mcol[:, :], bank[0:128, 0:1])
    banks.consumed(bi, dve)

    if fused:
        p.wait(qs, p.cc)
        p.wait(qs, pe, pe.n)
        p.wait(qs, dve, dve.n)
        qs.inc(qs.h.dma_start(
            out=V_s[:, :].rearrange("p (tt v) -> p tt v", v=1024),
            in_=IO["ag_v"].rearrange("(tt p) v -> p tt v", p=128)))
    p.wait(qg, pe, pe.n)
    qg.inc(qg.h.dma_start(out=sb3(wst, NDT, 1024), in_=fm3(wp_d)))
    for qb in range(4):
        q0 = qb * 512
        p.wait(qs, pe, pe.n)
        qs.inc(qs.h.dma_start(out=sb3(xc, NDT, 512), in_=fm3(q_dram[:, q0:q0 + 512])))
        kbufs = [kc, kc2]
        k_guard = [None, None]
        k_load = [0, 0]

        def k5_load_k(kc4l):
            par2 = kc4l % 2
            if k_guard[par2] is not None:
                p.wait(qs, k_guard[par2][0], k_guard[par2][1])
            k0l = kc4l * 512
            if fused:
                blk = k0l // HT
                kk0 = k0l % HT
                qs.inc(qs.h.dma_start(out=sb3(kbufs[par2], NDT, 512),
                                      in_=fm3(IO["ag_k"][blk * D:(blk + 1) * D, kk0:kk0 + 512])))
            else:
                qs.inc(qs.h.dma_start(out=sb3(kbufs[par2], NDT, 512),
                                      in_=fm3(kq_dram[:, k0l:k0l + 512])))
            k_load[par2] = qs.n

        k5_load_k(0)
        for kc4 in range(8):
            par = kc4 % 2
            if kc4 + 1 < 8:
                k5_load_k(kc4 + 1)
            p.wait(pe, qs, k_load[par])
            p.wait(pe, dve, dve.n)
            for ktl in range(4):
                kt = kc4 * 4 + ktl
                bi, bank = banks.get()
                for dt in range(NDT):
                    last = pe.h.matmul(bank[:, :],
                                kbufs[par][:, dt * 512 + ktl * 128: dt * 512 + (ktl + 1) * 128],
                                xc[:, dt * 512:(dt + 1) * 512],
                                start=(dt == 0), stop=(dt == NDT - 1))
                pe.inc(last)
                p.wait(act, pe, pe.n)
                act.h.activation(PT[:, kt * 512:(kt + 1) * 512], bank[:, :], AF.Exp,
                                 bias=mcol[:, 0:1])
                banks.consumed(bi, act)
            k_guard[par] = (pe, pe.n)
        p.wait(pe, act, act.n)
        p.wait(pe, dve, dve.n)
        for kt in range(32):
            last = pe.h.matmul(zps[0:1, :], onesc[:, 0:1], PT[:, kt * 512:(kt + 1) * 512],
                               start=(kt == 0), stop=(kt == 31))
        pe.inc(last)
        p.wait(dve, pe, pe.n)
        dve.h.reciprocal(nrm[0:1, :], zps[0:1, :])
        p.wait(pe, dve, dve.n)
        bi, bank = banks.get()
        last = pe.h.matmul(bank[:, :], ones1f[:, :], nrm[0:1, :], start=True, stop=True)
        pe.inc(last)
        p.wait(dve, pe, pe.n)
        dve.h.tensor_copy(rzb[:, :], bank[:, :])
        banks.consumed(bi, dve)
        for dvt in range(NDT):
            bi, bank = banks.get()
            for kt in range(32):
                last = pe.h.matmul(bank[:, :],
                                   V_s[:, kt * 1024 + dvt * 128: kt * 1024 + (dvt + 1) * 128],
                                   PT[:, kt * 512:(kt + 1) * 512],
                                   start=(kt == 0), stop=(kt == 31))
            pe.inc(last)
            p.wait(dve, pe, pe.n)
            dve.h.tensor_tensor(ybuf[:, dvt * 512:(dvt + 1) * 512], bank[:, :],
                                rzb[:, :], ALU.mult)
            banks.consumed(bi, dve)
        p.wait(qs, dve, dve.n)
        qs.inc(qs.h.dma_start(out=sb3(x3cb, NDT, 512), in_=fm3(xm_d[:, q0:q0 + 512])))
        p.wait(pe, dve, dve.n)
        p.wait(pe, qg, qg.n)
        for mt in range(NDT):
            bi, bank = banks.get()
            for dt in range(NDT):
                last = pe.h.matmul(bank[:, :],
                                   wst[:, dt * 1024 + mt * 128: dt * 1024 + (mt + 1) * 128],
                                   ybuf[:, dt * 512:(dt + 1) * 512],
                                   start=(dt == 0), stop=(dt == NDT - 1))
            pe.inc(last)
            p.wait(dve, pe, pe.n)
            p.wait(dve, qs, qs.n)
            dve.h.scalar_tensor_tensor(xab[:, mt * 512:(mt + 1) * 512], bank[:, :],
                                       bp_s[:, mt:mt + 1],
                                       x3cb[:, mt * 512:(mt + 1) * 512],
                                       ALU.add, ALU.add)
            banks.consumed(bi, dve)
        p.wait(act, dve, dve.n)
        row_stats(p, banks, xab, sq_sc, onesc, rows_s, rows_q, 512, HTl=512)
        p.wait(dve, pe, pe.n)
        ln_finalize_rows(p, rows_s, rows_q, tmp_row, 512, float(D), 1e-5)
        p.wait(pe, dve, dve.n)
        bi1, bankR = banks.get()
        pe.h.matmul(bankR[:, :], ones1f[:, :], rows_q[0:1, 0:512], start=True, stop=True)
        bi2, bankM = banks.get()
        last = pe.h.matmul(bankM[:, :], ones1f[:, :], rows_s[0:1, 0:512], start=True, stop=True)
        pe.inc(last)
        p.wait(dve, pe, pe.n)
        for dt in range(NDT):
            sl = xab[:, dt * 512:(dt + 1) * 512]
            dve.h.tensor_tensor(tsc[:, :], sl, bankR[:, :], ALU.mult)
            dve.h.tensor_tensor(sl, tsc[:, :], bankM[:, :], ALU.add)
        banks.consumed(bi1, dve)
        banks.consumed(bi2, dve)
        p.wait(qs, dve, dve.n)
        qs.inc(qs.h.dma_start(out=fm3(x3_d[:, q0:q0 + 512]), in_=sb3(xab, NDT, 512)))


def build_k5():
    p = Prog()
    IO = {n: p.din(n, s) for n, s in [("xfull", [D, TFULL]), ("xmy", [D, HT]),
          ("wq", [D, D]), ("wk", [D, D]), ("wv", [D, D]), ("bq_c", [128, NDT]),
          ("bk_c", [128, NDT]), ("bv_r", [1, D]), ("wp", [D, D]), ("bp_c", [128, NDT])]}
    IO["x3"] = p.dout("x3", [D, HT])
    IO["k_sc"] = p.dint("k_sc", [D, TFULL], BF16).ap()
    IO["q_sc"] = p.dint("q_sc", [D, HT], BF16).ap()
    p.push(); emit_k5(p, IO); p.pop()
    return p.finish()


def k5_inputs(inputs, x2_full):
    c = np.ascontiguousarray
    col = lambda v: c(v.reshape(-1, 128).T)
    wqkv = inputs["L1_attn_w_qkv"]
    bqkv = inputs["L1_attn_b_qkv"]
    base = dict(
        wq=c(wqkv[:, 0:1024]), wk=c(wqkv[:, 1024:2048]), wv=c(wqkv[:, 2048:3072]),
        bq_c=col(bqkv[0:1024]), bk_c=col(bqkv[1024:2048]), bv_r=c(bqkv[2048:3072].reshape(1, -1)),
        wp=c(inputs["L1_attn_w_proj"]), bp_c=col(inputs["L1_attn_b_proj"]))
    maps = []
    for core in range(8):
        b, h = core // 2, core % 2
        m = dict(base)
        m["xfull"] = c(x2_full[b].T)
        m["xmy"] = c(x2_full[b, h * HT:(h + 1) * HT, :].T)
        maps.append(m)
    return maps


def emit_k3(p, IO, h1_loader=None, ctx_bf16=False):
    pe, act, dve, qs, qg = p.pe, p.act, p.dve, p.qs, p.qg
    h1_d = IO.get("h1f"); wq_d = IO["wq_my"]; wk_d = IO["wk_my"]; wv_d = IO["wv_my"]
    triu_d = IO["triu"]; ctx_d = IO["ctxo"]

    qf = p.sb("qf", [128, 4 * TFULL], BF16)
    kf = p.sb("kf", [128, 4 * TFULL], BF16)
    V_s = p.sb("V_s", [128, 32 * 520], BF16)
    PT = p.sb("PT", [128, 32 * 512], BF16)
    wq16 = p.sb("wq16", [128, NDT * 512], BF16)
    wk16 = p.sb("wk16", [128, NDT * 512], BF16)
    wv16 = p.sb("wv16", [128, NDT * 512], BF16)
    hc = p.sb("hc", [128, NDT * 512], BF16)
    sqs = p.sb("sqs", [128, 512], BF16)
    triu = p.sb("triu_s", [128, 128], BF16)
    qmax8 = p.sb("qmax8", [1, 32], F32)
    kmax8 = p.sb("kmax8", [1, 32], F32)
    nm8 = p.sb("nm8", [1, 8], F32)
    mcols = p.sb("mcols", [128, 8], F32)
    nrm = p.sb("nrm", [1, 512], F32)
    rzb = p.sb("rzb", [128, 512], F32)
    cstage = p.sb("cstage", [128, 512], BF16 if ctx_bf16 else F32)
    ones1f = p.sb("ones1f", [1, 128], F32)
    onesc = p.sb("onesc", [128, 1], BF16)

    banks = Pool8(p, n=8)

    dve.h.memset(ones1f[:, :], 1.0)
    dve.h.memset(onesc[:, :], 1.0)
    dve.h.memset(qmax8[:, :], 0.0)
    dve.h.memset(kmax8[:, :], 0.0)
    dve.h.memset(V_s[:, :], 1.0)
    qs.inc(qs.h.dma_start(out=triu[:, :], in_=triu_d))
    qg.inc(qg.h.dma_start(out=sb3(wq16, NDT, 512), in_=fm3(wq_d)))
    qg.inc(qg.h.dma_start(out=sb3(wk16, NDT, 512), in_=fm3(wk_d)))
    qg.inc(qg.h.dma_start(out=sb3(wv16, NDT, 512), in_=fm3(wv_d)))

    p.wait(pe, dve, dve.n)
    p.wait(pe, qg, qg.n)
    for tc in range(8):
        t0 = tc * 512
        p.wait(qs, pe, pe.n)
        if h1_loader is None:
            qs.inc(qs.h.dma_start(out=sb3(hc, NDT, 512), in_=fm3(h1_d[:, t0:t0 + 512])))
        else:
            h1_loader(p, hc, t0)
        p.wait(pe, qs, qs.n)
        for which, wbuf, dst, scale, mx in (
                (0, wq16, qf, 0.125, qmax8), (1, wk16, kf, 1.0, kmax8)):
            for hp in range(4):
                bi, bank = banks.get()
                for dt in range(NDT):
                    last = pe.h.matmul(bank[:, :],
                                       wbuf[:, dt * 512 + hp * 128: dt * 512 + (hp + 1) * 128],
                                       hc[:, dt * 512:(dt + 1) * 512],
                                       start=(dt == 0), stop=(dt == NDT - 1))
                pe.inc(last)
                p.wait(act, pe, pe.n)
                if scale == 1.0:
                    act.h.activation(dst[:, hp * TFULL + t0: hp * TFULL + t0 + 512],
                                     bank[:, :], AF.Copy)
                else:
                    act.h.activation(dst[:, hp * TFULL + t0: hp * TFULL + t0 + 512],
                                     bank[:, :], AF.Identity, scale=scale)
                banks.consumed(bi, act)
                p.wait(act, pe, pe.n)
                act.h.activation(sqs[:, :], dst[:, hp * TFULL + t0: hp * TFULL + t0 + 512],
                                 AF.Square)
                p.wait(pe, act, act.n)
                for lane in range(2):
                    h8 = hp * 2 + lane
                    bi2, bank2 = banks.get()
                    last = pe.h.matmul(bank2[0:1, 0:512], onesc[lane * 64:lane * 64 + 64, 0:1],
                                       sqs[lane * 64:lane * 64 + 64, :],
                                       start=True, stop=True,
                                       tile_position=(lane * 64, 0))
                    pe.inc(last)
                    p.wait(dve, pe, pe.n)
                    dve.h.reduce_max(mx[0:1, 8 + h8:9 + h8], bank2[0:1, 0:512], axis=AX)
                    dve.h.tensor_tensor(mx[0:1, h8:h8 + 1], mx[0:1, h8:h8 + 1],
                                        mx[0:1, 8 + h8:9 + h8], ALU.max)
                    banks.consumed(bi2, dve)
        p.wait(pe, dve, dve.n)
        for ttl in range(4):
            tt = tc * 4 + ttl
            bi, bank = banks.get()
            for dt in range(NDT):
                last = pe.h.matmul(bank[:, :],
                                   hc[:, dt * 512 + ttl * 128: dt * 512 + (ttl + 1) * 128],
                                   wv16[:, dt * 512:(dt + 1) * 512],
                                   start=(dt == 0), stop=(dt == NDT - 1))
            pe.inc(last)
            p.wait(dve, pe, pe.n)
            dve.h.tensor_copy(
                V_s[:, tt * 520: tt * 520 + 520].rearrange("p (h v) -> p h v", v=65)[:, :, 0:64],
                bank[:, :].rearrange("p (h v) -> p h v", v=64))
            banks.consumed(bi, dve)

    p.wait(act, dve, dve.n)
    act.h.activation(qmax8[0:1, 16:24], qmax8[0:1, 0:8], AF.Sqrt)
    act.h.activation(kmax8[0:1, 16:24], kmax8[0:1, 0:8], AF.Sqrt)
    p.wait(dve, act, act.n)
    dve.h.tensor_tensor(nm8[0:1, :], qmax8[0:1, 16:24], kmax8[0:1, 16:24], ALU.mult)
    dve.h.tensor_scalar_mul(nm8[0:1, :], nm8[0:1, :], -0.125)
    p.wait(pe, dve, dve.n)
    bi, bank = banks.get()
    last = pe.h.matmul(bank[0:128, 0:8], ones1f[:, :], nm8[0:1, :], start=True, stop=True)
    pe.inc(last)
    p.wait(dve, pe, pe.n)
    dve.h.tensor_copy(mcols[:, :], bank[0:128, 0:8])
    banks.consumed(bi, dve)

    for h8 in range(8):
        hp, lane = h8 // 2, h8 % 2
        b0 = lane * 64
        for qb in range(8):
            q0 = qb * 512
            nkt = 4 * qb + 4
            p.wait(pe, dve, dve.n)
            for kt in range(nkt):
                bi, bank = banks.get()
                last = pe.h.matmul(bank[:, :],
                                   kf[b0:b0 + 64, hp * TFULL + kt * 128: hp * TFULL + (kt + 1) * 128],
                                   qf[b0:b0 + 64, hp * TFULL + q0: hp * TFULL + q0 + 512],
                                   start=True, stop=True, tile_position=(b0, 0))
                pe.inc(last)
                p.wait(act, pe, pe.n)
                act.h.activation(PT[:, kt * 512:(kt + 1) * 512], bank[:, :], AF.Exp,
                                 bias=mcols[:, h8:h8 + 1])
                banks.consumed(bi, act)
                jj = kt - 4 * qb
                if jj >= 0:
                    p.wait(dve, act, act.n)
                    if jj > 0:
                        dve.h.memset(PT[:, kt * 512: kt * 512 + jj * 128], 0.0)
                    dve.h.tensor_tensor(PT[:, kt * 512 + jj * 128: kt * 512 + (jj + 1) * 128],
                                        PT[:, kt * 512 + jj * 128: kt * 512 + (jj + 1) * 128],
                                        triu[:, :], ALU.mult)
            p.wait(pe, act, act.n)
            p.wait(pe, dve, dve.n)
            bi, bank = banks.get()
            for kt in range(nkt):
                last = pe.h.matmul(bank[0:65, :],
                                   V_s[:, kt * 520 + h8 * 65: kt * 520 + (h8 + 1) * 65],
                                   PT[:, kt * 512:(kt + 1) * 512],
                                   start=(kt == 0), stop=(kt == nkt - 1))
            pe.inc(last)
            p.wait(dve, pe, pe.n)
            dve.h.reciprocal(nrm[0:1, :], bank[64:65, :])
            p.wait(pe, dve, dve.n)
            bi2, bank2 = banks.get()
            last = pe.h.matmul(bank2[0:64, :], ones1f[0:1, 0:64], nrm[0:1, :],
                               start=True, stop=True)
            pe.inc(last)
            p.wait(dve, pe, pe.n)
            dve.h.tensor_copy(rzb[0:64, :], bank2[0:64, :])
            p.wait(dve, qs, qs.n)
            dve.h.tensor_tensor(cstage[0:64, :], bank[0:64, :], rzb[0:64, :], ALU.mult)
            banks.consumed(bi, dve)
            banks.consumed(bi2, dve)
            p.wait(qs, dve, dve.n)
            qs.inc(qs.h.dma_start(out=ctx_d[h8 * 64:(h8 + 1) * 64, q0:q0 + 512],
                                  in_=cstage[0:64, :]))


def build_k3():
    p = Prog()
    IO = {}
    IO["h1f"] = p.nc.dram_tensor("h1f", [D, TFULL], BF16, kind="ExternalInput").ap()
    IO["wq_my"] = p.din("wq_my", [D, 512])
    IO["wk_my"] = p.din("wk_my", [D, 512])
    IO["wv_my"] = p.din("wv_my", [D, 512])
    IO["triu"] = p.nc.dram_tensor("triu", [128, 128], BF16, kind="ExternalInput").ap()
    IO["ctxo"] = p.dout("ctxo", [512, TFULL])
    p.push(); emit_k3(p, IO); p.pop()
    return p.finish()


def k3_inputs(inputs, h1_full_bf16, triu):
    c = np.ascontiguousarray
    maps = []
    for core in range(8):
        b, r = core // 2, core % 2
        hs = r * 512
        m = dict(
            h1f=c(h1_full_bf16[b].T),
            wq_my=c(inputs["L0_ffn_wq"][:, hs:hs + 512]),
            wk_my=c(inputs["L0_ffn_wk"][:, hs:hs + 512]),
            wv_my=c(inputs["L0_ffn_wv"][:, hs:hs + 512]),
            triu=triu)
        maps.append(m)
    return maps


def emit_k1(p, IO, attn_bf16=False):
    pe, act, dve, qs, qg = p.pe, p.act, p.dve, p.qs, p.qg
    x0_d = IO["x0f"]; l1w_d = IO["ln1w_c"]; l1b_d = IO["ln1b_c"]
    wq_d = IO["wq_my"]; wk_d = IO["wk_my"]; wv_d = IO["wv_my"]; dec_d = IO["dec_my"]
    qb_d = IO["qb_c"]; kb_d = IO["kb_c"]; vb_d = IO["vb_r"]; db_d = IO["db_r"]
    ui_d = IO["u_incl"]; ur_d = IO["u_rvex"]; id_d = IO["identb"]
    attn_d = IO["attn"]
    outf_dram = IO["outf"]

    qf = p.sb("qf", [128, 4 * TFULL], BF16)
    kf = p.sb("kf", [128, 4 * TFULL], BF16)
    KT = p.sb("KT", [128, 32 * 512], BF16)
    V_s = p.sb("V_s", [128, 32 * 520], BF16)
    wq16 = p.sb("wq16", [128, NDT * 512], BF16)
    wk16 = p.sb("wk16", [128, NDT * 512], BF16)
    wv16 = p.sb("wv16", [128, NDT * 512], BF16)
    dec16 = p.sb("dec16", [128, NDT * 8], BF16)
    xc32 = p.sb("xc32", [128, NDT * 512], F32)
    xnc = p.sb("xnc", [128, NDT * 512], BF16)
    sq_sc = p.sb("sq_sc", [128, NDT * 256], BF16)
    rows_s = p.sb("rows_s", [1, 512], F32)
    rows_q = p.sb("rows_q", [1, 512], F32)
    tmp_row = p.sb("tmp_row", [1, 512], F32)
    tsc = p.sb("tsc", [128, 512], F32)
    ll_tm = p.sb("ll_tm", [128, 32 * 8], F32)
    Sst = p.sb("Sst", [64, 8 * 65], F32)
    outb = p.sb("outb", [128, 520], F32)
    astage = p.sb("astage", [128, 512], BF16 if attn_bf16 else F32)
    l1w_s = p.sb("l1w_s", [128, NDT], F32)
    l1b_s = p.sb("l1b_s", [128, NDT], F32)
    qb_s = p.sb("qb_s", [128, 4], F32)
    nqb_s = p.sb("nqb_s", [128, 4], F32)
    kb_s = p.sb("kb_s", [128, 4], F32)
    nkb_s = p.sb("nkb_s", [128, 4], F32)
    vb_s = p.sb("vb_s", [1, 512], BF16)
    db_s = p.sb("db_s", [1, 8], BF16)
    ui_s = p.sb("ui_s", [128, 128], F32)
    ur_s = p.sb("ur_s", [128, 128], F32)
    idb_s = p.sb("idb_s", [128, 128], BF16)
    ones1f = p.sb("ones1f", [1, 128], F32)
    onescf = p.sb("onescf", [128, 1], F32)
    onesc = p.sb("onesc", [128, 1], BF16)
    ones1b = p.sb("ones1b", [1, 128], BF16)

    banks = Pool8(p, n=8)

    for s, dd in [(l1w_s, l1w_d), (l1b_s, l1b_d), (qb_s, qb_d), (kb_s, kb_d),
                  (ui_s, ui_d), (ur_s, ur_d)]:
        qs.inc(qs.h.dma_start(out=s[:, :], in_=dd))
    qs.inc(qs.h.dma_start(out=idb_s[:, :], in_=id_d))
    for s, dd in [(wq16, wq_d), (wk16, wk_d), (wv16, wv_d)]:
        qg.inc(qg.h.dma_start(out=sb3(s, NDT, 512), in_=fm3(dd)))
    qg.inc(qg.h.dma_start(out=sb3(dec16, NDT, 8), in_=fm3(dec_d)))
    qg.inc(qg.h.dma_start(out=vb_s[:, :], in_=vb_d))
    qg.inc(qg.h.dma_start(out=db_s[:, :], in_=db_d))
    dve.h.memset(ones1f[:, :], 1.0)
    dve.h.memset(ones1b[:, :], 1.0)
    dve.h.memset(onescf[:, :], 1.0)
    dve.h.memset(onesc[:, :], 1.0)
    dve.h.memset(Sst[:, :], 0.0)
    dve.h.memset(V_s[:, :], 1.0)
    p.wait(act, qs, qs.n)
    act.h.activation(nqb_s[:, :], qb_s[:, :], AF.Identity, scale=-1.0)
    act.h.activation(nkb_s[:, :], kb_s[:, :], AF.Identity, scale=-1.0)

    p.wait(pe, dve, dve.n)
    p.wait(pe, qg, qg.n)
    p.wait(pe, qs, qs.n)
    for tc in range(8):
        t0 = tc * 512
        p.wait(qs, dve, dve.n)
        qs.inc(qs.h.dma_start(out=sb3(xc32, NDT, 512), in_=fm3(x0_d[:, t0:t0 + 512])))
        p.wait(act, qs, qs.n)
        p.wait(act, dve, dve.n)
        p.wait(dve, qs, qs.n)
        row_stats(p, banks, xc32, sq_sc, onesc, rows_s, rows_q, 512, HTl=512, sqw=256, cw0=256)
        p.wait(dve, pe, pe.n)
        ln_finalize_rows(p, rows_s, rows_q, tmp_row, 512, float(D), 1e-5)
        p.wait(pe, dve, dve.n)
        bi1, bankR = banks.get()
        pe.h.matmul(bankR[:, :], ones1f[:, :], rows_q[0:1, 0:512], start=True, stop=True)
        bi2, bankM = banks.get()
        last = pe.h.matmul(bankM[:, :], ones1f[:, :], rows_s[0:1, 0:512], start=True, stop=True)
        pe.inc(last)
        p.wait(dve, pe, pe.n)
        for dt in range(NDT):
            sl = xc32[:, dt * 512:(dt + 1) * 512]
            dve.h.tensor_tensor(tsc[:, :], sl, bankR[:, :], ALU.mult)
            dve.h.tensor_tensor(tsc[:, :], tsc[:, :], bankM[:, :], ALU.add)
            dve.h.tensor_scalar(xnc[:, dt * 512:(dt + 1) * 512], tsc[:, :],
                                l1w_s[:, dt:dt + 1], l1b_s[:, dt:dt + 1],
                                ALU.mult, ALU.add)
        banks.consumed(bi1, dve)
        banks.consumed(bi2, dve)
        p.wait(pe, dve, dve.n)
        for wbuf, dst, bcol, nbcol in ((wq16, qf, qb_s, nqb_s), (wk16, kf, kb_s, nkb_s)):
            for hp in range(4):
                bi, bank = banks.get()
                for dt in range(NDT):
                    last = pe.h.matmul(bank[:, :],
                                       wbuf[:, dt * 512 + hp * 128: dt * 512 + (hp + 1) * 128],
                                       xnc[:, dt * 512:(dt + 1) * 512],
                                       start=(dt == 0), stop=(dt == NDT - 1))
                pe.inc(last)
                p.wait(act, pe, pe.n)
                p.wait(act, dve, dve.n)
                phsc = sq_sc[:, 0:512]
                phsc2 = sq_sc[:, 512:1024]
                act.h.activation(phsc[:, :], bank[:, :], AF.Relu, bias=bcol[:, hp:hp + 1])
                act.h.activation(phsc2[:, :], bank[:, :], AF.Relu, scale=-1.0,
                                 bias=nbcol[:, hp:hp + 1])
                act.h.activation(phsc2[:, :], phsc2[:, :], AF.Exp, scale=-1.0)
                banks.consumed(bi, act)
                p.wait(dve, act, act.n)
                dve.h.tensor_tensor(dst[:, hp * TFULL + t0: hp * TFULL + t0 + 512],
                                    phsc[:, :], phsc2[:, :], ALU.add)
        p.wait(pe, dve, dve.n)
        for hp in range(4):
            for ttl in range(4):
                tt = tc * 4 + ttl
                bi, bank = banks.get()
                tview = bank[:, 0:64].bitcast(BF16)
                last = pe.h.transpose(tview,
                                      kf[:, hp * TFULL + t0 + ttl * 128: hp * TFULL + t0 + (ttl + 1) * 128],
                                      idb_s[:, :])
                pe.inc(last)
                p.wait(dve, pe, pe.n)
                dve.h.tensor_copy(KT[:, tt * 512 + hp * 128: tt * 512 + (hp + 1) * 128],
                                  tview)
                banks.consumed(bi, dve)
        p.wait(pe, dve, dve.n)
        for ttl in range(4):
            tt = tc * 4 + ttl
            bi, bank = banks.get()
            for dt in range(NDT):
                pe.h.matmul(bank[:, :],
                            xnc[:, dt * 512 + ttl * 128: dt * 512 + (ttl + 1) * 128],
                            wv16[:, dt * 512:(dt + 1) * 512],
                            start=(dt == 0), stop=False)
            last = pe.h.matmul(bank[:, :], ones1b[:, :],
                               vb_s[:, :], start=False, stop=True)
            pe.inc(last)
            p.wait(dve, pe, pe.n)
            dve.h.tensor_copy(
                V_s[:, tt * 520: tt * 520 + 520].rearrange("p (h v) -> p h v", v=65)[:, :, 0:64],
                bank[:, :].rearrange("p (h v) -> p h v", v=64))
            banks.consumed(bi, dve)
            bi, bank = banks.get()
            for dt in range(NDT):
                pe.h.matmul(bank[:, 0:8],
                            xnc[:, dt * 512 + ttl * 128: dt * 512 + (ttl + 1) * 128],
                            dec16[:, dt * 8:(dt + 1) * 8],
                            start=(dt == 0), stop=False)
            last = pe.h.matmul(bank[:, 0:8], ones1b[:, :],
                               db_s[:, :], start=False, stop=True)
            pe.inc(last)
            p.wait(act, pe, pe.n)
            p.wait(act, dve, dve.n)
            act.h.activation(tsc[:, 0:8], bank[:, 0:8], AF.Sigmoid)
            banks.consumed(bi, act)
            p.wait(dve, act, act.n)
            dve.h.tensor_scalar_max(ll_tm[:, tt * 8:(tt + 1) * 8], tsc[:, 0:8], 1e-8)

    p.wait(act, dve, dve.n)
    act.h.activation(ll_tm[:, :], ll_tm[:, :], AF.Ln)

    cA = p.sb("cA", [128, 32 * 8], F32)
    eA = p.sb("eA", [128, 32 * 8], F32)
    nllA = p.sb("nllA", [128, 32 * 8], F32)
    lamrA = p.sb("lamrA", [1, 32 * 8], F32)
    lamcA = p.sb("lamcA", [64, 32 * 8], F32)
    p.wait(pe, act, act.n)
    p.wait(pe, dve, dve.n)
    for tt in range(32):
        lls = ll_tm[:, tt * 8:(tt + 1) * 8]
        biA, bA = banks.get()
        pe.h.matmul(bA[:, 0:8], ui_s[:, :], lls, start=True, stop=True)
        pe.h.matmul(bA[:, 8:16], ur_s[:, :], lls, start=True, stop=True)
        last = pe.h.matmul(bA[0:1, 16:24], onescf[:, 0:1], lls, start=True, stop=True)
        pe.inc(last)
        p.wait(act, pe, pe.n)
        act.h.activation(cA[:, tt * 8:(tt + 1) * 8], bA[:, 0:8], AF.Exp)
        act.h.activation(eA[:, tt * 8:(tt + 1) * 8], bA[:, 8:16], AF.Exp)
        act.h.activation(lamrA[0:1, tt * 8:(tt + 1) * 8], bA[0:1, 16:24], AF.Exp)
        banks.consumed(biA, act)
    p.wait(dve, act, act.n)
    dve.h.tensor_scalar_mul(nllA[:, :], ll_tm[:, :], -1.0)
    p.wait(pe, act, act.n)
    biL, bL = banks.get()
    last = pe.h.matmul(bL[0:64, 0:256], ones1f[0:1, 0:64], lamrA[0:1, :],
                       start=True, stop=True)
    pe.inc(last)
    p.wait(dve, pe, pe.n)
    dve.h.tensor_copy(lamcA[:, :], bL[0:64, 0:256])
    banks.consumed(biL, dve)

    EPS = 1e-6
    S16a = p.sb("S16a", [128, 8 * 65], BF16)
    Ve8 = p.sb("Ve8", [128, 8 * 65], BF16)
    Dm8 = p.sb("Dm8", [128, 4 * 128], BF16)
    StD8 = p.sb("StD8", [128, 4 * 128], BF16)
    for sweep in range(K1_SWEEPS):
        fwd = (sweep == 0)
        order = list(range(32))[:K1_NTT] if fwd else list(range(31, -1, -1))[:K1_NTT]
        p.wait(dve, act, act.n)
        dve.h.memset(Sst[:, :], 0.0)
        for tt in order:
          if True:
            c_tm = cA[:, tt * 8:(tt + 1) * 8]
            e_tm = eA[:, tt * 8:(tt + 1) * 8]
            nll = nllA[:, tt * 8:(tt + 1) * 8]
            lamc = lamcA[:, tt * 8:(tt + 1) * 8]
          if True:
            if fwd:
                p.wait(dve, qs, qs.n)
                dve.h.memset(outb[:, :], 0.0)
            else:
                qs.inc(qs.h.dma_start(out=outb[:, :],
                                      in_=outf_dram.ap()[tt * 128:(tt + 1) * 128, :]))
                p.wait(dve, qs, qs.n)
            if "inter" in K1_STAGES:
                sv = Sst[:, :].rearrange("p (h v) -> p h v", v=65)
                d0 = S16a[0:64, :].rearrange("p (h v) -> p h v", v=65)
                d1 = S16a[64:128, :].rearrange("p (h v) -> p h v", v=65)
                dve.h.tensor_copy(d0[:, 0::2, :], sv[:, 0::2, :])
                dve.h.tensor_copy(d1[:, 1::2, :], sv[:, 1::2, :])
            p.wait(pe, dve, dve.n)
            ibks = []
            for h8 in (range(8) if "inter" in K1_STAGES else []):
                hp, lane = h8 // 2, h8 % 2
                b0 = lane * 64
                bi5, bk = banks.get()
                last = pe.h.matmul(bk[:, 0:65],
                                   qf[b0:b0 + 64, hp * TFULL + tt * 128: hp * TFULL + (tt + 1) * 128],
                                   S16a[b0:b0 + 64, h8 * 65:(h8 + 1) * 65],
                                   start=True, stop=True, tile_position=(b0, 0))
                ibks.append((bi5, bk))
            if "inter" in K1_STAGES:
                pe.inc(last)
            p.wait(dve, pe, pe.n)
            for h8 in (range(8) if "inter" in K1_STAGES else []):
                bi5, bk = ibks[h8]
                scol = c_tm[:, h8:h8 + 1] if fwd else e_tm[:, h8:h8 + 1]
                dve.h.scalar_tensor_tensor(outb[:, h8 * 65:(h8 + 1) * 65],
                                           bk[:, 0:65], scol,
                                           outb[:, h8 * 65:(h8 + 1) * 65],
                                           ALU.mult, ALU.add)
                banks.consumed(bi5, dve)
            if fwd and "intra" in K1_STAGES:
                p.wait(pe, dve, dve.n)
                for grp in range(2):
                    hs = range(grp * 4, grp * 4 + 4)
                    sbks = []
                    for h8 in hs:
                        hp, lane = h8 // 2, h8 % 2
                        b0 = lane * 64
                        biS, bk = banks.get()
                        sl = qf[b0:b0 + 64, hp * TFULL + tt * 128: hp * TFULL + (tt + 1) * 128]
                        kl = kf[b0:b0 + 64, hp * TFULL + tt * 128: hp * TFULL + (tt + 1) * 128]
                        pe.h.matmul(bk[:, 0:128], kl, sl,
                                    start=True, stop=True, tile_position=(b0, 0))
                        sbks.append((biS, bk))
                    dbks = []
                    for h8 in hs:
                        biDl, bk = banks.get()
                        pe.h.matmul(bk[:, 0:128], ui_s[:, :],
                                    ll_tm[:, tt * 8 + h8: tt * 8 + h8 + 1].broadcast_to([128, 128]),
                                    start=True, stop=False)
                        last = pe.h.matmul(bk[:, 0:128],
                                           nll[:, h8:h8 + 1].broadcast_to([128, 128]),
                                           ui_s[:, :], start=False, stop=True)
                        dbks.append((biDl, bk))
                    pe.inc(last)
                    p.wait(act, pe, pe.n)
                    for j, h8 in enumerate(hs):
                        biDl, bk = dbks[j]
                        act.h.activation(StD8[:, j * 128:(j + 1) * 128], bk[:, 0:128], AF.Abs)
                        act.h.activation(Dm8[:, j * 128:(j + 1) * 128],
                                         StD8[:, j * 128:(j + 1) * 128], AF.Exp, scale=-1.0)
                        banks.consumed(biDl, act)
                    p.wait(dve, act, act.n)
                    for j, h8 in enumerate(hs):
                        biS, bk = sbks[j]
                        dve.h.tensor_tensor(StD8[:, j * 128:(j + 1) * 128],
                                            bk[:, 0:128],
                                            Dm8[:, j * 128:(j + 1) * 128], ALU.mult)
                        banks.consumed(biS, dve)
                    p.wait(pe, dve, dve.n)
                    obks = []
                    for j, h8 in enumerate(hs):
                        bi8, bk = banks.get()
                        last = pe.h.matmul(bk[:, 0:65], StD8[:, j * 128:(j + 1) * 128],
                                           V_s[:, tt * 520 + h8 * 65: tt * 520 + (h8 + 1) * 65],
                                           start=True, stop=True)
                        obks.append((bi8, bk))
                    pe.inc(last)
                    p.wait(dve, pe, pe.n)
                    for j, h8 in enumerate(hs):
                        bi8, bk = obks[j]
                        dve.h.tensor_tensor(outb[:, h8 * 65:(h8 + 1) * 65],
                                            outb[:, h8 * 65:(h8 + 1) * 65], bk[:, 0:65], ALU.add)
                        banks.consumed(bi8, dve)
            for h8 in (range(8) if "update" in K1_STAGES else []):
                ucol = e_tm[:, h8:h8 + 1] if fwd else c_tm[:, h8:h8 + 1]
                dve.h.tensor_scalar(Ve8[:, h8 * 65:(h8 + 1) * 65],
                                    V_s[:, tt * 520 + h8 * 65: tt * 520 + (h8 + 1) * 65],
                                    ucol, None, ALU.mult)
            p.wait(pe, dve, dve.n)
            ubks = []
            for h8 in (range(8) if "update" in K1_STAGES else []):
                hp, lane = h8 // 2, h8 % 2
                b0 = lane * 64
                bi9, bk = banks.get()
                last = pe.h.matmul(bk[0:64, 0:65],
                                   KT[:, tt * 512 + hp * 128 + b0: tt * 512 + hp * 128 + b0 + 64],
                                   Ve8[:, h8 * 65:(h8 + 1) * 65], start=True, stop=True)
                ubks.append((bi9, bk))
            if "update" in K1_STAGES:
                pe.inc(last)
            p.wait(dve, pe, pe.n)
            for h8 in (range(8) if "update" in K1_STAGES else []):
                bi9, bk = ubks[h8]
                S_h = Sst[:, h8 * 65:(h8 + 1) * 65]
                dve.h.scalar_tensor_tensor(S_h, S_h, lamc[:, h8:h8 + 1],
                                           bk[0:64, 0:65], ALU.mult, ALU.add)
                banks.consumed(bi9, dve)
            if fwd:
                p.wait(qs, dve, dve.n)
                qs.inc(qs.h.dma_start(out=outf_dram.ap()[tt * 128:(tt + 1) * 128, :],
                                      in_=outb[:, :]))
            else:
                p.wait(dve, qs, qs.n)
                dve.h.tensor_scalar_max(
                    tsc[:, 0:8],
                    outb[:, 0:520].rearrange("p (h v) -> p h v", v=65)[:, :, 64:65]
                        .rearrange("p h v -> p (h v)"), EPS)
                dve.h.reciprocal(tsc[:, 0:8], tsc[:, 0:8])
                for h8 in range(8):
                    dve.h.tensor_scalar(astage[:, h8 * 64:(h8 + 1) * 64],
                                        outb[:, h8 * 65: h8 * 65 + 64],
                                        tsc[:, h8:h8 + 1], None, ALU.mult)
                p.wait(qs, dve, dve.n)
                qs.inc(qs.h.dma_start(out=attn_d[tt * 128:(tt + 1) * 128, :],
                                      in_=astage[:, :]))


def build_k1():
    p = Prog()
    IO = {n: p.din(n, s) for n, s in [("x0f", [D, TFULL]), ("ln1w_c", [128, NDT]),
          ("ln1b_c", [128, NDT]), ("wq_my", [D, 512]), ("wk_my", [D, 512]),
          ("wv_my", [D, 512]), ("dec_my", [D, 8]), ("qb_c", [128, 4]),
          ("kb_c", [128, 4]), ("vb_r", [1, 512]), ("db_r", [1, 8]),
          ("u_incl", [128, 128]), ("u_rvex", [128, 128])]}
    IO["identb"] = p.nc.dram_tensor("identb", [128, 128], BF16, kind="ExternalInput").ap()
    IO["attn"] = p.nc.dram_tensor("attn", [TFULL, 512], BF16, kind="ExternalOutput").ap()
    IO["outf"] = p.dint("outf", [TFULL, 520])
    p.push(); emit_k1(p, IO, attn_bf16=True); p.pop()
    return p.finish()


def k1_inputs(inputs):
    c = np.ascontiguousarray
    col = lambda v: c(v.reshape(-1, 128).T)
    t = np.arange(128)
    u_incl = (t[:, None] <= t[None, :]).astype(np.float32)
    u_rvex = (t[:, None] > t[None, :]).astype(np.float32)
    import ml_dtypes
    identb = np.eye(128, dtype=ml_dtypes.bfloat16)
    maps = []
    for core in range(8):
        b, r = core // 2, core % 2
        hs = r * 512
        m = dict(
            x0f=c(np.asarray(inputs["L0_attn_x"])[b].T),
            ln1w_c=col(inputs["L0_attn_ln1_w"]), ln1b_c=col(inputs["L0_attn_ln1_b"]),
            wq_my=c(inputs["L0_attn_q_w"][:, hs:hs + 512]),
            wk_my=c(inputs["L0_attn_k_w"][:, hs:hs + 512]),
            wv_my=c(inputs["L0_attn_v_w"][:, hs:hs + 512]),
            dec_my=c(inputs["L0_attn_dec_w"][:, r * 8:(r + 1) * 8]),
            qb_c=c(inputs["L0_attn_q_b"][hs:hs + 512].reshape(4, 128).T),
            kb_c=c(inputs["L0_attn_k_b"][hs:hs + 512].reshape(4, 128).T),
            vb_r=c(inputs["L0_attn_v_b"][hs:hs + 512].reshape(1, 512)),
            db_r=c(inputs["L0_attn_dec_b"][r * 8:(r + 1) * 8].reshape(1, 8)),
            u_incl=u_incl, u_rvex=u_rvex, identb=identb)
        maps.append(m)
    return maps


GROUPS = [[0, 1], [2, 3], [4, 5], [6, 7]]


def _ag(p, in_h, out_h):
    g = p.nc.gpsimd
    g.wait_ge(p.qs.sem, p.qs.n)
    g.wait_ge(p.qg.sem, p.qg.n)
    ins = g.collective_compute(
        "AllGather", mybir.AluOpType.bypass,
        replica_groups=GROUPS,
        ins=[in_h.ap().opt()],
        outs=[out_h.ap().opt()],
    )
    ins.then_inc(p.cc.sem, 1)
    p.cc.n += 1


def build_fused(dbg=False):
    p = Prog()
    nc = p.nc
    if dbg:
        _dint = p.dint
        def dint_dbg(name, shape, dt=F32):
            if name in ("x1_i", "x2_i", "x3_i", "ctx2_pre", "ctx4_pre"):
                class H:
                    def __init__(self, ap): self._ap = ap
                    def ap(self): return self._ap
                return H(p.nc.dram_tensor(name, list(shape), dt, kind="ExternalOutput").ap())
            return _dint(name, shape, dt)
        p.dint = dint_dbg
    p.cc = Eng(p, nc.gpsimd, "cc")

    def dins(pairs):
        return {n: p.din(n, s) for n, s in pairs}

    IO1 = dins([("x0f", [D, TFULL]), ("ln1w_c", [128, NDT]), ("ln1b_c", [128, NDT]),
                ("wq_my", [D, 512]), ("wk_my", [D, 512]), ("wv_my", [D, 512]),
                ("dec_my", [D, 8]), ("qb_c", [128, 4]), ("kb_c", [128, 4]),
                ("vb_r", [1, 512]), ("db_r", [1, 8]), ("u_incl", [128, 128]),
                ("u_rvex", [128, 128])])
    IO1["identb"] = nc.dram_tensor("identb", [128, 128], BF16, kind="ExternalInput").ap()
    IO1["outf"] = p.dint("outf", [TFULL, 520])
    attn_b = p.dint("attn_b", [TFULL, 512], BF16)
    IO1["attn"] = attn_b.ap()

    IO2 = dins([("x0my", [D, HT]), ("ow", [D, D]), ("ob_c", [128, NDT]),
                ("ln2w_c", [128, NDT]), ("ln2b_c", [128, NDT]), ("f1w", [D, 4096]),
                ("f1b_c", [128, 32]), ("f2w", [4096, D]), ("f2b_c", [128, NDT]),
                ("l1s_c", [128, NDT]), ("l1h_c", [128, NDT])])
    IO2["x0"] = IO2.pop("x0my")
    x1_i = p.dint("x1_i", [D, HT])
    h1_i = p.dint("h1_i", [D, HT], BF16)
    IO2["x1"] = x1_i.ap()
    IO2["h1"] = h1_i.ap()

    IO3 = {"wq_my": p.din("wq3_my", [D, 512]), "wk_my": p.din("wk3_my", [D, 512]),
           "wv_my": p.din("wv3_my", [D, 512])}
    IO3["triu"] = nc.dram_tensor("triu", [128, 128], BF16, kind="ExternalInput").ap()
    ctx3_i = p.dint("ctx3_i", [512, TFULL], BF16)
    IO3["ctxo"] = ctx3_i.ap()

    IO4 = {"wo": p.din("wo4", [D, D]), "bo_c": p.din("bo4_c", [128, NDT]),
           "l2s_c": p.din("l2s_c", [128, NDT]), "l2h_c": p.din("l2h_c", [128, NDT]),
           "gate_w": p.din("gate_w", [D, 8]), "w1": p.din("w1", [8, D, 2048]),
           "w2": p.din("w2", [8, D, 2048]), "w3": p.din("w3", [8, 2048, D]),
           "ident": p.din("ident", [128, 128])}
    IO4["x1"] = x1_i.ap()
    x2_i = p.dint("x2_i", [D, HT])
    IO4["x2"] = x2_i.ap()

    IO5 = {"wq": p.din("wq5", [D, D]), "wk": p.din("wk5", [D, D]),
           "wv": p.din("wv5", [D, D]), "bq_c": p.din("bq5_c", [128, NDT]),
           "bk_c": p.din("bk5_c", [128, NDT]), "bv_r": p.din("bv5_r", [1, D]),
           "wp": p.din("wp5", [D, D]), "bp_c": p.din("bp5_c", [128, NDT])}
    IO5["xmy"] = x2_i.ap()
    k_b = p.dint("k_b", [D, HT], BF16)
    v_b = p.dint("v_b", [HT, D], BF16)
    kmax_b = p.dint("kmax_b", [1, 4])
    ag_k = p.dint("ag_k", [2 * D, HT], BF16)
    ag_v = p.dint("ag_v", [TFULL, D], BF16)
    ag_km = p.dint("ag_km", [2, 4])
    IO5["k_sc"] = k_b.ap()
    IO5["v_bounce"] = v_b.ap()
    IO5["kmax_b"] = kmax_b.ap()
    IO5["ag_k"] = ag_k.ap()
    IO5["ag_v"] = ag_v.ap()
    IO5["ag_kmax"] = ag_km.ap()
    IO5["q_sc"] = p.dint("q_sc", [D, HT], BF16).ap()
    x3_i = p.dint("x3_i", [D, HT])
    IO5["x3"] = x3_i.ap()

    def mid_cb(p):
        _ag(p, k_b, ag_k)
        _ag(p, v_b, ag_v)
        _ag(p, kmax_b, ag_km)
    IO5["mid_cb"] = mid_cb

    IO6 = dins([("wr_t", [D, 8]), ("br_r", [1, 8]), ("w1t", [8, D, 4096]),
                ("b1c", [8, 128, 32]), ("w2t", [8, 4096, D]), ("b2r", [8, 1, D]),
                ("lnw_c", [128, 8]), ("lnb_c", [128, 8])])
    IO6["ident"] = IO4["ident"]
    IO6["x"] = x3_i.ap()
    IO6["y"] = p.dout("y", [D, HT])

    myt0_d = p.din("my_t0", [1, 1], I32)
    ag1 = p.dint("ag1", [2 * TFULL, 512], BF16)
    ctx2_pre = p.dint("ctx2_pre", [2 * HT, 512], BF16)
    ag2 = p.dint("ag2", [2 * D, HT], BF16)
    ag3 = p.dint("ag3", [2 * 512, TFULL], BF16)
    ctx4_pre = p.dint("ctx4_pre", [D, HT], BF16)

    myt0_s = p.sb("myt0_s", [1, 1], I32)
    p.qg.inc(p.qg.h.dma_start(out=myt0_s[:, :], in_=myt0_d))

    p.push(); emit_k1(p, IO1, attn_bf16=True); p.pop()
    p.barrier()
    _ag(p, attn_b, ag1)
    g = nc.gpsimd
    g.wait_ge(p.cc.sem, p.cc.n)
    g.wait_ge(p.qg.sem, p.qg.n)
    off = g.value_load(myt0_s[0:1, 0:1])
    for r in range(2):
        p.qg.h.dma_start(out=ctx2_pre.ap()[r * HT:(r + 1) * HT, :],
                         in_=ag1.ap()[r * TFULL:(r + 1) * TFULL][bass.ds(off, HT), :])

    def loader2(p, c16):
        p.wait(p.qs, p.qg, p.qg.n)
        for r in range(2):
            for cs in range(4):
                dt = r * 4 + cs
                p.qs.h.dma_start(
                    out=c16[:, dt * HT:(dt + 1) * HT],
                    in_=ctx2_pre.ap()[r * HT:(r + 1) * HT, cs * 128:(cs + 1) * 128],
                    transpose=True)

    p.push(); emit_k2(p, IO2, ctx_loader=loader2); p.pop()
    p.barrier()
    _ag(p, h1_i, ag2)

    def loader3(p, hc, t0):
        p.wait(p.qs, p.cc)
        blk = t0 // HT
        c0 = t0 % HT
        p.qs.h.dma_start(out=sb3(hc, NDT, 512),
                         in_=fm3(ag2.ap()[blk * D:(blk + 1) * D, c0:c0 + 512]))

    p.push(); emit_k3(p, IO3, h1_loader=loader3, ctx_bf16=True); p.pop()
    p.barrier()
    _ag(p, ctx3_i, ag3)
    g.wait_ge(p.cc.sem, p.cc.n)
    off2 = g.value_load(myt0_s[0:1, 0:1])
    p.qg.h.dma_start(out=ctx4_pre.ap(), in_=ag3.ap()[:, bass.ds(off2, HT)])

    def loader4(p, c16):
        p.wait(p.qg, p.qg, p.qg.n)
        p.qg.h.dma_start(out=sb3(c16, NDT, HT), in_=fm3(ctx4_pre.ap()))

    p.push(); emit_k4(p, IO4, ctx_loader=loader4); p.pop()
    p.barrier()

    p.push(); emit_k5(p, IO5, fused=True); p.pop()
    p.barrier()

    p.push(); emit_k6(p, IO6); p.pop()
    if dbg:
        p.barrier()
        for nm, h, shape, dt in [("dbg_attn", attn_b, [TFULL, 512], BF16),
                                 ("dbg_h1", h1_i, [D, HT], BF16),
                                 ("dbg_ctx3", ctx3_i, [512, TFULL], BF16)]:
            dd = nc.dram_tensor(nm, shape, dt, kind="ExternalOutput").ap()
            p.qs.h.dma_start(out=dd, in_=h.ap())
    return p.finish()


def fused_inputs(inputs):
    import ml_dtypes
    c = np.ascontiguousarray
    col = lambda v: c(v.reshape(-1, 128).T)
    x0 = np.asarray(inputs["L0_attn_x"])
    t = np.arange(128)
    u_incl = (t[:, None] <= t[None, :]).astype(np.float32)
    u_rvex = (t[:, None] > t[None, :]).astype(np.float32)
    identb = np.eye(128, dtype=ml_dtypes.bfloat16)
    triu = np.triu(np.ones((128, 128))).astype(ml_dtypes.bfloat16)
    wqkv = inputs["L1_attn_w_qkv"]
    bqkv = inputs["L1_attn_b_qkv"]
    base = dict(
        ln1w_c=col(inputs["L0_attn_ln1_w"]), ln1b_c=col(inputs["L0_attn_ln1_b"]),
        u_incl=u_incl, u_rvex=u_rvex, identb=identb, triu=triu,
        ow=c(inputs["L0_attn_o_w"]), ob_c=col(inputs["L0_attn_o_b"]),
        ln2w_c=col(inputs["L0_attn_ln2_w"]), ln2b_c=col(inputs["L0_attn_ln2_b"]),
        f1w=c(inputs["L0_attn_f1_w"]), f1b_c=col(inputs["L0_attn_f1_b"]),
        f2w=c(inputs["L0_attn_f2_w"]), f2b_c=col(inputs["L0_attn_f2_b"]),
        l1s_c=col(inputs["L0_ffn_ln1_scale"]), l1h_c=col(inputs["L0_ffn_ln1_shift"]),
        wo4=c(inputs["L0_ffn_wo"]), bo4_c=col(inputs["L0_ffn_bo"]),
        l2s_c=col(inputs["L0_ffn_ln2_scale"]), l2h_c=col(inputs["L0_ffn_ln2_shift"]),
        gate_w=c(inputs["L0_ffn_gate_w"]),
        w1=c(inputs["L0_ffn_w1"]), w2=c(inputs["L0_ffn_w2"]), w3=c(inputs["L0_ffn_w3"]),
        ident=np.eye(128, dtype=np.float32),
        wq5=c(wqkv[:, 0:1024]), wk5=c(wqkv[:, 1024:2048]), wv5=c(wqkv[:, 2048:3072]),
        bq5_c=col(bqkv[0:1024]), bk5_c=col(bqkv[1024:2048]),
        bv5_r=c(bqkv[2048:3072].reshape(1, -1)),
        wp5=c(inputs["L1_attn_w_proj"]), bp5_c=col(inputs["L1_attn_b_proj"]),
        wr_t=c(inputs["L1_ffn_Wr"].T), br_r=c(inputs["L1_ffn_br"].reshape(1, 8)),
        w1t=c(inputs["L1_ffn_W1"].transpose(0, 2, 1)),
        b1c=c(inputs["L1_ffn_b1"].reshape(8, 32, 128).transpose(0, 2, 1)),
        w2t=c(inputs["L1_ffn_W2"].transpose(0, 2, 1)),
        b2r=c(inputs["L1_ffn_b2"].reshape(8, 1, 1024)),
        lnw_c=c(inputs["L1_ffn_ln_w"].reshape(8, 128).T),
        lnb_c=c(inputs["L1_ffn_ln_b"].reshape(8, 128).T))
    maps = []
    for core in range(8):
        b, r = core // 2, core % 2
        hs = r * 512
        m = dict(base)
        m["x0f"] = c(x0[b].T)
        m["x0my"] = c(x0[b, r * HT:(r + 1) * HT, :].T)
        m["wq_my"] = c(inputs["L0_attn_q_w"][:, hs:hs + 512])
        m["wk_my"] = c(inputs["L0_attn_k_w"][:, hs:hs + 512])
        m["wv_my"] = c(inputs["L0_attn_v_w"][:, hs:hs + 512])
        m["dec_my"] = c(inputs["L0_attn_dec_w"][:, r * 8:(r + 1) * 8])
        m["qb_c"] = c(inputs["L0_attn_q_b"][hs:hs + 512].reshape(4, 128).T)
        m["kb_c"] = c(inputs["L0_attn_k_b"][hs:hs + 512].reshape(4, 128).T)
        m["vb_r"] = c(inputs["L0_attn_v_b"][hs:hs + 512].reshape(1, 512))
        m["db_r"] = c(inputs["L0_attn_dec_b"][r * 8:(r + 1) * 8].reshape(1, 8))
        m["wq3_my"] = c(inputs["L0_ffn_wq"][:, hs:hs + 512])
        m["wk3_my"] = c(inputs["L0_ffn_wk"][:, hs:hs + 512])
        m["wv3_my"] = c(inputs["L0_ffn_wv"][:, hs:hs + 512])
        m["my_t0"] = np.array([[r * HT]], dtype=np.int32)
        maps.append(m)
    return maps


def kernel_fused(inputs, dbg=False, res_out=None):
    nc = build_fused(dbg=dbg)
    res = run_bass_kernel_spmd(nc, fused_inputs(inputs), core_ids=list(range(8)))
    if res_out is not None:
        res_out.append(res)
    out = np.zeros((4, TFULL, D), dtype=np.float32)
    for core in range(8):
        b, h = core // 2, core % 2
        out[b, h * HT:(h + 1) * HT, :] = res.results[core]["y"].T
    return out


def kernel(**inputs):
    inputs = {k: np.asarray(v) for k, v in inputs.items()}
    return kernel_fused(inputs)
